# version 04
# baseline (speedup 1.0000x reference)
import numpy as np
import ml_dtypes
from contextlib import ExitStack

import concourse.bass as bass
import concourse.mybir as mybir
from concourse.bass_utils import run_bass_kernel_spmd

F32 = mybir.dt.float32
BF16 = mybir.dt.bfloat16
U32 = mybir.dt.uint32
AF = mybir.ActivationFunctionType
ALU = mybir.AluOpType
AX = mybir.AxisListType

B = 2
TT = 8192
C = 1024
H = 16
KD = 64
CH = 128
TOK = 2048
NCH = TOK // CH
SLAB = 512
NSLAB = TOK // SLAB
HEAD_DIV = 8.0
NCORES = 8
EPS_GN = 1e-5
EPS_LN = 1e-5
NEXP = 8
EI = 512
SI = 1024


class Ctr:
    def __init__(self, ctx, nc, name):
        self.h = ctx.enter_context(nc.semaphore(name))
        self.n = 0
        self.dma = False


class Prog:
    ENGINES = ("tensor", "vector", "scalar", "gpsimd", "sync")

    def __init__(self, nc, ctx):
        self.nc = nc
        self.ctx = ctx
        self.q = {e: [] for e in self.ENGINES}

    def sem(self, name):
        return Ctr(self.ctx, self.nc, name)

    def wait(self, eng, ctr, val=None):
        v = ctr.n if (val is None or ctr.dma) else val
        if v > 0:
            self.q[eng].append(lambda e, c=ctr, v=v: e.wait_ge(c.h, v))

    MM_NOPS = 12

    def mm(self, psum, lhsT, rhs, start, stop, inc=None, tile_position=None):
        def fn(e, psum=psum, lhsT=lhsT, rhs=rhs, start=start, stop=stop, inc=inc,
               tp=tile_position):
            ins = e.matmul(psum, lhsT, rhs, start=start, stop=stop, tile_position=tp)
            if inc is not None:
                ins.then_inc(inc.h, 1)
            for _ in range(Prog.MM_NOPS):
                e.nop()
        self.q["tensor"].append(fn)
        if inc is not None:
            inc.n += 1

    def op(self, eng, fn, inc=None):
        def wrapped(e, fn=fn, inc=inc):
            ins = fn(e)
            if inc is not None:
                ins.then_inc(inc.h, 1)
        self.q[eng].append(wrapped)
        if inc is not None:
            inc.n += 1

    def dma(self, eng, ctr, out, in_, transpose=False, **kw):
        ctr.dma = True
        def fn(e, out=out, in_=in_, transpose=transpose, kw=kw):
            e.dma_start(out, in_, transpose=transpose, **kw).then_inc(ctr.h, 16)
        self.q[eng].append(fn)
        ctr.n += 16

    def build_block(self):
        block = self.ctx.enter_context(self.nc.Block())
        q = self.q

        @block.tensor
        def _(e):
            for fn in q["tensor"]:
                fn(e)

        @block.vector
        def _(e):
            for fn in q["vector"]:
                fn(e)

        @block.scalar
        def _(e):
            for fn in q["scalar"]:
                fn(e)

        @block.gpsimd
        def _(e):
            for fn in q["gpsimd"]:
                fn(e)

        @block.sync
        def _(e):
            for fn in q["sync"]:
                fn(e)


def _head(h):
    return slice((h % 2) * KD, (h % 2) * KD + KD), h // 2



def build_l1(debug=False):
    nc = bass.Bass("TRN2", target_bir_lowering=False)
    xp = nc.dram_tensor("xp", [TOK + 1, C], F32, kind="ExternalInput").ap()
    w_in = {nm: nc.dram_tensor(nm, [C, C], F32, kind="ExternalInput").ap()
            for nm in ("wr", "wk", "wv", "wg")}
    tm_in = nc.dram_tensor("tm", [4, C], F32, kind="ExternalInput").ap()
    td_in = nc.dram_tensor("td", [1, H], F32, kind="ExternalInput").ap()
    fa_in = nc.dram_tensor("fa", [1, H], F32, kind="ExternalInput").ap()

    o_sc = nc.dram_tensor("o_sc", [128, 512], F32, kind="ExternalOutput").ap()
    o_xo = nc.dram_tensor("o_xo", [TOK, C], BF16, kind="ExternalOutput").ap()
    o_rwb = nc.dram_tensor("o_rwb", [128, 8 * TOK], BF16, kind="ExternalOutput").ap()
    o_g = nc.dram_tensor("o_g", [TOK, C], BF16, kind="ExternalOutput").ap()

    x_bf = nc.dram_tensor("x_bf", [TOK + 1, C], BF16, kind="Internal").ap()
    w_bf = {nm: nc.dram_tensor(nm + "_bf", [C, C], BF16, kind="Internal").ap()
            for nm in ("wr", "wk", "wv", "wg")}
    kc_dram = nc.dram_tensor("kc_dram", [C, TOK], BF16, kind="Internal").ap()
    sc1 = nc.dram_tensor("sc1", [1, 64], F32, kind="Internal").ap()

    dbg = {}
    if debug:
        for nm, shp, dt_ in [("d_wmat", [128, H * CH], F32), ("d_wb", [128, 8 * CH], BF16),
                             ("d_wk", [128, H], F32), ("d_ws", [128, 512], F32),
                             ("d_ldbc", [128, H], F32),
                             ("d_mixr", [128, 8 * SLAB], BF16), ("d_mixk", [128, 8 * SLAB], BF16),
                             ("d_rsl", [128, 8 * SLAB], BF16), ("d_kcs", [128, 8 * SLAB], BF16),
                             ("d_kts", [128, 4 * C], BF16), ("d_vts", [128, 4 * C], BF16),
                             ("d_xc", [128, 8 * SLAB], BF16), ("d_xxc", [128, 8 * SLAB], BF16)]:
            dbg[nm] = nc.dram_tensor(nm, shp, dt_, kind="ExternalOutput").ap()

    with ExitStack() as ctx:
        p = Prog(nc, ctx)
        sb = lambda n, s, d=F32: ctx.enter_context(nc.sbuf_tensor(n, s, d))
        psl = lambda n, s: ctx.enter_context(nc.psum_tensor(n, s, F32))

        wT = {nm: sb(nm + "T", [128, 8 * C], BF16) for nm in ("wr", "wk", "wv", "wg")}
        xc = sb("xc", [128, 8 * SLAB], BF16)
        xxc = sb("xxc", [128, 8 * SLAB], BF16)
        mixed = {nm: sb("mx_" + nm, [128, 8 * SLAB], BF16) for nm in ("r", "k", "v", "g")}
        r_sl = sb("r_sl", [128, 8 * SLAB], BF16)
        rwb_sl = sb("rwb_sl", [128, 8 * SLAB], BF16)
        kcs = sb("kcs", [128, 8 * SLAB], BF16)
        kts = sb("kts", [128, 4 * C], BF16)
        kwk = sb("kwk", [128, C], BF16)
        vts = sb("vts", [128, 4 * C], BF16)
        g_sl = sb("g_sl", [128, 4 * C], BF16)
        xo_sl = sb("xo_sl", [128, 4 * C], BF16)
        attw = sb("attw", [128, 2 * CH], BF16)
        s_sb = sb("s_sb", [128, 512], F32)
        s_tmp = sb("s_tmp", [128, 512], F32)
        s_bf = sb("s_bf", [128, 512], BF16)

        tm_cols = sb("tm_cols", [128, 32], F32)
        td_row = sb("td_row", [1, H], F32)
        ld_row = sb("ld_row", [1, H], F32)
        ws_row = sb("ws_row", [1, H], F32)
        fa_row = sb("fa_row", [1, H], F32)
        ld_bc = sb("ld_bc", [128, H], F32)
        u_bc = sb("u_bc", [128, H], F32)
        ws_bc = sb("ws_bc", [128, H], F32)
        ldcol8 = sb("ldcol8", [128, 8], F32)
        wmatT = sb("wmatT", [128, H * CH], F32)
        wb_ct = sb("wb_ct", [128, 8 * CH], BF16)
        wk_cols = sb("wk_cols", [128, H], F32)
        ws_tile = sb("ws_tile", [128, 512], F32)
        iota_d = sb("iota_d", [128, CH], F32)
        iota_f = sb("iota_f", [128, CH], F32)
        clampd = sb("clampd", [128, CH], F32)
        rev_col = sb("rev_col", [128, 1], F32)
        ones128 = sb("ones128", [128, CH], F32)
        eye = sb("eye", [128, CH], F32)

        ps_proj = [psl("ps_proj%d" % i, [128, SLAB]) for i in range(2)]
        ps_att = [psl("ps_att%d" % i, [128, CH]) for i in range(2)]
        ps_out = psl("ps_out", [128, C])
        ps_A = psl("ps_A", [128, 512])

        s_cast = p.sem("s_cast")
        s_ld = p.sem("s_ld")
        s_xld = p.sem("s_xld")
        s_tr = p.sem("s_tr")
        s_kd = p.sem("s_kd")
        s_kc = p.sem("s_kc")
        s_mm = p.sem("s_mm")
        s_v = p.sem("s_v")
        s_a = p.sem("s_a")
        s_g = p.sem("s_g")
        s_o = p.sem("s_o")

        p.dma("gpsimd", s_cast, x_bf, xp)
        cast_x = s_cast.n
        for nm in ("wr", "wk", "wv", "wg"):
            p.dma("gpsimd", s_cast, w_bf[nm], w_in[nm])
        cast_w = s_cast.n

        p.dma("sync", s_ld, td_row[0:1, :], td_in)
        p.dma("sync", s_ld, fa_row[0:1, :], fa_in)
        p.dma("sync", s_ld, tm_cols[:, :],
              tm_in.rearrange("m (p c) -> c (m p)", c=128),
              allow_slow_non_contiguous=True)
        ld0 = s_ld.n

        p.wait("scalar", s_ld, ld0)
        p.op("scalar", lambda e: e.activation(ld_row[0:1, :], td_row[0:1, :], AF.Exp), s_a)
        a_exp = s_a.n
        p.wait("vector", s_a, a_exp)
        p.op("vector", lambda e: e.tensor_scalar_mul(ld_row[0:1, :], ld_row[0:1, :], -1.0), s_v)
        v_ld = s_v.n
        p.wait("scalar", s_v, v_ld)
        p.op("scalar", lambda e: e.activation(ws_row[0:1, :], ld_row[0:1, :], AF.Exp,
                                              scale=float(CH)), s_a)
        a_ws = s_a.n
        p.wait("sync", s_v, v_ld)
        p.dma("sync", s_kd, sc1[0:1, 0:H], ld_row[0:1, :])
        p.wait("sync", s_a, a_ws)
        p.dma("sync", s_kd, sc1[0:1, 16:16 + H], ws_row[0:1, :])
        p.wait("sync", s_ld, ld0)
        p.dma("sync", s_kd, sc1[0:1, 32:32 + H], fa_row[0:1, :])
        kd0 = s_kd.n

        p.wait("gpsimd", s_kd, kd0)
        p.dma("gpsimd", s_cast, ld_bc[:, :], sc1[0:1, 0:H].partition_broadcast(128))
        p.dma("gpsimd", s_cast, ws_bc[:, :], sc1[0:1, 16:16 + H].partition_broadcast(128))
        p.dma("gpsimd", s_cast, u_bc[:, :], sc1[0:1, 32:32 + H].partition_broadcast(128))
        for pb in range(8):
            p.dma("gpsimd", s_cast, ldcol8[0:64, pb:pb + 1],
                  sc1[0:1, 2 * pb:2 * pb + 1].partition_broadcast(64))
            p.dma("gpsimd", s_cast, ldcol8[64:128, pb:pb + 1],
                  sc1[0:1, 2 * pb + 1:2 * pb + 2].partition_broadcast(64))
        cast_bc = s_cast.n

        p.op("gpsimd", lambda e: e.iota(iota_d[:, :], pattern=[[1, CH]], base=-1,
                                        channel_multiplier=-1,
                                        allow_small_or_imprecise_dtypes=True), s_g)
        p.op("gpsimd", lambda e: e.iota(iota_f[:, :], pattern=[[1, CH]], base=0,
                                        channel_multiplier=0,
                                        allow_small_or_imprecise_dtypes=True), s_g)
        p.op("gpsimd", lambda e: e.iota(rev_col[:, 0:1], pattern=[[0, 1]], base=127,
                                        channel_multiplier=-1,
                                        allow_small_or_imprecise_dtypes=True), s_g)
        p.op("gpsimd", lambda e: e.memset(ones128[:, :], 1.0), s_g)
        g_iota = s_g.n
        p.wait("gpsimd", s_g, g_iota)
        p.op("gpsimd", lambda e: e.affine_select(eye[:, :], ones128[:, :],
                                                 pattern=[[1, CH]], compare_op=ALU.is_equal,
                                                 fill=0.0, base=0, channel_multiplier=-1), s_g)
        g_eye = s_g.n

        p.wait("vector", s_g, g_iota)
        p.op("vector", lambda e: e.tensor_scalar_max(clampd[:, :], iota_d[:, :], 0.0), s_v)
        v_clamp = s_v.n

        p.wait("scalar", s_v, v_clamp)
        p.wait("scalar", s_cast, cast_bc)
        for h in range(H):
            sl = slice(h * CH, (h + 1) * CH)
            p.op("scalar", lambda e, sl=sl, h=h: e.activation(
                wmatT[:, sl], clampd[:, :], AF.Exp, scale=ld_bc[:, h:h + 1]), s_a)
        a_wpow = s_a.n
        p.wait("gpsimd", s_a, a_wpow)
        for h in range(H):
            sl = slice(h * CH, (h + 1) * CH)
            p.op("gpsimd", lambda e, sl=sl: e.affine_select(
                wmatT[:, sl], wmatT[:, sl], pattern=[[1, CH]], compare_op=ALU.is_ge,
                fill=0.0, base=-1, channel_multiplier=-1), s_g)
        g_wmask = s_g.n
        p.wait("vector", s_g, g_wmask)
        p.wait("vector", s_cast, cast_bc)
        for h in range(H):
            sl = slice(h * CH, (h + 1) * CH)
            p.op("vector", lambda e, sl=sl, h=h: e.scalar_tensor_tensor(
                wmatT[:, sl], eye[:, :], u_bc[:, h:h + 1], wmatT[:, sl], ALU.mult, ALU.add), s_v)
        v_wmat = s_v.n

        p.wait("scalar", s_cast, cast_bc)
        p.wait("scalar", s_g, g_iota)
        for pb in range(8):
            sl = slice(pb * CH, (pb + 1) * CH)
            p.op("scalar", lambda e, sl=sl, pb=pb: e.activation(
                wb_ct[:, sl], iota_f[:, :], AF.Exp, scale=ldcol8[:, pb:pb + 1]), s_a)
        a_wb = s_a.n

        for h in range(H):
            p.op("scalar", lambda e, h=h: e.activation(
                wk_cols[:, h:h + 1], rev_col[:, 0:1], AF.Exp, scale=ld_bc[:, h:h + 1]), s_a)
        a_wk = s_a.n

        p.wait("vector", s_cast, cast_bc)
        for hp in range(8):
            for half in range(2):
                rows = slice(half * 64, half * 64 + 64)
                colh = 2 * hp + half
                p.op("vector", lambda e, rows=rows, hp=hp, colh=colh: e.tensor_scalar_mul(
                    ws_tile[rows, hp * 64:(hp + 1) * 64], ones128[0:64, 0:64],
                    ws_bc[rows, colh:colh + 1]), s_v)
        p.op("vector", lambda e: e.memset(s_sb[:, :], 0.0), s_v)
        p.op("vector", lambda e: e.memset(s_bf[:, :], 0.0), s_v)
        v_consts = s_v.n

        if debug:
            p.wait("sync", s_v, s_v.n)
            p.wait("sync", s_a, s_a.n)
            p.wait("sync", s_g, s_g.n)
            p.dma("sync", s_o, dbg["d_wmat"], wmatT[:, :])
            p.dma("sync", s_o, dbg["d_wb"], wb_ct[:, :])
            p.dma("sync", s_o, dbg["d_wk"], wk_cols[:, :])
            p.dma("sync", s_o, dbg["d_ws"], ws_tile[:, :])
            p.dma("sync", s_o, dbg["d_ldbc"], ld_bc[:, :])

        p.wait("scalar", s_cast, cast_w)
        for nm in ("wr", "wk", "wv", "wg"):
            for ib in range(8):
                p.dma("scalar", s_tr, wT[nm][:, ib * C:(ib + 1) * C],
                      w_bf[nm][0:C, ib * 128:(ib + 1) * 128], transpose=True)
        tr_w = s_tr.n

        for s in range(NSLAB):
            t0 = s * SLAB
            p.wait("scalar", s_o, s_o.n)
            p.wait("vector", s_o, s_o.n)
            prev_mix_v = s_v.n
            p.wait("scalar", s_cast, cast_x)
            p.wait("scalar", s_v, prev_mix_v)
            for pb in range(8):
                cs = slice(pb * 128, (pb + 1) * 128)
                p.dma("scalar", s_xld, xc[:, pb * SLAB:(pb + 1) * SLAB],
                      x_bf[1 + t0:1 + t0 + SLAB, cs], transpose=True)
                p.dma("scalar", s_xld, xxc[:, pb * SLAB:(pb + 1) * SLAB],
                      x_bf[t0:t0 + SLAB, cs], transpose=True)
            xld = s_xld.n

            p.wait("vector", s_xld, xld)
            p.wait("vector", s_mm, s_mm.n)
            for mi, nm in enumerate(("r", "k", "v", "g")):
                for pb in range(8):
                    sl = slice(pb * SLAB, (pb + 1) * SLAB)
                    p.op("vector", lambda e, sl=sl, nm=nm: e.tensor_sub(
                        mixed[nm][:, sl], xc[:, sl], xxc[:, sl]), s_v)
            v_sub = s_v.n
            p.wait("vector", s_v, v_sub)
            for mi, nm in enumerate(("r", "k", "v", "g")):
                for pb in range(8):
                    sl = slice(pb * SLAB, (pb + 1) * SLAB)
                    tcol = tm_cols[:, mi * 8 + pb:mi * 8 + pb + 1]
                    p.op("vector", lambda e, sl=sl, tcol=tcol, nm=nm: e.scalar_tensor_tensor(
                        mixed[nm][:, sl], mixed[nm][:, sl], tcol, xxc[:, sl],
                        ALU.mult, ALU.add), s_v)
            v_mix = s_v.n

            def cmaj(nm_w, nm_x, dst):
                for ob in range(8):
                    bank = ps_proj[ob % 2]
                    p.wait("tensor", s_a, s_a.n)
                    p.wait("tensor", s_v, v_mix)
                    p.wait("tensor", s_tr, tr_w)
                    for ib in range(8):
                        p.mm(bank[:, :],
                             wT[nm_w][:, ib * C + ob * 128: ib * C + (ob + 1) * 128],
                             mixed[nm_x][:, ib * SLAB:(ib + 1) * SLAB],
                             start=(ib == 0), stop=(ib == 7),
                             inc=(s_mm if ib == 7 else None))
                    p.wait("scalar", s_mm, s_mm.n)
                    sl = slice(ob * SLAB, (ob + 1) * SLAB)
                    p.op("scalar", lambda e, bank=bank, dst=dst, sl=sl:
                         e.copy(dst[:, sl], bank[:, :]), s_a)

            def tmaj(nm_w, nm_x, dst, act=None):
                for tb in range(4):
                    for oh in range(2):
                        bank = ps_proj[(tb * 2 + oh) % 2]
                        p.wait("tensor", s_a, s_a.n)
                        p.wait("tensor", s_v, v_mix)
                        p.wait("tensor", s_tr, tr_w)
                        for ib in range(8):
                            p.mm(bank[:, :],
                                 mixed[nm_x][:, ib * SLAB + tb * 128: ib * SLAB + (tb + 1) * 128],
                                 wT[nm_w][:, ib * C + oh * 512: ib * C + (oh + 1) * 512],
                                 start=(ib == 0), stop=(ib == 7),
                                 inc=(s_mm if ib == 7 else None))
                        p.wait("scalar", s_mm, s_mm.n)
                        sl = slice(tb * C + oh * 512, tb * C + (oh + 1) * 512)
                        if act is None:
                            p.op("scalar", lambda e, bank=bank, sl=sl, dst=dst:
                                 e.copy(dst[:, sl], bank[:, :]), s_a)
                        else:
                            p.op("scalar", lambda e, bank=bank, sl=sl, dst=dst, act=act:
                                 e.activation(dst[:, sl], bank[:, :], act), s_a)

            cmaj("wr", "r", r_sl)
            cmaj("wk", "k", kcs)
            a_rk = s_a.n
            tmaj("wv", "v", vts)
            tmaj("wg", "g", g_sl, act=AF.Silu)
            a_proj = s_a.n

            p.wait("vector", s_a, a_rk)
            p.wait("vector", s_a, a_wb)
            for pb in range(8):
                for cnn in range(4):
                    sl = slice(pb * SLAB + cnn * CH, pb * SLAB + (cnn + 1) * CH)
                    wsl = slice(pb * CH, (pb + 1) * CH)
                    p.op("vector", lambda e, sl=sl, wsl=wsl: e.tensor_mul(
                        rwb_sl[:, sl], r_sl[:, sl], wb_ct[:, wsl]), s_v)
            v_rwb = s_v.n

            p.wait("sync", s_a, a_rk)
            p.dma("sync", s_kc,
                  kc_dram[0:C, t0:t0 + SLAB].rearrange("(p a) t -> a p t", a=128),
                  kcs.rearrange("a (p t) -> a p t", t=SLAB))
            kwrt = s_kc.n
            p.wait("scalar", s_kc, kwrt)
            p.wait("scalar", s_mm, s_mm.n)
            for tb in range(4):
                p.dma("scalar", s_tr, kts[:, tb * C:(tb + 1) * C],
                      kc_dram[0:C, t0 + tb * 128:t0 + (tb + 1) * 128], transpose=True)
            tr_k = s_tr.n

            if debug and s == 0:
                p.wait("sync", s_v, s_v.n)
                p.wait("sync", s_a, s_a.n)
                p.wait("sync", s_tr, s_tr.n)
                p.dma("sync", s_o, dbg["d_xc"], xc[:, :])
                p.dma("sync", s_o, dbg["d_xxc"], xxc[:, :])
                p.dma("sync", s_o, dbg["d_mixr"], mixed["r"][:, :])
                p.dma("sync", s_o, dbg["d_mixk"], mixed["k"][:, :])
                p.dma("sync", s_o, dbg["d_rsl"], r_sl[:, :])
                p.dma("sync", s_o, dbg["d_kcs"], kcs[:, :])
                p.dma("sync", s_o, dbg["d_kts"], kts[:, :])
                p.dma("sync", s_o, dbg["d_vts"], vts[:, :])
                p.wait("vector", s_o, s_o.n)
                p.wait("tensor", s_o, s_o.n)
                p.wait("sync", s_o, s_o.n)
                p.wait("scalar", s_o, s_o.n)
                p.wait("gpsimd", s_o, s_o.n)

            for cnn in range(4):
                p.wait("vector", s_tr, tr_k)
                p.wait("vector", s_a, a_wk)
                p.wait("vector", s_mm, s_mm.n)
                for h in range(H):
                    p.op("vector", lambda e, h=h, cnn=cnn: e.tensor_scalar_mul(
                        kwk[:, h * KD:(h + 1) * KD],
                        kts[:, cnn * C + h * KD:cnn * C + (h + 1) * KD],
                        wk_cols[:, h:h + 1]), s_v)
                v_kwk = s_v.n

                attw_v = {}
                for h in range(H):
                    rows, pb = _head(h)
                    rsl = slice(pb * SLAB + cnn * CH, pb * SLAB + (cnn + 1) * CH)
                    bank = ps_att[h % 2]
                    if h >= 2:
                        p.wait("tensor", s_v, attw_v[h - 2])
                    p.wait("tensor", s_a, a_rk)
                    p.wait("tensor", s_v, v_rwb)
                    p.mm(bank[:, :], kcs[rows, rsl], r_sl[rows, rsl],
                         start=True, stop=True, inc=s_mm)
                    att_mm = s_mm.n
                    p.wait("vector", s_mm, att_mm)
                    p.wait("vector", s_v, v_wmat)
                    half = h % 2
                    p.op("vector", lambda e, bank=bank, h=h, half=half: e.tensor_mul(
                        attw[:, half * CH:(half + 1) * CH], bank[:, :],
                        wmatT[:, h * CH:(h + 1) * CH]), s_v)
                    attw_v[h] = s_v.n

                    p.wait("tensor", s_v, attw_v[h])
                    vsl = slice(cnn * C + h * KD, cnn * C + (h + 1) * KD)
                    p.mm(ps_out[:, h * KD:(h + 1) * KD],
                         attw[:, half * CH:(half + 1) * CH], vts[:, vsl],
                         start=True, stop=False)
                    srows = slice((h % 2) * KD, (h % 2) * KD + KD)
                    scols = slice((h // 2) * KD, (h // 2 + 1) * KD)
                    p.mm(ps_out[:, h * KD:(h + 1) * KD],
                         rwb_sl[rows, rsl], s_bf[srows, scols],
                         start=False, stop=True, inc=s_mm)
                    p.wait("tensor", s_v, v_kwk)
                    p.mm(ps_A[srows, scols], kwk[:, h * KD:(h + 1) * KD], vts[:, vsl],
                         start=True, stop=True, inc=s_mm,
                         tile_position=(0, (h % 2) * KD))
                mm_chunk = s_mm.n

                p.wait("vector", s_mm, mm_chunk)
                p.op("vector", lambda e: e.tensor_mul(s_tmp[:, :], s_sb[:, :], ws_tile[:, :]), s_v)
                p.wait("vector", s_v, s_v.n)
                p.op("vector", lambda e: e.tensor_add(s_sb[:, :], s_tmp[:, :], ps_A[:, :]), s_v)
                v_supd = s_v.n
                p.wait("vector", s_v, v_supd)
                p.op("vector", lambda e: e.tensor_copy(s_bf[:, :], s_sb[:, :]), s_v)
                v_sbf = s_v.n

                p.wait("scalar", s_mm, mm_chunk)
                p.op("scalar", lambda e, cnn=cnn: e.mul(
                    xo_sl[:, cnn * C:(cnn + 1) * C], ps_out[:, :], 1.0 / HEAD_DIV), s_a)
                a_xo = s_a.n
                p.wait("tensor", s_a, a_xo)
                p.wait("tensor", s_v, v_sbf)

            a_slab = s_a.n
            v_slab = s_v.n
            p.wait("sync", s_a, a_slab)
            p.wait("sync", s_v, v_slab)
            p.dma("sync", s_o,
                  o_xo[t0:t0 + SLAB, :].rearrange("(tb p) c -> p tb c", p=128),
                  xo_sl.rearrange("a (tb c) -> a tb c", c=C))
            p.dma("sync", s_o,
                  o_g[t0:t0 + SLAB, :].rearrange("(tb p) c -> p tb c", p=128),
                  g_sl.rearrange("a (tb c) -> a tb c", c=C))
            p.dma("sync", s_o,
                  o_rwb.rearrange("a (p t) -> a p t", t=TOK)[:, :, t0:t0 + SLAB],
                  rwb_sl.rearrange("a (p t) -> a p t", t=SLAB))

        p.wait("sync", s_v, s_v.n)
        p.dma("sync", s_o, o_sc, s_sb[:, :])
        p.wait("sync", s_o, s_o.n)
        p.build_block()
    return nc



def build_l2():
    nc = bass.Bass("TRN2", target_bir_lowering=False)
    x_in = nc.dram_tensor("x", [TOK, C], F32, kind="ExternalInput").ap()
    xo_in = nc.dram_tensor("xo", [TOK, C], BF16, kind="ExternalInput").ap()
    rwb_in = nc.dram_tensor("rwb", [128, 8 * TOK], BF16, kind="ExternalInput").ap()
    g_in = nc.dram_tensor("g", [TOK, C], BF16, kind="ExternalInput").ap()
    sall_in = nc.dram_tensor("sall", [512, 512], F32, kind="ExternalInput").ap()
    coefk_in = nc.dram_tensor("coefk", [1, 4], F32, kind="ExternalInput").ap()
    mask_in = nc.dram_tensor("maskk", [1, 4], F32, kind="ExternalInput").ap()
    wo_in = nc.dram_tensor("wo", [C, C], F32, kind="ExternalInput").ap()
    td_in = nc.dram_tensor("td", [1, H], F32, kind="ExternalInput").ap()
    lnw_in = nc.dram_tensor("lnw", [1, C], F32, kind="ExternalInput").ap()
    lnb_in = nc.dram_tensor("lnb", [1, C], F32, kind="ExternalInput").ap()
    gw_in = nc.dram_tensor("gw", [16, C], F32, kind="ExternalInput").ap()
    lb_in = nc.dram_tensor("lb", [1, 8], F32, kind="ExternalInput").ap()

    o_xmid = nc.dram_tensor("o_xmid", [TOK, C], F32, kind="ExternalOutput").ap()
    o_m = nc.dram_tensor("o_m", [TOK, 16], F32, kind="ExternalOutput").ap()
    o_stats = nc.dram_tensor("o_stats", [1, 16], F32, kind="ExternalOutput").ap()

    wo_bf = nc.dram_tensor("wo_bf", [C, C], BF16, kind="Internal").ap()
    gw_bf = nc.dram_tensor("gw_bf", [16, C], BF16, kind="Internal").ap()
    yg_dram = nc.dram_tensor("yg_dram", [TOK, C], BF16, kind="Internal").ap()
    xmb_dram = nc.dram_tensor("xmb_dram", [TOK, C], BF16, kind="Internal").ap()
    sc2 = nc.dram_tensor("sc2", [1, 128], F32, kind="Internal").ap()

    with ExitStack() as ctx:
        p = Prog(nc, ctx)
        sb = lambda n, s, d=F32: ctx.enter_context(nc.sbuf_tensor(n, s, d))
        psl = lambda n, s: ctx.enter_context(nc.psum_tensor(n, s, F32))

        rwb = sb("rwb_s", [128, 8 * TOK], BF16)
        xo_res = sb("xo_res", [128, NCH * C], BF16)
        g_res = sb("g_res", [128, NCH * C], BF16)
        woT = sb("woT", [128, 8 * C], BF16)
        s_tiles = [sb("s_t%d" % i, [128, 512]) for i in range(4)]
        s_e = sb("s_e", [128, 512])
        s_ebf = sb("s_ebf", [128, 512], BF16)
        ws_tile = sb("ws_tile", [128, 512], F32)
        y_t = sb("y_t", [128, C])
        sq_t = sb("sq_t", [128, C])
        gn16 = sb("gn16", [128, 16])
        gn16b = sb("gn16b", [128, 16])
        rstd16 = sb("rstd16", [128, 16])
        yg_sl = sb("yg_sl", [128, 4 * C], BF16)
        ygc = sb("ygc", [128, 8 * SLAB], BF16)
        x_sl = sb("x_sl", [128, 4 * C], F32)
        xm_t = sb("xm_t", [128, C], F32)
        xmb_sl = sb("xmb_sl", [128, 4 * C], BF16)
        col1 = sb("col1", [128, 1])
        col2 = sb("col2", [128, 1])
        lnw_bc = sb("lnw_bc", [128, C], F32)
        lnb_bc = sb("lnb_bc", [128, C], F32)
        td_row = sb("td_row", [1, H], F32)
        ld_row = sb("ld_row", [1, H], F32)
        ws_row = sb("ws_row", [1, H], F32)
        ws_bc = sb("ws_bc", [128, H], F32)
        coefk_sb = sb("coefk_sb", [1, 4], F32)
        mask_sb = sb("mask_sb", [1, 4], F32)
        coefrow = sb("coefrow", [1, 4 * H], F32)
        coef_bc = sb("coef_bc", [128, 4 * H], F32)
        ones64 = sb("ones64", [64, 64], F32)
        gwT = sb("gwT", [128, 8 * 16], BF16)
        xmc_t = sb("xmc_t", [128, 8 * 128], BF16)
        iota8 = sb("iota8", [128, 8], F32)
        lb_bc = sb("lb_bc", [128, 8], F32)
        lg = sb("lg", [128, 8], F32)
        scores = sb("scores", [128, 8], F32)
        mx8 = sb("mx8", [128, 8], F32)
        mi8 = sb("mi8", [128, 8], U32)
        mif = sb("mif", [128, 8], F32)
        eq1 = sb("eq1", [128, 8], F32)
        eq2 = sb("eq2", [128, 8], F32)
        mw = sb("mw", [128, 16], F32)
        ones_col = sb("ones_col", [128, 1], F32)
        den = sb("den", [128, 1], F32)
        w1c = sb("w1c", [128, 1], F32)
        w2c = sb("w2c", [128, 1], F32)
        st_sb = sb("st_sb", [1, 16], F32)

        ps_corr = psl("ps_corr", [128, C])
        ps_proj = [psl("ps_proj%d" % i, [128, SLAB]) for i in range(2)]
        ps_g = psl("ps_g", [128, 8])
        ps_st = psl("ps_st", [1, 16])

        s_cast = p.sem("s_cast")
        s_ld = p.sem("s_ld")
        s_tr = p.sem("s_tr")
        s_kd = p.sem("s_kd")
        s_mm = p.sem("s_mm")
        s_v = p.sem("s_v")
        s_a = p.sem("s_a")
        s_g = p.sem("s_g")
        s_o = p.sem("s_o")

        p.dma("gpsimd", s_cast, wo_bf, wo_in)
        p.dma("gpsimd", s_cast, gw_bf, gw_in)
        cast_w = s_cast.n
        p.dma("gpsimd", s_cast, lnw_bc[:, :], lnw_in[0:1, :].partition_broadcast(128))
        p.dma("gpsimd", s_cast, lnb_bc[:, :], lnb_in[0:1, :].partition_broadcast(128))
        p.dma("gpsimd", s_cast, lb_bc[:, :], lb_in[0:1, :].partition_broadcast(128))
        cast_ln = s_cast.n

        p.dma("sync", s_ld, rwb[:, :], rwb_in)
        p.dma("sync", s_ld,
              xo_res.rearrange("a (tb c) -> a tb c", c=C),
              xo_in.rearrange("(tb p) c -> p tb c", p=128))
        p.dma("sync", s_ld,
              g_res.rearrange("a (tb c) -> a tb c", c=C),
              g_in.rearrange("(tb p) c -> p tb c", p=128))
        for i in range(4):
            p.dma("sync", s_ld, s_tiles[i][:, :], sall_in[i * 128:(i + 1) * 128, :])
        p.dma("sync", s_ld, td_row[0:1, :], td_in)
        p.dma("sync", s_ld, coefk_sb[0:1, :], coefk_in)
        p.dma("sync", s_ld, mask_sb[0:1, :], mask_in)
        ld0 = s_ld.n

        p.wait("scalar", s_ld, ld0)
        p.op("scalar", lambda e: e.activation(ld_row[0:1, :], td_row[0:1, :], AF.Exp), s_a)
        p.wait("vector", s_a, s_a.n)
        p.op("vector", lambda e: e.tensor_scalar_mul(ld_row[0:1, :], ld_row[0:1, :], -1.0), s_v)
        v_ld = s_v.n
        p.wait("scalar", s_v, v_ld)
        p.op("scalar", lambda e: e.activation(ws_row[0:1, :], ld_row[0:1, :], AF.Exp,
                                              scale=float(CH)), s_a)
        for qq in range(4):
            p.op("scalar", lambda e, qq=qq: e.activation(
                coefrow[0:1, qq * H:(qq + 1) * H], ld_row[0:1, :], AF.Exp,
                scale=coefk_sb[0:1, qq:qq + 1]), s_a)
        a_coef = s_a.n
        p.wait("vector", s_a, a_coef)
        for qq in range(4):
            p.op("vector", lambda e, qq=qq: e.tensor_scalar_mul(
                coefrow[0:1, qq * H:(qq + 1) * H], coefrow[0:1, qq * H:(qq + 1) * H],
                mask_sb[0:1, qq:qq + 1]), s_v)
        v_coef = s_v.n
        p.wait("sync", s_a, a_coef)
        p.dma("sync", s_kd, sc2[0:1, 0:H], ws_row[0:1, :])
        p.wait("sync", s_v, v_coef)
        p.dma("sync", s_kd, sc2[0:1, 16:16 + 4 * H], coefrow[0:1, :])
        kd0 = s_kd.n
        p.wait("gpsimd", s_kd, kd0)
        p.dma("gpsimd", s_cast, ws_bc[:, :], sc2[0:1, 0:H].partition_broadcast(128))
        p.dma("gpsimd", s_cast, coef_bc[:, :], sc2[0:1, 16:16 + 4 * H].partition_broadcast(128))
        cast_bc = s_cast.n

        p.op("gpsimd", lambda e: e.memset(ones64[:, :], 1.0), s_g)
        p.op("gpsimd", lambda e: e.iota(iota8[:, :], pattern=[[1, 8]], base=0,
                                        channel_multiplier=0,
                                        allow_small_or_imprecise_dtypes=True), s_g)
        p.op("gpsimd", lambda e: e.memset(ones_col[:, :], 1.0), s_g)
        g_0 = s_g.n

        p.wait("vector", s_cast, cast_bc)
        p.wait("vector", s_g, g_0)
        for hp in range(8):
            for half in range(2):
                rows = slice(half * 64, half * 64 + 64)
                colh = 2 * hp + half
                p.op("vector", lambda e, rows=rows, hp=hp, colh=colh: e.tensor_scalar_mul(
                    ws_tile[rows, hp * 64:(hp + 1) * 64], ones64[:, :],
                    ws_bc[rows, colh:colh + 1]), s_v)
        p.op("vector", lambda e: e.memset(s_e[:, :], 0.0), s_v)
        p.wait("vector", s_ld, ld0)
        p.wait("vector", s_v, s_v.n)
        for qq in range(4):
            for hp in range(8):
                for half in range(2):
                    rows = slice(half * 64, half * 64 + 64)
                    cols = slice(hp * 64, (hp + 1) * 64)
                    colh = qq * H + 2 * hp + half
                    if qq > 0:
                        p.wait("vector", s_v, s_v.n)
                    p.op("vector", lambda e, qq=qq, rows=rows, cols=cols, colh=colh:
                         e.scalar_tensor_tensor(s_e[rows, cols], s_tiles[qq][rows, cols],
                                                coef_bc[rows, colh:colh + 1], s_e[rows, cols],
                                                ALU.mult, ALU.add), s_v)
        p.wait("vector", s_v, s_v.n)
        p.op("vector", lambda e: e.tensor_copy(s_ebf[:, :], s_e[:, :]), s_v)
        v_se = s_v.n

        p.wait("scalar", s_cast, cast_w)
        for ib in range(8):
            p.dma("scalar", s_tr, woT[:, ib * C:(ib + 1) * C],
                  wo_bf[0:C, ib * 128:(ib + 1) * 128], transpose=True)
        for ib in range(8):
            p.dma("scalar", s_tr, gwT[:, ib * 16:(ib + 1) * 16],
                  gw_bf[0:16, ib * 128:(ib + 1) * 128], transpose=True)
        tr_w = s_tr.n

        for j in range(NCH):
            s_slab = j // 4
            cnn = j % 4
            p.wait("tensor", s_v, v_se)
            p.wait("tensor", s_ld, ld0)
            p.wait("tensor", s_v, s_v.n)
            for h in range(H):
                rows, pb = _head(h)
                rsl = slice(pb * TOK + j * CH, pb * TOK + (j + 1) * CH)
                srows = slice((h % 2) * KD, (h % 2) * KD + KD)
                scols = slice((h // 2) * KD, (h // 2 + 1) * KD)
                p.mm(ps_corr[:, h * KD:(h + 1) * KD], rwb[rows, rsl], s_ebf[srows, scols],
                     start=True, stop=True, inc=s_mm)
            mm_c = s_mm.n
            p.wait("vector", s_mm, mm_c)
            p.op("vector", lambda e, j=j: e.scalar_tensor_tensor(
                y_t[:, :], ps_corr[:, :], 1.0 / HEAD_DIV, xo_res[:, j * C:(j + 1) * C],
                ALU.mult, ALU.add), s_v)
            v_y = s_v.n
            p.wait("vector", s_v, v_y)
            p.op("vector", lambda e: e.tensor_mul(s_e[:, :], s_e[:, :], ws_tile[:, :]), s_v)
            p.wait("vector", s_v, s_v.n)
            p.op("vector", lambda e: e.tensor_copy(s_ebf[:, :], s_e[:, :]), s_v)
            v_se = s_v.n

            p.wait("vector", s_v, v_y)
            p.op("vector", lambda e: e.tensor_reduce(
                gn16[:, :], y_t.rearrange("p (h k) -> p h k", k=KD), axis=AX.X, op=ALU.add), s_v)
            p.wait("vector", s_v, s_v.n)
            p.op("vector", lambda e: e.tensor_scalar_mul(gn16[:, :], gn16[:, :], -1.0 / KD), s_v)
            v_nm = s_v.n
            p.wait("vector", s_v, v_nm)
            for h in range(H):
                sl = slice(h * KD, (h + 1) * KD)
                p.op("vector", lambda e, sl=sl, h=h: e.tensor_scalar_add(
                    y_t[:, sl], y_t[:, sl], gn16[:, h:h + 1]), s_v)
            v_yc = s_v.n
            p.wait("vector", s_v, v_yc)
            p.op("vector", lambda e: e.tensor_mul(sq_t[:, :], y_t[:, :], y_t[:, :]), s_v)
            p.wait("vector", s_v, s_v.n)
            p.op("vector", lambda e: e.tensor_reduce(
                gn16b[:, :], sq_t.rearrange("p (h k) -> p h k", k=KD), axis=AX.X, op=ALU.add), s_v)
            p.wait("vector", s_v, s_v.n)
            p.op("vector", lambda e: e.tensor_scalar(
                gn16b[:, :], gn16b[:, :], 1.0 / KD, EPS_GN, ALU.mult, ALU.add), s_v)
            p.wait("vector", s_v, s_v.n)
            p.op("vector", lambda e: e.reciprocal(gn16b[:, :], gn16b[:, :]), s_v)
            v_rcp = s_v.n
            p.wait("scalar", s_v, v_rcp)
            p.op("scalar", lambda e: e.activation(rstd16[:, :], gn16b[:, :], AF.Sqrt), s_a)
            a_rstd = s_a.n
            p.wait("vector", s_a, a_rstd)
            for h in range(H):
                sl = slice(h * KD, (h + 1) * KD)
                p.op("vector", lambda e, sl=sl, h=h: e.tensor_scalar_mul(
                    y_t[:, sl], y_t[:, sl], rstd16[:, h:h + 1]), s_v)
            p.wait("vector", s_v, s_v.n)
            p.wait("vector", s_cast, cast_ln)
            p.op("vector", lambda e: e.tensor_mul(y_t[:, :], y_t[:, :], lnw_bc[:, :]), s_v)
            p.wait("vector", s_v, s_v.n)
            p.op("vector", lambda e: e.tensor_add(y_t[:, :], y_t[:, :], lnb_bc[:, :]), s_v)
            p.wait("vector", s_v, s_v.n)
            p.op("vector", lambda e, j=j, cnn=cnn: e.tensor_mul(
                yg_sl[:, cnn * C:(cnn + 1) * C], y_t[:, :], g_res[:, j * C:(j + 1) * C]), s_v)
            v_yg = s_v.n

            if cnn == 3:
                t0 = s_slab * SLAB
                p.wait("sync", s_v, v_yg)
                p.dma("sync", s_kd,
                      yg_dram[t0:t0 + SLAB, :].rearrange("(tb p) c -> p tb c", p=128),
                      yg_sl.rearrange("a (tb c) -> a tb c", c=C))
                kw1 = s_kd.n
                p.wait("scalar", s_kd, kw1)
                p.wait("scalar", s_mm, s_mm.n)
                for pb in range(8):
                    cs = slice(pb * 128, (pb + 1) * 128)
                    p.dma("scalar", s_tr, ygc[:, pb * SLAB:(pb + 1) * SLAB],
                          yg_dram[t0:t0 + SLAB, cs], transpose=True)
                tr_yg = s_tr.n
                p.wait("sync", s_v, s_v.n)
                p.dma("sync", s_ld,
                      x_sl.rearrange("p (tb c) -> p tb c", c=C),
                      x_in[t0:t0 + SLAB, :].rearrange("(tb p) c -> p tb c", p=128))
                x_ld = s_ld.n

                p.wait("vector", s_kd, s_kd.n)
                p.wait("vector", s_o, s_o.n)
                for tb in range(4):
                    for oh in range(2):
                        bank = ps_proj[(tb * 2 + oh) % 2]
                        p.wait("tensor", s_v, s_v.n)
                        p.wait("tensor", s_tr, tr_yg)
                        for ib in range(8):
                            p.mm(bank[:, :],
                                 ygc[:, ib * SLAB + tb * 128: ib * SLAB + (tb + 1) * 128],
                                 woT[:, ib * C + oh * 512: ib * C + (oh + 1) * 512],
                                 start=(ib == 0), stop=(ib == 7),
                                 inc=(s_mm if ib == 7 else None))
                        p.wait("vector", s_mm, s_mm.n)
                        p.wait("vector", s_ld, x_ld)
                        p.wait("vector", s_o, s_o.n)
                        sl = slice(tb * C + oh * 512, tb * C + (oh + 1) * 512)
                        osl = slice(oh * 512, (oh + 1) * 512)
                        p.op("vector", lambda e, bank=bank, sl=sl, osl=osl: e.tensor_add(
                            xm_t[:, osl], x_sl[:, sl], bank[:, :]), s_v)
                    v_r1 = s_v.n
                    p.wait("vector", s_v, v_r1)
                    p.op("vector", lambda e: e.tensor_reduce(
                        col1[:, 0:1], xm_t[:, :], axis=AX.X, op=ALU.add), s_v)
                    p.wait("vector", s_v, s_v.n)
                    p.op("vector", lambda e: e.tensor_scalar_mul(col1[:, 0:1], col1[:, 0:1], -1.0 / C), s_v)
                    p.wait("vector", s_v, s_v.n)
                    p.op("vector", lambda e: e.tensor_scalar_add(xm_t[:, :], xm_t[:, :], col1[:, 0:1]), s_v)
                    p.wait("vector", s_v, s_v.n)
                    p.op("vector", lambda e: e.tensor_mul(sq_t[:, :], xm_t[:, :], xm_t[:, :]), s_v)
                    p.wait("vector", s_v, s_v.n)
                    p.op("vector", lambda e: e.tensor_reduce(
                        col2[:, 0:1], sq_t[:, :], axis=AX.X, op=ALU.add), s_v)
                    p.wait("vector", s_v, s_v.n)
                    p.op("vector", lambda e: e.tensor_scalar(
                        col2[:, 0:1], col2[:, 0:1], 1.0 / C, EPS_LN, ALU.mult, ALU.add), s_v)
                    p.wait("vector", s_v, s_v.n)
                    p.op("vector", lambda e: e.reciprocal(col2[:, 0:1], col2[:, 0:1]), s_v)
                    v_l1 = s_v.n
                    p.wait("scalar", s_v, v_l1)
                    p.op("scalar", lambda e: e.activation(col2[:, 0:1], col2[:, 0:1], AF.Sqrt), s_a)
                    a_l1 = s_a.n
                    p.wait("vector", s_a, a_l1)
                    p.op("vector", lambda e: e.tensor_scalar_mul(xm_t[:, :], xm_t[:, :], col2[:, 0:1]), s_v)
                    p.wait("vector", s_v, s_v.n)
                    p.op("vector", lambda e, tb=tb: e.tensor_copy(
                        xmb_sl[:, tb * C:(tb + 1) * C], xm_t[:, :]), s_v)
                    v_xm = s_v.n
                    p.wait("sync", s_v, v_xm)
                    p.dma("sync", s_o, o_xmid[t0 + tb * 128:t0 + (tb + 1) * 128, :], xm_t[:, :])
                p.wait("sync", s_v, s_v.n)
                p.dma("sync", s_kd,
                      xmb_dram[t0:t0 + SLAB, :].rearrange("(tb p) c -> p tb c", p=128),
                      xmb_sl.rearrange("a (tb c) -> a tb c", c=C))

        xmb_done = s_kd.n

        p.op("gpsimd", lambda e: e.memset(mw[:, :], 0.0), s_g)
        g_mw = s_g.n
        for ti in range(NCH):
            t0 = ti * CH
            p.wait("scalar", s_kd, xmb_done)
            p.wait("scalar", s_mm, s_mm.n)
            for ib in range(8):
                cs = slice(ib * 128, (ib + 1) * 128)
                p.dma("scalar", s_tr, xmc_t[:, ib * 128:(ib + 1) * 128],
                      xmb_dram[t0:t0 + CH, cs], transpose=True)
            tr_xm = s_tr.n
            p.wait("tensor", s_tr, tr_xm)
            p.wait("tensor", s_tr, tr_w)
            p.wait("tensor", s_v, s_v.n)
            for ib in range(8):
                p.mm(ps_g[:, :], xmc_t[:, ib * 128:(ib + 1) * 128], gwT[:, ib * 16:ib * 16 + 8],
                     start=(ib == 0), stop=(ib == 7), inc=(s_mm if ib == 7 else None))
            p.wait("vector", s_mm, s_mm.n)
            p.wait("vector", s_cast, cast_ln)
            p.op("vector", lambda e: e.tensor_add(lg[:, :], ps_g[:, :], lb_bc[:, :]), s_v)
            v_lg = s_v.n
            p.wait("scalar", s_v, v_lg)
            p.op("scalar", lambda e: e.activation(scores[:, :], lg[:, :], AF.Sigmoid), s_a)
            a_sc = s_a.n
            p.wait("vector", s_a, a_sc)
            p.op("vector", lambda e: e.max(mx8[:, :], scores[:, :]), s_v)
            p.wait("vector", s_v, s_v.n)
            p.op("vector", lambda e: e.max_index(mi8[:, :], mx8[:, :], scores[:, :]), s_v)
            p.wait("vector", s_v, s_v.n)
            p.op("vector", lambda e: e.tensor_copy(mif[:, :], mi8[:, :]), s_v)
            p.wait("vector", s_v, s_v.n)
            p.op("vector", lambda e: e.scalar_tensor_tensor(
                den[:, 0:1], mx8[:, 0:1], 1e-8, mx8[:, 1:2], ALU.add, ALU.add), s_v)
            p.wait("vector", s_v, s_v.n)
            p.op("vector", lambda e: e.reciprocal(den[:, 0:1], den[:, 0:1]), s_v)
            p.wait("vector", s_v, s_v.n)
            p.op("vector", lambda e: e.tensor_mul(w1c[:, 0:1], mx8[:, 0:1], den[:, 0:1]), s_v)
            p.op("vector", lambda e: e.tensor_mul(w2c[:, 0:1], mx8[:, 1:2], den[:, 0:1]), s_v)
            p.wait("vector", s_v, s_v.n)
            p.wait("vector", s_g, g_mw)
            p.op("vector", lambda e: e.tensor_scalar(
                eq1[:, :], iota8[:, :], mif[:, 0:1], None, ALU.is_equal), s_v)
            p.op("vector", lambda e: e.tensor_scalar(
                eq2[:, :], iota8[:, :], mif[:, 1:2], None, ALU.is_equal), s_v)
            p.wait("vector", s_v, s_v.n)
            p.wait("vector", s_mm, s_mm.n)
            p.wait("vector", s_o, s_o.n)
            p.op("vector", lambda e: e.tensor_add(mw[:, 0:8], eq1[:, :], eq2[:, :]), s_v)
            p.wait("vector", s_v, s_v.n)
            p.op("vector", lambda e: e.tensor_scalar_mul(mw[:, 8:16], eq1[:, :], w1c[:, 0:1]), s_v)
            p.wait("vector", s_v, s_v.n)
            p.op("vector", lambda e: e.scalar_tensor_tensor(
                mw[:, 8:16], eq2[:, :], w2c[:, 0:1], mw[:, 8:16], ALU.mult, ALU.add), s_v)
            v_mw = s_v.n
            p.wait("tensor", s_v, v_mw)
            p.wait("tensor", s_g, g_0)
            p.mm(ps_st[0:1, :], ones_col[:, 0:1], mw[:, :],
                 start=(ti == 0), stop=(ti == NCH - 1), inc=s_mm)
            p.wait("sync", s_v, v_mw)
            p.dma("sync", s_o, o_m[t0:t0 + CH, :], mw[:, 0:16])
        p.wait("scalar", s_mm, s_mm.n)
        p.op("scalar", lambda e: e.copy(st_sb[0:1, :], ps_st[0:1, :]), s_a)
        p.wait("sync", s_a, s_a.n)
        p.dma("sync", s_o, o_stats, st_sb[0:1, :])
        p.wait("sync", s_o, s_o.n)
        p.build_block()
    return nc



def build_l3():
    nc = bass.Bass("TRN2", target_bir_lowering=False)
    xm_in = nc.dram_tensor("xm", [TOK, C], F32, kind="ExternalInput").ap()
    m_in = nc.dram_tensor("m", [TOK, 128], F32, kind="ExternalInput").ap()
    stats_in = nc.dram_tensor("stats", [8, 16], F32, kind="ExternalInput").ap()
    eg_in = nc.dram_tensor("eg", [NEXP, EI, C], F32, kind="ExternalInput").ap()
    eu_in = nc.dram_tensor("eu", [NEXP, EI, C], F32, kind="ExternalInput").ap()
    ed_in = nc.dram_tensor("ed", [NEXP, C, EI], F32, kind="ExternalInput").ap()
    sg_in = nc.dram_tensor("sg", [SI, C], F32, kind="ExternalInput").ap()
    su_in = nc.dram_tensor("su", [SI, C], F32, kind="ExternalInput").ap()
    sd_in = nc.dram_tensor("sd", [C, SI], F32, kind="ExternalInput").ap()

    o_out = nc.dram_tensor("o_out", [TOK, C], F32, kind="ExternalOutput").ap()

    xm_bf = nc.dram_tensor("xm_bf", [TOK, C], BF16, kind="Internal").ap()
    m_bf = nc.dram_tensor("m_bf", [TOK, 128], BF16, kind="Internal").ap()
    eg_bf = nc.dram_tensor("eg_bf", [NEXP, EI, C], BF16, kind="Internal").ap()
    eu_bf = nc.dram_tensor("eu_bf", [NEXP, EI, C], BF16, kind="Internal").ap()
    ed_bf = nc.dram_tensor("ed_bf", [NEXP, C, EI], BF16, kind="Internal").ap()
    sg_bf = nc.dram_tensor("sg_bf", [SI, C], BF16, kind="Internal").ap()
    su_bf = nc.dram_tensor("su_bf", [SI, C], BF16, kind="Internal").ap()
    sd_bf = nc.dram_tensor("sd_bf", [C, SI], BF16, kind="Internal").ap()
    selwm_dram = nc.dram_tensor("selwm_dram", [16, TOK], BF16, kind="Internal").ap()
    sc3 = nc.dram_tensor("sc3", [1, 16], F32, kind="Internal").ap()

    with ExitStack() as ctx:
        p = Prog(nc, ctx)
        sb = lambda n, s, d=F32: ctx.enter_context(nc.sbuf_tensor(n, s, d))
        psl = lambda n, s: ctx.enter_context(nc.psum_tensor(n, s, F32))

        xmc = sb("xmc", [128, 8 * TOK], BF16)
        he_buf = [sb("he%d" % e, [128, 4 * SLAB], BF16) for e in range(NEXP)]
        hsh = sb("hsh", [128, 8 * SLAB], BF16)
        wg_buf = sb("wg_buf", [128, 8 * C], BF16)
        wu_buf = sb("wu_buf", [128, 8 * C], BF16)
        wd_buf = [sb("wd_buf%d" % i, [128, 8 * C], BF16) for i in range(2)]
        a_sl = sb("a_sl", [128, 8 * SLAB], BF16)
        u_sl = sb("u_sl", [128, 8 * SLAB], BF16)
        selwm_bc = sb("selwm_bc", [128, SLAB], BF16)
        xm_sl = sb("xm_sl", [128, 4 * C], F32)
        r2_t = sb("r2_t", [128, C], F32)
        sq_t = sb("sq_t", [128, C], F32)
        col1 = sb("col1", [128, 1], F32)
        col2 = sb("col2", [128, 1], F32)
        mT = sb("mT", [128, TOK], BF16)
        st_in = sb("st_in", [8, 16], F32)
        st_sb = sb("st_sb", [1, 16], F32)
        wm_row = sb("wm_row", [1, 8], F32)
        wmcol = sb("wmcol", [16, 1], F32)
        ones8 = sb("ones8", [8, 1], F32)

        ps_dn = [psl("ps_dn%d" % i, [128, C]) for i in range(4)]

        s_cast = p.sem("s_cast")
        s_csel = p.sem("s_csel")
        s_lds = p.sem("s_lds")
        s_ld = p.sem("s_ld")
        s_trm = p.sem("s_trm")
        s_trx = p.sem("s_trx")
        s_tr = p.sem("s_tr")
        s_trd = p.sem("s_trd")
        s_kd = p.sem("s_kd")
        s_mm = p.sem("s_mm")
        s_v = p.sem("s_v")
        s_a = p.sem("s_a")
        s_g = p.sem("s_g")
        s_o = p.sem("s_o")

        p.dma("gpsimd", s_cast, xm_bf, xm_in)
        p.dma("gpsimd", s_cast, m_bf, m_in)
        cast_xm = s_cast.n
        for e_ in range(NEXP):
            p.dma("gpsimd", s_cast, eg_bf[e_], eg_in[e_])
            p.dma("gpsimd", s_cast, eu_bf[e_], eu_in[e_])
            p.dma("gpsimd", s_cast, ed_bf[e_], ed_in[e_])
        p.dma("gpsimd", s_cast, sg_bf, sg_in)
        p.dma("gpsimd", s_cast, su_bf, su_in)
        p.dma("gpsimd", s_cast, sd_bf, sd_in)
        cast_all = s_cast.n

        p.dma("sync", s_lds, st_in[:, :], stats_in)
        ld_st = s_lds.n
        p.op("gpsimd", lambda e: e.memset(ones8[:, :], 1.0), s_g)
        p.op("gpsimd", lambda e: e.memset(wmcol[:, :], 0.0), s_g)
        g_0 = s_g.n

        p.wait("tensor", s_lds, ld_st)
        p.wait("tensor", s_g, g_0)
        p.mm(ps_dn[3][0:1, 0:16], ones8[:, 0:1], st_in[:, :], start=True, stop=True, inc=s_mm)
        p.wait("scalar", s_mm, s_mm.n)
        p.op("scalar", lambda e: e.copy(st_sb[0:1, :], ps_dn[3][0:1, 0:16]), s_a)
        a_st = s_a.n
        p.wait("vector", s_a, a_st)
        p.op("vector", lambda e: e.tensor_scalar_max(st_sb[0:1, 0:8], st_sb[0:1, 0:8], 1.0), s_v)
        p.wait("vector", s_v, s_v.n)
        p.op("vector", lambda e: e.reciprocal(st_sb[0:1, 0:8], st_sb[0:1, 0:8]), s_v)
        p.wait("vector", s_v, s_v.n)
        p.op("vector", lambda e: e.tensor_mul(wm_row[0:1, :], st_sb[0:1, 8:16], st_sb[0:1, 0:8]), s_v)
        v_wm = s_v.n
        p.wait("sync", s_v, v_wm)
        p.dma("sync", s_kd, sc3[0:1, 0:8], wm_row[0:1, :])
        kd_wm = s_kd.n
        p.wait("sync", s_kd, kd_wm)
        p.wait("sync", s_g, g_0)
        p.dma("sync", s_kd, wmcol[0:8, 0:1], sc3[0:1, 0:8].rearrange("a b -> b a"),
              allow_slow_non_contiguous=True)
        kd_wc = s_kd.n

        p.wait("scalar", s_cast, cast_xm)
        p.dma("scalar", s_trm, mT[:, :], m_bf[0:TOK, 0:128], transpose=True)
        tr_m = s_trm.n
        p.wait("vector", s_trm, tr_m)
        p.wait("vector", s_kd, kd_wc)
        p.op("vector", lambda e: e.tensor_scalar_mul(mT[0:16, :], mT[0:16, :], wmcol[0:16, 0:1]), s_v)
        v_swm = s_v.n
        p.wait("sync", s_v, v_swm)
        p.dma("sync", s_kd, selwm_dram, mT[0:16, :])
        kd_swm = s_kd.n

        for pb in range(8):
            cs = slice(pb * 128, (pb + 1) * 128)
            p.dma("scalar", s_trx, xmc[:, pb * TOK:(pb + 1) * TOK],
                  xm_bf[0:TOK, cs], transpose=True)
        tr_xm = s_trx.n

        for tau in range(4):
            t0 = tau * SLAB
            p.wait("sync", s_v, s_v.n)
            p.dma("sync", s_ld,
                  xm_sl.rearrange("p (tb c) -> p tb c", c=C),
                  xm_in[t0:t0 + SLAB, :].rearrange("(tb p) c -> p tb c", p=128))
            x_ld = s_ld.n

            for ex in range(NEXP + 1):
                shared = (ex == NEXP)
                nob = 8 if shared else 4
                g_src = sg_bf if shared else eg_bf[ex]
                u_src = su_bf if shared else eu_bf[ex]
                p.wait("scalar", s_cast, cast_all)
                p.wait("scalar", s_mm, s_mm.n)
                for ib in range(8):
                    p.dma("scalar", s_tr, wg_buf[:, ib * (nob * 128):(ib + 1) * (nob * 128)],
                          g_src[0:nob * 128, ib * 128:(ib + 1) * 128], transpose=True)
                    p.dma("scalar", s_tr, wu_buf[:, ib * (nob * 128):(ib + 1) * (nob * 128)],
                          u_src[0:nob * 128, ib * 128:(ib + 1) * 128], transpose=True)
                tr_gu = s_tr.n
                if not shared:
                    p.wait("gpsimd", s_kd, kd_swm)
                    p.wait("gpsimd", s_v, s_v.n)
                    p.dma("gpsimd", s_csel, selwm_bc[:, :],
                          selwm_dram[ex:ex + 1, t0:t0 + SLAB].partition_broadcast(128))
                sel_cast = s_csel.n

                for ob in range(nob):
                    bank = ps_dn[0][:, 0:512] if (ob % 2 == 0) else ps_dn[1][:, 0:512]
                    p.wait("tensor", s_v, s_v.n)
                    p.wait("tensor", s_a, s_a.n)
                    p.wait("tensor", s_tr, tr_gu)
                    p.wait("tensor", s_trx, tr_xm)
                    for ib in range(8):
                        p.mm(bank,
                             wg_buf[:, ib * (nob * 128) + ob * 128: ib * (nob * 128) + (ob + 1) * 128],
                             xmc[:, ib * TOK + t0: ib * TOK + t0 + SLAB],
                             start=(ib == 0), stop=(ib == 7),
                             inc=(s_mm if ib == 7 else None))
                    p.wait("scalar", s_mm, s_mm.n)
                    p.op("scalar", lambda e, bank=bank, ob=ob: e.activation(
                        a_sl[:, ob * SLAB:(ob + 1) * SLAB], bank, AF.Silu), s_a)
                    bank2 = ps_dn[0][:, 512:1024] if (ob % 2 == 0) else ps_dn[1][:, 512:1024]
                    p.wait("tensor", s_a, s_a.n)
                    for ib in range(8):
                        p.mm(bank2,
                             wu_buf[:, ib * (nob * 128) + ob * 128: ib * (nob * 128) + (ob + 1) * 128],
                             xmc[:, ib * TOK + t0: ib * TOK + t0 + SLAB],
                             start=(ib == 0), stop=(ib == 7),
                             inc=(s_mm if ib == 7 else None))
                    p.wait("scalar", s_mm, s_mm.n)
                    p.op("scalar", lambda e, bank2=bank2, ob=ob: e.copy(
                        u_sl[:, ob * SLAB:(ob + 1) * SLAB], bank2), s_a)
                a_gu = s_a.n
                p.wait("vector", s_a, a_gu)
                p.wait("vector", s_csel, sel_cast)
                dst = hsh if shared else he_buf[ex]
                for ob in range(nob):
                    sl = slice(ob * SLAB, (ob + 1) * SLAB)
                    p.op("vector", lambda e, sl=sl, dst=dst: e.tensor_mul(
                        dst[:, sl], a_sl[:, sl], u_sl[:, sl]), s_v)
                p.wait("vector", s_v, s_v.n)
                if not shared:
                    for ob in range(nob):
                        sl = slice(ob * SLAB, (ob + 1) * SLAB)
                        p.op("vector", lambda e, sl=sl, dst=dst: e.tensor_mul(
                            dst[:, sl], dst[:, sl], selwm_bc[:, :]), s_v)
            v_h = s_v.n

            for ex in range(NEXP + 1):
                shared = (ex == NEXP)
                nib = 8 if shared else 4
                d_src = sd_bf if shared else ed_bf[ex]
                buf = wd_buf[ex % 2]
                p.wait("scalar", s_mm, s_mm.n)
                for ib in range(nib):
                    p.dma("scalar", s_trd, buf[:, ib * C:(ib + 1) * C],
                          d_src[0:C, ib * 128:(ib + 1) * 128], transpose=True)
                tr_d = s_trd.n
                p.wait("tensor", s_trd, tr_d)
                p.wait("tensor", s_v, v_h)
                if ex == 0:
                    p.wait("tensor", s_a, s_a.n)
                hsrc = hsh if shared else he_buf[ex]
                for tb in range(4):
                    for oh in range(2):
                        for ib in range(nib):
                            last_of_ex = (tb == 3 and oh == 1 and ib == nib - 1)
                            p.mm(ps_dn[tb][:, oh * 512:(oh + 1) * 512],
                                 hsrc[:, ib * SLAB + tb * 128: ib * SLAB + (tb + 1) * 128],
                                 buf[:, ib * C + oh * 512: ib * C + (oh + 1) * 512],
                                 start=(ex == 0 and ib == 0),
                                 stop=(ex == NEXP and ib == nib - 1),
                                 inc=(s_mm if last_of_ex else None))
            mm_dn = s_mm.n

            for tb in range(4):
                p.wait("vector", s_mm, mm_dn)
                p.wait("vector", s_ld, x_ld)
                p.op("vector", lambda e, tb=tb: e.tensor_add(
                    r2_t[:, :], xm_sl[:, tb * C:(tb + 1) * C], ps_dn[tb][:, :]), s_v)
                p.wait("vector", s_v, s_v.n)
                p.op("vector", lambda e: e.tensor_reduce(
                    col1[:, 0:1], r2_t[:, :], axis=AX.X, op=ALU.add), s_v)
                p.wait("vector", s_v, s_v.n)
                p.op("vector", lambda e: e.tensor_scalar_mul(col1[:, 0:1], col1[:, 0:1], -1.0 / C), s_v)
                p.wait("vector", s_v, s_v.n)
                p.op("vector", lambda e: e.tensor_scalar_add(r2_t[:, :], r2_t[:, :], col1[:, 0:1]), s_v)
                p.wait("vector", s_v, s_v.n)
                p.op("vector", lambda e: e.tensor_mul(sq_t[:, :], r2_t[:, :], r2_t[:, :]), s_v)
                p.wait("vector", s_v, s_v.n)
                p.op("vector", lambda e: e.tensor_reduce(
                    col2[:, 0:1], sq_t[:, :], axis=AX.X, op=ALU.add), s_v)
                p.wait("vector", s_v, s_v.n)
                p.op("vector", lambda e: e.tensor_scalar(
                    col2[:, 0:1], col2[:, 0:1], 1.0 / C, EPS_LN, ALU.mult, ALU.add), s_v)
                p.wait("vector", s_v, s_v.n)
                p.op("vector", lambda e: e.reciprocal(col2[:, 0:1], col2[:, 0:1]), s_v)
                v_l2 = s_v.n
                p.wait("scalar", s_v, v_l2)
                p.op("scalar", lambda e: e.activation(col2[:, 0:1], col2[:, 0:1], AF.Sqrt), s_a)
                a_l2 = s_a.n
                p.wait("vector", s_a, a_l2)
                p.op("vector", lambda e: e.tensor_scalar_mul(r2_t[:, :], r2_t[:, :], col2[:, 0:1]), s_v)
                v_out = s_v.n
                p.wait("sync", s_v, v_out)
                p.dma("sync", s_o, o_out[t0 + tb * 128:t0 + (tb + 1) * 128, :], r2_t[:, :])
                p.wait("vector", s_o, s_o.n)

        p.wait("sync", s_o, s_o.n)
        p.build_block()
    return nc



def kernel(**inputs):
    x = np.ascontiguousarray(np.asarray(inputs["L0_attn_xq"], np.float32))
    tm = np.ascontiguousarray(np.stack([
        np.asarray(inputs["L0_attn_time_mix_r"], np.float32).reshape(C),
        np.asarray(inputs["L0_attn_time_mix_k"], np.float32).reshape(C),
        np.asarray(inputs["L0_attn_time_mix_v"], np.float32).reshape(C),
        np.asarray(inputs["L0_attn_time_mix_g"], np.float32).reshape(C),
    ]))
    td = np.ascontiguousarray(np.asarray(inputs["L0_attn_time_decay"], np.float32).reshape(1, H))
    fa = np.ascontiguousarray(np.asarray(inputs["L0_attn_time_faaaa"], np.float32).reshape(1, H))
    f32c = lambda k: np.ascontiguousarray(np.asarray(inputs[k], np.float32))

    nc1 = build_l1()
    maps1 = []
    for c in range(NCORES):
        b, q = c // 4, c % 4
        t0 = q * TOK
        xp = np.zeros((TOK + 1, C), np.float32)
        xp[1:] = x[b, t0:t0 + TOK]
        if q > 0:
            xp[0] = x[b, t0 - 1]
        maps1.append({
            "xp": xp,
            "wr": f32c("L0_attn_W_r"), "wk": f32c("L0_attn_W_k"),
            "wv": f32c("L0_attn_W_v"), "wg": f32c("L0_attn_W_g"),
            "tm": tm, "td": td, "fa": fa,
        })
    res1 = run_bass_kernel_spmd(nc1, maps1, core_ids=list(range(NCORES)))
    r1 = res1.results

    nc2 = build_l2()
    maps2 = []
    for c in range(NCORES):
        b, q = c // 4, c % 4
        t0 = q * TOK
        sall = np.concatenate([np.asarray(r1[b * 4 + qq]["o_sc"], np.float32)
                               for qq in range(4)], axis=0)
        coefk = np.zeros((1, 4), np.float32)
        maskk = np.zeros((1, 4), np.float32)
        for qq in range(q):
            coefk[0, qq] = float(TOK * (q - 1 - qq))
            maskk[0, qq] = 1.0
        maps2.append({
            "x": x[b, t0:t0 + TOK],
            "xo": np.asarray(r1[c]["o_xo"]).view(np.uint16).view(ml_dtypes.bfloat16)
                  if np.asarray(r1[c]["o_xo"]).dtype != ml_dtypes.bfloat16
                  else np.asarray(r1[c]["o_xo"]),
            "rwb": np.asarray(r1[c]["o_rwb"]),
            "g": np.asarray(r1[c]["o_g"]),
            "sall": sall, "coefk": coefk, "maskk": maskk,
            "wo": f32c("L0_attn_W_o"), "td": td,
            "lnw": np.asarray(inputs["L0_attn_ln_w"], np.float32).reshape(1, C),
            "lnb": np.asarray(inputs["L0_attn_ln_b"], np.float32).reshape(1, C),
            "gw": np.concatenate([f32c("L0_ffn_gate_w"),
                                  np.zeros((8, C), np.float32)], axis=0),
            "lb": np.asarray(inputs["L0_ffn_lb_bias"], np.float32).reshape(1, 8),
        })
    res2 = run_bass_kernel_spmd(nc2, maps2, core_ids=list(range(NCORES)))
    r2 = res2.results

    nc3 = build_l3()
    stats_all = np.concatenate([np.asarray(r2[c]["o_stats"], np.float32)
                                for c in range(NCORES)], axis=0)
    maps3 = []
    for c in range(NCORES):
        maps3.append({
            "xm": np.asarray(r2[c]["o_xmid"], np.float32),
            "m": np.concatenate([np.asarray(r2[c]["o_m"], np.float32),
                                 np.zeros((TOK, 112), np.float32)], axis=1),
            "stats": stats_all,
            "eg": f32c("L0_ffn_expert_gate_w"),
            "eu": f32c("L0_ffn_expert_up_w"),
            "ed": f32c("L0_ffn_expert_down_w"),
            "sg": f32c("L0_ffn_shared_gate_w"),
            "su": f32c("L0_ffn_shared_up_w"),
            "sd": f32c("L0_ffn_shared_down_w"),
        })
    res3 = run_bass_kernel_spmd(nc3, maps3, core_ids=list(range(NCORES)))
    r3 = res3.results

    out = np.zeros((B, TT, C), np.float32)
    for c in range(NCORES):
        b, q = c // 4, c % 4
        out[b, q * TOK:(q + 1) * TOK] = np.asarray(r3[c]["o_out"], np.float32)
    return out


# version 05
# speedup vs baseline: 1.2465x; 1.2465x over previous
import numpy as np
import ml_dtypes
from contextlib import ExitStack

import concourse.bass as bass
import concourse.mybir as mybir
from concourse.bass_utils import run_bass_kernel_spmd

F32 = mybir.dt.float32
BF16 = mybir.dt.bfloat16
U32 = mybir.dt.uint32
AF = mybir.ActivationFunctionType
ALU = mybir.AluOpType
AX = mybir.AxisListType

B = 2
TT = 8192
C = 1024
H = 16
KD = 64
CH = 128
TOK = 2048
NCH = TOK // CH
SLAB = 512
NSLAB = TOK // SLAB
HEAD_DIV = 8.0
NCORES = 8
EPS_GN = 1e-5
EPS_LN = 1e-5
NEXP = 8
EI = 512
SI = 1024


class Ctr:
    def __init__(self, ctx, nc, name):
        self.h = ctx.enter_context(nc.semaphore(name))
        self.n = 0
        self.dma = False


class Prog:
    ENGINES = ("tensor", "vector", "scalar", "gpsimd", "sync")

    def __init__(self, nc, ctx):
        self.nc = nc
        self.ctx = ctx
        self.q = {e: [] for e in self.ENGINES}

    def sem(self, name):
        return Ctr(self.ctx, self.nc, name)

    def wait(self, eng, ctr, val=None):
        v = ctr.n if (val is None or ctr.dma) else val
        if v > 0:
            self.q[eng].append(lambda e, c=ctr, v=v: e.wait_ge(c.h, v))

    MM_NOPS = 12

    def mm(self, psum, lhsT, rhs, start, stop, inc=None, tile_position=None):
        def fn(e, psum=psum, lhsT=lhsT, rhs=rhs, start=start, stop=stop, inc=inc,
               tp=tile_position):
            ins = e.matmul(psum, lhsT, rhs, start=start, stop=stop, tile_position=tp)
            if inc is not None:
                ins.then_inc(inc.h, 1)
            for _ in range(Prog.MM_NOPS):
                e.nop()
        self.q["tensor"].append(fn)
        if inc is not None:
            inc.n += 1

    def op(self, eng, fn, inc=None):
        def wrapped(e, fn=fn, inc=inc):
            ins = fn(e)
            if inc is not None:
                ins.then_inc(inc.h, 1)
        self.q[eng].append(wrapped)
        if inc is not None:
            inc.n += 1

    def dma(self, eng, ctr, out, in_, transpose=False, **kw):
        ctr.dma = True
        def fn(e, out=out, in_=in_, transpose=transpose, kw=kw):
            e.dma_start(out, in_, transpose=transpose, **kw).then_inc(ctr.h, 16)
        self.q[eng].append(fn)
        ctr.n += 16

    def build_block(self):
        block = self.ctx.enter_context(self.nc.Block())
        q = self.q

        @block.tensor
        def _(e):
            for fn in q["tensor"]:
                fn(e)

        @block.vector
        def _(e):
            for fn in q["vector"]:
                fn(e)

        @block.scalar
        def _(e):
            for fn in q["scalar"]:
                fn(e)

        @block.gpsimd
        def _(e):
            for fn in q["gpsimd"]:
                fn(e)

        @block.sync
        def _(e):
            for fn in q["sync"]:
                fn(e)


def _head(h):
    return slice((h % 2) * KD, (h % 2) * KD + KD), h // 2



def build_l1(debug=False):
    nc = bass.Bass("TRN2", target_bir_lowering=False)
    xp = nc.dram_tensor("xp", [TOK + 1, C], F32, kind="ExternalInput").ap()
    w_in = {nm: nc.dram_tensor(nm, [C, C], F32, kind="ExternalInput").ap()
            for nm in ("wr", "wk", "wv", "wg")}
    tm_in = nc.dram_tensor("tm", [4, C], F32, kind="ExternalInput").ap()
    td_in = nc.dram_tensor("td", [1, H], F32, kind="ExternalInput").ap()
    fa_in = nc.dram_tensor("fa", [1, H], F32, kind="ExternalInput").ap()

    o_sc = nc.dram_tensor("o_sc", [128, 512], F32, kind="ExternalOutput").ap()
    o_xo = nc.dram_tensor("o_xo", [TOK, C], BF16, kind="ExternalOutput").ap()
    o_rwb = nc.dram_tensor("o_rwb", [128, 8 * TOK], BF16, kind="ExternalOutput").ap()
    o_g = nc.dram_tensor("o_g", [TOK, C], BF16, kind="ExternalOutput").ap()

    x_bf = nc.dram_tensor("x_bf", [TOK + 1, C], BF16, kind="Internal").ap()
    w_bf = {nm: nc.dram_tensor(nm + "_bf", [C, C], BF16, kind="Internal").ap()
            for nm in ("wr", "wk", "wv", "wg")}
    kc_dram = nc.dram_tensor("kc_dram", [C, TOK], BF16, kind="Internal").ap()
    sc1 = nc.dram_tensor("sc1", [1, 64], F32, kind="Internal").ap()

    dbg = {}
    if debug:
        for nm, shp, dt_ in [("d_wmat", [128, H * CH], F32), ("d_wb", [128, 8 * CH], BF16),
                             ("d_wk", [128, H], F32), ("d_ws", [128, 512], F32),
                             ("d_ldbc", [128, H], F32),
                             ("d_mixr", [128, 8 * SLAB], BF16), ("d_mixk", [128, 8 * SLAB], BF16),
                             ("d_rsl", [128, 8 * SLAB], BF16), ("d_kcs", [128, 8 * SLAB], BF16),
                             ("d_kts", [128, 4 * C], BF16), ("d_vts", [128, 4 * C], BF16),
                             ("d_xc", [128, 8 * SLAB], BF16), ("d_xxc", [128, 8 * SLAB], BF16)]:
            dbg[nm] = nc.dram_tensor(nm, shp, dt_, kind="ExternalOutput").ap()

    with ExitStack() as ctx:
        p = Prog(nc, ctx)
        sb = lambda n, s, d=F32: ctx.enter_context(nc.sbuf_tensor(n, s, d))
        psl = lambda n, s: ctx.enter_context(nc.psum_tensor(n, s, F32))

        wT = {nm: sb(nm + "T", [128, 8 * C], BF16) for nm in ("wr", "wk", "wv", "wg")}
        xc = sb("xc", [128, 8 * SLAB], BF16)
        xxc = sb("xxc", [128, 8 * SLAB], BF16)
        mixed = {nm: sb("mx_" + nm, [128, 8 * SLAB], BF16) for nm in ("r", "k", "v", "g")}
        r_sl = sb("r_sl", [128, 8 * SLAB], BF16)
        rwb_sl = sb("rwb_sl", [128, 8 * SLAB], BF16)
        kcs = sb("kcs", [128, 8 * SLAB], BF16)
        kts = sb("kts", [128, 4 * C], BF16)
        kwk = sb("kwk", [128, C], BF16)
        vts = sb("vts", [128, 4 * C], BF16)
        g_sl = sb("g_sl", [128, 4 * C], BF16)
        xo_sl = sb("xo_sl", [128, 4 * C], BF16)
        attw = sb("attw", [128, 2 * CH], BF16)
        s_sb = sb("s_sb", [128, 512], F32)
        s_tmp = sb("s_tmp", [128, 512], F32)
        s_bf = sb("s_bf", [128, 512], BF16)

        tm_cols = sb("tm_cols", [128, 32], F32)
        td_row = sb("td_row", [1, H], F32)
        ld_row = sb("ld_row", [1, H], F32)
        ws_row = sb("ws_row", [1, H], F32)
        fa_row = sb("fa_row", [1, H], F32)
        ld_bc = sb("ld_bc", [128, H], F32)
        u_bc = sb("u_bc", [128, H], F32)
        ws_bc = sb("ws_bc", [128, H], F32)
        ldcol8 = sb("ldcol8", [128, 8], F32)
        wmatT = sb("wmatT", [128, H * CH], F32)
        wb_ct = sb("wb_ct", [128, 8 * CH], BF16)
        wk_cols = sb("wk_cols", [128, H], F32)
        ws_tile = sb("ws_tile", [128, 512], F32)
        iota_d = sb("iota_d", [128, CH], F32)
        iota_f = sb("iota_f", [128, CH], F32)
        clampd = sb("clampd", [128, CH], F32)
        rev_col = sb("rev_col", [128, 1], F32)
        ones128 = sb("ones128", [128, CH], F32)
        eye = sb("eye", [128, CH], F32)

        ps_proj = [psl("ps_proj%d" % i, [128, SLAB]) for i in range(2)]
        ps_att = [psl("ps_att%d" % i, [128, CH]) for i in range(2)]
        ps_out = psl("ps_out", [128, C])
        ps_A = psl("ps_A", [128, 512])

        s_cast = p.sem("s_cast")
        s_ld = p.sem("s_ld")
        s_xld = p.sem("s_xld")
        s_tr = p.sem("s_tr")
        s_kd = p.sem("s_kd")
        s_kc = p.sem("s_kc")
        s_mm = p.sem("s_mm")
        s_v = p.sem("s_v")
        s_a = p.sem("s_a")
        s_g = p.sem("s_g")
        s_o = p.sem("s_o")

        p.dma("gpsimd", s_cast, x_bf, xp)
        cast_x = s_cast.n
        for nm in ("wr", "wk", "wv", "wg"):
            p.dma("gpsimd", s_cast, w_bf[nm], w_in[nm])
        cast_w = s_cast.n

        p.dma("sync", s_ld, td_row[0:1, :], td_in)
        p.dma("sync", s_ld, fa_row[0:1, :], fa_in)
        p.dma("sync", s_ld, tm_cols[:, :],
              tm_in.rearrange("m (p c) -> c (m p)", c=128),
              allow_slow_non_contiguous=True)
        ld0 = s_ld.n

        p.wait("scalar", s_ld, ld0)
        p.op("scalar", lambda e: e.activation(ld_row[0:1, :], td_row[0:1, :], AF.Exp), s_a)
        a_exp = s_a.n
        p.wait("vector", s_a, a_exp)
        p.op("vector", lambda e: e.tensor_scalar_mul(ld_row[0:1, :], ld_row[0:1, :], -1.0), s_v)
        v_ld = s_v.n
        p.wait("scalar", s_v, v_ld)
        p.op("scalar", lambda e: e.activation(ws_row[0:1, :], ld_row[0:1, :], AF.Exp,
                                              scale=float(CH)), s_a)
        a_ws = s_a.n
        p.wait("sync", s_v, v_ld)
        p.dma("sync", s_kd, sc1[0:1, 0:H], ld_row[0:1, :])
        p.wait("sync", s_a, a_ws)
        p.dma("sync", s_kd, sc1[0:1, 16:16 + H], ws_row[0:1, :])
        p.wait("sync", s_ld, ld0)
        p.dma("sync", s_kd, sc1[0:1, 32:32 + H], fa_row[0:1, :])
        kd0 = s_kd.n

        p.wait("gpsimd", s_kd, kd0)
        p.dma("gpsimd", s_cast, ld_bc[:, :], sc1[0:1, 0:H].partition_broadcast(128))
        p.dma("gpsimd", s_cast, ws_bc[:, :], sc1[0:1, 16:16 + H].partition_broadcast(128))
        p.dma("gpsimd", s_cast, u_bc[:, :], sc1[0:1, 32:32 + H].partition_broadcast(128))
        for pb in range(8):
            p.dma("gpsimd", s_cast, ldcol8[0:64, pb:pb + 1],
                  sc1[0:1, 2 * pb:2 * pb + 1].partition_broadcast(64))
            p.dma("gpsimd", s_cast, ldcol8[64:128, pb:pb + 1],
                  sc1[0:1, 2 * pb + 1:2 * pb + 2].partition_broadcast(64))
        cast_bc = s_cast.n

        p.op("gpsimd", lambda e: e.iota(iota_d[:, :], pattern=[[1, CH]], base=-1,
                                        channel_multiplier=-1,
                                        allow_small_or_imprecise_dtypes=True), s_g)
        p.op("gpsimd", lambda e: e.iota(iota_f[:, :], pattern=[[1, CH]], base=0,
                                        channel_multiplier=0,
                                        allow_small_or_imprecise_dtypes=True), s_g)
        p.op("gpsimd", lambda e: e.iota(rev_col[:, 0:1], pattern=[[0, 1]], base=127,
                                        channel_multiplier=-1,
                                        allow_small_or_imprecise_dtypes=True), s_g)
        p.op("gpsimd", lambda e: e.memset(ones128[:, :], 1.0), s_g)
        g_iota = s_g.n
        p.wait("gpsimd", s_g, g_iota)
        p.op("gpsimd", lambda e: e.affine_select(eye[:, :], ones128[:, :],
                                                 pattern=[[1, CH]], compare_op=ALU.is_equal,
                                                 fill=0.0, base=0, channel_multiplier=-1), s_g)
        g_eye = s_g.n

        p.wait("vector", s_g, g_iota)
        p.op("vector", lambda e: e.tensor_scalar_max(clampd[:, :], iota_d[:, :], 0.0), s_v)
        v_clamp = s_v.n

        p.wait("scalar", s_v, v_clamp)
        p.wait("scalar", s_cast, cast_bc)
        for h in range(H):
            sl = slice(h * CH, (h + 1) * CH)
            p.op("scalar", lambda e, sl=sl, h=h: e.activation(
                wmatT[:, sl], clampd[:, :], AF.Exp, scale=ld_bc[:, h:h + 1]), s_a)
        a_wpow = s_a.n
        p.wait("gpsimd", s_a, a_wpow)
        for h in range(H):
            sl = slice(h * CH, (h + 1) * CH)
            p.op("gpsimd", lambda e, sl=sl: e.affine_select(
                wmatT[:, sl], wmatT[:, sl], pattern=[[1, CH]], compare_op=ALU.is_ge,
                fill=0.0, base=-1, channel_multiplier=-1), s_g)
        g_wmask = s_g.n
        p.wait("vector", s_g, g_wmask)
        p.wait("vector", s_cast, cast_bc)
        for h in range(H):
            sl = slice(h * CH, (h + 1) * CH)
            p.op("vector", lambda e, sl=sl, h=h: e.scalar_tensor_tensor(
                wmatT[:, sl], eye[:, :], u_bc[:, h:h + 1], wmatT[:, sl], ALU.mult, ALU.add), s_v)
        v_wmat = s_v.n

        p.wait("scalar", s_cast, cast_bc)
        p.wait("scalar", s_g, g_iota)
        for pb in range(8):
            sl = slice(pb * CH, (pb + 1) * CH)
            p.op("scalar", lambda e, sl=sl, pb=pb: e.activation(
                wb_ct[:, sl], iota_f[:, :], AF.Exp, scale=ldcol8[:, pb:pb + 1]), s_a)
        a_wb = s_a.n

        for h in range(H):
            p.op("scalar", lambda e, h=h: e.activation(
                wk_cols[:, h:h + 1], rev_col[:, 0:1], AF.Exp, scale=ld_bc[:, h:h + 1]), s_a)
        a_wk = s_a.n

        p.wait("vector", s_cast, cast_bc)
        for hp in range(8):
            for half in range(2):
                rows = slice(half * 64, half * 64 + 64)
                colh = 2 * hp + half
                p.op("vector", lambda e, rows=rows, hp=hp, colh=colh: e.tensor_scalar_mul(
                    ws_tile[rows, hp * 64:(hp + 1) * 64], ones128[0:64, 0:64],
                    ws_bc[rows, colh:colh + 1]), s_v)
        p.op("vector", lambda e: e.memset(s_sb[:, :], 0.0), s_v)
        p.op("vector", lambda e: e.memset(s_bf[:, :], 0.0), s_v)
        v_consts = s_v.n

        if debug:
            p.wait("sync", s_v, s_v.n)
            p.wait("sync", s_a, s_a.n)
            p.wait("sync", s_g, s_g.n)
            p.dma("sync", s_o, dbg["d_wmat"], wmatT[:, :])
            p.dma("sync", s_o, dbg["d_wb"], wb_ct[:, :])
            p.dma("sync", s_o, dbg["d_wk"], wk_cols[:, :])
            p.dma("sync", s_o, dbg["d_ws"], ws_tile[:, :])
            p.dma("sync", s_o, dbg["d_ldbc"], ld_bc[:, :])

        p.wait("scalar", s_cast, cast_w)
        for nm in ("wr", "wk", "wv", "wg"):
            for ib in range(8):
                p.dma("scalar", s_tr, wT[nm][:, ib * C:(ib + 1) * C],
                      w_bf[nm][0:C, ib * 128:(ib + 1) * 128], transpose=True)
        tr_w = s_tr.n

        bank_evac = [0, 0]
        for s in range(NSLAB):
            t0 = s * SLAB
            p.wait("scalar", s_o, s_o.n)
            p.wait("vector", s_o, s_o.n)
            prev_mix_v = s_v.n
            p.wait("scalar", s_cast, cast_x)
            p.wait("scalar", s_v, prev_mix_v)
            for pb in range(8):
                cs = slice(pb * 128, (pb + 1) * 128)
                p.dma("scalar", s_xld, xc[:, pb * SLAB:(pb + 1) * SLAB],
                      x_bf[1 + t0:1 + t0 + SLAB, cs], transpose=True)
                p.dma("scalar", s_xld, xxc[:, pb * SLAB:(pb + 1) * SLAB],
                      x_bf[t0:t0 + SLAB, cs], transpose=True)
            xld = s_xld.n

            p.wait("vector", s_xld, xld)
            p.wait("vector", s_mm, s_mm.n)
            for mi, nm in enumerate(("r", "k", "v", "g")):
                for pb in range(8):
                    sl = slice(pb * SLAB, (pb + 1) * SLAB)
                    p.op("vector", lambda e, sl=sl, nm=nm: e.tensor_sub(
                        mixed[nm][:, sl], xc[:, sl], xxc[:, sl]), s_v)
            v_sub = s_v.n
            p.wait("vector", s_v, v_sub)
            for mi, nm in enumerate(("r", "k", "v", "g")):
                for pb in range(8):
                    sl = slice(pb * SLAB, (pb + 1) * SLAB)
                    tcol = tm_cols[:, mi * 8 + pb:mi * 8 + pb + 1]
                    p.op("vector", lambda e, sl=sl, tcol=tcol, nm=nm: e.scalar_tensor_tensor(
                        mixed[nm][:, sl], mixed[nm][:, sl], tcol, xxc[:, sl],
                        ALU.mult, ALU.add), s_v)
            v_mix = s_v.n

            def cmaj(nm_w, nm_x, dst):
                for ob in range(8):
                    bk = ob % 2
                    bank = ps_proj[bk]
                    p.wait("tensor", s_a, bank_evac[bk])
                    p.wait("tensor", s_v, v_mix)
                    p.wait("tensor", s_tr, tr_w)
                    for ib in range(8):
                        p.mm(bank[:, :],
                             wT[nm_w][:, ib * C + ob * 128: ib * C + (ob + 1) * 128],
                             mixed[nm_x][:, ib * SLAB:(ib + 1) * SLAB],
                             start=(ib == 0), stop=(ib == 7),
                             inc=(s_mm if ib == 7 else None))
                    p.wait("scalar", s_mm, s_mm.n)
                    sl = slice(ob * SLAB, (ob + 1) * SLAB)
                    p.op("scalar", lambda e, bank=bank, dst=dst, sl=sl:
                         e.copy(dst[:, sl], bank[:, :]), s_a)
                    bank_evac[bk] = s_a.n

            def tmaj(nm_w, nm_x, dst, act=None):
                for tb in range(4):
                    for oh in range(2):
                        bk = (tb * 2 + oh) % 2
                        bank = ps_proj[bk]
                        p.wait("tensor", s_a, bank_evac[bk])
                        p.wait("tensor", s_v, v_mix)
                        p.wait("tensor", s_tr, tr_w)
                        for ib in range(8):
                            p.mm(bank[:, :],
                                 mixed[nm_x][:, ib * SLAB + tb * 128: ib * SLAB + (tb + 1) * 128],
                                 wT[nm_w][:, ib * C + oh * 512: ib * C + (oh + 1) * 512],
                                 start=(ib == 0), stop=(ib == 7),
                                 inc=(s_mm if ib == 7 else None))
                        p.wait("scalar", s_mm, s_mm.n)
                        sl = slice(tb * C + oh * 512, tb * C + (oh + 1) * 512)
                        if act is None:
                            p.op("scalar", lambda e, bank=bank, sl=sl, dst=dst:
                                 e.copy(dst[:, sl], bank[:, :]), s_a)
                        else:
                            p.op("scalar", lambda e, bank=bank, sl=sl, dst=dst, act=act:
                                 e.activation(dst[:, sl], bank[:, :], act), s_a)
                        bank_evac[bk] = s_a.n

            cmaj("wr", "r", r_sl)
            cmaj("wk", "k", kcs)
            a_rk = s_a.n
            tmaj("wv", "v", vts)
            tmaj("wg", "g", g_sl, act=AF.Silu)
            a_proj = s_a.n

            p.wait("vector", s_a, a_rk)
            p.wait("vector", s_a, a_wb)
            for pb in range(8):
                for cnn in range(4):
                    sl = slice(pb * SLAB + cnn * CH, pb * SLAB + (cnn + 1) * CH)
                    wsl = slice(pb * CH, (pb + 1) * CH)
                    p.op("vector", lambda e, sl=sl, wsl=wsl: e.tensor_mul(
                        rwb_sl[:, sl], r_sl[:, sl], wb_ct[:, wsl]), s_v)
            v_rwb = s_v.n

            p.wait("sync", s_a, a_rk)
            p.dma("sync", s_kc,
                  kc_dram[0:C, t0:t0 + SLAB].rearrange("(p a) t -> a p t", a=128),
                  kcs.rearrange("a (p t) -> a p t", t=SLAB))
            kwrt = s_kc.n
            p.wait("scalar", s_kc, kwrt)
            p.wait("scalar", s_mm, s_mm.n)
            for tb in range(4):
                p.dma("scalar", s_tr, kts[:, tb * C:(tb + 1) * C],
                      kc_dram[0:C, t0 + tb * 128:t0 + (tb + 1) * 128], transpose=True)
            tr_k = s_tr.n

            if debug and s == 0:
                p.wait("sync", s_v, s_v.n)
                p.wait("sync", s_a, s_a.n)
                p.wait("sync", s_tr, s_tr.n)
                p.dma("sync", s_o, dbg["d_xc"], xc[:, :])
                p.dma("sync", s_o, dbg["d_xxc"], xxc[:, :])
                p.dma("sync", s_o, dbg["d_mixr"], mixed["r"][:, :])
                p.dma("sync", s_o, dbg["d_mixk"], mixed["k"][:, :])
                p.dma("sync", s_o, dbg["d_rsl"], r_sl[:, :])
                p.dma("sync", s_o, dbg["d_kcs"], kcs[:, :])
                p.dma("sync", s_o, dbg["d_kts"], kts[:, :])
                p.dma("sync", s_o, dbg["d_vts"], vts[:, :])
                p.wait("vector", s_o, s_o.n)
                p.wait("tensor", s_o, s_o.n)
                p.wait("sync", s_o, s_o.n)
                p.wait("scalar", s_o, s_o.n)
                p.wait("gpsimd", s_o, s_o.n)

            for cnn in range(4):
                p.wait("vector", s_tr, tr_k)
                p.wait("vector", s_a, a_wk)
                p.wait("vector", s_mm, s_mm.n)
                for h in range(H):
                    p.op("vector", lambda e, h=h, cnn=cnn: e.tensor_scalar_mul(
                        kwk[:, h * KD:(h + 1) * KD],
                        kts[:, cnn * C + h * KD:cnn * C + (h + 1) * KD],
                        wk_cols[:, h:h + 1]), s_v)
                v_kwk = s_v.n

                attw_v = {}
                for h in range(H):
                    rows, pb = _head(h)
                    rsl = slice(pb * SLAB + cnn * CH, pb * SLAB + (cnn + 1) * CH)
                    bank = ps_att[h % 2]
                    if h >= 2:
                        p.wait("tensor", s_v, attw_v[h - 2])
                    p.wait("tensor", s_a, a_rk)
                    p.wait("tensor", s_v, v_rwb)
                    p.mm(bank[:, :], kcs[rows, rsl], r_sl[rows, rsl],
                         start=True, stop=True, inc=s_mm)
                    att_mm = s_mm.n
                    p.wait("vector", s_mm, att_mm)
                    p.wait("vector", s_v, v_wmat)
                    half = h % 2
                    p.op("vector", lambda e, bank=bank, h=h, half=half: e.tensor_mul(
                        attw[:, half * CH:(half + 1) * CH], bank[:, :],
                        wmatT[:, h * CH:(h + 1) * CH]), s_v)
                    attw_v[h] = s_v.n

                    p.wait("tensor", s_v, attw_v[h])
                    vsl = slice(cnn * C + h * KD, cnn * C + (h + 1) * KD)
                    p.mm(ps_out[:, h * KD:(h + 1) * KD],
                         attw[:, half * CH:(half + 1) * CH], vts[:, vsl],
                         start=True, stop=False)
                    srows = slice((h % 2) * KD, (h % 2) * KD + KD)
                    scols = slice((h // 2) * KD, (h // 2 + 1) * KD)
                    p.mm(ps_out[:, h * KD:(h + 1) * KD],
                         rwb_sl[rows, rsl], s_bf[srows, scols],
                         start=False, stop=True, inc=s_mm)
                    p.wait("tensor", s_v, v_kwk)
                    p.mm(ps_A[srows, scols], kwk[:, h * KD:(h + 1) * KD], vts[:, vsl],
                         start=True, stop=True, inc=s_mm,
                         tile_position=(0, (h % 2) * KD))
                mm_chunk = s_mm.n

                p.wait("vector", s_mm, mm_chunk)
                p.op("vector", lambda e: e.tensor_mul(s_tmp[:, :], s_sb[:, :], ws_tile[:, :]), s_v)
                p.wait("vector", s_v, s_v.n)
                p.op("vector", lambda e: e.tensor_add(s_sb[:, :], s_tmp[:, :], ps_A[:, :]), s_v)
                v_supd = s_v.n
                p.wait("vector", s_v, v_supd)
                p.op("vector", lambda e: e.tensor_copy(s_bf[:, :], s_sb[:, :]), s_v)
                v_sbf = s_v.n

                p.wait("scalar", s_mm, mm_chunk)
                p.op("scalar", lambda e, cnn=cnn: e.mul(
                    xo_sl[:, cnn * C:(cnn + 1) * C], ps_out[:, :], 1.0 / HEAD_DIV), s_a)
                a_xo = s_a.n
                p.wait("tensor", s_a, a_xo)
                p.wait("tensor", s_v, v_sbf)

            a_slab = s_a.n
            v_slab = s_v.n
            p.wait("sync", s_a, a_slab)
            p.wait("sync", s_v, v_slab)
            p.dma("sync", s_o,
                  o_xo[t0:t0 + SLAB, :].rearrange("(tb p) c -> p tb c", p=128),
                  xo_sl.rearrange("a (tb c) -> a tb c", c=C))
            p.dma("sync", s_o,
                  o_g[t0:t0 + SLAB, :].rearrange("(tb p) c -> p tb c", p=128),
                  g_sl.rearrange("a (tb c) -> a tb c", c=C))
            p.dma("sync", s_o,
                  o_rwb.rearrange("a (p t) -> a p t", t=TOK)[:, :, t0:t0 + SLAB],
                  rwb_sl.rearrange("a (p t) -> a p t", t=SLAB))

        p.wait("sync", s_v, s_v.n)
        p.dma("sync", s_o, o_sc, s_sb[:, :])
        p.wait("sync", s_o, s_o.n)
        p.build_block()
    return nc



def build_l2():
    nc = bass.Bass("TRN2", target_bir_lowering=False)
    x_in = nc.dram_tensor("x", [TOK, C], F32, kind="ExternalInput").ap()
    xo_in = nc.dram_tensor("xo", [TOK, C], BF16, kind="ExternalInput").ap()
    rwb_in = nc.dram_tensor("rwb", [128, 8 * TOK], BF16, kind="ExternalInput").ap()
    g_in = nc.dram_tensor("g", [TOK, C], BF16, kind="ExternalInput").ap()
    sall_in = nc.dram_tensor("sall", [512, 512], F32, kind="ExternalInput").ap()
    coefk_in = nc.dram_tensor("coefk", [1, 4], F32, kind="ExternalInput").ap()
    mask_in = nc.dram_tensor("maskk", [1, 4], F32, kind="ExternalInput").ap()
    wo_in = nc.dram_tensor("wo", [C, C], F32, kind="ExternalInput").ap()
    td_in = nc.dram_tensor("td", [1, H], F32, kind="ExternalInput").ap()
    lnw_in = nc.dram_tensor("lnw", [1, C], F32, kind="ExternalInput").ap()
    lnb_in = nc.dram_tensor("lnb", [1, C], F32, kind="ExternalInput").ap()
    gw_in = nc.dram_tensor("gw", [16, C], F32, kind="ExternalInput").ap()
    lb_in = nc.dram_tensor("lb", [1, 8], F32, kind="ExternalInput").ap()

    o_xmid = nc.dram_tensor("o_xmid", [TOK, C], F32, kind="ExternalOutput").ap()
    o_m = nc.dram_tensor("o_m", [TOK, 16], F32, kind="ExternalOutput").ap()
    o_stats = nc.dram_tensor("o_stats", [1, 16], F32, kind="ExternalOutput").ap()

    wo_bf = nc.dram_tensor("wo_bf", [C, C], BF16, kind="Internal").ap()
    gw_bf = nc.dram_tensor("gw_bf", [16, C], BF16, kind="Internal").ap()
    yg_dram = nc.dram_tensor("yg_dram", [TOK, C], BF16, kind="Internal").ap()
    xmb_dram = nc.dram_tensor("xmb_dram", [TOK, C], BF16, kind="Internal").ap()
    sc2 = nc.dram_tensor("sc2", [1, 128], F32, kind="Internal").ap()

    with ExitStack() as ctx:
        p = Prog(nc, ctx)
        sb = lambda n, s, d=F32: ctx.enter_context(nc.sbuf_tensor(n, s, d))
        psl = lambda n, s: ctx.enter_context(nc.psum_tensor(n, s, F32))

        rwb = sb("rwb_s", [128, 8 * TOK], BF16)
        xo_res = sb("xo_res", [128, NCH * C], BF16)
        g_res = sb("g_res", [128, NCH * C], BF16)
        woT = sb("woT", [128, 8 * C], BF16)
        s_tiles = [sb("s_t%d" % i, [128, 512]) for i in range(4)]
        s_e = sb("s_e", [128, 512])
        s_ebf = sb("s_ebf", [128, 512], BF16)
        ws_tile = sb("ws_tile", [128, 512], F32)
        y_t = sb("y_t", [128, C])
        sq_t = sb("sq_t", [128, C])
        gn16 = sb("gn16", [128, 16])
        gn16b = sb("gn16b", [128, 16])
        rstd16 = sb("rstd16", [128, 16])
        yg_sl = sb("yg_sl", [128, 4 * C], BF16)
        ygc = sb("ygc", [128, 8 * SLAB], BF16)
        x_sl = sb("x_sl", [128, 4 * C], F32)
        xm_t = sb("xm_t", [128, C], F32)
        xmb_sl = sb("xmb_sl", [128, 4 * C], BF16)
        col1 = sb("col1", [128, 1])
        col2 = sb("col2", [128, 1])
        lnw_bc = sb("lnw_bc", [128, C], F32)
        lnb_bc = sb("lnb_bc", [128, C], F32)
        td_row = sb("td_row", [1, H], F32)
        ld_row = sb("ld_row", [1, H], F32)
        ws_row = sb("ws_row", [1, H], F32)
        ws_bc = sb("ws_bc", [128, H], F32)
        coefk_sb = sb("coefk_sb", [1, 4], F32)
        mask_sb = sb("mask_sb", [1, 4], F32)
        coefrow = sb("coefrow", [1, 4 * H], F32)
        coef_bc = sb("coef_bc", [128, 4 * H], F32)
        ones64 = sb("ones64", [64, 64], F32)
        gwT = sb("gwT", [128, 8 * 16], BF16)
        xmc_t = sb("xmc_t", [128, 8 * 128], BF16)
        iota8 = sb("iota8", [128, 8], F32)
        lb_bc = sb("lb_bc", [128, 8], F32)
        lg = sb("lg", [128, 8], F32)
        scores = sb("scores", [128, 8], F32)
        mx8 = sb("mx8", [128, 8], F32)
        mi8 = sb("mi8", [128, 8], U32)
        mif = sb("mif", [128, 8], F32)
        eq1 = sb("eq1", [128, 8], F32)
        eq2 = sb("eq2", [128, 8], F32)
        mw = sb("mw", [128, 16], F32)
        ones_col = sb("ones_col", [128, 1], F32)
        den = sb("den", [128, 1], F32)
        w1c = sb("w1c", [128, 1], F32)
        w2c = sb("w2c", [128, 1], F32)
        st_sb = sb("st_sb", [1, 16], F32)

        ps_corr = psl("ps_corr", [128, C])
        ps_proj = [psl("ps_proj%d" % i, [128, SLAB]) for i in range(2)]
        ps_g = psl("ps_g", [128, 8])
        ps_st = psl("ps_st", [1, 16])

        s_cast = p.sem("s_cast")
        s_ld = p.sem("s_ld")
        s_tr = p.sem("s_tr")
        s_kd = p.sem("s_kd")
        s_mm = p.sem("s_mm")
        s_v = p.sem("s_v")
        s_a = p.sem("s_a")
        s_g = p.sem("s_g")
        s_o = p.sem("s_o")

        p.dma("gpsimd", s_cast, wo_bf, wo_in)
        p.dma("gpsimd", s_cast, gw_bf, gw_in)
        cast_w = s_cast.n
        p.dma("gpsimd", s_cast, lnw_bc[:, :], lnw_in[0:1, :].partition_broadcast(128))
        p.dma("gpsimd", s_cast, lnb_bc[:, :], lnb_in[0:1, :].partition_broadcast(128))
        p.dma("gpsimd", s_cast, lb_bc[:, :], lb_in[0:1, :].partition_broadcast(128))
        cast_ln = s_cast.n

        p.dma("sync", s_ld, rwb[:, :], rwb_in)
        p.dma("sync", s_ld,
              xo_res.rearrange("a (tb c) -> a tb c", c=C),
              xo_in.rearrange("(tb p) c -> p tb c", p=128))
        p.dma("sync", s_ld,
              g_res.rearrange("a (tb c) -> a tb c", c=C),
              g_in.rearrange("(tb p) c -> p tb c", p=128))
        for i in range(4):
            p.dma("sync", s_ld, s_tiles[i][:, :], sall_in[i * 128:(i + 1) * 128, :])
        p.dma("sync", s_ld, td_row[0:1, :], td_in)
        p.dma("sync", s_ld, coefk_sb[0:1, :], coefk_in)
        p.dma("sync", s_ld, mask_sb[0:1, :], mask_in)
        ld0 = s_ld.n

        p.wait("scalar", s_ld, ld0)
        p.op("scalar", lambda e: e.activation(ld_row[0:1, :], td_row[0:1, :], AF.Exp), s_a)
        p.wait("vector", s_a, s_a.n)
        p.op("vector", lambda e: e.tensor_scalar_mul(ld_row[0:1, :], ld_row[0:1, :], -1.0), s_v)
        v_ld = s_v.n
        p.wait("scalar", s_v, v_ld)
        p.op("scalar", lambda e: e.activation(ws_row[0:1, :], ld_row[0:1, :], AF.Exp,
                                              scale=float(CH)), s_a)
        for qq in range(4):
            p.op("scalar", lambda e, qq=qq: e.activation(
                coefrow[0:1, qq * H:(qq + 1) * H], ld_row[0:1, :], AF.Exp,
                scale=coefk_sb[0:1, qq:qq + 1]), s_a)
        a_coef = s_a.n
        p.wait("vector", s_a, a_coef)
        for qq in range(4):
            p.op("vector", lambda e, qq=qq: e.tensor_scalar_mul(
                coefrow[0:1, qq * H:(qq + 1) * H], coefrow[0:1, qq * H:(qq + 1) * H],
                mask_sb[0:1, qq:qq + 1]), s_v)
        v_coef = s_v.n
        p.wait("sync", s_a, a_coef)
        p.dma("sync", s_kd, sc2[0:1, 0:H], ws_row[0:1, :])
        p.wait("sync", s_v, v_coef)
        p.dma("sync", s_kd, sc2[0:1, 16:16 + 4 * H], coefrow[0:1, :])
        kd0 = s_kd.n
        p.wait("gpsimd", s_kd, kd0)
        p.dma("gpsimd", s_cast, ws_bc[:, :], sc2[0:1, 0:H].partition_broadcast(128))
        p.dma("gpsimd", s_cast, coef_bc[:, :], sc2[0:1, 16:16 + 4 * H].partition_broadcast(128))
        cast_bc = s_cast.n

        p.op("gpsimd", lambda e: e.memset(ones64[:, :], 1.0), s_g)
        p.op("gpsimd", lambda e: e.iota(iota8[:, :], pattern=[[1, 8]], base=0,
                                        channel_multiplier=0,
                                        allow_small_or_imprecise_dtypes=True), s_g)
        p.op("gpsimd", lambda e: e.memset(ones_col[:, :], 1.0), s_g)
        g_0 = s_g.n

        p.wait("vector", s_cast, cast_bc)
        p.wait("vector", s_g, g_0)
        for hp in range(8):
            for half in range(2):
                rows = slice(half * 64, half * 64 + 64)
                colh = 2 * hp + half
                p.op("vector", lambda e, rows=rows, hp=hp, colh=colh: e.tensor_scalar_mul(
                    ws_tile[rows, hp * 64:(hp + 1) * 64], ones64[:, :],
                    ws_bc[rows, colh:colh + 1]), s_v)
        p.op("vector", lambda e: e.memset(s_e[:, :], 0.0), s_v)
        p.wait("vector", s_ld, ld0)
        p.wait("vector", s_v, s_v.n)
        for qq in range(4):
            for hp in range(8):
                for half in range(2):
                    rows = slice(half * 64, half * 64 + 64)
                    cols = slice(hp * 64, (hp + 1) * 64)
                    colh = qq * H + 2 * hp + half
                    if qq > 0:
                        p.wait("vector", s_v, s_v.n)
                    p.op("vector", lambda e, qq=qq, rows=rows, cols=cols, colh=colh:
                         e.scalar_tensor_tensor(s_e[rows, cols], s_tiles[qq][rows, cols],
                                                coef_bc[rows, colh:colh + 1], s_e[rows, cols],
                                                ALU.mult, ALU.add), s_v)
        p.wait("vector", s_v, s_v.n)
        p.op("vector", lambda e: e.tensor_copy(s_ebf[:, :], s_e[:, :]), s_v)
        v_se = s_v.n

        p.wait("scalar", s_cast, cast_w)
        for ib in range(8):
            p.dma("scalar", s_tr, woT[:, ib * C:(ib + 1) * C],
                  wo_bf[0:C, ib * 128:(ib + 1) * 128], transpose=True)
        for ib in range(8):
            p.dma("scalar", s_tr, gwT[:, ib * 16:(ib + 1) * 16],
                  gw_bf[0:16, ib * 128:(ib + 1) * 128], transpose=True)
        tr_w = s_tr.n

        for j in range(NCH):
            s_slab = j // 4
            cnn = j % 4
            p.wait("tensor", s_v, v_se)
            p.wait("tensor", s_ld, ld0)
            p.wait("tensor", s_v, s_v.n)
            for h in range(H):
                rows, pb = _head(h)
                rsl = slice(pb * TOK + j * CH, pb * TOK + (j + 1) * CH)
                srows = slice((h % 2) * KD, (h % 2) * KD + KD)
                scols = slice((h // 2) * KD, (h // 2 + 1) * KD)
                p.mm(ps_corr[:, h * KD:(h + 1) * KD], rwb[rows, rsl], s_ebf[srows, scols],
                     start=True, stop=True, inc=s_mm)
            mm_c = s_mm.n
            p.wait("vector", s_mm, mm_c)
            p.op("vector", lambda e, j=j: e.scalar_tensor_tensor(
                y_t[:, :], ps_corr[:, :], 1.0 / HEAD_DIV, xo_res[:, j * C:(j + 1) * C],
                ALU.mult, ALU.add), s_v)
            v_y = s_v.n
            p.wait("vector", s_v, v_y)
            p.op("vector", lambda e: e.tensor_mul(s_e[:, :], s_e[:, :], ws_tile[:, :]), s_v)
            p.wait("vector", s_v, s_v.n)
            p.op("vector", lambda e: e.tensor_copy(s_ebf[:, :], s_e[:, :]), s_v)
            v_se = s_v.n

            p.wait("vector", s_v, v_y)
            p.op("vector", lambda e: e.tensor_reduce(
                gn16[:, :], y_t.rearrange("p (h k) -> p h k", k=KD), axis=AX.X, op=ALU.add), s_v)
            p.wait("vector", s_v, s_v.n)
            p.op("vector", lambda e: e.tensor_scalar_mul(gn16[:, :], gn16[:, :], -1.0 / KD), s_v)
            v_nm = s_v.n
            p.wait("vector", s_v, v_nm)
            for h in range(H):
                sl = slice(h * KD, (h + 1) * KD)
                p.op("vector", lambda e, sl=sl, h=h: e.tensor_scalar_add(
                    y_t[:, sl], y_t[:, sl], gn16[:, h:h + 1]), s_v)
            v_yc = s_v.n
            p.wait("vector", s_v, v_yc)
            p.op("vector", lambda e: e.tensor_mul(sq_t[:, :], y_t[:, :], y_t[:, :]), s_v)
            p.wait("vector", s_v, s_v.n)
            p.op("vector", lambda e: e.tensor_reduce(
                gn16b[:, :], sq_t.rearrange("p (h k) -> p h k", k=KD), axis=AX.X, op=ALU.add), s_v)
            p.wait("vector", s_v, s_v.n)
            p.op("vector", lambda e: e.tensor_scalar(
                gn16b[:, :], gn16b[:, :], 1.0 / KD, EPS_GN, ALU.mult, ALU.add), s_v)
            p.wait("vector", s_v, s_v.n)
            p.op("vector", lambda e: e.reciprocal(gn16b[:, :], gn16b[:, :]), s_v)
            v_rcp = s_v.n
            p.wait("scalar", s_v, v_rcp)
            p.op("scalar", lambda e: e.activation(rstd16[:, :], gn16b[:, :], AF.Sqrt), s_a)
            a_rstd = s_a.n
            p.wait("vector", s_a, a_rstd)
            for h in range(H):
                sl = slice(h * KD, (h + 1) * KD)
                p.op("vector", lambda e, sl=sl, h=h: e.tensor_scalar_mul(
                    y_t[:, sl], y_t[:, sl], rstd16[:, h:h + 1]), s_v)
            p.wait("vector", s_v, s_v.n)
            p.wait("vector", s_cast, cast_ln)
            p.op("vector", lambda e: e.tensor_mul(y_t[:, :], y_t[:, :], lnw_bc[:, :]), s_v)
            p.wait("vector", s_v, s_v.n)
            p.op("vector", lambda e: e.tensor_add(y_t[:, :], y_t[:, :], lnb_bc[:, :]), s_v)
            p.wait("vector", s_v, s_v.n)
            p.op("vector", lambda e, j=j, cnn=cnn: e.tensor_mul(
                yg_sl[:, cnn * C:(cnn + 1) * C], y_t[:, :], g_res[:, j * C:(j + 1) * C]), s_v)
            v_yg = s_v.n

            if cnn == 3:
                t0 = s_slab * SLAB
                p.wait("sync", s_v, v_yg)
                p.dma("sync", s_kd,
                      yg_dram[t0:t0 + SLAB, :].rearrange("(tb p) c -> p tb c", p=128),
                      yg_sl.rearrange("a (tb c) -> a tb c", c=C))
                kw1 = s_kd.n
                p.wait("scalar", s_kd, kw1)
                p.wait("scalar", s_mm, s_mm.n)
                for pb in range(8):
                    cs = slice(pb * 128, (pb + 1) * 128)
                    p.dma("scalar", s_tr, ygc[:, pb * SLAB:(pb + 1) * SLAB],
                          yg_dram[t0:t0 + SLAB, cs], transpose=True)
                tr_yg = s_tr.n
                p.wait("sync", s_v, s_v.n)
                p.dma("sync", s_ld,
                      x_sl.rearrange("p (tb c) -> p tb c", c=C),
                      x_in[t0:t0 + SLAB, :].rearrange("(tb p) c -> p tb c", p=128))
                x_ld = s_ld.n

                p.wait("vector", s_kd, s_kd.n)
                p.wait("vector", s_o, s_o.n)
                for tb in range(4):
                    for oh in range(2):
                        bank = ps_proj[(tb * 2 + oh) % 2]
                        p.wait("tensor", s_v, s_v.n)
                        p.wait("tensor", s_tr, tr_yg)
                        for ib in range(8):
                            p.mm(bank[:, :],
                                 ygc[:, ib * SLAB + tb * 128: ib * SLAB + (tb + 1) * 128],
                                 woT[:, ib * C + oh * 512: ib * C + (oh + 1) * 512],
                                 start=(ib == 0), stop=(ib == 7),
                                 inc=(s_mm if ib == 7 else None))
                        p.wait("vector", s_mm, s_mm.n)
                        p.wait("vector", s_ld, x_ld)
                        p.wait("vector", s_o, s_o.n)
                        sl = slice(tb * C + oh * 512, tb * C + (oh + 1) * 512)
                        osl = slice(oh * 512, (oh + 1) * 512)
                        p.op("vector", lambda e, bank=bank, sl=sl, osl=osl: e.tensor_add(
                            xm_t[:, osl], x_sl[:, sl], bank[:, :]), s_v)
                    v_r1 = s_v.n
                    p.wait("vector", s_v, v_r1)
                    p.op("vector", lambda e: e.tensor_reduce(
                        col1[:, 0:1], xm_t[:, :], axis=AX.X, op=ALU.add), s_v)
                    p.wait("vector", s_v, s_v.n)
                    p.op("vector", lambda e: e.tensor_scalar_mul(col1[:, 0:1], col1[:, 0:1], -1.0 / C), s_v)
                    p.wait("vector", s_v, s_v.n)
                    p.op("vector", lambda e: e.tensor_scalar_add(xm_t[:, :], xm_t[:, :], col1[:, 0:1]), s_v)
                    p.wait("vector", s_v, s_v.n)
                    p.op("vector", lambda e: e.tensor_mul(sq_t[:, :], xm_t[:, :], xm_t[:, :]), s_v)
                    p.wait("vector", s_v, s_v.n)
                    p.op("vector", lambda e: e.tensor_reduce(
                        col2[:, 0:1], sq_t[:, :], axis=AX.X, op=ALU.add), s_v)
                    p.wait("vector", s_v, s_v.n)
                    p.op("vector", lambda e: e.tensor_scalar(
                        col2[:, 0:1], col2[:, 0:1], 1.0 / C, EPS_LN, ALU.mult, ALU.add), s_v)
                    p.wait("vector", s_v, s_v.n)
                    p.op("vector", lambda e: e.reciprocal(col2[:, 0:1], col2[:, 0:1]), s_v)
                    v_l1 = s_v.n
                    p.wait("scalar", s_v, v_l1)
                    p.op("scalar", lambda e: e.activation(col2[:, 0:1], col2[:, 0:1], AF.Sqrt), s_a)
                    a_l1 = s_a.n
                    p.wait("vector", s_a, a_l1)
                    p.op("vector", lambda e: e.tensor_scalar_mul(xm_t[:, :], xm_t[:, :], col2[:, 0:1]), s_v)
                    p.wait("vector", s_v, s_v.n)
                    p.op("vector", lambda e, tb=tb: e.tensor_copy(
                        xmb_sl[:, tb * C:(tb + 1) * C], xm_t[:, :]), s_v)
                    v_xm = s_v.n
                    p.wait("sync", s_v, v_xm)
                    p.dma("sync", s_o, o_xmid[t0 + tb * 128:t0 + (tb + 1) * 128, :], xm_t[:, :])
                p.wait("sync", s_v, s_v.n)
                p.dma("sync", s_kd,
                      xmb_dram[t0:t0 + SLAB, :].rearrange("(tb p) c -> p tb c", p=128),
                      xmb_sl.rearrange("a (tb c) -> a tb c", c=C))

        xmb_done = s_kd.n

        p.op("gpsimd", lambda e: e.memset(mw[:, :], 0.0), s_g)
        g_mw = s_g.n
        for ti in range(NCH):
            t0 = ti * CH
            p.wait("scalar", s_kd, xmb_done)
            p.wait("scalar", s_mm, s_mm.n)
            for ib in range(8):
                cs = slice(ib * 128, (ib + 1) * 128)
                p.dma("scalar", s_tr, xmc_t[:, ib * 128:(ib + 1) * 128],
                      xmb_dram[t0:t0 + CH, cs], transpose=True)
            tr_xm = s_tr.n
            p.wait("tensor", s_tr, tr_xm)
            p.wait("tensor", s_tr, tr_w)
            p.wait("tensor", s_v, s_v.n)
            for ib in range(8):
                p.mm(ps_g[:, :], xmc_t[:, ib * 128:(ib + 1) * 128], gwT[:, ib * 16:ib * 16 + 8],
                     start=(ib == 0), stop=(ib == 7), inc=(s_mm if ib == 7 else None))
            p.wait("vector", s_mm, s_mm.n)
            p.wait("vector", s_cast, cast_ln)
            p.op("vector", lambda e: e.tensor_add(lg[:, :], ps_g[:, :], lb_bc[:, :]), s_v)
            v_lg = s_v.n
            p.wait("scalar", s_v, v_lg)
            p.op("scalar", lambda e: e.activation(scores[:, :], lg[:, :], AF.Sigmoid), s_a)
            a_sc = s_a.n
            p.wait("vector", s_a, a_sc)
            p.op("vector", lambda e: e.max(mx8[:, :], scores[:, :]), s_v)
            p.wait("vector", s_v, s_v.n)
            p.op("vector", lambda e: e.max_index(mi8[:, :], mx8[:, :], scores[:, :]), s_v)
            p.wait("vector", s_v, s_v.n)
            p.op("vector", lambda e: e.tensor_copy(mif[:, :], mi8[:, :]), s_v)
            p.wait("vector", s_v, s_v.n)
            p.op("vector", lambda e: e.scalar_tensor_tensor(
                den[:, 0:1], mx8[:, 0:1], 1e-8, mx8[:, 1:2], ALU.add, ALU.add), s_v)
            p.wait("vector", s_v, s_v.n)
            p.op("vector", lambda e: e.reciprocal(den[:, 0:1], den[:, 0:1]), s_v)
            p.wait("vector", s_v, s_v.n)
            p.op("vector", lambda e: e.tensor_mul(w1c[:, 0:1], mx8[:, 0:1], den[:, 0:1]), s_v)
            p.op("vector", lambda e: e.tensor_mul(w2c[:, 0:1], mx8[:, 1:2], den[:, 0:1]), s_v)
            p.wait("vector", s_v, s_v.n)
            p.wait("vector", s_g, g_mw)
            p.op("vector", lambda e: e.tensor_scalar(
                eq1[:, :], iota8[:, :], mif[:, 0:1], None, ALU.is_equal), s_v)
            p.op("vector", lambda e: e.tensor_scalar(
                eq2[:, :], iota8[:, :], mif[:, 1:2], None, ALU.is_equal), s_v)
            p.wait("vector", s_v, s_v.n)
            p.wait("vector", s_mm, s_mm.n)
            p.wait("vector", s_o, s_o.n)
            p.op("vector", lambda e: e.tensor_add(mw[:, 0:8], eq1[:, :], eq2[:, :]), s_v)
            p.wait("vector", s_v, s_v.n)
            p.op("vector", lambda e: e.tensor_scalar_mul(mw[:, 8:16], eq1[:, :], w1c[:, 0:1]), s_v)
            p.wait("vector", s_v, s_v.n)
            p.op("vector", lambda e: e.scalar_tensor_tensor(
                mw[:, 8:16], eq2[:, :], w2c[:, 0:1], mw[:, 8:16], ALU.mult, ALU.add), s_v)
            v_mw = s_v.n
            p.wait("tensor", s_v, v_mw)
            p.wait("tensor", s_g, g_0)
            p.mm(ps_st[0:1, :], ones_col[:, 0:1], mw[:, :],
                 start=(ti == 0), stop=(ti == NCH - 1), inc=s_mm)
            p.wait("sync", s_v, v_mw)
            p.dma("sync", s_o, o_m[t0:t0 + CH, :], mw[:, 0:16])
        p.wait("scalar", s_mm, s_mm.n)
        p.op("scalar", lambda e: e.copy(st_sb[0:1, :], ps_st[0:1, :]), s_a)
        p.wait("sync", s_a, s_a.n)
        p.dma("sync", s_o, o_stats, st_sb[0:1, :])
        p.wait("sync", s_o, s_o.n)
        p.build_block()
    return nc



def build_l3():
    nc = bass.Bass("TRN2", target_bir_lowering=False)
    xm_in = nc.dram_tensor("xm", [TOK, C], F32, kind="ExternalInput").ap()
    m_in = nc.dram_tensor("m", [TOK, 128], F32, kind="ExternalInput").ap()
    stats_in = nc.dram_tensor("stats", [8, 16], F32, kind="ExternalInput").ap()
    eg_in = nc.dram_tensor("eg", [NEXP, EI, C], F32, kind="ExternalInput").ap()
    eu_in = nc.dram_tensor("eu", [NEXP, EI, C], F32, kind="ExternalInput").ap()
    ed_in = nc.dram_tensor("ed", [NEXP, C, EI], F32, kind="ExternalInput").ap()
    sg_in = nc.dram_tensor("sg", [SI, C], F32, kind="ExternalInput").ap()
    su_in = nc.dram_tensor("su", [SI, C], F32, kind="ExternalInput").ap()
    sd_in = nc.dram_tensor("sd", [C, SI], F32, kind="ExternalInput").ap()

    o_out = nc.dram_tensor("o_out", [TOK, C], F32, kind="ExternalOutput").ap()

    xm_bf = nc.dram_tensor("xm_bf", [TOK, C], BF16, kind="Internal").ap()
    m_bf = nc.dram_tensor("m_bf", [TOK, 128], BF16, kind="Internal").ap()
    eg_bf = nc.dram_tensor("eg_bf", [NEXP, EI, C], BF16, kind="Internal").ap()
    eu_bf = nc.dram_tensor("eu_bf", [NEXP, EI, C], BF16, kind="Internal").ap()
    ed_bf = nc.dram_tensor("ed_bf", [NEXP, C, EI], BF16, kind="Internal").ap()
    sg_bf = nc.dram_tensor("sg_bf", [SI, C], BF16, kind="Internal").ap()
    su_bf = nc.dram_tensor("su_bf", [SI, C], BF16, kind="Internal").ap()
    sd_bf = nc.dram_tensor("sd_bf", [C, SI], BF16, kind="Internal").ap()
    selwm_dram = nc.dram_tensor("selwm_dram", [16, TOK], BF16, kind="Internal").ap()
    sc3 = nc.dram_tensor("sc3", [1, 16], F32, kind="Internal").ap()

    with ExitStack() as ctx:
        p = Prog(nc, ctx)
        sb = lambda n, s, d=F32: ctx.enter_context(nc.sbuf_tensor(n, s, d))
        psl = lambda n, s: ctx.enter_context(nc.psum_tensor(n, s, F32))

        xmc = sb("xmc", [128, 8 * TOK], BF16)
        he_buf = [sb("he%d" % e, [128, 4 * SLAB], BF16) for e in range(NEXP)]
        hsh = sb("hsh", [128, 8 * SLAB], BF16)
        wg_buf = [sb("wg_buf0", [128, 8 * 512], BF16), sb("wg_buf1", [128, 8 * 512], BF16),
                  sb("wg_bufS", [128, 8 * C], BF16)]
        wu_buf = [sb("wu_buf0", [128, 8 * 512], BF16), sb("wu_buf1", [128, 8 * 512], BF16),
                  sb("wu_bufS", [128, 8 * C], BF16)]
        wd_buf = [sb("wd_buf0", [128, 4 * C], BF16), sb("wd_buf1", [128, 4 * C], BF16),
                  sb("wd_bufS", [128, 8 * C], BF16)]
        a_sl = sb("a_sl", [128, 8 * SLAB], BF16)
        u_sl = sb("u_sl", [128, 8 * SLAB], BF16)
        selwm_bc = sb("selwm_bc", [128, SLAB], BF16)
        xm_sl = sb("xm_sl", [128, 4 * C], F32)
        r2_t = sb("r2_t", [128, C], F32)
        col1 = sb("col1", [128, 1], F32)
        col2 = sb("col2", [128, 1], F32)
        st_in = sb("st_in", [8, 16], F32)
        st_sb = sb("st_sb", [1, 16], F32)
        wm_row = sb("wm_row", [1, 8], F32)
        wmcol = sb("wmcol", [16, 1], F32)
        ones8 = sb("ones8", [8, 1], F32)

        ps_dn = [psl("ps_dn%d" % i, [128, C]) for i in range(4)]

        s_cast = p.sem("s_cast")
        s_csel = p.sem("s_csel")
        s_gu = [p.sem("s_gu%d" % i) for i in range(3)]
        s_wd = [p.sem("s_wd%d" % i) for i in range(3)]
        s_lds = p.sem("s_lds")
        s_ld = p.sem("s_ld")
        s_trm = p.sem("s_trm")
        s_trx = p.sem("s_trx")
        s_tr = p.sem("s_tr")
        s_trd = p.sem("s_trd")
        s_kd = p.sem("s_kd")
        s_mm = p.sem("s_mm")
        s_v = p.sem("s_v")
        s_a = p.sem("s_a")
        s_g = p.sem("s_g")
        s_o = p.sem("s_o")

        p.dma("gpsimd", s_cast, xm_bf, xm_in)
        p.dma("gpsimd", s_cast, m_bf, m_in)
        cast_xm = s_cast.n
        for e_ in range(NEXP):
            p.dma("gpsimd", s_cast, eg_bf[e_], eg_in[e_])
            p.dma("gpsimd", s_cast, eu_bf[e_], eu_in[e_])
            p.dma("gpsimd", s_cast, ed_bf[e_], ed_in[e_])
        p.dma("gpsimd", s_cast, sg_bf, sg_in)
        p.dma("gpsimd", s_cast, su_bf, su_in)
        p.dma("gpsimd", s_cast, sd_bf, sd_in)
        cast_all = s_cast.n

        p.dma("sync", s_lds, st_in[:, :], stats_in)
        ld_st = s_lds.n
        p.op("gpsimd", lambda e: e.memset(ones8[:, :], 1.0), s_g)
        p.op("gpsimd", lambda e: e.memset(wmcol[:, :], 0.0), s_g)
        g_0 = s_g.n

        p.wait("tensor", s_lds, ld_st)
        p.wait("tensor", s_g, g_0)
        p.mm(ps_dn[3][0:1, 0:16], ones8[:, 0:1], st_in[:, :], start=True, stop=True, inc=s_mm)
        p.wait("scalar", s_mm, s_mm.n)
        p.op("scalar", lambda e: e.copy(st_sb[0:1, :], ps_dn[3][0:1, 0:16]), s_a)
        a_st = s_a.n
        p.wait("vector", s_a, a_st)
        p.op("vector", lambda e: e.tensor_scalar_max(st_sb[0:1, 0:8], st_sb[0:1, 0:8], 1.0), s_v)
        p.wait("vector", s_v, s_v.n)
        p.op("vector", lambda e: e.reciprocal(st_sb[0:1, 0:8], st_sb[0:1, 0:8]), s_v)
        p.wait("vector", s_v, s_v.n)
        p.op("vector", lambda e: e.tensor_mul(wm_row[0:1, :], st_sb[0:1, 8:16], st_sb[0:1, 0:8]), s_v)
        v_wm = s_v.n
        p.wait("sync", s_v, v_wm)
        p.dma("sync", s_kd, sc3[0:1, 0:8], wm_row[0:1, :])
        kd_wm = s_kd.n
        p.wait("sync", s_kd, kd_wm)
        p.wait("sync", s_g, g_0)
        p.dma("sync", s_kd, wmcol[0:8, 0:1], sc3[0:1, 0:8].rearrange("a b -> b a"),
              allow_slow_non_contiguous=True)
        kd_wc = s_kd.n

        mTv = a_sl.rearrange("p (a b) -> p (a b)") if False else a_sl
        p.wait("scalar", s_cast, cast_xm)
        p.dma("scalar", s_trm, mTv[:, 0:TOK], m_bf[0:TOK, 0:128], transpose=True)
        tr_m = s_trm.n
        p.wait("vector", s_trm, tr_m)
        p.wait("vector", s_kd, kd_wc)
        p.op("vector", lambda e: e.tensor_scalar_mul(mTv[0:16, 0:TOK], mTv[0:16, 0:TOK], wmcol[0:16, 0:1]), s_v)
        v_swm = s_v.n
        p.wait("sync", s_v, v_swm)
        p.dma("sync", s_kd, selwm_dram, mTv[0:16, 0:TOK])
        kd_swm = s_kd.n

        for pb in range(8):
            cs = slice(pb * 128, (pb + 1) * 128)
            p.dma("scalar", s_trx, xmc[:, pb * TOK:(pb + 1) * TOK],
                  xm_bf[0:TOK, cs], transpose=True)
        tr_xm = s_trx.n

        gu_used = [0, 0, 0]
        gu_tr = [0, 0, 0]
        wd_used = [0, 0, 0]
        wd_tr = [0, 0, 0]
        NE = NEXP + 1

        def _bi(ex_abs):
            return 2 if (ex_abs % NE) == NEXP else (ex_abs % 2)

        def load_gu(ex_abs):
            ex = ex_abs % NE
            bi = _bi(ex_abs)
            shared = (ex == NEXP)
            nob = 8 if shared else 4
            g_src = sg_bf if shared else eg_bf[ex]
            u_src = su_bf if shared else eu_bf[ex]
            p.wait("scalar", s_cast, cast_all)
            p.wait("scalar", s_mm, gu_used[bi])
            for ib in range(8):
                p.dma("scalar", s_gu[bi], wg_buf[bi][:, ib * (nob * 128):(ib + 1) * (nob * 128)],
                      g_src[0:nob * 128, ib * 128:(ib + 1) * 128], transpose=True)
                p.dma("scalar", s_gu[bi], wu_buf[bi][:, ib * (nob * 128):(ib + 1) * (nob * 128)],
                      u_src[0:nob * 128, ib * 128:(ib + 1) * 128], transpose=True)
            gu_tr[bi] = s_gu[bi].n

        def load_wd(ex_abs):
            ex = ex_abs % NE
            bi = _bi(ex_abs)
            shared = (ex == NEXP)
            nib = 8 if shared else 4
            d_src = sd_bf if shared else ed_bf[ex]
            p.wait("scalar", s_cast, cast_all)
            p.wait("scalar", s_mm, wd_used[bi])
            for ib in range(nib):
                p.dma("scalar", s_wd[bi], wd_buf[bi][:, ib * C:(ib + 1) * C],
                      d_src[0:C, ib * 128:(ib + 1) * 128], transpose=True)
            wd_tr[bi] = s_wd[bi].n

        load_gu(0)
        load_gu(1)
        p.wait("scalar", s_kd, s_kd.n)
        for tau in range(4):
            t0 = tau * SLAB
            p.wait("sync", s_v, s_v.n)
            p.dma("sync", s_ld,
                  xm_sl.rearrange("p (tb c) -> p tb c", c=C),
                  xm_in[t0:t0 + SLAB, :].rearrange("(tb p) c -> p tb c", p=128))
            x_ld = s_ld.n

            for ex in range(NE):
                ex_abs = tau * NE + ex
                bi = _bi(ex_abs)
                shared = (ex == NEXP)
                nob = 8 if shared else 4
                if not shared:
                    p.wait("gpsimd", s_kd, s_kd.n)
                    p.wait("gpsimd", s_v, s_v.n)
                    p.dma("gpsimd", s_csel, selwm_bc[:, :],
                          selwm_dram[ex:ex + 1, t0:t0 + SLAB].partition_broadcast(128))
                sel_cast = s_csel.n

                for ob in range(nob):
                    bank = ps_dn[0][:, 0:512] if (ob % 2 == 0) else ps_dn[1][:, 0:512]
                    p.wait("tensor", s_v, s_v.n)
                    p.wait("tensor", s_a, s_a.n)
                    p.wait("tensor", s_gu[bi], gu_tr[bi])
                    p.wait("tensor", s_trx, tr_xm)
                    for ib in range(8):
                        p.mm(bank,
                             wg_buf[bi][:, ib * (nob * 128) + ob * 128: ib * (nob * 128) + (ob + 1) * 128],
                             xmc[:, ib * TOK + t0: ib * TOK + t0 + SLAB],
                             start=(ib == 0), stop=(ib == 7),
                             inc=(s_mm if ib == 7 else None))
                    p.wait("scalar", s_mm, s_mm.n)
                    p.op("scalar", lambda e, bank=bank, ob=ob: e.activation(
                        a_sl[:, ob * SLAB:(ob + 1) * SLAB], bank, AF.Silu), s_a)
                    bank2 = ps_dn[0][:, 512:1024] if (ob % 2 == 0) else ps_dn[1][:, 512:1024]
                    p.wait("tensor", s_a, s_a.n)
                    for ib in range(8):
                        p.mm(bank2,
                             wu_buf[bi][:, ib * (nob * 128) + ob * 128: ib * (nob * 128) + (ob + 1) * 128],
                             xmc[:, ib * TOK + t0: ib * TOK + t0 + SLAB],
                             start=(ib == 0), stop=(ib == 7),
                             inc=(s_mm if ib == 7 else None))
                    p.wait("scalar", s_mm, s_mm.n)
                    p.op("scalar", lambda e, bank2=bank2, ob=ob: e.copy(
                        u_sl[:, ob * SLAB:(ob + 1) * SLAB], bank2), s_a)
                gu_used[bi] = s_mm.n
                a_gu = s_a.n
                if ex_abs + 2 < 4 * NE:
                    load_gu(ex_abs + 2)
                if tau == 0 and ex == NE - 2:
                    load_wd(0)
                if tau == 0 and ex == NE - 1:
                    load_wd(1)
                p.wait("vector", s_a, a_gu)
                p.wait("vector", s_csel, sel_cast)
                dst = hsh if shared else he_buf[ex]
                for ob in range(nob):
                    sl = slice(ob * SLAB, (ob + 1) * SLAB)
                    p.op("vector", lambda e, sl=sl, dst=dst: e.tensor_mul(
                        dst[:, sl], a_sl[:, sl], u_sl[:, sl]), s_v)
                p.wait("vector", s_v, s_v.n)
                if not shared:
                    for ob in range(nob):
                        sl = slice(ob * SLAB, (ob + 1) * SLAB)
                        p.op("vector", lambda e, sl=sl, dst=dst: e.tensor_mul(
                            dst[:, sl], dst[:, sl], selwm_bc[:, :]), s_v)
            v_h = s_v.n

            for ex in range(NE):
                ex_abs = tau * NE + ex
                bi = _bi(ex_abs)
                shared = (ex == NEXP)
                nib = 8 if shared else 4
                buf = wd_buf[bi]
                p.wait("tensor", s_wd[bi], wd_tr[bi])
                p.wait("tensor", s_v, v_h)
                if ex == 0:
                    p.wait("tensor", s_a, s_a.n)
                hsrc = hsh if shared else he_buf[ex]
                for tb in range(4):
                    for oh in range(2):
                        for ib in range(nib):
                            last_of_ex = (tb == 3 and oh == 1 and ib == nib - 1)
                            p.mm(ps_dn[tb][:, oh * 512:(oh + 1) * 512],
                                 hsrc[:, ib * SLAB + tb * 128: ib * SLAB + (tb + 1) * 128],
                                 buf[:, ib * C + oh * 512: ib * C + (oh + 1) * 512],
                                 start=(ex == 0 and ib == 0),
                                 stop=(ex == NEXP and ib == nib - 1),
                                 inc=(s_mm if last_of_ex else None))
                wd_used[bi] = s_mm.n
                if ex_abs + 2 < 4 * NE:
                    load_wd(ex_abs + 2)
            mm_dn = s_mm.n

            for tb in range(4):
                p.wait("vector", s_mm, mm_dn)
                p.wait("vector", s_ld, x_ld)
                p.op("vector", lambda e, tb=tb: e.tensor_add(
                    r2_t[:, :], xm_sl[:, tb * C:(tb + 1) * C], ps_dn[tb][:, :]), s_v)
                p.wait("vector", s_v, s_v.n)
                p.op("vector", lambda e: e.tensor_reduce(
                    col1[:, 0:1], r2_t[:, :], axis=AX.X, op=ALU.add), s_v)
                p.wait("vector", s_v, s_v.n)
                p.op("vector", lambda e: e.tensor_scalar_mul(col1[:, 0:1], col1[:, 0:1], -1.0 / C), s_v)
                p.wait("vector", s_v, s_v.n)
                p.op("vector", lambda e: e.tensor_scalar_add(r2_t[:, :], r2_t[:, :], col1[:, 0:1]), s_v)
                p.wait("scalar", s_v, s_v.n)
                p.op("scalar", lambda e, tb=tb: e.activation(
                    xm_sl[:, tb * C:(tb + 1) * C], r2_t[:, :], AF.Square,
                    accum_out=col2[:, 0:1]), s_a)
                p.wait("vector", s_a, s_a.n)
                p.op("vector", lambda e: e.tensor_scalar(
                    col2[:, 0:1], col2[:, 0:1], 1.0 / C, EPS_LN, ALU.mult, ALU.add), s_v)
                p.wait("vector", s_v, s_v.n)
                p.op("vector", lambda e: e.reciprocal(col2[:, 0:1], col2[:, 0:1]), s_v)
                v_l2 = s_v.n
                p.wait("scalar", s_v, v_l2)
                p.op("scalar", lambda e: e.activation(col2[:, 0:1], col2[:, 0:1], AF.Sqrt), s_a)
                a_l2 = s_a.n
                p.wait("vector", s_a, a_l2)
                p.op("vector", lambda e: e.tensor_scalar_mul(r2_t[:, :], r2_t[:, :], col2[:, 0:1]), s_v)
                v_out = s_v.n
                p.wait("sync", s_v, v_out)
                p.dma("sync", s_o, o_out[t0 + tb * 128:t0 + (tb + 1) * 128, :], r2_t[:, :])
                p.wait("vector", s_o, s_o.n)

        p.wait("sync", s_o, s_o.n)
        p.build_block()
    return nc



def kernel(**inputs):
    x = np.ascontiguousarray(np.asarray(inputs["L0_attn_xq"], np.float32))
    tm = np.ascontiguousarray(np.stack([
        np.asarray(inputs["L0_attn_time_mix_r"], np.float32).reshape(C),
        np.asarray(inputs["L0_attn_time_mix_k"], np.float32).reshape(C),
        np.asarray(inputs["L0_attn_time_mix_v"], np.float32).reshape(C),
        np.asarray(inputs["L0_attn_time_mix_g"], np.float32).reshape(C),
    ]))
    td = np.ascontiguousarray(np.asarray(inputs["L0_attn_time_decay"], np.float32).reshape(1, H))
    fa = np.ascontiguousarray(np.asarray(inputs["L0_attn_time_faaaa"], np.float32).reshape(1, H))
    f32c = lambda k: np.ascontiguousarray(np.asarray(inputs[k], np.float32))

    nc1 = build_l1()
    maps1 = []
    for c in range(NCORES):
        b, q = c // 4, c % 4
        t0 = q * TOK
        xp = np.zeros((TOK + 1, C), np.float32)
        xp[1:] = x[b, t0:t0 + TOK]
        if q > 0:
            xp[0] = x[b, t0 - 1]
        maps1.append({
            "xp": xp,
            "wr": f32c("L0_attn_W_r"), "wk": f32c("L0_attn_W_k"),
            "wv": f32c("L0_attn_W_v"), "wg": f32c("L0_attn_W_g"),
            "tm": tm, "td": td, "fa": fa,
        })
    res1 = run_bass_kernel_spmd(nc1, maps1, core_ids=list(range(NCORES)))
    r1 = res1.results

    nc2 = build_l2()
    maps2 = []
    for c in range(NCORES):
        b, q = c // 4, c % 4
        t0 = q * TOK
        sall = np.concatenate([np.asarray(r1[b * 4 + qq]["o_sc"], np.float32)
                               for qq in range(4)], axis=0)
        coefk = np.zeros((1, 4), np.float32)
        maskk = np.zeros((1, 4), np.float32)
        for qq in range(q):
            coefk[0, qq] = float(TOK * (q - 1 - qq))
            maskk[0, qq] = 1.0
        maps2.append({
            "x": x[b, t0:t0 + TOK],
            "xo": np.asarray(r1[c]["o_xo"]).view(np.uint16).view(ml_dtypes.bfloat16)
                  if np.asarray(r1[c]["o_xo"]).dtype != ml_dtypes.bfloat16
                  else np.asarray(r1[c]["o_xo"]),
            "rwb": np.asarray(r1[c]["o_rwb"]),
            "g": np.asarray(r1[c]["o_g"]),
            "sall": sall, "coefk": coefk, "maskk": maskk,
            "wo": f32c("L0_attn_W_o"), "td": td,
            "lnw": np.asarray(inputs["L0_attn_ln_w"], np.float32).reshape(1, C),
            "lnb": np.asarray(inputs["L0_attn_ln_b"], np.float32).reshape(1, C),
            "gw": np.concatenate([f32c("L0_ffn_gate_w"),
                                  np.zeros((8, C), np.float32)], axis=0),
            "lb": np.asarray(inputs["L0_ffn_lb_bias"], np.float32).reshape(1, 8),
        })
    res2 = run_bass_kernel_spmd(nc2, maps2, core_ids=list(range(NCORES)))
    r2 = res2.results

    nc3 = build_l3()
    stats_all = np.concatenate([np.asarray(r2[c]["o_stats"], np.float32)
                                for c in range(NCORES)], axis=0)
    maps3 = []
    for c in range(NCORES):
        maps3.append({
            "xm": np.asarray(r2[c]["o_xmid"], np.float32),
            "m": np.concatenate([np.asarray(r2[c]["o_m"], np.float32),
                                 np.zeros((TOK, 112), np.float32)], axis=1),
            "stats": stats_all,
            "eg": f32c("L0_ffn_expert_gate_w"),
            "eu": f32c("L0_ffn_expert_up_w"),
            "ed": f32c("L0_ffn_expert_down_w"),
            "sg": f32c("L0_ffn_shared_gate_w"),
            "su": f32c("L0_ffn_shared_up_w"),
            "sd": f32c("L0_ffn_shared_down_w"),
        })
    res3 = run_bass_kernel_spmd(nc3, maps3, core_ids=list(range(NCORES)))
    r3 = res3.results

    out = np.zeros((B, TT, C), np.float32)
    for c in range(NCORES):
        b, q = c // 4, c % 4
        out[b, q * TOK:(q + 1) * TOK] = np.asarray(r3[c]["o_out"], np.float32)
    return out


# version 07
# speedup vs baseline: 1.8468x; 1.4815x over previous
import numpy as np
import ml_dtypes
from contextlib import ExitStack

import concourse.bass as bass
import concourse.mybir as mybir
from concourse.bass_utils import run_bass_kernel_spmd

F32 = mybir.dt.float32
BF16 = mybir.dt.bfloat16
U32 = mybir.dt.uint32
AF = mybir.ActivationFunctionType
ALU = mybir.AluOpType
AX = mybir.AxisListType

B = 2
TT = 8192
C = 1024
H = 16
KD = 64
CH = 128
TOK = 2048
NCH = TOK // CH
SLAB = 512
NSLAB = TOK // SLAB
HEAD_DIV = 8.0
NCORES = 8
EPS_GN = 1e-5
EPS_LN = 1e-5
NEXP = 8
EI = 512
SI = 1024


class Ctr:
    def __init__(self, ctx, nc, name):
        self.h = ctx.enter_context(nc.semaphore(name))
        self.n = 0
        self.dma = False


class Prog:
    ENGINES = ("tensor", "vector", "scalar", "gpsimd", "sync")

    def __init__(self, nc, ctx):
        self.nc = nc
        self.ctx = ctx
        self.q = {e: [] for e in self.ENGINES}

    def sem(self, name):
        return Ctr(self.ctx, self.nc, name)

    def wait(self, eng, ctr, val=None):
        v = ctr.n if (val is None or ctr.dma) else val
        if v > 0:
            self.q[eng].append(lambda e, c=ctr, v=v: e.wait_ge(c.h, v))

    def mm(self, psum, lhsT, rhs, start, stop, inc=None, tile_position=None):
        try:
            nfree = rhs.shape[-1]
        except Exception:
            nfree = 512
        k = 8 if nfree >= 256 else 4

        def fn(e, psum=psum, lhsT=lhsT, rhs=rhs, start=start, stop=stop, inc=inc,
               tp=tile_position, k=k):
            ins = e.matmul(psum, lhsT, rhs, start=start, stop=stop, tile_position=tp)
            if inc is not None:
                ins.then_inc(inc.h, 1)
            for _ in range(k):
                e.nop()
        self.q["tensor"].append(fn)
        if inc is not None:
            inc.n += 1

    def op(self, eng, fn, inc=None):
        def wrapped(e, fn=fn, inc=inc):
            ins = fn(e)
            if inc is not None:
                ins.then_inc(inc.h, 1)
        self.q[eng].append(wrapped)
        if inc is not None:
            inc.n += 1

    def dma(self, eng, ctr, out, in_, transpose=False, **kw):
        ctr.dma = True
        def fn(e, out=out, in_=in_, transpose=transpose, kw=kw):
            e.dma_start(out, in_, transpose=transpose, **kw).then_inc(ctr.h, 16)
        self.q[eng].append(fn)
        ctr.n += 16

    def build_block(self):
        block = self.ctx.enter_context(self.nc.Block())
        q = self.q

        @block.tensor
        def _(e):
            for fn in q["tensor"]:
                fn(e)

        @block.vector
        def _(e):
            for fn in q["vector"]:
                fn(e)

        @block.scalar
        def _(e):
            for fn in q["scalar"]:
                fn(e)

        @block.gpsimd
        def _(e):
            for fn in q["gpsimd"]:
                fn(e)

        @block.sync
        def _(e):
            for fn in q["sync"]:
                fn(e)


def _head(h):
    return slice((h % 2) * KD, (h % 2) * KD + KD), h // 2



def build_l1(debug=False):
    nc = bass.Bass("TRN2", target_bir_lowering=False)
    xp = nc.dram_tensor("xp", [TOK + 1, C], F32, kind="ExternalInput").ap()
    w_in = {nm: nc.dram_tensor(nm, [C, C], F32, kind="ExternalInput").ap()
            for nm in ("wr", "wk", "wv", "wg")}
    tm_in = nc.dram_tensor("tm", [4, C], F32, kind="ExternalInput").ap()
    td_in = nc.dram_tensor("td", [1, H], F32, kind="ExternalInput").ap()
    fa_in = nc.dram_tensor("fa", [1, H], F32, kind="ExternalInput").ap()

    o_sc = nc.dram_tensor("o_sc", [128, 512], F32, kind="ExternalOutput").ap()
    o_xo = nc.dram_tensor("o_xo", [TOK, C], BF16, kind="ExternalOutput").ap()
    o_rwb = nc.dram_tensor("o_rwb", [128, 8 * TOK], BF16, kind="ExternalOutput").ap()
    o_g = nc.dram_tensor("o_g", [TOK, C], BF16, kind="ExternalOutput").ap()

    x_bf = nc.dram_tensor("x_bf", [TOK + 1, C], BF16, kind="Internal").ap()
    w_bf = {nm: nc.dram_tensor(nm + "_bf", [C, C], BF16, kind="Internal").ap()
            for nm in ("wr", "wk", "wv", "wg")}
    kc_dram = nc.dram_tensor("kc_dram", [C, TOK], BF16, kind="Internal").ap()
    sc1 = nc.dram_tensor("sc1", [1, 64], F32, kind="Internal").ap()

    dbg = {}
    if debug:
        for nm, shp, dt_ in [("d_wmat", [128, H * CH], F32), ("d_wb", [128, 8 * CH], BF16),
                             ("d_wk", [128, H], F32), ("d_ws", [128, 512], F32),
                             ("d_ldbc", [128, H], F32),
                             ("d_mixr", [128, 8 * SLAB], BF16), ("d_mixk", [128, 8 * SLAB], BF16),
                             ("d_rsl", [128, 8 * SLAB], BF16), ("d_kcs", [128, 8 * SLAB], BF16),
                             ("d_kts", [128, 4 * C], BF16), ("d_vts", [128, 4 * C], BF16),
                             ("d_xc", [128, 8 * SLAB], BF16), ("d_xxc", [128, 8 * SLAB], BF16)]:
            dbg[nm] = nc.dram_tensor(nm, shp, dt_, kind="ExternalOutput").ap()

    with ExitStack() as ctx:
        p = Prog(nc, ctx)
        sb = lambda n, s, d=F32: ctx.enter_context(nc.sbuf_tensor(n, s, d))
        psl = lambda n, s: ctx.enter_context(nc.psum_tensor(n, s, F32))

        wT = {nm: sb(nm + "T", [128, 8 * C], BF16) for nm in ("wr", "wk", "wv", "wg")}
        xc = sb("xc", [128, 8 * SLAB], BF16)
        xxc = sb("xxc", [128, 8 * SLAB], BF16)
        mixed = {nm: sb("mx_" + nm, [128, 8 * SLAB], BF16) for nm in ("r", "k", "v", "g")}
        r_sl = sb("r_sl", [128, 8 * SLAB], BF16)
        rwb_sl = sb("rwb_sl", [128, 8 * SLAB], BF16)
        kcs = sb("kcs", [128, 8 * SLAB], BF16)
        kts = sb("kts", [128, 4 * C], BF16)
        kwk = sb("kwk", [128, C], BF16)
        vts = sb("vts", [128, 4 * C], BF16)
        g_sl = sb("g_sl", [128, 4 * C], BF16)
        xo_sl = sb("xo_sl", [128, 4 * C], BF16)
        attw = sb("attw", [128, 2 * CH], BF16)
        s_sb = sb("s_sb", [128, 512], F32)
        s_tmp = sb("s_tmp", [128, 512], F32)
        s_bf = sb("s_bf", [128, 512], BF16)

        tm_cols = sb("tm_cols", [128, 32], F32)
        td_row = sb("td_row", [1, H], F32)
        ld_row = sb("ld_row", [1, H], F32)
        ws_row = sb("ws_row", [1, H], F32)
        fa_row = sb("fa_row", [1, H], F32)
        ld_bc = sb("ld_bc", [128, H], F32)
        u_bc = sb("u_bc", [128, H], F32)
        ws_bc = sb("ws_bc", [128, H], F32)
        ldcol8 = sb("ldcol8", [128, 8], F32)
        wmatT = sb("wmatT", [128, H * CH], F32)
        wb_ct = sb("wb_ct", [128, 8 * CH], BF16)
        wk_cols = sb("wk_cols", [128, H], F32)
        ws_tile = sb("ws_tile", [128, 512], F32)
        iota_d = sb("iota_d", [128, CH], F32)
        iota_f = sb("iota_f", [128, CH], F32)
        clampd = sb("clampd", [128, CH], F32)
        rev_col = sb("rev_col", [128, 1], F32)
        ones128 = sb("ones128", [128, CH], F32)
        eye = sb("eye", [128, CH], F32)

        ps_proj = [psl("ps_proj%d" % i, [128, SLAB]) for i in range(2)]
        ps_att = [psl("ps_att%d" % i, [128, CH]) for i in range(2)]
        ps_out = psl("ps_out", [128, C])
        ps_A = psl("ps_A", [128, 512])

        s_cast = p.sem("s_cast")
        s_ld = p.sem("s_ld")
        s_xld = p.sem("s_xld")
        s_tr = p.sem("s_tr")
        s_kd = p.sem("s_kd")
        s_kc = p.sem("s_kc")
        s_mm = p.sem("s_mm")
        s_v = p.sem("s_v")
        s_a = p.sem("s_a")
        s_g = p.sem("s_g")
        s_o = p.sem("s_o")

        p.dma("gpsimd", s_cast, x_bf, xp)
        cast_x = s_cast.n
        for nm in ("wr", "wk", "wv", "wg"):
            p.dma("gpsimd", s_cast, w_bf[nm], w_in[nm])
        cast_w = s_cast.n

        p.dma("sync", s_ld, td_row[0:1, :], td_in)
        p.dma("sync", s_ld, fa_row[0:1, :], fa_in)
        p.dma("sync", s_ld, tm_cols[:, :],
              tm_in.rearrange("m (p c) -> c (m p)", c=128),
              allow_slow_non_contiguous=True)
        ld0 = s_ld.n

        p.wait("scalar", s_ld, ld0)
        p.op("scalar", lambda e: e.activation(ld_row[0:1, :], td_row[0:1, :], AF.Exp), s_a)
        a_exp = s_a.n
        p.wait("vector", s_a, a_exp)
        p.op("vector", lambda e: e.tensor_scalar_mul(ld_row[0:1, :], ld_row[0:1, :], -1.0), s_v)
        v_ld = s_v.n
        p.wait("scalar", s_v, v_ld)
        p.op("scalar", lambda e: e.activation(ws_row[0:1, :], ld_row[0:1, :], AF.Exp,
                                              scale=float(CH)), s_a)
        a_ws = s_a.n
        p.wait("sync", s_v, v_ld)
        p.dma("sync", s_kd, sc1[0:1, 0:H], ld_row[0:1, :])
        p.wait("sync", s_a, a_ws)
        p.dma("sync", s_kd, sc1[0:1, 16:16 + H], ws_row[0:1, :])
        p.wait("sync", s_ld, ld0)
        p.dma("sync", s_kd, sc1[0:1, 32:32 + H], fa_row[0:1, :])
        kd0 = s_kd.n

        p.wait("gpsimd", s_kd, kd0)
        p.dma("gpsimd", s_cast, ld_bc[:, :], sc1[0:1, 0:H].partition_broadcast(128))
        p.dma("gpsimd", s_cast, ws_bc[:, :], sc1[0:1, 16:16 + H].partition_broadcast(128))
        p.dma("gpsimd", s_cast, u_bc[:, :], sc1[0:1, 32:32 + H].partition_broadcast(128))
        for pb in range(8):
            p.dma("gpsimd", s_cast, ldcol8[0:64, pb:pb + 1],
                  sc1[0:1, 2 * pb:2 * pb + 1].partition_broadcast(64))
            p.dma("gpsimd", s_cast, ldcol8[64:128, pb:pb + 1],
                  sc1[0:1, 2 * pb + 1:2 * pb + 2].partition_broadcast(64))
        cast_bc = s_cast.n

        p.op("gpsimd", lambda e: e.iota(iota_d[:, :], pattern=[[1, CH]], base=-1,
                                        channel_multiplier=-1,
                                        allow_small_or_imprecise_dtypes=True), s_g)
        p.op("gpsimd", lambda e: e.iota(iota_f[:, :], pattern=[[1, CH]], base=0,
                                        channel_multiplier=0,
                                        allow_small_or_imprecise_dtypes=True), s_g)
        p.op("gpsimd", lambda e: e.iota(rev_col[:, 0:1], pattern=[[0, 1]], base=127,
                                        channel_multiplier=-1,
                                        allow_small_or_imprecise_dtypes=True), s_g)
        p.op("gpsimd", lambda e: e.memset(ones128[:, :], 1.0), s_g)
        g_iota = s_g.n
        p.wait("gpsimd", s_g, g_iota)
        p.op("gpsimd", lambda e: e.affine_select(eye[:, :], ones128[:, :],
                                                 pattern=[[1, CH]], compare_op=ALU.is_equal,
                                                 fill=0.0, base=0, channel_multiplier=-1), s_g)
        g_eye = s_g.n

        p.wait("vector", s_g, g_iota)
        p.op("vector", lambda e: e.tensor_scalar_max(clampd[:, :], iota_d[:, :], 0.0), s_v)
        v_clamp = s_v.n

        p.wait("scalar", s_v, v_clamp)
        p.wait("scalar", s_cast, cast_bc)
        for h in range(H):
            sl = slice(h * CH, (h + 1) * CH)
            p.op("scalar", lambda e, sl=sl, h=h: e.activation(
                wmatT[:, sl], clampd[:, :], AF.Exp, scale=ld_bc[:, h:h + 1]), s_a)
        a_wpow = s_a.n
        p.wait("gpsimd", s_a, a_wpow)
        for h in range(H):
            sl = slice(h * CH, (h + 1) * CH)
            p.op("gpsimd", lambda e, sl=sl: e.affine_select(
                wmatT[:, sl], wmatT[:, sl], pattern=[[1, CH]], compare_op=ALU.is_ge,
                fill=0.0, base=-1, channel_multiplier=-1), s_g)
        g_wmask = s_g.n
        p.wait("vector", s_g, g_wmask)
        p.wait("vector", s_cast, cast_bc)
        for h in range(H):
            sl = slice(h * CH, (h + 1) * CH)
            p.op("vector", lambda e, sl=sl, h=h: e.scalar_tensor_tensor(
                wmatT[:, sl], eye[:, :], u_bc[:, h:h + 1], wmatT[:, sl], ALU.mult, ALU.add), s_v)
        v_wmat = s_v.n

        p.wait("scalar", s_cast, cast_bc)
        p.wait("scalar", s_g, g_iota)
        for pb in range(8):
            sl = slice(pb * CH, (pb + 1) * CH)
            p.op("scalar", lambda e, sl=sl, pb=pb: e.activation(
                wb_ct[:, sl], iota_f[:, :], AF.Exp, scale=ldcol8[:, pb:pb + 1]), s_a)
        a_wb = s_a.n

        for h in range(H):
            p.op("scalar", lambda e, h=h: e.activation(
                wk_cols[:, h:h + 1], rev_col[:, 0:1], AF.Exp, scale=ld_bc[:, h:h + 1]), s_a)
        a_wk = s_a.n

        p.wait("vector", s_cast, cast_bc)
        for hp in range(8):
            for half in range(2):
                rows = slice(half * 64, half * 64 + 64)
                colh = 2 * hp + half
                p.op("vector", lambda e, rows=rows, hp=hp, colh=colh: e.tensor_scalar_mul(
                    ws_tile[rows, hp * 64:(hp + 1) * 64], ones128[0:64, 0:64],
                    ws_bc[rows, colh:colh + 1]), s_v)
        p.op("vector", lambda e: e.memset(s_sb[:, :], 0.0), s_v)
        p.op("vector", lambda e: e.memset(s_bf[:, :], 0.0), s_v)
        v_consts = s_v.n

        if debug:
            p.wait("sync", s_v, s_v.n)
            p.wait("sync", s_a, s_a.n)
            p.wait("sync", s_g, s_g.n)
            p.dma("sync", s_o, dbg["d_wmat"], wmatT[:, :])
            p.dma("sync", s_o, dbg["d_wb"], wb_ct[:, :])
            p.dma("sync", s_o, dbg["d_wk"], wk_cols[:, :])
            p.dma("sync", s_o, dbg["d_ws"], ws_tile[:, :])
            p.dma("sync", s_o, dbg["d_ldbc"], ld_bc[:, :])

        p.wait("scalar", s_cast, cast_w)
        for nm in ("wr", "wk", "wv", "wg"):
            for ib in range(8):
                p.dma("scalar", s_tr, wT[nm][:, ib * C:(ib + 1) * C],
                      w_bf[nm][0:C, ib * 128:(ib + 1) * 128], transpose=True)
        tr_w = s_tr.n

        bank_evac = [0, 0]
        for s in range(NSLAB):
            t0 = s * SLAB
            p.wait("scalar", s_o, s_o.n)
            p.wait("vector", s_o, s_o.n)
            prev_mix_v = s_v.n
            p.wait("scalar", s_cast, cast_x)
            p.wait("scalar", s_v, prev_mix_v)
            for pb in range(8):
                cs = slice(pb * 128, (pb + 1) * 128)
                p.dma("scalar", s_xld, xc[:, pb * SLAB:(pb + 1) * SLAB],
                      x_bf[1 + t0:1 + t0 + SLAB, cs], transpose=True)
                p.dma("scalar", s_xld, xxc[:, pb * SLAB:(pb + 1) * SLAB],
                      x_bf[t0:t0 + SLAB, cs], transpose=True)
            xld = s_xld.n

            p.wait("vector", s_xld, xld)
            p.wait("vector", s_mm, s_mm.n)
            for mi, nm in enumerate(("r", "k", "v", "g")):
                for pb in range(8):
                    sl = slice(pb * SLAB, (pb + 1) * SLAB)
                    p.op("vector", lambda e, sl=sl, nm=nm: e.tensor_sub(
                        mixed[nm][:, sl], xc[:, sl], xxc[:, sl]), s_v)
            v_sub = s_v.n
            p.wait("vector", s_v, v_sub)
            for mi, nm in enumerate(("r", "k", "v", "g")):
                for pb in range(8):
                    sl = slice(pb * SLAB, (pb + 1) * SLAB)
                    tcol = tm_cols[:, mi * 8 + pb:mi * 8 + pb + 1]
                    p.op("vector", lambda e, sl=sl, tcol=tcol, nm=nm: e.scalar_tensor_tensor(
                        mixed[nm][:, sl], mixed[nm][:, sl], tcol, xxc[:, sl],
                        ALU.mult, ALU.add), s_v)
            v_mix = s_v.n

            def cmaj(nm_w, nm_x, dst):
                for ob in range(8):
                    bk = ob % 2
                    bank = ps_proj[bk]
                    p.wait("tensor", s_a, bank_evac[bk])
                    p.wait("tensor", s_v, v_mix)
                    p.wait("tensor", s_tr, tr_w)
                    for ib in range(8):
                        p.mm(bank[:, :],
                             wT[nm_w][:, ib * C + ob * 128: ib * C + (ob + 1) * 128],
                             mixed[nm_x][:, ib * SLAB:(ib + 1) * SLAB],
                             start=(ib == 0), stop=(ib == 7),
                             inc=(s_mm if ib == 7 else None))
                    p.wait("scalar", s_mm, s_mm.n)
                    sl = slice(ob * SLAB, (ob + 1) * SLAB)
                    p.op("scalar", lambda e, bank=bank, dst=dst, sl=sl:
                         e.copy(dst[:, sl], bank[:, :]), s_a)
                    bank_evac[bk] = s_a.n

            def tmaj(nm_w, nm_x, dst, act=None):
                for tb in range(4):
                    for oh in range(2):
                        bk = (tb * 2 + oh) % 2
                        bank = ps_proj[bk]
                        p.wait("tensor", s_a, bank_evac[bk])
                        p.wait("tensor", s_v, v_mix)
                        p.wait("tensor", s_tr, tr_w)
                        for ib in range(8):
                            p.mm(bank[:, :],
                                 mixed[nm_x][:, ib * SLAB + tb * 128: ib * SLAB + (tb + 1) * 128],
                                 wT[nm_w][:, ib * C + oh * 512: ib * C + (oh + 1) * 512],
                                 start=(ib == 0), stop=(ib == 7),
                                 inc=(s_mm if ib == 7 else None))
                        p.wait("scalar", s_mm, s_mm.n)
                        sl = slice(tb * C + oh * 512, tb * C + (oh + 1) * 512)
                        if act is None:
                            p.op("scalar", lambda e, bank=bank, sl=sl, dst=dst:
                                 e.copy(dst[:, sl], bank[:, :]), s_a)
                        else:
                            p.op("scalar", lambda e, bank=bank, sl=sl, dst=dst, act=act:
                                 e.activation(dst[:, sl], bank[:, :], act), s_a)
                        bank_evac[bk] = s_a.n

            cmaj("wr", "r", r_sl)
            cmaj("wk", "k", kcs)
            a_rk = s_a.n
            tmaj("wv", "v", vts)
            tmaj("wg", "g", g_sl, act=AF.Silu)
            a_proj = s_a.n

            p.wait("vector", s_a, a_rk)
            p.wait("vector", s_a, a_wb)
            for pb in range(8):
                for cnn in range(4):
                    sl = slice(pb * SLAB + cnn * CH, pb * SLAB + (cnn + 1) * CH)
                    wsl = slice(pb * CH, (pb + 1) * CH)
                    p.op("vector", lambda e, sl=sl, wsl=wsl: e.tensor_mul(
                        rwb_sl[:, sl], r_sl[:, sl], wb_ct[:, wsl]), s_v)
            v_rwb = s_v.n

            p.wait("sync", s_a, a_rk)
            p.dma("sync", s_kc,
                  kc_dram[0:C, t0:t0 + SLAB].rearrange("(p a) t -> a p t", a=128),
                  kcs.rearrange("a (p t) -> a p t", t=SLAB))
            kwrt = s_kc.n
            p.wait("scalar", s_kc, kwrt)
            p.wait("scalar", s_mm, s_mm.n)
            for tb in range(4):
                p.dma("scalar", s_tr, kts[:, tb * C:(tb + 1) * C],
                      kc_dram[0:C, t0 + tb * 128:t0 + (tb + 1) * 128], transpose=True)
            tr_k = s_tr.n

            if debug and s == 0:
                p.wait("sync", s_v, s_v.n)
                p.wait("sync", s_a, s_a.n)
                p.wait("sync", s_tr, s_tr.n)
                p.dma("sync", s_o, dbg["d_xc"], xc[:, :])
                p.dma("sync", s_o, dbg["d_xxc"], xxc[:, :])
                p.dma("sync", s_o, dbg["d_mixr"], mixed["r"][:, :])
                p.dma("sync", s_o, dbg["d_mixk"], mixed["k"][:, :])
                p.dma("sync", s_o, dbg["d_rsl"], r_sl[:, :])
                p.dma("sync", s_o, dbg["d_kcs"], kcs[:, :])
                p.dma("sync", s_o, dbg["d_kts"], kts[:, :])
                p.dma("sync", s_o, dbg["d_vts"], vts[:, :])
                p.wait("vector", s_o, s_o.n)
                p.wait("tensor", s_o, s_o.n)
                p.wait("sync", s_o, s_o.n)
                p.wait("scalar", s_o, s_o.n)
                p.wait("gpsimd", s_o, s_o.n)

            for cnn in range(4):
                p.wait("vector", s_tr, tr_k)
                p.wait("vector", s_a, a_wk)
                p.wait("vector", s_mm, s_mm.n)
                for h in range(H):
                    p.op("vector", lambda e, h=h, cnn=cnn: e.tensor_scalar_mul(
                        kwk[:, h * KD:(h + 1) * KD],
                        kts[:, cnn * C + h * KD:cnn * C + (h + 1) * KD],
                        wk_cols[:, h:h + 1]), s_v)
                v_kwk = s_v.n

                attw_v = {}
                for h in range(H):
                    rows, pb = _head(h)
                    rsl = slice(pb * SLAB + cnn * CH, pb * SLAB + (cnn + 1) * CH)
                    bank = ps_att[h % 2]
                    if h >= 2:
                        p.wait("tensor", s_v, attw_v[h - 2])
                    p.wait("tensor", s_a, a_rk)
                    p.wait("tensor", s_v, v_rwb)
                    p.mm(bank[:, :], kcs[rows, rsl], r_sl[rows, rsl],
                         start=True, stop=True, inc=s_mm)
                    att_mm = s_mm.n
                    p.wait("vector", s_mm, att_mm)
                    p.wait("vector", s_v, v_wmat)
                    half = h % 2
                    p.op("vector", lambda e, bank=bank, h=h, half=half: e.tensor_mul(
                        attw[:, half * CH:(half + 1) * CH], bank[:, :],
                        wmatT[:, h * CH:(h + 1) * CH]), s_v)
                    attw_v[h] = s_v.n

                    p.wait("tensor", s_v, attw_v[h])
                    vsl = slice(cnn * C + h * KD, cnn * C + (h + 1) * KD)
                    p.mm(ps_out[:, h * KD:(h + 1) * KD],
                         attw[:, half * CH:(half + 1) * CH], vts[:, vsl],
                         start=True, stop=False)
                    srows = slice((h % 2) * KD, (h % 2) * KD + KD)
                    scols = slice((h // 2) * KD, (h // 2 + 1) * KD)
                    p.mm(ps_out[:, h * KD:(h + 1) * KD],
                         rwb_sl[rows, rsl], s_bf[srows, scols],
                         start=False, stop=True, inc=s_mm)
                    p.wait("tensor", s_v, v_kwk)
                    p.mm(ps_A[srows, scols], kwk[:, h * KD:(h + 1) * KD], vts[:, vsl],
                         start=True, stop=True, inc=s_mm,
                         tile_position=(0, (h % 2) * KD))
                mm_chunk = s_mm.n

                p.wait("vector", s_mm, mm_chunk)
                p.op("vector", lambda e: e.tensor_mul(s_tmp[:, :], s_sb[:, :], ws_tile[:, :]), s_v)
                p.wait("vector", s_v, s_v.n)
                p.op("vector", lambda e: e.tensor_add(s_sb[:, :], s_tmp[:, :], ps_A[:, :]), s_v)
                v_supd = s_v.n
                p.wait("vector", s_v, v_supd)
                p.op("vector", lambda e: e.tensor_copy(s_bf[:, :], s_sb[:, :]), s_v)
                v_sbf = s_v.n

                p.wait("scalar", s_mm, mm_chunk)
                p.op("scalar", lambda e, cnn=cnn: e.mul(
                    xo_sl[:, cnn * C:(cnn + 1) * C], ps_out[:, :], 1.0 / HEAD_DIV), s_a)
                a_xo = s_a.n
                p.wait("tensor", s_a, a_xo)
                p.wait("tensor", s_v, v_sbf)

            a_slab = s_a.n
            v_slab = s_v.n
            p.wait("sync", s_a, a_slab)
            p.wait("sync", s_v, v_slab)
            p.dma("sync", s_o,
                  o_xo[t0:t0 + SLAB, :].rearrange("(tb p) c -> p tb c", p=128),
                  xo_sl.rearrange("a (tb c) -> a tb c", c=C))
            p.dma("sync", s_o,
                  o_g[t0:t0 + SLAB, :].rearrange("(tb p) c -> p tb c", p=128),
                  g_sl.rearrange("a (tb c) -> a tb c", c=C))
            p.dma("sync", s_o,
                  o_rwb.rearrange("a (p t) -> a p t", t=TOK)[:, :, t0:t0 + SLAB],
                  rwb_sl.rearrange("a (p t) -> a p t", t=SLAB))

        p.wait("sync", s_v, s_v.n)
        p.dma("sync", s_o, o_sc, s_sb[:, :])
        p.wait("sync", s_o, s_o.n)
        p.build_block()
    return nc



def build_l2():
    nc = bass.Bass("TRN2", target_bir_lowering=False)
    x_in = nc.dram_tensor("x", [TOK, C], F32, kind="ExternalInput").ap()
    xo_in = nc.dram_tensor("xo", [TOK, C], BF16, kind="ExternalInput").ap()
    rwb_in = nc.dram_tensor("rwb", [128, 8 * TOK], BF16, kind="ExternalInput").ap()
    g_in = nc.dram_tensor("g", [TOK, C], BF16, kind="ExternalInput").ap()
    sall_in = nc.dram_tensor("sall", [512, 512], F32, kind="ExternalInput").ap()
    coefk_in = nc.dram_tensor("coefk", [1, 4], F32, kind="ExternalInput").ap()
    mask_in = nc.dram_tensor("maskk", [1, 4], F32, kind="ExternalInput").ap()
    wo_in = nc.dram_tensor("wo", [C, C], F32, kind="ExternalInput").ap()
    td_in = nc.dram_tensor("td", [1, H], F32, kind="ExternalInput").ap()
    lnw_in = nc.dram_tensor("lnw", [1, C], F32, kind="ExternalInput").ap()
    lnb_in = nc.dram_tensor("lnb", [1, C], F32, kind="ExternalInput").ap()
    gw_in = nc.dram_tensor("gw", [16, C], F32, kind="ExternalInput").ap()
    lb_in = nc.dram_tensor("lb", [1, 8], F32, kind="ExternalInput").ap()

    o_xmid = nc.dram_tensor("o_xmid", [TOK, C], F32, kind="ExternalOutput").ap()
    o_m = nc.dram_tensor("o_m", [TOK, 16], F32, kind="ExternalOutput").ap()
    o_stats = nc.dram_tensor("o_stats", [1, 16], F32, kind="ExternalOutput").ap()

    wo_bf = nc.dram_tensor("wo_bf", [C, C], BF16, kind="Internal").ap()
    gw_bf = nc.dram_tensor("gw_bf", [16, C], BF16, kind="Internal").ap()
    yg_dram = nc.dram_tensor("yg_dram", [TOK, C], BF16, kind="Internal").ap()
    xmb_dram = nc.dram_tensor("xmb_dram", [TOK, C], BF16, kind="Internal").ap()
    sc2 = nc.dram_tensor("sc2", [1, 128], F32, kind="Internal").ap()

    with ExitStack() as ctx:
        p = Prog(nc, ctx)
        sb = lambda n, s, d=F32: ctx.enter_context(nc.sbuf_tensor(n, s, d))
        psl = lambda n, s: ctx.enter_context(nc.psum_tensor(n, s, F32))

        rwb = sb("rwb_s", [128, 8 * TOK], BF16)
        xo_res = sb("xo_res", [128, NCH * C], BF16)
        g_res = sb("g_res", [128, NCH * C], BF16)
        woT = sb("woT", [128, 8 * C], BF16)
        s_tiles = [sb("s_t%d" % i, [128, 512]) for i in range(4)]
        s_e = sb("s_e", [128, 512])
        s_ebf = sb("s_ebf", [128, 512], BF16)
        ws_tile = sb("ws_tile", [128, 512], F32)
        y_t = sb("y_t", [128, C])
        sq_t = sb("sq_t", [128, C])
        gn16 = sb("gn16", [128, 16])
        gn16b = sb("gn16b", [128, 16])
        rstd16 = sb("rstd16", [128, 16])
        yg_sl = sb("yg_sl", [128, 4 * C], BF16)
        ygc = sb("ygc", [128, 8 * SLAB], BF16)
        x_sl = sb("x_sl", [128, 4 * C], F32)
        xm_t = sb("xm_t", [128, C], F32)
        xmb_sl = sb("xmb_sl", [128, 4 * C], BF16)
        col1 = sb("col1", [128, 1])
        col2 = sb("col2", [128, 1])
        lnw_bc = sb("lnw_bc", [128, C], F32)
        lnb_bc = sb("lnb_bc", [128, C], F32)
        td_row = sb("td_row", [1, H], F32)
        ld_row = sb("ld_row", [1, H], F32)
        ws_row = sb("ws_row", [1, H], F32)
        ws_bc = sb("ws_bc", [128, H], F32)
        coefk_sb = sb("coefk_sb", [1, 4], F32)
        mask_sb = sb("mask_sb", [1, 4], F32)
        coefrow = sb("coefrow", [1, 4 * H], F32)
        coef_bc = sb("coef_bc", [128, 4 * H], F32)
        ones64 = sb("ones64", [64, 64], F32)
        gwT = sb("gwT", [128, 8 * 16], BF16)
        xmc_t = sb("xmc_t", [128, 8 * 128], BF16)
        iota8 = sb("iota8", [128, 8], F32)
        lb_bc = sb("lb_bc", [128, 8], F32)
        lg = sb("lg", [128, 8], F32)
        scores = sb("scores", [128, 8], F32)
        mx8 = sb("mx8", [128, 8], F32)
        mi8 = sb("mi8", [128, 8], U32)
        mif = sb("mif", [128, 8], F32)
        eq1 = sb("eq1", [128, 8], F32)
        eq2 = sb("eq2", [128, 8], F32)
        mw = sb("mw", [128, 16], F32)
        ones_col = sb("ones_col", [128, 1], F32)
        den = sb("den", [128, 1], F32)
        w1c = sb("w1c", [128, 1], F32)
        w2c = sb("w2c", [128, 1], F32)
        st_sb = sb("st_sb", [1, 16], F32)

        ps_corr = psl("ps_corr", [128, C])
        ps_proj = [psl("ps_proj%d" % i, [128, SLAB]) for i in range(2)]
        ps_g = psl("ps_g", [128, 8])
        ps_st = psl("ps_st", [1, 16])

        s_cast = p.sem("s_cast")
        s_ld = p.sem("s_ld")
        s_tr = p.sem("s_tr")
        s_kd = p.sem("s_kd")
        s_mm = p.sem("s_mm")
        s_v = p.sem("s_v")
        s_a = p.sem("s_a")
        s_g = p.sem("s_g")
        s_o = p.sem("s_o")

        p.dma("gpsimd", s_cast, wo_bf, wo_in)
        p.dma("gpsimd", s_cast, gw_bf, gw_in)
        cast_w = s_cast.n
        p.dma("gpsimd", s_cast, lnw_bc[:, :], lnw_in[0:1, :].partition_broadcast(128))
        p.dma("gpsimd", s_cast, lnb_bc[:, :], lnb_in[0:1, :].partition_broadcast(128))
        p.dma("gpsimd", s_cast, lb_bc[:, :], lb_in[0:1, :].partition_broadcast(128))
        cast_ln = s_cast.n

        p.dma("sync", s_ld, rwb[:, :], rwb_in)
        p.dma("sync", s_ld,
              xo_res.rearrange("a (tb c) -> a tb c", c=C),
              xo_in.rearrange("(tb p) c -> p tb c", p=128))
        p.dma("sync", s_ld,
              g_res.rearrange("a (tb c) -> a tb c", c=C),
              g_in.rearrange("(tb p) c -> p tb c", p=128))
        for i in range(4):
            p.dma("sync", s_ld, s_tiles[i][:, :], sall_in[i * 128:(i + 1) * 128, :])
        p.dma("sync", s_ld, td_row[0:1, :], td_in)
        p.dma("sync", s_ld, coefk_sb[0:1, :], coefk_in)
        p.dma("sync", s_ld, mask_sb[0:1, :], mask_in)
        ld0 = s_ld.n

        p.wait("scalar", s_ld, ld0)
        p.op("scalar", lambda e: e.activation(ld_row[0:1, :], td_row[0:1, :], AF.Exp), s_a)
        p.wait("vector", s_a, s_a.n)
        p.op("vector", lambda e: e.tensor_scalar_mul(ld_row[0:1, :], ld_row[0:1, :], -1.0), s_v)
        v_ld = s_v.n
        p.wait("scalar", s_v, v_ld)
        p.op("scalar", lambda e: e.activation(ws_row[0:1, :], ld_row[0:1, :], AF.Exp,
                                              scale=float(CH)), s_a)
        for qq in range(4):
            p.op("scalar", lambda e, qq=qq: e.activation(
                coefrow[0:1, qq * H:(qq + 1) * H], ld_row[0:1, :], AF.Exp,
                scale=coefk_sb[0:1, qq:qq + 1]), s_a)
        a_coef = s_a.n
        p.wait("vector", s_a, a_coef)
        for qq in range(4):
            p.op("vector", lambda e, qq=qq: e.tensor_scalar_mul(
                coefrow[0:1, qq * H:(qq + 1) * H], coefrow[0:1, qq * H:(qq + 1) * H],
                mask_sb[0:1, qq:qq + 1]), s_v)
        v_coef = s_v.n
        p.wait("sync", s_a, a_coef)
        p.dma("sync", s_kd, sc2[0:1, 0:H], ws_row[0:1, :])
        p.wait("sync", s_v, v_coef)
        p.dma("sync", s_kd, sc2[0:1, 16:16 + 4 * H], coefrow[0:1, :])
        kd0 = s_kd.n
        p.wait("gpsimd", s_kd, kd0)
        p.dma("gpsimd", s_cast, ws_bc[:, :], sc2[0:1, 0:H].partition_broadcast(128))
        p.dma("gpsimd", s_cast, coef_bc[:, :], sc2[0:1, 16:16 + 4 * H].partition_broadcast(128))
        cast_bc = s_cast.n

        p.op("gpsimd", lambda e: e.memset(ones64[:, :], 1.0), s_g)
        p.op("gpsimd", lambda e: e.iota(iota8[:, :], pattern=[[1, 8]], base=0,
                                        channel_multiplier=0,
                                        allow_small_or_imprecise_dtypes=True), s_g)
        p.op("gpsimd", lambda e: e.memset(ones_col[:, :], 1.0), s_g)
        g_0 = s_g.n

        p.wait("vector", s_cast, cast_bc)
        p.wait("vector", s_g, g_0)
        for hp in range(8):
            for half in range(2):
                rows = slice(half * 64, half * 64 + 64)
                colh = 2 * hp + half
                p.op("vector", lambda e, rows=rows, hp=hp, colh=colh: e.tensor_scalar_mul(
                    ws_tile[rows, hp * 64:(hp + 1) * 64], ones64[:, :],
                    ws_bc[rows, colh:colh + 1]), s_v)
        p.op("vector", lambda e: e.memset(s_e[:, :], 0.0), s_v)
        p.wait("vector", s_ld, ld0)
        p.wait("vector", s_v, s_v.n)
        for qq in range(4):
            for hp in range(8):
                for half in range(2):
                    rows = slice(half * 64, half * 64 + 64)
                    cols = slice(hp * 64, (hp + 1) * 64)
                    colh = qq * H + 2 * hp + half
                    if qq > 0:
                        p.wait("vector", s_v, s_v.n)
                    p.op("vector", lambda e, qq=qq, rows=rows, cols=cols, colh=colh:
                         e.scalar_tensor_tensor(s_e[rows, cols], s_tiles[qq][rows, cols],
                                                coef_bc[rows, colh:colh + 1], s_e[rows, cols],
                                                ALU.mult, ALU.add), s_v)
        p.wait("vector", s_v, s_v.n)
        p.op("vector", lambda e: e.tensor_copy(s_ebf[:, :], s_e[:, :]), s_v)
        v_se = s_v.n

        p.wait("scalar", s_cast, cast_w)
        for ib in range(8):
            p.dma("scalar", s_tr, woT[:, ib * C:(ib + 1) * C],
                  wo_bf[0:C, ib * 128:(ib + 1) * 128], transpose=True)
        for ib in range(8):
            p.dma("scalar", s_tr, gwT[:, ib * 16:(ib + 1) * 16],
                  gw_bf[0:16, ib * 128:(ib + 1) * 128], transpose=True)
        tr_w = s_tr.n

        for j in range(NCH):
            s_slab = j // 4
            cnn = j % 4
            p.wait("tensor", s_v, v_se)
            p.wait("tensor", s_ld, ld0)
            p.wait("tensor", s_v, s_v.n)
            for h in range(H):
                rows, pb = _head(h)
                rsl = slice(pb * TOK + j * CH, pb * TOK + (j + 1) * CH)
                srows = slice((h % 2) * KD, (h % 2) * KD + KD)
                scols = slice((h // 2) * KD, (h // 2 + 1) * KD)
                p.mm(ps_corr[:, h * KD:(h + 1) * KD], rwb[rows, rsl], s_ebf[srows, scols],
                     start=True, stop=True, inc=s_mm)
            mm_c = s_mm.n
            p.wait("vector", s_mm, mm_c)
            p.op("vector", lambda e, j=j: e.scalar_tensor_tensor(
                y_t[:, :], ps_corr[:, :], 1.0 / HEAD_DIV, xo_res[:, j * C:(j + 1) * C],
                ALU.mult, ALU.add), s_v)
            v_y = s_v.n
            p.wait("vector", s_v, v_y)
            p.op("vector", lambda e: e.tensor_mul(s_e[:, :], s_e[:, :], ws_tile[:, :]), s_v)
            p.wait("vector", s_v, s_v.n)
            p.op("vector", lambda e: e.tensor_copy(s_ebf[:, :], s_e[:, :]), s_v)
            v_se = s_v.n

            p.wait("vector", s_v, v_y)
            p.op("vector", lambda e: e.tensor_reduce(
                gn16[:, :], y_t.rearrange("p (h k) -> p h k", k=KD), axis=AX.X, op=ALU.add), s_v)
            p.wait("vector", s_v, s_v.n)
            p.op("vector", lambda e: e.tensor_scalar_mul(gn16[:, :], gn16[:, :], -1.0 / KD), s_v)
            v_nm = s_v.n
            p.wait("vector", s_v, v_nm)
            for h in range(H):
                sl = slice(h * KD, (h + 1) * KD)
                p.op("vector", lambda e, sl=sl, h=h: e.tensor_scalar_add(
                    y_t[:, sl], y_t[:, sl], gn16[:, h:h + 1]), s_v)
            v_yc = s_v.n
            p.wait("vector", s_v, v_yc)
            p.op("vector", lambda e: e.tensor_mul(sq_t[:, :], y_t[:, :], y_t[:, :]), s_v)
            p.wait("vector", s_v, s_v.n)
            p.op("vector", lambda e: e.tensor_reduce(
                gn16b[:, :], sq_t.rearrange("p (h k) -> p h k", k=KD), axis=AX.X, op=ALU.add), s_v)
            p.wait("vector", s_v, s_v.n)
            p.op("vector", lambda e: e.tensor_scalar(
                gn16b[:, :], gn16b[:, :], 1.0 / KD, EPS_GN, ALU.mult, ALU.add), s_v)
            p.wait("vector", s_v, s_v.n)
            p.op("vector", lambda e: e.reciprocal(gn16b[:, :], gn16b[:, :]), s_v)
            v_rcp = s_v.n
            p.wait("scalar", s_v, v_rcp)
            p.op("scalar", lambda e: e.activation(rstd16[:, :], gn16b[:, :], AF.Sqrt), s_a)
            a_rstd = s_a.n
            p.wait("vector", s_a, a_rstd)
            for h in range(H):
                sl = slice(h * KD, (h + 1) * KD)
                p.op("vector", lambda e, sl=sl, h=h: e.tensor_scalar_mul(
                    y_t[:, sl], y_t[:, sl], rstd16[:, h:h + 1]), s_v)
            p.wait("vector", s_v, s_v.n)
            p.wait("vector", s_cast, cast_ln)
            p.op("vector", lambda e: e.tensor_mul(y_t[:, :], y_t[:, :], lnw_bc[:, :]), s_v)
            p.wait("vector", s_v, s_v.n)
            p.op("vector", lambda e: e.tensor_add(y_t[:, :], y_t[:, :], lnb_bc[:, :]), s_v)
            p.wait("vector", s_v, s_v.n)
            p.op("vector", lambda e, j=j, cnn=cnn: e.tensor_mul(
                yg_sl[:, cnn * C:(cnn + 1) * C], y_t[:, :], g_res[:, j * C:(j + 1) * C]), s_v)
            v_yg = s_v.n

            if cnn == 3:
                t0 = s_slab * SLAB
                p.wait("sync", s_v, v_yg)
                p.dma("sync", s_kd,
                      yg_dram[t0:t0 + SLAB, :].rearrange("(tb p) c -> p tb c", p=128),
                      yg_sl.rearrange("a (tb c) -> a tb c", c=C))
                kw1 = s_kd.n
                p.wait("scalar", s_kd, kw1)
                p.wait("scalar", s_mm, s_mm.n)
                for pb in range(8):
                    cs = slice(pb * 128, (pb + 1) * 128)
                    p.dma("scalar", s_tr, ygc[:, pb * SLAB:(pb + 1) * SLAB],
                          yg_dram[t0:t0 + SLAB, cs], transpose=True)
                tr_yg = s_tr.n
                p.wait("sync", s_v, s_v.n)
                p.dma("sync", s_ld,
                      x_sl.rearrange("p (tb c) -> p tb c", c=C),
                      x_in[t0:t0 + SLAB, :].rearrange("(tb p) c -> p tb c", p=128))
                x_ld = s_ld.n

                p.wait("vector", s_kd, s_kd.n)
                p.wait("vector", s_o, s_o.n)
                for tb in range(4):
                    for oh in range(2):
                        bank = ps_proj[(tb * 2 + oh) % 2]
                        p.wait("tensor", s_v, s_v.n)
                        p.wait("tensor", s_tr, tr_yg)
                        for ib in range(8):
                            p.mm(bank[:, :],
                                 ygc[:, ib * SLAB + tb * 128: ib * SLAB + (tb + 1) * 128],
                                 woT[:, ib * C + oh * 512: ib * C + (oh + 1) * 512],
                                 start=(ib == 0), stop=(ib == 7),
                                 inc=(s_mm if ib == 7 else None))
                        p.wait("vector", s_mm, s_mm.n)
                        p.wait("vector", s_ld, x_ld)
                        p.wait("vector", s_o, s_o.n)
                        sl = slice(tb * C + oh * 512, tb * C + (oh + 1) * 512)
                        osl = slice(oh * 512, (oh + 1) * 512)
                        p.op("vector", lambda e, bank=bank, sl=sl, osl=osl: e.tensor_add(
                            xm_t[:, osl], x_sl[:, sl], bank[:, :]), s_v)
                    v_r1 = s_v.n
                    p.wait("vector", s_v, v_r1)
                    p.op("vector", lambda e: e.tensor_reduce(
                        col1[:, 0:1], xm_t[:, :], axis=AX.X, op=ALU.add), s_v)
                    p.wait("vector", s_v, s_v.n)
                    p.op("vector", lambda e: e.tensor_scalar_mul(col1[:, 0:1], col1[:, 0:1], -1.0 / C), s_v)
                    p.wait("vector", s_v, s_v.n)
                    p.op("vector", lambda e: e.tensor_scalar_add(xm_t[:, :], xm_t[:, :], col1[:, 0:1]), s_v)
                    p.wait("vector", s_v, s_v.n)
                    p.op("vector", lambda e: e.tensor_mul(sq_t[:, :], xm_t[:, :], xm_t[:, :]), s_v)
                    p.wait("vector", s_v, s_v.n)
                    p.op("vector", lambda e: e.tensor_reduce(
                        col2[:, 0:1], sq_t[:, :], axis=AX.X, op=ALU.add), s_v)
                    p.wait("vector", s_v, s_v.n)
                    p.op("vector", lambda e: e.tensor_scalar(
                        col2[:, 0:1], col2[:, 0:1], 1.0 / C, EPS_LN, ALU.mult, ALU.add), s_v)
                    p.wait("vector", s_v, s_v.n)
                    p.op("vector", lambda e: e.reciprocal(col2[:, 0:1], col2[:, 0:1]), s_v)
                    v_l1 = s_v.n
                    p.wait("scalar", s_v, v_l1)
                    p.op("scalar", lambda e: e.activation(col2[:, 0:1], col2[:, 0:1], AF.Sqrt), s_a)
                    a_l1 = s_a.n
                    p.wait("vector", s_a, a_l1)
                    p.op("vector", lambda e: e.tensor_scalar_mul(xm_t[:, :], xm_t[:, :], col2[:, 0:1]), s_v)
                    p.wait("vector", s_v, s_v.n)
                    p.op("vector", lambda e, tb=tb: e.tensor_copy(
                        xmb_sl[:, tb * C:(tb + 1) * C], xm_t[:, :]), s_v)
                    v_xm = s_v.n
                    p.wait("sync", s_v, v_xm)
                    p.dma("sync", s_o, o_xmid[t0 + tb * 128:t0 + (tb + 1) * 128, :], xm_t[:, :])
                p.wait("sync", s_v, s_v.n)
                p.dma("sync", s_kd,
                      xmb_dram[t0:t0 + SLAB, :].rearrange("(tb p) c -> p tb c", p=128),
                      xmb_sl.rearrange("a (tb c) -> a tb c", c=C))

        xmb_done = s_kd.n

        p.op("gpsimd", lambda e: e.memset(mw[:, :], 0.0), s_g)
        g_mw = s_g.n
        for ti in range(NCH):
            t0 = ti * CH
            p.wait("scalar", s_kd, xmb_done)
            p.wait("scalar", s_mm, s_mm.n)
            for ib in range(8):
                cs = slice(ib * 128, (ib + 1) * 128)
                p.dma("scalar", s_tr, xmc_t[:, ib * 128:(ib + 1) * 128],
                      xmb_dram[t0:t0 + CH, cs], transpose=True)
            tr_xm = s_tr.n
            p.wait("tensor", s_tr, tr_xm)
            p.wait("tensor", s_tr, tr_w)
            p.wait("tensor", s_v, s_v.n)
            for ib in range(8):
                p.mm(ps_g[:, :], xmc_t[:, ib * 128:(ib + 1) * 128], gwT[:, ib * 16:ib * 16 + 8],
                     start=(ib == 0), stop=(ib == 7), inc=(s_mm if ib == 7 else None))
            p.wait("vector", s_mm, s_mm.n)
            p.wait("vector", s_cast, cast_ln)
            p.op("vector", lambda e: e.tensor_add(lg[:, :], ps_g[:, :], lb_bc[:, :]), s_v)
            v_lg = s_v.n
            p.wait("scalar", s_v, v_lg)
            p.op("scalar", lambda e: e.activation(scores[:, :], lg[:, :], AF.Sigmoid), s_a)
            a_sc = s_a.n
            p.wait("vector", s_a, a_sc)
            p.op("vector", lambda e: e.max(mx8[:, :], scores[:, :]), s_v)
            p.wait("vector", s_v, s_v.n)
            p.op("vector", lambda e: e.max_index(mi8[:, :], mx8[:, :], scores[:, :]), s_v)
            p.wait("vector", s_v, s_v.n)
            p.op("vector", lambda e: e.tensor_copy(mif[:, :], mi8[:, :]), s_v)
            p.wait("vector", s_v, s_v.n)
            p.op("vector", lambda e: e.scalar_tensor_tensor(
                den[:, 0:1], mx8[:, 0:1], 1e-8, mx8[:, 1:2], ALU.add, ALU.add), s_v)
            p.wait("vector", s_v, s_v.n)
            p.op("vector", lambda e: e.reciprocal(den[:, 0:1], den[:, 0:1]), s_v)
            p.wait("vector", s_v, s_v.n)
            p.op("vector", lambda e: e.tensor_mul(w1c[:, 0:1], mx8[:, 0:1], den[:, 0:1]), s_v)
            p.op("vector", lambda e: e.tensor_mul(w2c[:, 0:1], mx8[:, 1:2], den[:, 0:1]), s_v)
            p.wait("vector", s_v, s_v.n)
            p.wait("vector", s_g, g_mw)
            p.op("vector", lambda e: e.tensor_scalar(
                eq1[:, :], iota8[:, :], mif[:, 0:1], None, ALU.is_equal), s_v)
            p.op("vector", lambda e: e.tensor_scalar(
                eq2[:, :], iota8[:, :], mif[:, 1:2], None, ALU.is_equal), s_v)
            p.wait("vector", s_v, s_v.n)
            p.wait("vector", s_mm, s_mm.n)
            p.wait("vector", s_o, s_o.n)
            p.op("vector", lambda e: e.tensor_add(mw[:, 0:8], eq1[:, :], eq2[:, :]), s_v)
            p.wait("vector", s_v, s_v.n)
            p.op("vector", lambda e: e.tensor_scalar_mul(mw[:, 8:16], eq1[:, :], w1c[:, 0:1]), s_v)
            p.wait("vector", s_v, s_v.n)
            p.op("vector", lambda e: e.scalar_tensor_tensor(
                mw[:, 8:16], eq2[:, :], w2c[:, 0:1], mw[:, 8:16], ALU.mult, ALU.add), s_v)
            v_mw = s_v.n
            p.wait("tensor", s_v, v_mw)
            p.wait("tensor", s_g, g_0)
            p.mm(ps_st[0:1, :], ones_col[:, 0:1], mw[:, :],
                 start=(ti == 0), stop=(ti == NCH - 1), inc=s_mm)
            p.wait("sync", s_v, v_mw)
            p.dma("sync", s_o, o_m[t0:t0 + CH, :], mw[:, 0:16])
        p.wait("scalar", s_mm, s_mm.n)
        p.op("scalar", lambda e: e.copy(st_sb[0:1, :], ps_st[0:1, :]), s_a)
        p.wait("sync", s_a, s_a.n)
        p.dma("sync", s_o, o_stats, st_sb[0:1, :])
        p.wait("sync", s_o, s_o.n)
        p.build_block()
    return nc



def build_l3():
    nc = bass.Bass("TRN2", target_bir_lowering=False)
    xm_in = nc.dram_tensor("xm", [TOK, C], F32, kind="ExternalInput").ap()
    m_in = nc.dram_tensor("m", [TOK, 128], F32, kind="ExternalInput").ap()
    stats_in = nc.dram_tensor("stats", [8, 16], F32, kind="ExternalInput").ap()
    eg_in = nc.dram_tensor("eg", [NEXP, EI, C], F32, kind="ExternalInput").ap()
    eu_in = nc.dram_tensor("eu", [NEXP, EI, C], F32, kind="ExternalInput").ap()
    ed_in = nc.dram_tensor("ed", [NEXP, C, EI], F32, kind="ExternalInput").ap()
    sg_in = nc.dram_tensor("sg", [SI, C], F32, kind="ExternalInput").ap()
    su_in = nc.dram_tensor("su", [SI, C], F32, kind="ExternalInput").ap()
    sd_in = nc.dram_tensor("sd", [C, SI], F32, kind="ExternalInput").ap()

    o_out = nc.dram_tensor("o_out", [TOK, C], F32, kind="ExternalOutput").ap()

    xm_bf = nc.dram_tensor("xm_bf", [TOK, C], BF16, kind="Internal").ap()
    m_bf = nc.dram_tensor("m_bf", [TOK, 128], BF16, kind="Internal").ap()
    eg_bf = nc.dram_tensor("eg_bf", [NEXP, EI, C], BF16, kind="Internal").ap()
    eu_bf = nc.dram_tensor("eu_bf", [NEXP, EI, C], BF16, kind="Internal").ap()
    ed_bf = nc.dram_tensor("ed_bf", [NEXP, C, EI], BF16, kind="Internal").ap()
    sg_bf = nc.dram_tensor("sg_bf", [SI, C], BF16, kind="Internal").ap()
    su_bf = nc.dram_tensor("su_bf", [SI, C], BF16, kind="Internal").ap()
    sd_bf = nc.dram_tensor("sd_bf", [C, SI], BF16, kind="Internal").ap()
    selwm_dram = nc.dram_tensor("selwm_dram", [16, TOK], BF16, kind="Internal").ap()
    guT_dram = nc.dram_tensor("guT_dram", [2 * (NEXP + 1), 128, 8 * C], BF16, kind="Internal").ap()
    wdT_dram = nc.dram_tensor("wdT_dram", [NEXP + 1, 128, 8 * C], BF16, kind="Internal").ap()
    sc3 = nc.dram_tensor("sc3", [1, 16], F32, kind="Internal").ap()

    with ExitStack() as ctx:
        p = Prog(nc, ctx)
        sb = lambda n, s, d=F32: ctx.enter_context(nc.sbuf_tensor(n, s, d))
        psl = lambda n, s: ctx.enter_context(nc.psum_tensor(n, s, F32))

        xmc = sb("xmc", [128, 8 * TOK], BF16)
        he_buf = [sb("he%d" % e, [128, 4 * SLAB], BF16) for e in range(NEXP)]
        hsh = sb("hsh", [128, 8 * SLAB], BF16)
        wg_buf = [sb("wg_buf0", [128, 8 * 512], BF16), sb("wg_buf1", [128, 8 * 512], BF16),
                  sb("wg_bufS", [128, 8 * C], BF16)]
        wu_buf = [sb("wu_buf0", [128, 8 * 512], BF16), sb("wu_buf1", [128, 8 * 512], BF16),
                  sb("wu_bufS", [128, 8 * C], BF16)]
        wd_buf = [sb("wd_buf0", [128, 4 * C], BF16), sb("wd_buf1", [128, 4 * C], BF16),
                  sb("wd_bufS", [128, 8 * C], BF16)]
        a_sl = sb("a_sl", [128, 8 * SLAB], BF16)
        u_sl = sb("u_sl", [128, 8 * SLAB], BF16)
        selwm_bc = sb("selwm_bc", [128, SLAB], BF16)
        xm_sl = sb("xm_sl", [128, 4 * C], F32)
        r2_t = sb("r2_t", [128, C], F32)
        col1 = sb("col1", [128, 1], F32)
        col2 = sb("col2", [128, 1], F32)
        st_in = sb("st_in", [8, 16], F32)
        st_sb = sb("st_sb", [1, 16], F32)
        wm_row = sb("wm_row", [1, 8], F32)
        wmcol = sb("wmcol", [16, 1], F32)
        ones8 = sb("ones8", [8, 1], F32)

        ps_dn = [psl("ps_dn%d" % i, [128, C]) for i in range(4)]

        s_cast = p.sem("s_cast")
        s_csel = p.sem("s_csel")
        s_gu = [p.sem("s_gu%d" % i) for i in range(3)]
        s_wd = [p.sem("s_wd%d" % i) for i in range(3)]
        s_lds = p.sem("s_lds")
        s_ld = p.sem("s_ld")
        s_trm = p.sem("s_trm")
        s_trx = p.sem("s_trx")
        s_tr = p.sem("s_tr")
        s_trd = p.sem("s_trd")
        s_kd = p.sem("s_kd")
        s_mm = p.sem("s_mm")
        s_v = p.sem("s_v")
        s_a = p.sem("s_a")
        s_g = p.sem("s_g")
        s_o = p.sem("s_o")

        p.dma("gpsimd", s_cast, xm_bf, xm_in)
        p.dma("gpsimd", s_cast, m_bf, m_in)
        cast_xm = s_cast.n
        for e_ in range(NEXP):
            p.dma("gpsimd", s_cast, eg_bf[e_], eg_in[e_])
            p.dma("gpsimd", s_cast, eu_bf[e_], eu_in[e_])
            p.dma("gpsimd", s_cast, ed_bf[e_], ed_in[e_])
        p.dma("gpsimd", s_cast, sg_bf, sg_in)
        p.dma("gpsimd", s_cast, su_bf, su_in)
        p.dma("gpsimd", s_cast, sd_bf, sd_in)
        cast_all = s_cast.n

        p.dma("sync", s_lds, st_in[:, :], stats_in)
        ld_st = s_lds.n
        p.op("gpsimd", lambda e: e.memset(ones8[:, :], 1.0), s_g)
        p.op("gpsimd", lambda e: e.memset(wmcol[:, :], 0.0), s_g)
        g_0 = s_g.n

        p.wait("tensor", s_lds, ld_st)
        p.wait("tensor", s_g, g_0)
        p.mm(ps_dn[3][0:1, 0:16], ones8[:, 0:1], st_in[:, :], start=True, stop=True, inc=s_mm)
        p.wait("scalar", s_mm, s_mm.n)
        p.op("scalar", lambda e: e.copy(st_sb[0:1, :], ps_dn[3][0:1, 0:16]), s_a)
        a_st = s_a.n
        p.wait("vector", s_a, a_st)
        p.op("vector", lambda e: e.tensor_scalar_max(st_sb[0:1, 0:8], st_sb[0:1, 0:8], 1.0), s_v)
        p.wait("vector", s_v, s_v.n)
        p.op("vector", lambda e: e.reciprocal(st_sb[0:1, 0:8], st_sb[0:1, 0:8]), s_v)
        p.wait("vector", s_v, s_v.n)
        p.op("vector", lambda e: e.tensor_mul(wm_row[0:1, :], st_sb[0:1, 8:16], st_sb[0:1, 0:8]), s_v)
        v_wm = s_v.n
        p.wait("sync", s_v, v_wm)
        p.dma("sync", s_kd, sc3[0:1, 0:8], wm_row[0:1, :])
        kd_wm = s_kd.n
        p.wait("sync", s_kd, kd_wm)
        p.wait("sync", s_g, g_0)
        p.dma("sync", s_kd, wmcol[0:8, 0:1], sc3[0:1, 0:8].rearrange("a b -> b a"),
              allow_slow_non_contiguous=True)
        kd_wc = s_kd.n

        mTv = a_sl.rearrange("p (a b) -> p (a b)") if False else a_sl
        p.wait("scalar", s_cast, cast_xm)
        p.dma("scalar", s_trm, mTv[:, 0:TOK], m_bf[0:TOK, 0:128], transpose=True)
        tr_m = s_trm.n
        p.wait("vector", s_trm, tr_m)
        p.wait("vector", s_kd, kd_wc)
        p.op("vector", lambda e: e.tensor_scalar_mul(mTv[0:16, 0:TOK], mTv[0:16, 0:TOK], wmcol[0:16, 0:1]), s_v)
        v_swm = s_v.n
        p.wait("sync", s_v, v_swm)
        p.dma("sync", s_kd, selwm_dram, mTv[0:16, 0:TOK])
        kd_swm = s_kd.n

        for pb in range(8):
            cs = slice(pb * 128, (pb + 1) * 128)
            p.dma("scalar", s_trx, xmc[:, pb * TOK:(pb + 1) * TOK],
                  xm_bf[0:TOK, cs], transpose=True)
        tr_xm = s_trx.n

        gu_used = [0, 0, 0]
        gu_tr = [0, 0, 0]
        wd_used = [0, 0, 0]
        wd_tr = [0, 0, 0]
        NE = NEXP + 1

        def _bi(ex_abs):
            return 2 if (ex_abs % NE) == NEXP else (ex_abs % 2)

        def load_gu(ex_abs):
            ex = ex_abs % NE
            tau = ex_abs // NE
            bi = _bi(ex_abs)
            shared = (ex == NEXP)
            nob = 8 if shared else 4
            ncols = nob * 128 * 8
            p.wait("scalar", s_cast, cast_all)
            p.wait("scalar", s_mm, gu_used[bi])
            if tau == 0:
                p.wait("scalar", s_kd, s_kd.n)
                g_src = sg_bf if shared else eg_bf[ex]
                u_src = su_bf if shared else eu_bf[ex]
                for ib in range(8):
                    p.dma("scalar", s_gu[bi], wg_buf[bi][:, ib * (nob * 128):(ib + 1) * (nob * 128)],
                          g_src[0:nob * 128, ib * 128:(ib + 1) * 128], transpose=True)
                    p.dma("scalar", s_gu[bi], wu_buf[bi][:, ib * (nob * 128):(ib + 1) * (nob * 128)],
                          u_src[0:nob * 128, ib * 128:(ib + 1) * 128], transpose=True)
                gu_tr[bi] = s_gu[bi].n
                p.wait("sync", s_gu[bi], s_gu[bi].n)
                p.dma("sync", s_kd, guT_dram[2 * ex, 0:128, 0:ncols], wg_buf[bi][:, 0:ncols])
                p.dma("sync", s_kd, guT_dram[2 * ex + 1, 0:128, 0:ncols], wu_buf[bi][:, 0:ncols])
            else:
                p.wait("scalar", s_kd, s_kd.n)
                p.dma("scalar", s_gu[bi], wg_buf[bi][:, 0:ncols], guT_dram[2 * ex, 0:128, 0:ncols])
                p.dma("scalar", s_gu[bi], wu_buf[bi][:, 0:ncols], guT_dram[2 * ex + 1, 0:128, 0:ncols])
                gu_tr[bi] = s_gu[bi].n

        def load_wd(ex_abs):
            ex = ex_abs % NE
            tau = ex_abs // NE
            bi = _bi(ex_abs)
            shared = (ex == NEXP)
            nib = 8 if shared else 4
            ncols = nib * C
            p.wait("scalar", s_cast, cast_all)
            p.wait("scalar", s_mm, wd_used[bi])
            if tau == 0:
                p.wait("scalar", s_kd, s_kd.n)
                d_src = sd_bf if shared else ed_bf[ex]
                for ib in range(nib):
                    p.dma("scalar", s_wd[bi], wd_buf[bi][:, ib * C:(ib + 1) * C],
                          d_src[0:C, ib * 128:(ib + 1) * 128], transpose=True)
                wd_tr[bi] = s_wd[bi].n
                p.wait("sync", s_wd[bi], s_wd[bi].n)
                p.dma("sync", s_kd, wdT_dram[ex, 0:128, 0:ncols], wd_buf[bi][:, 0:ncols])
            else:
                p.wait("scalar", s_kd, s_kd.n)
                p.dma("scalar", s_wd[bi], wd_buf[bi][:, 0:ncols], wdT_dram[ex, 0:128, 0:ncols])
                wd_tr[bi] = s_wd[bi].n

        d1_evac = [[0, 0], [0, 0]]
        psum01_read = 0
        load_gu(0)
        load_gu(1)
        p.wait("scalar", s_kd, s_kd.n)
        for tau in range(4):
            t0 = tau * SLAB
            p.wait("sync", s_v, s_v.n)
            p.dma("sync", s_ld,
                  xm_sl.rearrange("p (tb c) -> p tb c", c=C),
                  xm_in[t0:t0 + SLAB, :].rearrange("(tb p) c -> p tb c", p=128))
            x_ld = s_ld.n

            p.wait("tensor", s_v, psum01_read)
            for ex in range(NE):
                h_prev = s_v.n
                ex_abs = tau * NE + ex
                bi = _bi(ex_abs)
                shared = (ex == NEXP)
                nob = 8 if shared else 4
                if not shared:
                    p.wait("gpsimd", s_kd, s_kd.n)
                    p.wait("gpsimd", s_v, s_v.n)
                    p.dma("gpsimd", s_csel, selwm_bc[:, :],
                          selwm_dram[ex:ex + 1, t0:t0 + SLAB].partition_broadcast(128))
                sel_cast = s_csel.n

                for ob in range(nob):
                    bk = ob % 2
                    bank = ps_dn[bk][:, 0:512]
                    p.wait("tensor", s_a, d1_evac[bk][0])
                    p.wait("tensor", s_gu[bi], gu_tr[bi])
                    p.wait("tensor", s_trx, tr_xm)
                    for ib in range(8):
                        p.mm(bank,
                             wg_buf[bi][:, ib * (nob * 128) + ob * 128: ib * (nob * 128) + (ob + 1) * 128],
                             xmc[:, ib * TOK + t0: ib * TOK + t0 + SLAB],
                             start=(ib == 0), stop=(ib == 7),
                             inc=(s_mm if ib == 7 else None))
                    p.wait("scalar", s_mm, s_mm.n)
                    p.wait("scalar", s_v, h_prev)
                    p.op("scalar", lambda e, bank=bank, ob=ob: e.activation(
                        a_sl[:, ob * SLAB:(ob + 1) * SLAB], bank, AF.Silu), s_a)
                    d1_evac[bk][0] = s_a.n
                    bank2 = ps_dn[bk][:, 512:1024]
                    p.wait("tensor", s_a, d1_evac[bk][1])
                    for ib in range(8):
                        p.mm(bank2,
                             wu_buf[bi][:, ib * (nob * 128) + ob * 128: ib * (nob * 128) + (ob + 1) * 128],
                             xmc[:, ib * TOK + t0: ib * TOK + t0 + SLAB],
                             start=(ib == 0), stop=(ib == 7),
                             inc=(s_mm if ib == 7 else None))
                    p.wait("scalar", s_mm, s_mm.n)
                    p.wait("scalar", s_v, h_prev)
                    p.op("scalar", lambda e, bank2=bank2, ob=ob: e.copy(
                        u_sl[:, ob * SLAB:(ob + 1) * SLAB], bank2), s_a)
                    d1_evac[bk][1] = s_a.n
                gu_used[bi] = s_mm.n
                a_gu = s_a.n
                if ex_abs + 2 < 4 * NE:
                    load_gu(ex_abs + 2)
                if tau == 0 and ex == NE - 2:
                    load_wd(0)
                if tau == 0 and ex == NE - 1:
                    load_wd(1)
                p.wait("vector", s_a, a_gu)
                p.wait("vector", s_csel, sel_cast)
                dst = hsh if shared else he_buf[ex]
                for ob in range(nob):
                    sl = slice(ob * SLAB, (ob + 1) * SLAB)
                    p.op("vector", lambda e, sl=sl, dst=dst: e.tensor_mul(
                        dst[:, sl], a_sl[:, sl], u_sl[:, sl]), s_v)
                p.wait("vector", s_v, s_v.n)
                if not shared:
                    for ob in range(nob):
                        sl = slice(ob * SLAB, (ob + 1) * SLAB)
                        p.op("vector", lambda e, sl=sl, dst=dst: e.tensor_mul(
                            dst[:, sl], dst[:, sl], selwm_bc[:, :]), s_v)
            v_h = s_v.n

            for ex in range(NE):
                ex_abs = tau * NE + ex
                bi = _bi(ex_abs)
                shared = (ex == NEXP)
                nib = 8 if shared else 4
                buf = wd_buf[bi]
                p.wait("tensor", s_wd[bi], wd_tr[bi])
                p.wait("tensor", s_v, v_h)
                if ex == 0:
                    p.wait("tensor", s_a, s_a.n)
                hsrc = hsh if shared else he_buf[ex]
                for tb in range(4):
                    for oh in range(2):
                        for ib in range(nib):
                            last_of_ex = (tb == 3 and oh == 1 and ib == nib - 1)
                            p.mm(ps_dn[tb][:, oh * 512:(oh + 1) * 512],
                                 hsrc[:, ib * SLAB + tb * 128: ib * SLAB + (tb + 1) * 128],
                                 buf[:, ib * C + oh * 512: ib * C + (oh + 1) * 512],
                                 start=(ex == 0 and ib == 0),
                                 stop=(ex == NEXP and ib == nib - 1),
                                 inc=(s_mm if last_of_ex else None))
                wd_used[bi] = s_mm.n
                if ex_abs + 2 < 4 * NE:
                    load_wd(ex_abs + 2)
            mm_dn = s_mm.n

            for tb in range(4):
                p.wait("vector", s_mm, mm_dn)
                p.wait("vector", s_ld, x_ld)
                p.op("vector", lambda e, tb=tb: e.tensor_add(
                    r2_t[:, :], xm_sl[:, tb * C:(tb + 1) * C], ps_dn[tb][:, :]), s_v)
                p.wait("vector", s_v, s_v.n)
                p.op("vector", lambda e: e.tensor_reduce(
                    col1[:, 0:1], r2_t[:, :], axis=AX.X, op=ALU.add), s_v)
                p.wait("vector", s_v, s_v.n)
                p.op("vector", lambda e: e.tensor_scalar_mul(col1[:, 0:1], col1[:, 0:1], -1.0 / C), s_v)
                p.wait("vector", s_v, s_v.n)
                p.op("vector", lambda e: e.tensor_scalar_add(r2_t[:, :], r2_t[:, :], col1[:, 0:1]), s_v)
                p.wait("scalar", s_v, s_v.n)
                p.op("scalar", lambda e, tb=tb: e.activation(
                    xm_sl[:, tb * C:(tb + 1) * C], r2_t[:, :], AF.Square,
                    accum_out=col2[:, 0:1]), s_a)
                p.wait("vector", s_a, s_a.n)
                p.op("vector", lambda e: e.tensor_scalar(
                    col2[:, 0:1], col2[:, 0:1], 1.0 / C, EPS_LN, ALU.mult, ALU.add), s_v)
                p.wait("vector", s_v, s_v.n)
                p.op("vector", lambda e: e.reciprocal(col2[:, 0:1], col2[:, 0:1]), s_v)
                v_l2 = s_v.n
                p.wait("scalar", s_v, v_l2)
                p.op("scalar", lambda e: e.activation(col2[:, 0:1], col2[:, 0:1], AF.Sqrt), s_a)
                a_l2 = s_a.n
                p.wait("vector", s_a, a_l2)
                p.op("vector", lambda e: e.tensor_scalar_mul(r2_t[:, :], r2_t[:, :], col2[:, 0:1]), s_v)
                v_out = s_v.n
                p.wait("sync", s_v, v_out)
                p.dma("sync", s_o, o_out[t0 + tb * 128:t0 + (tb + 1) * 128, :], r2_t[:, :])
                p.wait("vector", s_o, s_o.n)
                if tb == 1:
                    psum01_read = s_v.n

        p.wait("sync", s_o, s_o.n)
        p.build_block()
    return nc



def kernel(**inputs):
    x = np.ascontiguousarray(np.asarray(inputs["L0_attn_xq"], np.float32))
    tm = np.ascontiguousarray(np.stack([
        np.asarray(inputs["L0_attn_time_mix_r"], np.float32).reshape(C),
        np.asarray(inputs["L0_attn_time_mix_k"], np.float32).reshape(C),
        np.asarray(inputs["L0_attn_time_mix_v"], np.float32).reshape(C),
        np.asarray(inputs["L0_attn_time_mix_g"], np.float32).reshape(C),
    ]))
    td = np.ascontiguousarray(np.asarray(inputs["L0_attn_time_decay"], np.float32).reshape(1, H))
    fa = np.ascontiguousarray(np.asarray(inputs["L0_attn_time_faaaa"], np.float32).reshape(1, H))
    f32c = lambda k: np.ascontiguousarray(np.asarray(inputs[k], np.float32))

    nc1 = build_l1()
    maps1 = []
    for c in range(NCORES):
        b, q = c // 4, c % 4
        t0 = q * TOK
        xp = np.zeros((TOK + 1, C), np.float32)
        xp[1:] = x[b, t0:t0 + TOK]
        if q > 0:
            xp[0] = x[b, t0 - 1]
        maps1.append({
            "xp": xp,
            "wr": f32c("L0_attn_W_r"), "wk": f32c("L0_attn_W_k"),
            "wv": f32c("L0_attn_W_v"), "wg": f32c("L0_attn_W_g"),
            "tm": tm, "td": td, "fa": fa,
        })
    res1 = run_bass_kernel_spmd(nc1, maps1, core_ids=list(range(NCORES)))
    r1 = res1.results

    nc2 = build_l2()
    maps2 = []
    for c in range(NCORES):
        b, q = c // 4, c % 4
        t0 = q * TOK
        sall = np.concatenate([np.asarray(r1[b * 4 + qq]["o_sc"], np.float32)
                               for qq in range(4)], axis=0)
        coefk = np.zeros((1, 4), np.float32)
        maskk = np.zeros((1, 4), np.float32)
        for qq in range(q):
            coefk[0, qq] = float(TOK * (q - 1 - qq))
            maskk[0, qq] = 1.0
        maps2.append({
            "x": x[b, t0:t0 + TOK],
            "xo": np.asarray(r1[c]["o_xo"]).view(np.uint16).view(ml_dtypes.bfloat16)
                  if np.asarray(r1[c]["o_xo"]).dtype != ml_dtypes.bfloat16
                  else np.asarray(r1[c]["o_xo"]),
            "rwb": np.asarray(r1[c]["o_rwb"]),
            "g": np.asarray(r1[c]["o_g"]),
            "sall": sall, "coefk": coefk, "maskk": maskk,
            "wo": f32c("L0_attn_W_o"), "td": td,
            "lnw": np.asarray(inputs["L0_attn_ln_w"], np.float32).reshape(1, C),
            "lnb": np.asarray(inputs["L0_attn_ln_b"], np.float32).reshape(1, C),
            "gw": np.concatenate([f32c("L0_ffn_gate_w"),
                                  np.zeros((8, C), np.float32)], axis=0),
            "lb": np.asarray(inputs["L0_ffn_lb_bias"], np.float32).reshape(1, 8),
        })
    res2 = run_bass_kernel_spmd(nc2, maps2, core_ids=list(range(NCORES)))
    r2 = res2.results

    nc3 = build_l3()
    stats_all = np.concatenate([np.asarray(r2[c]["o_stats"], np.float32)
                                for c in range(NCORES)], axis=0)
    maps3 = []
    for c in range(NCORES):
        maps3.append({
            "xm": np.asarray(r2[c]["o_xmid"], np.float32),
            "m": np.concatenate([np.asarray(r2[c]["o_m"], np.float32),
                                 np.zeros((TOK, 112), np.float32)], axis=1),
            "stats": stats_all,
            "eg": f32c("L0_ffn_expert_gate_w"),
            "eu": f32c("L0_ffn_expert_up_w"),
            "ed": f32c("L0_ffn_expert_down_w"),
            "sg": f32c("L0_ffn_shared_gate_w"),
            "su": f32c("L0_ffn_shared_up_w"),
            "sd": f32c("L0_ffn_shared_down_w"),
        })
    res3 = run_bass_kernel_spmd(nc3, maps3, core_ids=list(range(NCORES)))
    r3 = res3.results

    out = np.zeros((B, TT, C), np.float32)
    for c in range(NCORES):
        b, q = c // 4, c % 4
        out[b, q * TOK:(q + 1) * TOK] = np.asarray(r3[c]["o_out"], np.float32)
    return out


# version 08
# speedup vs baseline: 1.8837x; 1.0200x over previous
import numpy as np
import ml_dtypes
from contextlib import ExitStack

import concourse.bass as bass
import concourse.mybir as mybir
from concourse.bass_utils import run_bass_kernel_spmd

F32 = mybir.dt.float32
BF16 = mybir.dt.bfloat16
U32 = mybir.dt.uint32
AF = mybir.ActivationFunctionType
ALU = mybir.AluOpType
AX = mybir.AxisListType

B = 2
TT = 8192
C = 1024
H = 16
KD = 64
CH = 128
TOK = 2048
NCH = TOK // CH
SLAB = 512
NSLAB = TOK // SLAB
HEAD_DIV = 8.0
NCORES = 8
EPS_GN = 1e-5
EPS_LN = 1e-5
NEXP = 8
EI = 512
SI = 1024


class Ctr:
    def __init__(self, ctx, nc, name):
        self.h = ctx.enter_context(nc.semaphore(name))
        self.n = 0
        self.dma = False


class Prog:
    ENGINES = ("tensor", "vector", "scalar", "gpsimd", "sync")

    def __init__(self, nc, ctx):
        self.nc = nc
        self.ctx = ctx
        self.q = {e: [] for e in self.ENGINES}

    def sem(self, name):
        return Ctr(self.ctx, self.nc, name)

    def wait(self, eng, ctr, val=None):
        v = ctr.n if (val is None or ctr.dma) else val
        if v > 0:
            self.q[eng].append(lambda e, c=ctr, v=v: e.wait_ge(c.h, v))

    def mm(self, psum, lhsT, rhs, start, stop, inc=None, tile_position=None):
        try:
            nfree = rhs.shape[-1]
        except Exception:
            nfree = 512
        k = 8 if nfree >= 256 else 4

        def fn(e, psum=psum, lhsT=lhsT, rhs=rhs, start=start, stop=stop, inc=inc,
               tp=tile_position, k=k):
            ins = e.matmul(psum, lhsT, rhs, start=start, stop=stop, tile_position=tp)
            if inc is not None:
                ins.then_inc(inc.h, 1)
            for _ in range(k):
                e.nop()
        self.q["tensor"].append(fn)
        if inc is not None:
            inc.n += 1

    def op(self, eng, fn, inc=None):
        def wrapped(e, fn=fn, inc=inc):
            ins = fn(e)
            if inc is not None:
                ins.then_inc(inc.h, 1)
        self.q[eng].append(wrapped)
        if inc is not None:
            inc.n += 1

    def dma(self, eng, ctr, out, in_, transpose=False, **kw):
        ctr.dma = True
        def fn(e, out=out, in_=in_, transpose=transpose, kw=kw):
            e.dma_start(out, in_, transpose=transpose, **kw).then_inc(ctr.h, 16)
        self.q[eng].append(fn)
        ctr.n += 16

    def build_block(self):
        block = self.ctx.enter_context(self.nc.Block())
        q = self.q

        @block.tensor
        def _(e):
            for fn in q["tensor"]:
                fn(e)

        @block.vector
        def _(e):
            for fn in q["vector"]:
                fn(e)

        @block.scalar
        def _(e):
            for fn in q["scalar"]:
                fn(e)

        @block.gpsimd
        def _(e):
            for fn in q["gpsimd"]:
                fn(e)

        @block.sync
        def _(e):
            for fn in q["sync"]:
                fn(e)


def _head(h):
    return slice((h % 2) * KD, (h % 2) * KD + KD), h // 2



def build_l1(debug=False):
    nc = bass.Bass("TRN2", target_bir_lowering=False)
    xp = nc.dram_tensor("xp", [TOK + 1, C], F32, kind="ExternalInput").ap()
    w_in = {nm: nc.dram_tensor(nm, [C, C], F32, kind="ExternalInput").ap()
            for nm in ("wr", "wk", "wv", "wg")}
    tm_in = nc.dram_tensor("tm", [4, C], F32, kind="ExternalInput").ap()
    td_in = nc.dram_tensor("td", [1, H], F32, kind="ExternalInput").ap()
    fa_in = nc.dram_tensor("fa", [1, H], F32, kind="ExternalInput").ap()

    o_sc = nc.dram_tensor("o_sc", [128, 512], F32, kind="ExternalOutput").ap()
    o_xo = nc.dram_tensor("o_xo", [TOK, C], BF16, kind="ExternalOutput").ap()
    o_rwb = nc.dram_tensor("o_rwb", [128, 8 * TOK], BF16, kind="ExternalOutput").ap()
    o_g = nc.dram_tensor("o_g", [TOK, C], BF16, kind="ExternalOutput").ap()

    x_bf = nc.dram_tensor("x_bf", [TOK + 1, C], BF16, kind="Internal").ap()
    w_bf = {nm: nc.dram_tensor(nm + "_bf", [C, C], BF16, kind="Internal").ap()
            for nm in ("wr", "wk", "wv", "wg")}
    kc_dram = nc.dram_tensor("kc_dram", [C, TOK], BF16, kind="Internal").ap()
    sc1 = nc.dram_tensor("sc1", [1, 64], F32, kind="Internal").ap()

    dbg = {}
    if debug:
        for nm, shp, dt_ in [("d_wmat", [128, H * CH], F32), ("d_wb", [128, 8 * CH], BF16),
                             ("d_wk", [128, H], F32), ("d_ws", [128, 512], F32),
                             ("d_ldbc", [128, H], F32),
                             ("d_mixr", [128, 8 * SLAB], BF16), ("d_mixk", [128, 8 * SLAB], BF16),
                             ("d_rsl", [128, 8 * SLAB], BF16), ("d_kcs", [128, 8 * SLAB], BF16),
                             ("d_kts", [128, 4 * C], BF16), ("d_vts", [128, 4 * C], BF16),
                             ("d_xc", [128, 8 * SLAB], BF16), ("d_xxc", [128, 8 * SLAB], BF16)]:
            dbg[nm] = nc.dram_tensor(nm, shp, dt_, kind="ExternalOutput").ap()

    with ExitStack() as ctx:
        p = Prog(nc, ctx)
        sb = lambda n, s, d=F32: ctx.enter_context(nc.sbuf_tensor(n, s, d))
        psl = lambda n, s: ctx.enter_context(nc.psum_tensor(n, s, F32))

        wT = {nm: sb(nm + "T", [128, 8 * C], BF16) for nm in ("wr", "wk", "wv", "wg")}
        xc = sb("xc", [128, 8 * SLAB], BF16)
        xxc = sb("xxc", [128, 8 * SLAB], BF16)
        mixed = {nm: sb("mx_" + nm, [128, 8 * SLAB], BF16) for nm in ("r", "k", "v", "g")}
        r_sl = sb("r_sl", [128, 8 * SLAB], BF16)
        rwb_sl = sb("rwb_sl", [128, 8 * SLAB], BF16)
        kcs = sb("kcs", [128, 8 * SLAB], BF16)
        kts = sb("kts", [128, 4 * C], BF16)
        kwk = sb("kwk", [128, C], BF16)
        vts = sb("vts", [128, 4 * C], BF16)
        g_sl = sb("g_sl", [128, 4 * C], BF16)
        xo_sl = sb("xo_sl", [128, 4 * C], BF16)
        attw = sb("attw", [128, 2 * CH], BF16)
        s_sb = sb("s_sb", [128, 512], F32)
        s_tmp = sb("s_tmp", [128, 512], F32)
        s_bf = sb("s_bf", [128, 512], BF16)

        tm_cols = sb("tm_cols", [128, 32], F32)
        td_row = sb("td_row", [1, H], F32)
        ld_row = sb("ld_row", [1, H], F32)
        ws_row = sb("ws_row", [1, H], F32)
        fa_row = sb("fa_row", [1, H], F32)
        ld_bc = sb("ld_bc", [128, H], F32)
        u_bc = sb("u_bc", [128, H], F32)
        ws_bc = sb("ws_bc", [128, H], F32)
        ldcol8 = sb("ldcol8", [128, 8], F32)
        wmatT = sb("wmatT", [128, H * CH], F32)
        wb_ct = sb("wb_ct", [128, 8 * CH], BF16)
        wk_cols = sb("wk_cols", [128, H], F32)
        ws_tile = sb("ws_tile", [128, 512], F32)
        iota_d = sb("iota_d", [128, CH], F32)
        iota_f = sb("iota_f", [128, CH], F32)
        clampd = sb("clampd", [128, CH], F32)
        rev_col = sb("rev_col", [128, 1], F32)
        ones128 = sb("ones128", [128, CH], F32)
        eye = sb("eye", [128, CH], F32)

        ps_proj = [psl("ps_proj%d" % i, [128, SLAB]) for i in range(2)]
        ps_att = [psl("ps_att%d" % i, [128, CH]) for i in range(2)]
        ps_out = psl("ps_out", [128, C])
        ps_A = psl("ps_A", [128, 512])

        s_cast = p.sem("s_cast")
        s_ld = p.sem("s_ld")
        s_xld = p.sem("s_xld")
        s_tr = p.sem("s_tr")
        s_kd = p.sem("s_kd")
        s_kc = p.sem("s_kc")
        s_mm = p.sem("s_mm")
        s_v = p.sem("s_v")
        s_a = p.sem("s_a")
        s_g = p.sem("s_g")
        s_o = p.sem("s_o")

        p.dma("gpsimd", s_cast, x_bf, xp)
        cast_x = s_cast.n
        for nm in ("wr", "wk", "wv", "wg"):
            p.dma("gpsimd", s_cast, w_bf[nm], w_in[nm])
        cast_w = s_cast.n

        p.dma("sync", s_ld, td_row[0:1, :], td_in)
        p.dma("sync", s_ld, fa_row[0:1, :], fa_in)
        p.dma("sync", s_ld, tm_cols[:, :],
              tm_in.rearrange("m (p c) -> c (m p)", c=128),
              allow_slow_non_contiguous=True)
        ld0 = s_ld.n

        p.wait("scalar", s_ld, ld0)
        p.op("scalar", lambda e: e.activation(ld_row[0:1, :], td_row[0:1, :], AF.Exp), s_a)
        a_exp = s_a.n
        p.wait("vector", s_a, a_exp)
        p.op("vector", lambda e: e.tensor_scalar_mul(ld_row[0:1, :], ld_row[0:1, :], -1.0), s_v)
        v_ld = s_v.n
        p.wait("scalar", s_v, v_ld)
        p.op("scalar", lambda e: e.activation(ws_row[0:1, :], ld_row[0:1, :], AF.Exp,
                                              scale=float(CH)), s_a)
        a_ws = s_a.n
        p.wait("sync", s_v, v_ld)
        p.dma("sync", s_kd, sc1[0:1, 0:H], ld_row[0:1, :])
        p.wait("sync", s_a, a_ws)
        p.dma("sync", s_kd, sc1[0:1, 16:16 + H], ws_row[0:1, :])
        p.wait("sync", s_ld, ld0)
        p.dma("sync", s_kd, sc1[0:1, 32:32 + H], fa_row[0:1, :])
        kd0 = s_kd.n

        p.wait("gpsimd", s_kd, kd0)
        p.dma("gpsimd", s_cast, ld_bc[:, :], sc1[0:1, 0:H].partition_broadcast(128))
        p.dma("gpsimd", s_cast, ws_bc[:, :], sc1[0:1, 16:16 + H].partition_broadcast(128))
        p.dma("gpsimd", s_cast, u_bc[:, :], sc1[0:1, 32:32 + H].partition_broadcast(128))
        for pb in range(8):
            p.dma("gpsimd", s_cast, ldcol8[0:64, pb:pb + 1],
                  sc1[0:1, 2 * pb:2 * pb + 1].partition_broadcast(64))
            p.dma("gpsimd", s_cast, ldcol8[64:128, pb:pb + 1],
                  sc1[0:1, 2 * pb + 1:2 * pb + 2].partition_broadcast(64))
        cast_bc = s_cast.n

        p.op("gpsimd", lambda e: e.iota(iota_d[:, :], pattern=[[1, CH]], base=-1,
                                        channel_multiplier=-1,
                                        allow_small_or_imprecise_dtypes=True), s_g)
        p.op("gpsimd", lambda e: e.iota(iota_f[:, :], pattern=[[1, CH]], base=0,
                                        channel_multiplier=0,
                                        allow_small_or_imprecise_dtypes=True), s_g)
        p.op("gpsimd", lambda e: e.iota(rev_col[:, 0:1], pattern=[[0, 1]], base=127,
                                        channel_multiplier=-1,
                                        allow_small_or_imprecise_dtypes=True), s_g)
        p.op("gpsimd", lambda e: e.memset(ones128[:, :], 1.0), s_g)
        g_iota = s_g.n
        p.wait("gpsimd", s_g, g_iota)
        p.op("gpsimd", lambda e: e.affine_select(eye[:, :], ones128[:, :],
                                                 pattern=[[1, CH]], compare_op=ALU.is_equal,
                                                 fill=0.0, base=0, channel_multiplier=-1), s_g)
        g_eye = s_g.n

        p.wait("vector", s_g, g_iota)
        p.op("vector", lambda e: e.tensor_scalar_max(clampd[:, :], iota_d[:, :], 0.0), s_v)
        v_clamp = s_v.n

        p.wait("scalar", s_v, v_clamp)
        p.wait("scalar", s_cast, cast_bc)
        for h in range(H):
            sl = slice(h * CH, (h + 1) * CH)
            p.op("scalar", lambda e, sl=sl, h=h: e.activation(
                wmatT[:, sl], clampd[:, :], AF.Exp, scale=ld_bc[:, h:h + 1]), s_a)
        a_wpow = s_a.n
        p.wait("gpsimd", s_a, a_wpow)
        for h in range(H):
            sl = slice(h * CH, (h + 1) * CH)
            p.op("gpsimd", lambda e, sl=sl: e.affine_select(
                wmatT[:, sl], wmatT[:, sl], pattern=[[1, CH]], compare_op=ALU.is_ge,
                fill=0.0, base=-1, channel_multiplier=-1), s_g)
        g_wmask = s_g.n
        p.wait("vector", s_g, g_wmask)
        p.wait("vector", s_cast, cast_bc)
        for h in range(H):
            sl = slice(h * CH, (h + 1) * CH)
            p.op("vector", lambda e, sl=sl, h=h: e.scalar_tensor_tensor(
                wmatT[:, sl], eye[:, :], u_bc[:, h:h + 1], wmatT[:, sl], ALU.mult, ALU.add), s_v)
        v_wmat = s_v.n

        p.wait("scalar", s_cast, cast_bc)
        p.wait("scalar", s_g, g_iota)
        for pb in range(8):
            sl = slice(pb * CH, (pb + 1) * CH)
            p.op("scalar", lambda e, sl=sl, pb=pb: e.activation(
                wb_ct[:, sl], iota_f[:, :], AF.Exp, scale=ldcol8[:, pb:pb + 1]), s_a)
        a_wb = s_a.n

        for h in range(H):
            p.op("scalar", lambda e, h=h: e.activation(
                wk_cols[:, h:h + 1], rev_col[:, 0:1], AF.Exp, scale=ld_bc[:, h:h + 1]), s_a)
        a_wk = s_a.n

        p.wait("vector", s_cast, cast_bc)
        for hp in range(8):
            for half in range(2):
                rows = slice(half * 64, half * 64 + 64)
                colh = 2 * hp + half
                p.op("vector", lambda e, rows=rows, hp=hp, colh=colh: e.tensor_scalar_mul(
                    ws_tile[rows, hp * 64:(hp + 1) * 64], ones128[0:64, 0:64],
                    ws_bc[rows, colh:colh + 1]), s_v)
        p.op("vector", lambda e: e.memset(s_sb[:, :], 0.0), s_v)
        p.op("vector", lambda e: e.memset(s_bf[:, :], 0.0), s_v)
        v_consts = s_v.n

        if debug:
            p.wait("sync", s_v, s_v.n)
            p.wait("sync", s_a, s_a.n)
            p.wait("sync", s_g, s_g.n)
            p.dma("sync", s_o, dbg["d_wmat"], wmatT[:, :])
            p.dma("sync", s_o, dbg["d_wb"], wb_ct[:, :])
            p.dma("sync", s_o, dbg["d_wk"], wk_cols[:, :])
            p.dma("sync", s_o, dbg["d_ws"], ws_tile[:, :])
            p.dma("sync", s_o, dbg["d_ldbc"], ld_bc[:, :])

        p.wait("scalar", s_cast, cast_w)
        for nm in ("wr", "wk", "wv", "wg"):
            for ib in range(8):
                p.dma("scalar", s_tr, wT[nm][:, ib * C:(ib + 1) * C],
                      w_bf[nm][0:C, ib * 128:(ib + 1) * 128], transpose=True)
        tr_w = s_tr.n

        bank_evac = [0, 0]
        mix_done_v = 0
        proj_done_mm = 0
        for s in range(NSLAB):
            t0 = s * SLAB
            p.wait("scalar", s_o, s_o.n)
            p.wait("vector", s_o, s_o.n)
            p.wait("scalar", s_cast, cast_x)
            p.wait("scalar", s_v, mix_done_v)
            for pb in range(8):
                cs = slice(pb * 128, (pb + 1) * 128)
                p.dma("scalar", s_xld, xc[:, pb * SLAB:(pb + 1) * SLAB],
                      x_bf[1 + t0:1 + t0 + SLAB, cs], transpose=True)
                p.dma("scalar", s_xld, xxc[:, pb * SLAB:(pb + 1) * SLAB],
                      x_bf[t0:t0 + SLAB, cs], transpose=True)
            xld = s_xld.n

            p.wait("vector", s_xld, xld)
            p.wait("vector", s_mm, proj_done_mm)
            for mi, nm in enumerate(("r", "k", "v", "g")):
                for pb in range(8):
                    sl = slice(pb * SLAB, (pb + 1) * SLAB)
                    p.op("vector", lambda e, sl=sl, nm=nm: e.tensor_sub(
                        mixed[nm][:, sl], xc[:, sl], xxc[:, sl]), s_v)
            v_sub = s_v.n
            p.wait("vector", s_v, v_sub)
            for mi, nm in enumerate(("r", "k", "v", "g")):
                for pb in range(8):
                    sl = slice(pb * SLAB, (pb + 1) * SLAB)
                    tcol = tm_cols[:, mi * 8 + pb:mi * 8 + pb + 1]
                    p.op("vector", lambda e, sl=sl, tcol=tcol, nm=nm: e.scalar_tensor_tensor(
                        mixed[nm][:, sl], mixed[nm][:, sl], tcol, xxc[:, sl],
                        ALU.mult, ALU.add), s_v)
            v_mix = s_v.n
            mix_done_v = v_mix

            def cmaj(nm_w, nm_x, dst):
                for ob in range(8):
                    bk = ob % 2
                    bank = ps_proj[bk]
                    p.wait("tensor", s_a, bank_evac[bk])
                    p.wait("tensor", s_v, v_mix)
                    p.wait("tensor", s_tr, tr_w)
                    for ib in range(8):
                        p.mm(bank[:, :],
                             wT[nm_w][:, ib * C + ob * 128: ib * C + (ob + 1) * 128],
                             mixed[nm_x][:, ib * SLAB:(ib + 1) * SLAB],
                             start=(ib == 0), stop=(ib == 7),
                             inc=(s_mm if ib == 7 else None))
                    p.wait("scalar", s_mm, s_mm.n)
                    sl = slice(ob * SLAB, (ob + 1) * SLAB)
                    p.op("scalar", lambda e, bank=bank, dst=dst, sl=sl:
                         e.copy(dst[:, sl], bank[:, :]), s_a)
                    bank_evac[bk] = s_a.n

            def tmaj(nm_w, nm_x, dst, act=None):
                for tb in range(4):
                    for oh in range(2):
                        bk = (tb * 2 + oh) % 2
                        bank = ps_proj[bk]
                        p.wait("tensor", s_a, bank_evac[bk])
                        p.wait("tensor", s_v, v_mix)
                        p.wait("tensor", s_tr, tr_w)
                        for ib in range(8):
                            p.mm(bank[:, :],
                                 mixed[nm_x][:, ib * SLAB + tb * 128: ib * SLAB + (tb + 1) * 128],
                                 wT[nm_w][:, ib * C + oh * 512: ib * C + (oh + 1) * 512],
                                 start=(ib == 0), stop=(ib == 7),
                                 inc=(s_mm if ib == 7 else None))
                        p.wait("scalar", s_mm, s_mm.n)
                        sl = slice(tb * C + oh * 512, tb * C + (oh + 1) * 512)
                        if act is None:
                            p.op("scalar", lambda e, bank=bank, sl=sl, dst=dst:
                                 e.copy(dst[:, sl], bank[:, :]), s_a)
                        else:
                            p.op("scalar", lambda e, bank=bank, sl=sl, dst=dst, act=act:
                                 e.activation(dst[:, sl], bank[:, :], act), s_a)
                        bank_evac[bk] = s_a.n

            cmaj("wr", "r", r_sl)
            cmaj("wk", "k", kcs)
            a_rk = s_a.n
            tmaj("wv", "v", vts)
            tmaj("wg", "g", g_sl, act=AF.Silu)
            a_proj = s_a.n
            proj_done_mm = s_mm.n

            p.wait("vector", s_a, a_rk)
            p.wait("vector", s_a, a_wb)
            for pb in range(8):
                for cnn in range(4):
                    sl = slice(pb * SLAB + cnn * CH, pb * SLAB + (cnn + 1) * CH)
                    wsl = slice(pb * CH, (pb + 1) * CH)
                    p.op("vector", lambda e, sl=sl, wsl=wsl: e.tensor_mul(
                        rwb_sl[:, sl], r_sl[:, sl], wb_ct[:, wsl]), s_v)
            v_rwb = s_v.n

            p.wait("sync", s_a, a_rk)
            p.dma("sync", s_kc,
                  kc_dram[0:C, t0:t0 + SLAB].rearrange("(p a) t -> a p t", a=128),
                  kcs.rearrange("a (p t) -> a p t", t=SLAB))
            kwrt = s_kc.n
            p.wait("scalar", s_kc, kwrt)
            p.wait("scalar", s_mm, s_mm.n)
            for tb in range(4):
                p.dma("scalar", s_tr, kts[:, tb * C:(tb + 1) * C],
                      kc_dram[0:C, t0 + tb * 128:t0 + (tb + 1) * 128], transpose=True)
            tr_k = s_tr.n

            if debug and s == 0:
                p.wait("sync", s_v, s_v.n)
                p.wait("sync", s_a, s_a.n)
                p.wait("sync", s_tr, s_tr.n)
                p.dma("sync", s_o, dbg["d_xc"], xc[:, :])
                p.dma("sync", s_o, dbg["d_xxc"], xxc[:, :])
                p.dma("sync", s_o, dbg["d_mixr"], mixed["r"][:, :])
                p.dma("sync", s_o, dbg["d_mixk"], mixed["k"][:, :])
                p.dma("sync", s_o, dbg["d_rsl"], r_sl[:, :])
                p.dma("sync", s_o, dbg["d_kcs"], kcs[:, :])
                p.dma("sync", s_o, dbg["d_kts"], kts[:, :])
                p.dma("sync", s_o, dbg["d_vts"], vts[:, :])
                p.wait("vector", s_o, s_o.n)
                p.wait("tensor", s_o, s_o.n)
                p.wait("sync", s_o, s_o.n)
                p.wait("scalar", s_o, s_o.n)
                p.wait("gpsimd", s_o, s_o.n)

            for cnn in range(4):
                p.wait("vector", s_tr, tr_k)
                p.wait("vector", s_a, a_wk)
                p.wait("vector", s_mm, s_mm.n)
                for h in range(H):
                    p.op("vector", lambda e, h=h, cnn=cnn: e.tensor_scalar_mul(
                        kwk[:, h * KD:(h + 1) * KD],
                        kts[:, cnn * C + h * KD:cnn * C + (h + 1) * KD],
                        wk_cols[:, h:h + 1]), s_v)
                v_kwk = s_v.n

                attw_v = {}
                for h in range(H):
                    rows, pb = _head(h)
                    rsl = slice(pb * SLAB + cnn * CH, pb * SLAB + (cnn + 1) * CH)
                    bank = ps_att[h % 2]
                    if h >= 2:
                        p.wait("tensor", s_v, attw_v[h - 2])
                    p.wait("tensor", s_a, a_rk)
                    p.wait("tensor", s_v, v_rwb)
                    p.mm(bank[:, :], kcs[rows, rsl], r_sl[rows, rsl],
                         start=True, stop=True, inc=s_mm)
                    att_mm = s_mm.n
                    p.wait("vector", s_mm, att_mm)
                    p.wait("vector", s_v, v_wmat)
                    half = h % 2
                    p.op("vector", lambda e, bank=bank, h=h, half=half: e.tensor_mul(
                        attw[:, half * CH:(half + 1) * CH], bank[:, :],
                        wmatT[:, h * CH:(h + 1) * CH]), s_v)
                    attw_v[h] = s_v.n

                    p.wait("tensor", s_v, attw_v[h])
                    vsl = slice(cnn * C + h * KD, cnn * C + (h + 1) * KD)
                    p.mm(ps_out[:, h * KD:(h + 1) * KD],
                         attw[:, half * CH:(half + 1) * CH], vts[:, vsl],
                         start=True, stop=False)
                    srows = slice((h % 2) * KD, (h % 2) * KD + KD)
                    scols = slice((h // 2) * KD, (h // 2 + 1) * KD)
                    p.mm(ps_out[:, h * KD:(h + 1) * KD],
                         rwb_sl[rows, rsl], s_bf[srows, scols],
                         start=False, stop=True, inc=s_mm)
                    p.wait("tensor", s_v, v_kwk)
                    p.mm(ps_A[srows, scols], kwk[:, h * KD:(h + 1) * KD], vts[:, vsl],
                         start=True, stop=True, inc=s_mm,
                         tile_position=(0, (h % 2) * KD))
                mm_chunk = s_mm.n

                p.wait("vector", s_mm, mm_chunk)
                p.op("vector", lambda e: e.tensor_mul(s_tmp[:, :], s_sb[:, :], ws_tile[:, :]), s_v)
                p.wait("vector", s_v, s_v.n)
                p.op("vector", lambda e: e.tensor_add(s_sb[:, :], s_tmp[:, :], ps_A[:, :]), s_v)
                v_supd = s_v.n
                p.wait("vector", s_v, v_supd)
                p.op("vector", lambda e: e.tensor_copy(s_bf[:, :], s_sb[:, :]), s_v)
                v_sbf = s_v.n

                p.wait("scalar", s_mm, mm_chunk)
                p.op("scalar", lambda e, cnn=cnn: e.mul(
                    xo_sl[:, cnn * C:(cnn + 1) * C], ps_out[:, :], 1.0 / HEAD_DIV), s_a)
                a_xo = s_a.n
                p.wait("tensor", s_a, a_xo)
                p.wait("tensor", s_v, v_sbf)

            a_slab = s_a.n
            v_slab = s_v.n
            p.wait("sync", s_a, a_slab)
            p.wait("sync", s_v, v_slab)
            p.dma("sync", s_o,
                  o_xo[t0:t0 + SLAB, :].rearrange("(tb p) c -> p tb c", p=128),
                  xo_sl.rearrange("a (tb c) -> a tb c", c=C))
            p.dma("sync", s_o,
                  o_g[t0:t0 + SLAB, :].rearrange("(tb p) c -> p tb c", p=128),
                  g_sl.rearrange("a (tb c) -> a tb c", c=C))
            p.dma("sync", s_o,
                  o_rwb.rearrange("a (p t) -> a p t", t=TOK)[:, :, t0:t0 + SLAB],
                  rwb_sl.rearrange("a (p t) -> a p t", t=SLAB))

        p.wait("sync", s_v, s_v.n)
        p.dma("sync", s_o, o_sc, s_sb[:, :])
        p.wait("sync", s_o, s_o.n)
        p.build_block()
    return nc



def build_l2():
    nc = bass.Bass("TRN2", target_bir_lowering=False)
    x_in = nc.dram_tensor("x", [TOK, C], F32, kind="ExternalInput").ap()
    xo_in = nc.dram_tensor("xo", [TOK, C], BF16, kind="ExternalInput").ap()
    rwb_in = nc.dram_tensor("rwb", [128, 8 * TOK], BF16, kind="ExternalInput").ap()
    g_in = nc.dram_tensor("g", [TOK, C], BF16, kind="ExternalInput").ap()
    sall_in = nc.dram_tensor("sall", [512, 512], F32, kind="ExternalInput").ap()
    coefk_in = nc.dram_tensor("coefk", [1, 4], F32, kind="ExternalInput").ap()
    mask_in = nc.dram_tensor("maskk", [1, 4], F32, kind="ExternalInput").ap()
    wo_in = nc.dram_tensor("wo", [C, C], F32, kind="ExternalInput").ap()
    td_in = nc.dram_tensor("td", [1, H], F32, kind="ExternalInput").ap()
    lnw_in = nc.dram_tensor("lnw", [1, C], F32, kind="ExternalInput").ap()
    lnb_in = nc.dram_tensor("lnb", [1, C], F32, kind="ExternalInput").ap()
    gw_in = nc.dram_tensor("gw", [16, C], F32, kind="ExternalInput").ap()
    lb_in = nc.dram_tensor("lb", [1, 8], F32, kind="ExternalInput").ap()

    o_xmid = nc.dram_tensor("o_xmid", [TOK, C], F32, kind="ExternalOutput").ap()
    o_m = nc.dram_tensor("o_m", [TOK, 16], F32, kind="ExternalOutput").ap()
    o_stats = nc.dram_tensor("o_stats", [1, 16], F32, kind="ExternalOutput").ap()

    wo_bf = nc.dram_tensor("wo_bf", [C, C], BF16, kind="Internal").ap()
    gw_bf = nc.dram_tensor("gw_bf", [16, C], BF16, kind="Internal").ap()
    yg_dram = nc.dram_tensor("yg_dram", [TOK, C], BF16, kind="Internal").ap()
    xmb_dram = nc.dram_tensor("xmb_dram", [TOK, C], BF16, kind="Internal").ap()
    sc2 = nc.dram_tensor("sc2", [1, 128], F32, kind="Internal").ap()

    with ExitStack() as ctx:
        p = Prog(nc, ctx)
        sb = lambda n, s, d=F32: ctx.enter_context(nc.sbuf_tensor(n, s, d))
        psl = lambda n, s: ctx.enter_context(nc.psum_tensor(n, s, F32))

        rwb = sb("rwb_s", [128, 8 * TOK], BF16)
        xo_res = sb("xo_res", [128, NCH * C], BF16)
        g_res = sb("g_res", [128, NCH * C], BF16)
        woT = sb("woT", [128, 8 * C], BF16)
        s_tiles = [sb("s_t%d" % i, [128, 512]) for i in range(4)]
        s_e = sb("s_e", [128, 512])
        s_ebf = sb("s_ebf", [128, 512], BF16)
        ws_tile = sb("ws_tile", [128, 512], F32)
        y_t = sb("y_t", [128, C])
        sq_t = sb("sq_t", [128, C])
        gn16 = sb("gn16", [128, 16])
        gn16b = sb("gn16b", [128, 16])
        rstd16 = sb("rstd16", [128, 16])
        yg_sl = sb("yg_sl", [128, 4 * C], BF16)
        ygc = sb("ygc", [128, 8 * SLAB], BF16)
        x_sl = sb("x_sl", [128, 4 * C], F32)
        xm_t = sb("xm_t", [128, C], F32)
        xmb_sl = sb("xmb_sl", [128, 4 * C], BF16)
        col1 = sb("col1", [128, 1])
        col2 = sb("col2", [128, 1])
        lnw_bc = sb("lnw_bc", [128, C], F32)
        lnb_bc = sb("lnb_bc", [128, C], F32)
        td_row = sb("td_row", [1, H], F32)
        ld_row = sb("ld_row", [1, H], F32)
        ws_row = sb("ws_row", [1, H], F32)
        ws_bc = sb("ws_bc", [128, H], F32)
        coefk_sb = sb("coefk_sb", [1, 4], F32)
        mask_sb = sb("mask_sb", [1, 4], F32)
        coefrow = sb("coefrow", [1, 4 * H], F32)
        coef_bc = sb("coef_bc", [128, 4 * H], F32)
        ones64 = sb("ones64", [64, 64], F32)
        gwT = sb("gwT", [128, 8 * 16], BF16)
        xmc_t = sb("xmc_t", [128, 8 * 128], BF16)
        iota8 = sb("iota8", [128, 8], F32)
        lb_bc = sb("lb_bc", [128, 8], F32)
        lg = sb("lg", [128, 8], F32)
        scores = sb("scores", [128, 8], F32)
        mx8 = sb("mx8", [128, 8], F32)
        mi8 = sb("mi8", [128, 8], U32)
        mif = sb("mif", [128, 8], F32)
        eq1 = sb("eq1", [128, 8], F32)
        eq2 = sb("eq2", [128, 8], F32)
        mw = sb("mw", [128, 16], F32)
        ones_col = sb("ones_col", [128, 1], F32)
        den = sb("den", [128, 1], F32)
        w1c = sb("w1c", [128, 1], F32)
        w2c = sb("w2c", [128, 1], F32)
        st_sb = sb("st_sb", [1, 16], F32)

        ps_corr = psl("ps_corr", [128, C])
        ps_proj = [psl("ps_proj%d" % i, [128, SLAB]) for i in range(2)]
        ps_g = psl("ps_g", [128, 8])
        ps_st = psl("ps_st", [1, 16])

        s_cast = p.sem("s_cast")
        s_ld = p.sem("s_ld")
        s_tr = p.sem("s_tr")
        s_kd = p.sem("s_kd")
        s_mm = p.sem("s_mm")
        s_v = p.sem("s_v")
        s_a = p.sem("s_a")
        s_g = p.sem("s_g")
        s_o = p.sem("s_o")

        p.dma("gpsimd", s_cast, wo_bf, wo_in)
        p.dma("gpsimd", s_cast, gw_bf, gw_in)
        cast_w = s_cast.n
        p.dma("gpsimd", s_cast, lnw_bc[:, :], lnw_in[0:1, :].partition_broadcast(128))
        p.dma("gpsimd", s_cast, lnb_bc[:, :], lnb_in[0:1, :].partition_broadcast(128))
        p.dma("gpsimd", s_cast, lb_bc[:, :], lb_in[0:1, :].partition_broadcast(128))
        cast_ln = s_cast.n

        p.dma("sync", s_ld, rwb[:, :], rwb_in)
        p.dma("sync", s_ld,
              xo_res.rearrange("a (tb c) -> a tb c", c=C),
              xo_in.rearrange("(tb p) c -> p tb c", p=128))
        p.dma("sync", s_ld,
              g_res.rearrange("a (tb c) -> a tb c", c=C),
              g_in.rearrange("(tb p) c -> p tb c", p=128))
        for i in range(4):
            p.dma("sync", s_ld, s_tiles[i][:, :], sall_in[i * 128:(i + 1) * 128, :])
        p.dma("sync", s_ld, td_row[0:1, :], td_in)
        p.dma("sync", s_ld, coefk_sb[0:1, :], coefk_in)
        p.dma("sync", s_ld, mask_sb[0:1, :], mask_in)
        ld0 = s_ld.n

        p.wait("scalar", s_ld, ld0)
        p.op("scalar", lambda e: e.activation(ld_row[0:1, :], td_row[0:1, :], AF.Exp), s_a)
        p.wait("vector", s_a, s_a.n)
        p.op("vector", lambda e: e.tensor_scalar_mul(ld_row[0:1, :], ld_row[0:1, :], -1.0), s_v)
        v_ld = s_v.n
        p.wait("scalar", s_v, v_ld)
        p.op("scalar", lambda e: e.activation(ws_row[0:1, :], ld_row[0:1, :], AF.Exp,
                                              scale=float(CH)), s_a)
        for qq in range(4):
            p.op("scalar", lambda e, qq=qq: e.activation(
                coefrow[0:1, qq * H:(qq + 1) * H], ld_row[0:1, :], AF.Exp,
                scale=coefk_sb[0:1, qq:qq + 1]), s_a)
        a_coef = s_a.n
        p.wait("vector", s_a, a_coef)
        for qq in range(4):
            p.op("vector", lambda e, qq=qq: e.tensor_scalar_mul(
                coefrow[0:1, qq * H:(qq + 1) * H], coefrow[0:1, qq * H:(qq + 1) * H],
                mask_sb[0:1, qq:qq + 1]), s_v)
        v_coef = s_v.n
        p.wait("sync", s_a, a_coef)
        p.dma("sync", s_kd, sc2[0:1, 0:H], ws_row[0:1, :])
        p.wait("sync", s_v, v_coef)
        p.dma("sync", s_kd, sc2[0:1, 16:16 + 4 * H], coefrow[0:1, :])
        kd0 = s_kd.n
        p.wait("gpsimd", s_kd, kd0)
        p.dma("gpsimd", s_cast, ws_bc[:, :], sc2[0:1, 0:H].partition_broadcast(128))
        p.dma("gpsimd", s_cast, coef_bc[:, :], sc2[0:1, 16:16 + 4 * H].partition_broadcast(128))
        cast_bc = s_cast.n

        p.op("gpsimd", lambda e: e.memset(ones64[:, :], 1.0), s_g)
        p.op("gpsimd", lambda e: e.iota(iota8[:, :], pattern=[[1, 8]], base=0,
                                        channel_multiplier=0,
                                        allow_small_or_imprecise_dtypes=True), s_g)
        p.op("gpsimd", lambda e: e.memset(ones_col[:, :], 1.0), s_g)
        g_0 = s_g.n

        p.wait("vector", s_cast, cast_bc)
        p.wait("vector", s_g, g_0)
        for hp in range(8):
            for half in range(2):
                rows = slice(half * 64, half * 64 + 64)
                colh = 2 * hp + half
                p.op("vector", lambda e, rows=rows, hp=hp, colh=colh: e.tensor_scalar_mul(
                    ws_tile[rows, hp * 64:(hp + 1) * 64], ones64[:, :],
                    ws_bc[rows, colh:colh + 1]), s_v)
        p.op("vector", lambda e: e.memset(s_e[:, :], 0.0), s_v)
        p.wait("vector", s_ld, ld0)
        p.wait("vector", s_v, s_v.n)
        for qq in range(4):
            for hp in range(8):
                for half in range(2):
                    rows = slice(half * 64, half * 64 + 64)
                    cols = slice(hp * 64, (hp + 1) * 64)
                    colh = qq * H + 2 * hp + half
                    if qq > 0:
                        p.wait("vector", s_v, s_v.n)
                    p.op("vector", lambda e, qq=qq, rows=rows, cols=cols, colh=colh:
                         e.scalar_tensor_tensor(s_e[rows, cols], s_tiles[qq][rows, cols],
                                                coef_bc[rows, colh:colh + 1], s_e[rows, cols],
                                                ALU.mult, ALU.add), s_v)
        p.wait("vector", s_v, s_v.n)
        p.op("vector", lambda e: e.tensor_copy(s_ebf[:, :], s_e[:, :]), s_v)
        v_se = s_v.n

        p.wait("scalar", s_cast, cast_w)
        for ib in range(8):
            p.dma("scalar", s_tr, woT[:, ib * C:(ib + 1) * C],
                  wo_bf[0:C, ib * 128:(ib + 1) * 128], transpose=True)
        for ib in range(8):
            p.dma("scalar", s_tr, gwT[:, ib * 16:(ib + 1) * 16],
                  gw_bf[0:16, ib * 128:(ib + 1) * 128], transpose=True)
        tr_w = s_tr.n

        for j in range(NCH):
            s_slab = j // 4
            cnn = j % 4
            p.wait("tensor", s_v, v_se)
            p.wait("tensor", s_ld, ld0)
            p.wait("tensor", s_v, s_v.n)
            for h in range(H):
                rows, pb = _head(h)
                rsl = slice(pb * TOK + j * CH, pb * TOK + (j + 1) * CH)
                srows = slice((h % 2) * KD, (h % 2) * KD + KD)
                scols = slice((h // 2) * KD, (h // 2 + 1) * KD)
                p.mm(ps_corr[:, h * KD:(h + 1) * KD], rwb[rows, rsl], s_ebf[srows, scols],
                     start=True, stop=True, inc=s_mm)
            mm_c = s_mm.n
            p.wait("vector", s_mm, mm_c)
            p.op("vector", lambda e, j=j: e.scalar_tensor_tensor(
                y_t[:, :], ps_corr[:, :], 1.0 / HEAD_DIV, xo_res[:, j * C:(j + 1) * C],
                ALU.mult, ALU.add), s_v)
            v_y = s_v.n
            p.wait("vector", s_v, v_y)
            p.op("vector", lambda e: e.tensor_mul(s_e[:, :], s_e[:, :], ws_tile[:, :]), s_v)
            p.wait("vector", s_v, s_v.n)
            p.op("vector", lambda e: e.tensor_copy(s_ebf[:, :], s_e[:, :]), s_v)
            v_se = s_v.n

            p.wait("vector", s_v, v_y)
            p.op("vector", lambda e: e.tensor_reduce(
                gn16[:, :], y_t.rearrange("p (h k) -> p h k", k=KD), axis=AX.X, op=ALU.add), s_v)
            p.wait("vector", s_v, s_v.n)
            p.op("vector", lambda e: e.tensor_scalar_mul(gn16[:, :], gn16[:, :], -1.0 / KD), s_v)
            v_nm = s_v.n
            p.wait("vector", s_v, v_nm)
            for h in range(H):
                sl = slice(h * KD, (h + 1) * KD)
                p.op("vector", lambda e, sl=sl, h=h: e.tensor_scalar_add(
                    y_t[:, sl], y_t[:, sl], gn16[:, h:h + 1]), s_v)
            v_yc = s_v.n
            p.wait("vector", s_v, v_yc)
            p.op("vector", lambda e: e.tensor_mul(sq_t[:, :], y_t[:, :], y_t[:, :]), s_v)
            p.wait("vector", s_v, s_v.n)
            p.op("vector", lambda e: e.tensor_reduce(
                gn16b[:, :], sq_t.rearrange("p (h k) -> p h k", k=KD), axis=AX.X, op=ALU.add), s_v)
            p.wait("vector", s_v, s_v.n)
            p.op("vector", lambda e: e.tensor_scalar(
                gn16b[:, :], gn16b[:, :], 1.0 / KD, EPS_GN, ALU.mult, ALU.add), s_v)
            p.wait("vector", s_v, s_v.n)
            p.op("vector", lambda e: e.reciprocal(gn16b[:, :], gn16b[:, :]), s_v)
            v_rcp = s_v.n
            p.wait("scalar", s_v, v_rcp)
            p.op("scalar", lambda e: e.activation(rstd16[:, :], gn16b[:, :], AF.Sqrt), s_a)
            a_rstd = s_a.n
            p.wait("vector", s_a, a_rstd)
            for h in range(H):
                sl = slice(h * KD, (h + 1) * KD)
                p.op("vector", lambda e, sl=sl, h=h: e.tensor_scalar_mul(
                    y_t[:, sl], y_t[:, sl], rstd16[:, h:h + 1]), s_v)
            p.wait("vector", s_v, s_v.n)
            p.wait("vector", s_cast, cast_ln)
            p.op("vector", lambda e: e.tensor_mul(y_t[:, :], y_t[:, :], lnw_bc[:, :]), s_v)
            p.wait("vector", s_v, s_v.n)
            p.op("vector", lambda e: e.tensor_add(y_t[:, :], y_t[:, :], lnb_bc[:, :]), s_v)
            p.wait("vector", s_v, s_v.n)
            p.op("vector", lambda e, j=j, cnn=cnn: e.tensor_mul(
                yg_sl[:, cnn * C:(cnn + 1) * C], y_t[:, :], g_res[:, j * C:(j + 1) * C]), s_v)
            v_yg = s_v.n

            if cnn == 3:
                t0 = s_slab * SLAB
                p.wait("sync", s_v, v_yg)
                p.dma("sync", s_kd,
                      yg_dram[t0:t0 + SLAB, :].rearrange("(tb p) c -> p tb c", p=128),
                      yg_sl.rearrange("a (tb c) -> a tb c", c=C))
                kw1 = s_kd.n
                p.wait("scalar", s_kd, kw1)
                p.wait("scalar", s_mm, s_mm.n)
                for pb in range(8):
                    cs = slice(pb * 128, (pb + 1) * 128)
                    p.dma("scalar", s_tr, ygc[:, pb * SLAB:(pb + 1) * SLAB],
                          yg_dram[t0:t0 + SLAB, cs], transpose=True)
                tr_yg = s_tr.n
                p.wait("sync", s_v, s_v.n)
                p.dma("sync", s_ld,
                      x_sl.rearrange("p (tb c) -> p tb c", c=C),
                      x_in[t0:t0 + SLAB, :].rearrange("(tb p) c -> p tb c", p=128))
                x_ld = s_ld.n

                p.wait("vector", s_kd, s_kd.n)
                p.wait("vector", s_o, s_o.n)
                for tb in range(4):
                    for oh in range(2):
                        bank = ps_proj[(tb * 2 + oh) % 2]
                        p.wait("tensor", s_v, s_v.n)
                        p.wait("tensor", s_tr, tr_yg)
                        for ib in range(8):
                            p.mm(bank[:, :],
                                 ygc[:, ib * SLAB + tb * 128: ib * SLAB + (tb + 1) * 128],
                                 woT[:, ib * C + oh * 512: ib * C + (oh + 1) * 512],
                                 start=(ib == 0), stop=(ib == 7),
                                 inc=(s_mm if ib == 7 else None))
                        p.wait("vector", s_mm, s_mm.n)
                        p.wait("vector", s_ld, x_ld)
                        p.wait("vector", s_o, s_o.n)
                        sl = slice(tb * C + oh * 512, tb * C + (oh + 1) * 512)
                        osl = slice(oh * 512, (oh + 1) * 512)
                        p.op("vector", lambda e, bank=bank, sl=sl, osl=osl: e.tensor_add(
                            xm_t[:, osl], x_sl[:, sl], bank[:, :]), s_v)
                    v_r1 = s_v.n
                    p.wait("vector", s_v, v_r1)
                    p.op("vector", lambda e: e.tensor_reduce(
                        col1[:, 0:1], xm_t[:, :], axis=AX.X, op=ALU.add), s_v)
                    p.wait("vector", s_v, s_v.n)
                    p.op("vector", lambda e: e.tensor_scalar_mul(col1[:, 0:1], col1[:, 0:1], -1.0 / C), s_v)
                    p.wait("vector", s_v, s_v.n)
                    p.op("vector", lambda e: e.tensor_scalar_add(xm_t[:, :], xm_t[:, :], col1[:, 0:1]), s_v)
                    p.wait("vector", s_v, s_v.n)
                    p.op("vector", lambda e: e.tensor_mul(sq_t[:, :], xm_t[:, :], xm_t[:, :]), s_v)
                    p.wait("vector", s_v, s_v.n)
                    p.op("vector", lambda e: e.tensor_reduce(
                        col2[:, 0:1], sq_t[:, :], axis=AX.X, op=ALU.add), s_v)
                    p.wait("vector", s_v, s_v.n)
                    p.op("vector", lambda e: e.tensor_scalar(
                        col2[:, 0:1], col2[:, 0:1], 1.0 / C, EPS_LN, ALU.mult, ALU.add), s_v)
                    p.wait("vector", s_v, s_v.n)
                    p.op("vector", lambda e: e.reciprocal(col2[:, 0:1], col2[:, 0:1]), s_v)
                    v_l1 = s_v.n
                    p.wait("scalar", s_v, v_l1)
                    p.op("scalar", lambda e: e.activation(col2[:, 0:1], col2[:, 0:1], AF.Sqrt), s_a)
                    a_l1 = s_a.n
                    p.wait("vector", s_a, a_l1)
                    p.op("vector", lambda e: e.tensor_scalar_mul(xm_t[:, :], xm_t[:, :], col2[:, 0:1]), s_v)
                    p.wait("vector", s_v, s_v.n)
                    p.op("vector", lambda e, tb=tb: e.tensor_copy(
                        xmb_sl[:, tb * C:(tb + 1) * C], xm_t[:, :]), s_v)
                    v_xm = s_v.n
                    p.wait("sync", s_v, v_xm)
                    p.dma("sync", s_o, o_xmid[t0 + tb * 128:t0 + (tb + 1) * 128, :], xm_t[:, :])
                p.wait("sync", s_v, s_v.n)
                p.dma("sync", s_kd,
                      xmb_dram[t0:t0 + SLAB, :].rearrange("(tb p) c -> p tb c", p=128),
                      xmb_sl.rearrange("a (tb c) -> a tb c", c=C))

        xmb_done = s_kd.n

        p.op("gpsimd", lambda e: e.memset(mw[:, :], 0.0), s_g)
        g_mw = s_g.n
        for ti in range(NCH):
            t0 = ti * CH
            p.wait("scalar", s_kd, xmb_done)
            p.wait("scalar", s_mm, s_mm.n)
            for ib in range(8):
                cs = slice(ib * 128, (ib + 1) * 128)
                p.dma("scalar", s_tr, xmc_t[:, ib * 128:(ib + 1) * 128],
                      xmb_dram[t0:t0 + CH, cs], transpose=True)
            tr_xm = s_tr.n
            p.wait("tensor", s_tr, tr_xm)
            p.wait("tensor", s_tr, tr_w)
            p.wait("tensor", s_v, s_v.n)
            for ib in range(8):
                p.mm(ps_g[:, :], xmc_t[:, ib * 128:(ib + 1) * 128], gwT[:, ib * 16:ib * 16 + 8],
                     start=(ib == 0), stop=(ib == 7), inc=(s_mm if ib == 7 else None))
            p.wait("vector", s_mm, s_mm.n)
            p.wait("vector", s_cast, cast_ln)
            p.op("vector", lambda e: e.tensor_add(lg[:, :], ps_g[:, :], lb_bc[:, :]), s_v)
            v_lg = s_v.n
            p.wait("scalar", s_v, v_lg)
            p.op("scalar", lambda e: e.activation(scores[:, :], lg[:, :], AF.Sigmoid), s_a)
            a_sc = s_a.n
            p.wait("vector", s_a, a_sc)
            p.op("vector", lambda e: e.max(mx8[:, :], scores[:, :]), s_v)
            p.wait("vector", s_v, s_v.n)
            p.op("vector", lambda e: e.max_index(mi8[:, :], mx8[:, :], scores[:, :]), s_v)
            p.wait("vector", s_v, s_v.n)
            p.op("vector", lambda e: e.tensor_copy(mif[:, :], mi8[:, :]), s_v)
            p.wait("vector", s_v, s_v.n)
            p.op("vector", lambda e: e.scalar_tensor_tensor(
                den[:, 0:1], mx8[:, 0:1], 1e-8, mx8[:, 1:2], ALU.add, ALU.add), s_v)
            p.wait("vector", s_v, s_v.n)
            p.op("vector", lambda e: e.reciprocal(den[:, 0:1], den[:, 0:1]), s_v)
            p.wait("vector", s_v, s_v.n)
            p.op("vector", lambda e: e.tensor_mul(w1c[:, 0:1], mx8[:, 0:1], den[:, 0:1]), s_v)
            p.op("vector", lambda e: e.tensor_mul(w2c[:, 0:1], mx8[:, 1:2], den[:, 0:1]), s_v)
            p.wait("vector", s_v, s_v.n)
            p.wait("vector", s_g, g_mw)
            p.op("vector", lambda e: e.tensor_scalar(
                eq1[:, :], iota8[:, :], mif[:, 0:1], None, ALU.is_equal), s_v)
            p.op("vector", lambda e: e.tensor_scalar(
                eq2[:, :], iota8[:, :], mif[:, 1:2], None, ALU.is_equal), s_v)
            p.wait("vector", s_v, s_v.n)
            p.wait("vector", s_mm, s_mm.n)
            p.wait("vector", s_o, s_o.n)
            p.op("vector", lambda e: e.tensor_add(mw[:, 0:8], eq1[:, :], eq2[:, :]), s_v)
            p.wait("vector", s_v, s_v.n)
            p.op("vector", lambda e: e.tensor_scalar_mul(mw[:, 8:16], eq1[:, :], w1c[:, 0:1]), s_v)
            p.wait("vector", s_v, s_v.n)
            p.op("vector", lambda e: e.scalar_tensor_tensor(
                mw[:, 8:16], eq2[:, :], w2c[:, 0:1], mw[:, 8:16], ALU.mult, ALU.add), s_v)
            v_mw = s_v.n
            p.wait("tensor", s_v, v_mw)
            p.wait("tensor", s_g, g_0)
            p.mm(ps_st[0:1, :], ones_col[:, 0:1], mw[:, :],
                 start=(ti == 0), stop=(ti == NCH - 1), inc=s_mm)
            p.wait("sync", s_v, v_mw)
            p.dma("sync", s_o, o_m[t0:t0 + CH, :], mw[:, 0:16])
        p.wait("scalar", s_mm, s_mm.n)
        p.op("scalar", lambda e: e.copy(st_sb[0:1, :], ps_st[0:1, :]), s_a)
        p.wait("sync", s_a, s_a.n)
        p.dma("sync", s_o, o_stats, st_sb[0:1, :])
        p.wait("sync", s_o, s_o.n)
        p.build_block()
    return nc



def build_l3():
    nc = bass.Bass("TRN2", target_bir_lowering=False)
    xm_in = nc.dram_tensor("xm", [TOK, C], F32, kind="ExternalInput").ap()
    m_in = nc.dram_tensor("m", [TOK, 128], F32, kind="ExternalInput").ap()
    stats_in = nc.dram_tensor("stats", [8, 16], F32, kind="ExternalInput").ap()
    eg_in = nc.dram_tensor("eg", [NEXP, EI, C], F32, kind="ExternalInput").ap()
    eu_in = nc.dram_tensor("eu", [NEXP, EI, C], F32, kind="ExternalInput").ap()
    ed_in = nc.dram_tensor("ed", [NEXP, C, EI], F32, kind="ExternalInput").ap()
    sg_in = nc.dram_tensor("sg", [SI, C], F32, kind="ExternalInput").ap()
    su_in = nc.dram_tensor("su", [SI, C], F32, kind="ExternalInput").ap()
    sd_in = nc.dram_tensor("sd", [C, SI], F32, kind="ExternalInput").ap()

    o_out = nc.dram_tensor("o_out", [TOK, C], F32, kind="ExternalOutput").ap()

    xm_bf = nc.dram_tensor("xm_bf", [TOK, C], BF16, kind="Internal").ap()
    m_bf = nc.dram_tensor("m_bf", [TOK, 128], BF16, kind="Internal").ap()
    eg_bf = nc.dram_tensor("eg_bf", [NEXP, EI, C], BF16, kind="Internal").ap()
    eu_bf = nc.dram_tensor("eu_bf", [NEXP, EI, C], BF16, kind="Internal").ap()
    ed_bf = nc.dram_tensor("ed_bf", [NEXP, C, EI], BF16, kind="Internal").ap()
    sg_bf = nc.dram_tensor("sg_bf", [SI, C], BF16, kind="Internal").ap()
    su_bf = nc.dram_tensor("su_bf", [SI, C], BF16, kind="Internal").ap()
    sd_bf = nc.dram_tensor("sd_bf", [C, SI], BF16, kind="Internal").ap()
    selwm_dram = nc.dram_tensor("selwm_dram", [16, TOK], BF16, kind="Internal").ap()
    guT_dram = nc.dram_tensor("guT_dram", [2 * (NEXP + 1), 128, 8 * C], BF16, kind="Internal").ap()
    wdT_dram = nc.dram_tensor("wdT_dram", [NEXP + 1, 128, 8 * C], BF16, kind="Internal").ap()
    sc3 = nc.dram_tensor("sc3", [1, 16], F32, kind="Internal").ap()

    with ExitStack() as ctx:
        p = Prog(nc, ctx)
        sb = lambda n, s, d=F32: ctx.enter_context(nc.sbuf_tensor(n, s, d))
        psl = lambda n, s: ctx.enter_context(nc.psum_tensor(n, s, F32))

        xmc = sb("xmc", [128, 8 * TOK], BF16)
        he_buf = [sb("he%d" % e, [128, 4 * SLAB], BF16) for e in range(NEXP)]
        hsh = sb("hsh", [128, 8 * SLAB], BF16)
        wg_buf = [sb("wg_buf0", [128, 8 * 512], BF16), sb("wg_buf1", [128, 8 * 512], BF16),
                  sb("wg_bufS", [128, 8 * C], BF16)]
        wu_buf = [sb("wu_buf0", [128, 8 * 512], BF16), sb("wu_buf1", [128, 8 * 512], BF16),
                  sb("wu_bufS", [128, 8 * C], BF16)]
        wd_buf = [sb("wd_buf0", [128, 4 * C], BF16), sb("wd_buf1", [128, 4 * C], BF16),
                  sb("wd_bufS", [128, 8 * C], BF16)]
        a_sl = sb("a_sl", [128, 8 * SLAB], BF16)
        u_sl = sb("u_sl", [128, 8 * SLAB], BF16)
        selwm_bc = sb("selwm_bc", [128, SLAB], BF16)
        xm_sl = sb("xm_sl", [128, 4 * C], F32)
        r2_t = sb("r2_t", [128, C], F32)
        col1 = sb("col1", [128, 1], F32)
        col2 = sb("col2", [128, 1], F32)
        st_in = sb("st_in", [8, 16], F32)
        st_sb = sb("st_sb", [1, 16], F32)
        wm_row = sb("wm_row", [1, 8], F32)
        wmcol = sb("wmcol", [16, 1], F32)
        ones8 = sb("ones8", [8, 1], F32)

        ps_dn = [psl("ps_dn%d" % i, [128, C]) for i in range(4)]

        s_cast = p.sem("s_cast")
        s_cex = [p.sem("s_cex%d" % i) for i in range(NEXP + 1)]
        s_csel = p.sem("s_csel")
        s_gu = [p.sem("s_gu%d" % i) for i in range(3)]
        s_wd = [p.sem("s_wd%d" % i) for i in range(3)]
        s_lds = p.sem("s_lds")
        s_ld = p.sem("s_ld")
        s_trm = p.sem("s_trm")
        s_trx = p.sem("s_trx")
        s_tr = p.sem("s_tr")
        s_trd = p.sem("s_trd")
        s_kd = p.sem("s_kd")
        s_mm = p.sem("s_mm")
        s_v = p.sem("s_v")
        s_a = p.sem("s_a")
        s_g = p.sem("s_g")
        s_o = p.sem("s_o")

        p.dma("gpsimd", s_cast, xm_bf, xm_in)
        p.dma("gpsimd", s_cast, m_bf, m_in)
        cast_xm = s_cast.n
        for e_ in range(NEXP):
            p.dma("gpsimd", s_cex[e_], eg_bf[e_], eg_in[e_])
            p.dma("gpsimd", s_cex[e_], eu_bf[e_], eu_in[e_])
            p.dma("gpsimd", s_cex[e_], ed_bf[e_], ed_in[e_])
        p.dma("gpsimd", s_cex[NEXP], sg_bf, sg_in)
        p.dma("gpsimd", s_cex[NEXP], su_bf, su_in)
        p.dma("gpsimd", s_cex[NEXP], sd_bf, sd_in)

        p.dma("sync", s_lds, st_in[:, :], stats_in)
        ld_st = s_lds.n
        p.op("gpsimd", lambda e: e.memset(ones8[:, :], 1.0), s_g)
        p.op("gpsimd", lambda e: e.memset(wmcol[:, :], 0.0), s_g)
        g_0 = s_g.n

        p.wait("tensor", s_lds, ld_st)
        p.wait("tensor", s_g, g_0)
        p.mm(ps_dn[3][0:1, 0:16], ones8[:, 0:1], st_in[:, :], start=True, stop=True, inc=s_mm)
        p.wait("scalar", s_mm, s_mm.n)
        p.op("scalar", lambda e: e.copy(st_sb[0:1, :], ps_dn[3][0:1, 0:16]), s_a)
        a_st = s_a.n
        p.wait("vector", s_a, a_st)
        p.op("vector", lambda e: e.tensor_scalar_max(st_sb[0:1, 0:8], st_sb[0:1, 0:8], 1.0), s_v)
        p.wait("vector", s_v, s_v.n)
        p.op("vector", lambda e: e.reciprocal(st_sb[0:1, 0:8], st_sb[0:1, 0:8]), s_v)
        p.wait("vector", s_v, s_v.n)
        p.op("vector", lambda e: e.tensor_mul(wm_row[0:1, :], st_sb[0:1, 8:16], st_sb[0:1, 0:8]), s_v)
        v_wm = s_v.n
        p.wait("sync", s_v, v_wm)
        p.dma("sync", s_kd, sc3[0:1, 0:8], wm_row[0:1, :])
        kd_wm = s_kd.n
        p.wait("sync", s_kd, kd_wm)
        p.wait("sync", s_g, g_0)
        p.dma("sync", s_kd, wmcol[0:8, 0:1], sc3[0:1, 0:8].rearrange("a b -> b a"),
              allow_slow_non_contiguous=True)
        kd_wc = s_kd.n

        mTv = a_sl.rearrange("p (a b) -> p (a b)") if False else a_sl
        p.wait("scalar", s_cast, cast_xm)
        p.dma("scalar", s_trm, mTv[:, 0:TOK], m_bf[0:TOK, 0:128], transpose=True)
        tr_m = s_trm.n
        p.wait("vector", s_trm, tr_m)
        p.wait("vector", s_kd, kd_wc)
        p.op("vector", lambda e: e.tensor_scalar_mul(mTv[0:16, 0:TOK], mTv[0:16, 0:TOK], wmcol[0:16, 0:1]), s_v)
        v_swm = s_v.n
        p.wait("sync", s_v, v_swm)
        p.dma("sync", s_kd, selwm_dram, mTv[0:16, 0:TOK])
        kd_swm = s_kd.n

        for pb in range(8):
            cs = slice(pb * 128, (pb + 1) * 128)
            p.dma("scalar", s_trx, xmc[:, pb * TOK:(pb + 1) * TOK],
                  xm_bf[0:TOK, cs], transpose=True)
        tr_xm = s_trx.n

        gu_used = [0, 0, 0]
        gu_tr = [0, 0, 0]
        wd_used = [0, 0, 0]
        wd_tr = [0, 0, 0]
        NE = NEXP + 1

        def _bi(ex_abs):
            return 2 if (ex_abs % NE) == NEXP else (ex_abs % 2)

        def load_gu(ex_abs):
            ex = ex_abs % NE
            tau = ex_abs // NE
            bi = _bi(ex_abs)
            shared = (ex == NEXP)
            nob = 8 if shared else 4
            ncols = nob * 128 * 8
            if tau == 0:
                p.wait("scalar", s_cex[ex], s_cex[ex].n)
            p.wait("scalar", s_mm, gu_used[bi])
            if tau == 0:
                p.wait("scalar", s_kd, s_kd.n)
                g_src = sg_bf if shared else eg_bf[ex]
                u_src = su_bf if shared else eu_bf[ex]
                for ib in range(8):
                    p.dma("scalar", s_gu[bi], wg_buf[bi][:, ib * (nob * 128):(ib + 1) * (nob * 128)],
                          g_src[0:nob * 128, ib * 128:(ib + 1) * 128], transpose=True)
                    p.dma("scalar", s_gu[bi], wu_buf[bi][:, ib * (nob * 128):(ib + 1) * (nob * 128)],
                          u_src[0:nob * 128, ib * 128:(ib + 1) * 128], transpose=True)
                gu_tr[bi] = s_gu[bi].n
                p.wait("sync", s_gu[bi], s_gu[bi].n)
                p.dma("sync", s_kd, guT_dram[2 * ex, 0:128, 0:ncols], wg_buf[bi][:, 0:ncols])
                p.dma("sync", s_kd, guT_dram[2 * ex + 1, 0:128, 0:ncols], wu_buf[bi][:, 0:ncols])
            else:
                p.wait("scalar", s_kd, s_kd.n)
                p.dma("scalar", s_gu[bi], wg_buf[bi][:, 0:ncols], guT_dram[2 * ex, 0:128, 0:ncols])
                p.dma("scalar", s_gu[bi], wu_buf[bi][:, 0:ncols], guT_dram[2 * ex + 1, 0:128, 0:ncols])
                gu_tr[bi] = s_gu[bi].n

        def load_wd(ex_abs):
            ex = ex_abs % NE
            tau = ex_abs // NE
            bi = _bi(ex_abs)
            shared = (ex == NEXP)
            nib = 8 if shared else 4
            ncols = nib * C
            if tau == 0:
                p.wait("scalar", s_cex[ex], s_cex[ex].n)
            p.wait("scalar", s_mm, wd_used[bi])
            if tau == 0:
                p.wait("scalar", s_kd, s_kd.n)
                d_src = sd_bf if shared else ed_bf[ex]
                for ib in range(nib):
                    p.dma("scalar", s_wd[bi], wd_buf[bi][:, ib * C:(ib + 1) * C],
                          d_src[0:C, ib * 128:(ib + 1) * 128], transpose=True)
                wd_tr[bi] = s_wd[bi].n
                p.wait("sync", s_wd[bi], s_wd[bi].n)
                p.dma("sync", s_kd, wdT_dram[ex, 0:128, 0:ncols], wd_buf[bi][:, 0:ncols])
            else:
                p.wait("scalar", s_kd, s_kd.n)
                p.dma("scalar", s_wd[bi], wd_buf[bi][:, 0:ncols], wdT_dram[ex, 0:128, 0:ncols])
                wd_tr[bi] = s_wd[bi].n

        d1_evac = [[0, 0], [0, 0]]
        psum01_read = 0
        load_gu(0)
        load_gu(1)
        p.wait("scalar", s_kd, s_kd.n)
        for tau in range(4):
            t0 = tau * SLAB
            p.wait("sync", s_v, s_v.n)
            p.dma("sync", s_ld,
                  xm_sl.rearrange("p (tb c) -> p tb c", c=C),
                  xm_in[t0:t0 + SLAB, :].rearrange("(tb p) c -> p tb c", p=128))
            x_ld = s_ld.n

            p.wait("tensor", s_v, psum01_read)
            for ex in range(NE):
                h_prev = s_v.n
                ex_abs = tau * NE + ex
                bi = _bi(ex_abs)
                shared = (ex == NEXP)
                nob = 8 if shared else 4
                if not shared:
                    p.wait("gpsimd", s_kd, s_kd.n)
                    p.wait("gpsimd", s_v, s_v.n)
                    p.dma("gpsimd", s_csel, selwm_bc[:, :],
                          selwm_dram[ex:ex + 1, t0:t0 + SLAB].partition_broadcast(128))
                sel_cast = s_csel.n

                for ob in range(nob):
                    bk = ob % 2
                    bank = ps_dn[bk][:, 0:512]
                    p.wait("tensor", s_a, d1_evac[bk][0])
                    p.wait("tensor", s_gu[bi], gu_tr[bi])
                    p.wait("tensor", s_trx, tr_xm)
                    for ib in range(8):
                        p.mm(bank,
                             wg_buf[bi][:, ib * (nob * 128) + ob * 128: ib * (nob * 128) + (ob + 1) * 128],
                             xmc[:, ib * TOK + t0: ib * TOK + t0 + SLAB],
                             start=(ib == 0), stop=(ib == 7),
                             inc=(s_mm if ib == 7 else None))
                    p.wait("scalar", s_mm, s_mm.n)
                    p.wait("scalar", s_v, h_prev)
                    p.op("scalar", lambda e, bank=bank, ob=ob: e.activation(
                        a_sl[:, ob * SLAB:(ob + 1) * SLAB], bank, AF.Silu), s_a)
                    d1_evac[bk][0] = s_a.n
                    bank2 = ps_dn[bk][:, 512:1024]
                    p.wait("tensor", s_a, d1_evac[bk][1])
                    for ib in range(8):
                        p.mm(bank2,
                             wu_buf[bi][:, ib * (nob * 128) + ob * 128: ib * (nob * 128) + (ob + 1) * 128],
                             xmc[:, ib * TOK + t0: ib * TOK + t0 + SLAB],
                             start=(ib == 0), stop=(ib == 7),
                             inc=(s_mm if ib == 7 else None))
                    p.wait("scalar", s_mm, s_mm.n)
                    p.wait("scalar", s_v, h_prev)
                    p.op("scalar", lambda e, bank2=bank2, ob=ob: e.copy(
                        u_sl[:, ob * SLAB:(ob + 1) * SLAB], bank2), s_a)
                    d1_evac[bk][1] = s_a.n
                gu_used[bi] = s_mm.n
                a_gu = s_a.n
                if ex_abs + 2 < 4 * NE:
                    load_gu(ex_abs + 2)
                if tau == 0 and ex == NE - 2:
                    load_wd(0)
                if tau == 0 and ex == NE - 1:
                    load_wd(1)
                p.wait("vector", s_a, a_gu)
                p.wait("vector", s_csel, sel_cast)
                dst = hsh if shared else he_buf[ex]
                for ob in range(nob):
                    sl = slice(ob * SLAB, (ob + 1) * SLAB)
                    p.op("vector", lambda e, sl=sl, dst=dst: e.tensor_mul(
                        dst[:, sl], a_sl[:, sl], u_sl[:, sl]), s_v)
                p.wait("vector", s_v, s_v.n)
                if not shared:
                    for ob in range(nob):
                        sl = slice(ob * SLAB, (ob + 1) * SLAB)
                        p.op("vector", lambda e, sl=sl, dst=dst: e.tensor_mul(
                            dst[:, sl], dst[:, sl], selwm_bc[:, :]), s_v)
            v_h = s_v.n

            for ex in range(NE):
                ex_abs = tau * NE + ex
                bi = _bi(ex_abs)
                shared = (ex == NEXP)
                nib = 8 if shared else 4
                buf = wd_buf[bi]
                p.wait("tensor", s_wd[bi], wd_tr[bi])
                p.wait("tensor", s_v, v_h)
                if ex == 0:
                    p.wait("tensor", s_a, s_a.n)
                hsrc = hsh if shared else he_buf[ex]
                for tb in range(4):
                    for oh in range(2):
                        for ib in range(nib):
                            last_of_ex = (tb == 3 and oh == 1 and ib == nib - 1)
                            p.mm(ps_dn[tb][:, oh * 512:(oh + 1) * 512],
                                 hsrc[:, ib * SLAB + tb * 128: ib * SLAB + (tb + 1) * 128],
                                 buf[:, ib * C + oh * 512: ib * C + (oh + 1) * 512],
                                 start=(ex == 0 and ib == 0),
                                 stop=(ex == NEXP and ib == nib - 1),
                                 inc=(s_mm if last_of_ex else None))
                wd_used[bi] = s_mm.n
                if ex_abs + 2 < 4 * NE:
                    load_wd(ex_abs + 2)
            mm_dn = s_mm.n

            for tb in range(4):
                p.wait("vector", s_mm, mm_dn)
                p.wait("vector", s_ld, x_ld)
                p.op("vector", lambda e, tb=tb: e.tensor_add(
                    r2_t[:, :], xm_sl[:, tb * C:(tb + 1) * C], ps_dn[tb][:, :]), s_v)
                p.wait("vector", s_v, s_v.n)
                p.op("vector", lambda e: e.tensor_reduce(
                    col1[:, 0:1], r2_t[:, :], axis=AX.X, op=ALU.add), s_v)
                p.wait("vector", s_v, s_v.n)
                p.op("vector", lambda e: e.tensor_scalar_mul(col1[:, 0:1], col1[:, 0:1], -1.0 / C), s_v)
                p.wait("vector", s_v, s_v.n)
                p.op("vector", lambda e: e.tensor_scalar_add(r2_t[:, :], r2_t[:, :], col1[:, 0:1]), s_v)
                p.wait("scalar", s_v, s_v.n)
                p.op("scalar", lambda e, tb=tb: e.activation(
                    xm_sl[:, tb * C:(tb + 1) * C], r2_t[:, :], AF.Square,
                    accum_out=col2[:, 0:1]), s_a)
                p.wait("vector", s_a, s_a.n)
                p.op("vector", lambda e: e.tensor_scalar(
                    col2[:, 0:1], col2[:, 0:1], 1.0 / C, EPS_LN, ALU.mult, ALU.add), s_v)
                p.wait("vector", s_v, s_v.n)
                p.op("vector", lambda e: e.reciprocal(col2[:, 0:1], col2[:, 0:1]), s_v)
                v_l2 = s_v.n
                p.wait("scalar", s_v, v_l2)
                p.op("scalar", lambda e: e.activation(col2[:, 0:1], col2[:, 0:1], AF.Sqrt), s_a)
                a_l2 = s_a.n
                p.wait("vector", s_a, a_l2)
                p.op("vector", lambda e: e.tensor_scalar_mul(r2_t[:, :], r2_t[:, :], col2[:, 0:1]), s_v)
                v_out = s_v.n
                p.wait("sync", s_v, v_out)
                p.dma("sync", s_o, o_out[t0 + tb * 128:t0 + (tb + 1) * 128, :], r2_t[:, :])
                p.wait("vector", s_o, s_o.n)
                if tb == 1:
                    psum01_read = s_v.n

        p.wait("sync", s_o, s_o.n)
        p.build_block()
    return nc



def kernel(**inputs):
    x = np.ascontiguousarray(np.asarray(inputs["L0_attn_xq"], np.float32))
    tm = np.ascontiguousarray(np.stack([
        np.asarray(inputs["L0_attn_time_mix_r"], np.float32).reshape(C),
        np.asarray(inputs["L0_attn_time_mix_k"], np.float32).reshape(C),
        np.asarray(inputs["L0_attn_time_mix_v"], np.float32).reshape(C),
        np.asarray(inputs["L0_attn_time_mix_g"], np.float32).reshape(C),
    ]))
    td = np.ascontiguousarray(np.asarray(inputs["L0_attn_time_decay"], np.float32).reshape(1, H))
    fa = np.ascontiguousarray(np.asarray(inputs["L0_attn_time_faaaa"], np.float32).reshape(1, H))
    f32c = lambda k: np.ascontiguousarray(np.asarray(inputs[k], np.float32))

    nc1 = build_l1()
    maps1 = []
    for c in range(NCORES):
        b, q = c // 4, c % 4
        t0 = q * TOK
        xp = np.zeros((TOK + 1, C), np.float32)
        xp[1:] = x[b, t0:t0 + TOK]
        if q > 0:
            xp[0] = x[b, t0 - 1]
        maps1.append({
            "xp": xp,
            "wr": f32c("L0_attn_W_r"), "wk": f32c("L0_attn_W_k"),
            "wv": f32c("L0_attn_W_v"), "wg": f32c("L0_attn_W_g"),
            "tm": tm, "td": td, "fa": fa,
        })
    res1 = run_bass_kernel_spmd(nc1, maps1, core_ids=list(range(NCORES)))
    r1 = res1.results

    nc2 = build_l2()
    maps2 = []
    for c in range(NCORES):
        b, q = c // 4, c % 4
        t0 = q * TOK
        sall = np.concatenate([np.asarray(r1[b * 4 + qq]["o_sc"], np.float32)
                               for qq in range(4)], axis=0)
        coefk = np.zeros((1, 4), np.float32)
        maskk = np.zeros((1, 4), np.float32)
        for qq in range(q):
            coefk[0, qq] = float(TOK * (q - 1 - qq))
            maskk[0, qq] = 1.0
        maps2.append({
            "x": x[b, t0:t0 + TOK],
            "xo": np.asarray(r1[c]["o_xo"]).view(np.uint16).view(ml_dtypes.bfloat16)
                  if np.asarray(r1[c]["o_xo"]).dtype != ml_dtypes.bfloat16
                  else np.asarray(r1[c]["o_xo"]),
            "rwb": np.asarray(r1[c]["o_rwb"]),
            "g": np.asarray(r1[c]["o_g"]),
            "sall": sall, "coefk": coefk, "maskk": maskk,
            "wo": f32c("L0_attn_W_o"), "td": td,
            "lnw": np.asarray(inputs["L0_attn_ln_w"], np.float32).reshape(1, C),
            "lnb": np.asarray(inputs["L0_attn_ln_b"], np.float32).reshape(1, C),
            "gw": np.concatenate([f32c("L0_ffn_gate_w"),
                                  np.zeros((8, C), np.float32)], axis=0),
            "lb": np.asarray(inputs["L0_ffn_lb_bias"], np.float32).reshape(1, 8),
        })
    res2 = run_bass_kernel_spmd(nc2, maps2, core_ids=list(range(NCORES)))
    r2 = res2.results

    nc3 = build_l3()
    stats_all = np.concatenate([np.asarray(r2[c]["o_stats"], np.float32)
                                for c in range(NCORES)], axis=0)
    maps3 = []
    for c in range(NCORES):
        maps3.append({
            "xm": np.asarray(r2[c]["o_xmid"], np.float32),
            "m": np.concatenate([np.asarray(r2[c]["o_m"], np.float32),
                                 np.zeros((TOK, 112), np.float32)], axis=1),
            "stats": stats_all,
            "eg": f32c("L0_ffn_expert_gate_w"),
            "eu": f32c("L0_ffn_expert_up_w"),
            "ed": f32c("L0_ffn_expert_down_w"),
            "sg": f32c("L0_ffn_shared_gate_w"),
            "su": f32c("L0_ffn_shared_up_w"),
            "sd": f32c("L0_ffn_shared_down_w"),
        })
    res3 = run_bass_kernel_spmd(nc3, maps3, core_ids=list(range(NCORES)))
    r3 = res3.results

    out = np.zeros((B, TT, C), np.float32)
    for c in range(NCORES):
        b, q = c // 4, c % 4
        out[b, q * TOK:(q + 1) * TOK] = np.asarray(r3[c]["o_out"], np.float32)
    return out


# version 09
# speedup vs baseline: 1.9615x; 1.0413x over previous
import numpy as np
import ml_dtypes
from contextlib import ExitStack

import concourse.bass as bass
import concourse.mybir as mybir
from concourse.bass_utils import run_bass_kernel_spmd

F32 = mybir.dt.float32
BF16 = mybir.dt.bfloat16
U32 = mybir.dt.uint32
AF = mybir.ActivationFunctionType
ALU = mybir.AluOpType
AX = mybir.AxisListType

B = 2
TT = 8192
C = 1024
H = 16
KD = 64
CH = 128
TOK = 2048
NCH = TOK // CH
SLAB = 512
NSLAB = TOK // SLAB
HEAD_DIV = 8.0
NCORES = 8
EPS_GN = 1e-5
EPS_LN = 1e-5
NEXP = 8
EI = 512
SI = 1024


class Ctr:
    def __init__(self, ctx, nc, name):
        self.h = ctx.enter_context(nc.semaphore(name))
        self.n = 0
        self.dma = False


class Prog:
    ENGINES = ("tensor", "vector", "scalar", "gpsimd", "sync")

    def __init__(self, nc, ctx):
        self.nc = nc
        self.ctx = ctx
        self.q = {e: [] for e in self.ENGINES}

    def sem(self, name):
        return Ctr(self.ctx, self.nc, name)

    def wait(self, eng, ctr, val=None):
        v = ctr.n if (val is None or ctr.dma) else val
        if v > 0:
            self.q[eng].append(lambda e, c=ctr, v=v: e.wait_ge(c.h, v))

    def mm(self, psum, lhsT, rhs, start, stop, inc=None, tile_position=None):
        try:
            nfree = rhs.shape[-1]
        except Exception:
            nfree = 512
        k = 8 if nfree >= 256 else 4

        def fn(e, psum=psum, lhsT=lhsT, rhs=rhs, start=start, stop=stop, inc=inc,
               tp=tile_position, k=k):
            ins = e.matmul(psum, lhsT, rhs, start=start, stop=stop, tile_position=tp)
            if inc is not None:
                ins.then_inc(inc.h, 1)
            for _ in range(k):
                e.nop()
        self.q["tensor"].append(fn)
        if inc is not None:
            inc.n += 1

    def op(self, eng, fn, inc=None):
        def wrapped(e, fn=fn, inc=inc):
            ins = fn(e)
            if inc is not None:
                ins.then_inc(inc.h, 1)
        self.q[eng].append(wrapped)
        if inc is not None:
            inc.n += 1

    def dma(self, eng, ctr, out, in_, transpose=False, **kw):
        ctr.dma = True
        def fn(e, out=out, in_=in_, transpose=transpose, kw=kw):
            e.dma_start(out, in_, transpose=transpose, **kw).then_inc(ctr.h, 16)
        self.q[eng].append(fn)
        ctr.n += 16

    def build_block(self):
        block = self.ctx.enter_context(self.nc.Block())
        q = self.q

        @block.tensor
        def _(e):
            for fn in q["tensor"]:
                fn(e)

        @block.vector
        def _(e):
            for fn in q["vector"]:
                fn(e)

        @block.scalar
        def _(e):
            for fn in q["scalar"]:
                fn(e)

        @block.gpsimd
        def _(e):
            for fn in q["gpsimd"]:
                fn(e)

        @block.sync
        def _(e):
            for fn in q["sync"]:
                fn(e)


def _head(h):
    return slice((h % 2) * KD, (h % 2) * KD + KD), h // 2



def build_l1(debug=False):
    nc = bass.Bass("TRN2", target_bir_lowering=False)
    xp = nc.dram_tensor("xp", [TOK + 1, C], F32, kind="ExternalInput").ap()
    w_in = {nm: nc.dram_tensor(nm, [C, C], F32, kind="ExternalInput").ap()
            for nm in ("wr", "wk", "wv", "wg")}
    tm_in = nc.dram_tensor("tm", [4, C], F32, kind="ExternalInput").ap()
    td_in = nc.dram_tensor("td", [1, H], F32, kind="ExternalInput").ap()
    fa_in = nc.dram_tensor("fa", [1, H], F32, kind="ExternalInput").ap()

    o_sc = nc.dram_tensor("o_sc", [128, 512], F32, kind="ExternalOutput").ap()
    o_xo = nc.dram_tensor("o_xo", [TOK, C], BF16, kind="ExternalOutput").ap()
    o_rwb = nc.dram_tensor("o_rwb", [128, 8 * TOK], BF16, kind="ExternalOutput").ap()
    o_g = nc.dram_tensor("o_g", [TOK, C], BF16, kind="ExternalOutput").ap()

    x_bf = nc.dram_tensor("x_bf", [TOK + 1, C], BF16, kind="Internal").ap()
    w_bf = {nm: nc.dram_tensor(nm + "_bf", [C, C], BF16, kind="Internal").ap()
            for nm in ("wr", "wk", "wv", "wg")}
    kc_dram = nc.dram_tensor("kc_dram", [C, TOK], BF16, kind="Internal").ap()
    sc1 = nc.dram_tensor("sc1", [1, 64], F32, kind="Internal").ap()

    dbg = {}
    if debug:
        for nm, shp, dt_ in [("d_wmat", [128, H * CH], F32), ("d_wb", [128, 8 * CH], BF16),
                             ("d_wk", [128, H], F32), ("d_ws", [128, 512], F32),
                             ("d_ldbc", [128, H], F32),
                             ("d_mixr", [128, 8 * SLAB], BF16), ("d_mixk", [128, 8 * SLAB], BF16),
                             ("d_rsl", [128, 8 * SLAB], BF16), ("d_kcs", [128, 8 * SLAB], BF16),
                             ("d_kts", [128, 4 * C], BF16), ("d_vts", [128, 4 * C], BF16),
                             ("d_xc", [128, 8 * SLAB], BF16), ("d_xxc", [128, 8 * SLAB], BF16)]:
            dbg[nm] = nc.dram_tensor(nm, shp, dt_, kind="ExternalOutput").ap()

    with ExitStack() as ctx:
        p = Prog(nc, ctx)
        sb = lambda n, s, d=F32: ctx.enter_context(nc.sbuf_tensor(n, s, d))
        psl = lambda n, s: ctx.enter_context(nc.psum_tensor(n, s, F32))

        wT = {nm: sb(nm + "T", [128, 8 * C], BF16) for nm in ("wr", "wk", "wv", "wg")}
        xc = sb("xc", [128, 8 * SLAB], BF16)
        xxc = sb("xxc", [128, 8 * SLAB], BF16)
        mixed = {nm: sb("mx_" + nm, [128, 8 * SLAB], BF16) for nm in ("r", "k", "v", "g")}
        r_sl = sb("r_sl", [128, 8 * SLAB], BF16)
        rwb_sl = sb("rwb_sl", [128, 8 * SLAB], BF16)
        kcs = sb("kcs", [128, 8 * SLAB], BF16)
        kts = sb("kts", [128, 4 * C], BF16)
        kwk = sb("kwk", [128, C], BF16)
        vts = sb("vts", [128, 4 * C], BF16)
        g_sl = sb("g_sl", [128, 4 * C], BF16)
        xo_sl = sb("xo_sl", [128, 4 * C], BF16)
        attw = sb("attw", [128, H * CH], BF16)
        s_sb = sb("s_sb", [128, 512], F32)
        s_tmp = sb("s_tmp", [128, 512], F32)
        s_bf = sb("s_bf", [128, 512], BF16)

        tm_cols = sb("tm_cols", [128, 32], F32)
        td_row = sb("td_row", [1, H], F32)
        ld_row = sb("ld_row", [1, H], F32)
        ws_row = sb("ws_row", [1, H], F32)
        fa_row = sb("fa_row", [1, H], F32)
        ld_bc = sb("ld_bc", [128, H], F32)
        u_bc = sb("u_bc", [128, H], F32)
        ws_bc = sb("ws_bc", [128, H], F32)
        ldcol8 = sb("ldcol8", [128, 8], F32)
        wmatT = sb("wmatT", [128, H * CH], F32)
        wb_ct = sb("wb_ct", [128, 8 * CH], BF16)
        wk_cols = sb("wk_cols", [128, H], F32)
        ws_tile = sb("ws_tile", [128, 512], F32)
        iota_d = sb("iota_d", [128, CH], F32)
        iota_f = sb("iota_f", [128, CH], F32)
        clampd = sb("clampd", [128, CH], F32)
        rev_col = sb("rev_col", [128, 1], F32)
        ones128 = sb("ones128", [128, CH], F32)
        eye = sb("eye", [128, CH], F32)

        ps_proj = [psl("ps_proj%d" % i, [128, SLAB]) for i in range(2)]
        ps_att = [psl("ps_att%d" % i, [128, 512]) for i in range(2)]
        ps_out = psl("ps_out", [128, C])
        ps_A = psl("ps_A", [128, 512])

        s_cast = p.sem("s_cast")
        s_ld = p.sem("s_ld")
        s_xld = p.sem("s_xld")
        s_tr = p.sem("s_tr")
        s_kd = p.sem("s_kd")
        s_kc = p.sem("s_kc")
        s_mm = p.sem("s_mm")
        s_v = p.sem("s_v")
        s_a = p.sem("s_a")
        s_g = p.sem("s_g")
        s_o = p.sem("s_o")

        p.dma("gpsimd", s_cast, x_bf, xp)
        cast_x = s_cast.n
        for nm in ("wr", "wk", "wv", "wg"):
            p.dma("gpsimd", s_cast, w_bf[nm], w_in[nm])
        cast_w = s_cast.n

        p.dma("sync", s_ld, td_row[0:1, :], td_in)
        p.dma("sync", s_ld, fa_row[0:1, :], fa_in)
        p.dma("sync", s_ld, tm_cols[:, :],
              tm_in.rearrange("m (p c) -> c (m p)", c=128),
              allow_slow_non_contiguous=True)
        ld0 = s_ld.n

        p.wait("scalar", s_ld, ld0)
        p.op("scalar", lambda e: e.activation(ld_row[0:1, :], td_row[0:1, :], AF.Exp), s_a)
        a_exp = s_a.n
        p.wait("vector", s_a, a_exp)
        p.op("vector", lambda e: e.tensor_scalar_mul(ld_row[0:1, :], ld_row[0:1, :], -1.0), s_v)
        v_ld = s_v.n
        p.wait("scalar", s_v, v_ld)
        p.op("scalar", lambda e: e.activation(ws_row[0:1, :], ld_row[0:1, :], AF.Exp,
                                              scale=float(CH)), s_a)
        a_ws = s_a.n
        p.wait("sync", s_v, v_ld)
        p.dma("sync", s_kd, sc1[0:1, 0:H], ld_row[0:1, :])
        p.wait("sync", s_a, a_ws)
        p.dma("sync", s_kd, sc1[0:1, 16:16 + H], ws_row[0:1, :])
        p.wait("sync", s_ld, ld0)
        p.dma("sync", s_kd, sc1[0:1, 32:32 + H], fa_row[0:1, :])
        kd0 = s_kd.n

        p.wait("gpsimd", s_kd, kd0)
        p.dma("gpsimd", s_cast, ld_bc[:, :], sc1[0:1, 0:H].partition_broadcast(128))
        p.dma("gpsimd", s_cast, ws_bc[:, :], sc1[0:1, 16:16 + H].partition_broadcast(128))
        p.dma("gpsimd", s_cast, u_bc[:, :], sc1[0:1, 32:32 + H].partition_broadcast(128))
        for pb in range(8):
            p.dma("gpsimd", s_cast, ldcol8[0:64, pb:pb + 1],
                  sc1[0:1, 2 * pb:2 * pb + 1].partition_broadcast(64))
            p.dma("gpsimd", s_cast, ldcol8[64:128, pb:pb + 1],
                  sc1[0:1, 2 * pb + 1:2 * pb + 2].partition_broadcast(64))
        cast_bc = s_cast.n

        p.op("gpsimd", lambda e: e.iota(iota_d[:, :], pattern=[[1, CH]], base=-1,
                                        channel_multiplier=-1,
                                        allow_small_or_imprecise_dtypes=True), s_g)
        p.op("gpsimd", lambda e: e.iota(iota_f[:, :], pattern=[[1, CH]], base=0,
                                        channel_multiplier=0,
                                        allow_small_or_imprecise_dtypes=True), s_g)
        p.op("gpsimd", lambda e: e.iota(rev_col[:, 0:1], pattern=[[0, 1]], base=127,
                                        channel_multiplier=-1,
                                        allow_small_or_imprecise_dtypes=True), s_g)
        p.op("gpsimd", lambda e: e.memset(ones128[:, :], 1.0), s_g)
        g_iota = s_g.n
        p.wait("gpsimd", s_g, g_iota)
        p.op("gpsimd", lambda e: e.affine_select(eye[:, :], ones128[:, :],
                                                 pattern=[[1, CH]], compare_op=ALU.is_equal,
                                                 fill=0.0, base=0, channel_multiplier=-1), s_g)
        g_eye = s_g.n

        p.wait("vector", s_g, g_iota)
        p.op("vector", lambda e: e.tensor_scalar_max(clampd[:, :], iota_d[:, :], 0.0), s_v)
        v_clamp = s_v.n

        p.wait("scalar", s_v, v_clamp)
        p.wait("scalar", s_cast, cast_bc)
        for h in range(H):
            sl = slice(h * CH, (h + 1) * CH)
            p.op("scalar", lambda e, sl=sl, h=h: e.activation(
                wmatT[:, sl], clampd[:, :], AF.Exp, scale=ld_bc[:, h:h + 1]), s_a)
        a_wpow = s_a.n
        p.wait("gpsimd", s_a, a_wpow)
        for h in range(H):
            sl = slice(h * CH, (h + 1) * CH)
            p.op("gpsimd", lambda e, sl=sl: e.affine_select(
                wmatT[:, sl], wmatT[:, sl], pattern=[[1, CH]], compare_op=ALU.is_ge,
                fill=0.0, base=-1, channel_multiplier=-1), s_g)
        g_wmask = s_g.n
        p.wait("vector", s_g, g_wmask)
        p.wait("vector", s_cast, cast_bc)
        for h in range(H):
            sl = slice(h * CH, (h + 1) * CH)
            p.op("vector", lambda e, sl=sl, h=h: e.scalar_tensor_tensor(
                wmatT[:, sl], eye[:, :], u_bc[:, h:h + 1], wmatT[:, sl], ALU.mult, ALU.add), s_v)
        v_wmat = s_v.n

        p.wait("scalar", s_cast, cast_bc)
        p.wait("scalar", s_g, g_iota)
        for pb in range(8):
            sl = slice(pb * CH, (pb + 1) * CH)
            p.op("scalar", lambda e, sl=sl, pb=pb: e.activation(
                wb_ct[:, sl], iota_f[:, :], AF.Exp, scale=ldcol8[:, pb:pb + 1]), s_a)
        a_wb = s_a.n

        for h in range(H):
            p.op("scalar", lambda e, h=h: e.activation(
                wk_cols[:, h:h + 1], rev_col[:, 0:1], AF.Exp, scale=ld_bc[:, h:h + 1]), s_a)
        a_wk = s_a.n

        p.wait("vector", s_cast, cast_bc)
        for hp in range(8):
            for half in range(2):
                rows = slice(half * 64, half * 64 + 64)
                colh = 2 * hp + half
                p.op("vector", lambda e, rows=rows, hp=hp, colh=colh: e.tensor_scalar_mul(
                    ws_tile[rows, hp * 64:(hp + 1) * 64], ones128[0:64, 0:64],
                    ws_bc[rows, colh:colh + 1]), s_v)
        p.op("vector", lambda e: e.memset(s_sb[:, :], 0.0), s_v)
        p.op("vector", lambda e: e.memset(s_bf[:, :], 0.0), s_v)
        v_consts = s_v.n

        if debug:
            p.wait("sync", s_v, s_v.n)
            p.wait("sync", s_a, s_a.n)
            p.wait("sync", s_g, s_g.n)
            p.dma("sync", s_o, dbg["d_wmat"], wmatT[:, :])
            p.dma("sync", s_o, dbg["d_wb"], wb_ct[:, :])
            p.dma("sync", s_o, dbg["d_wk"], wk_cols[:, :])
            p.dma("sync", s_o, dbg["d_ws"], ws_tile[:, :])
            p.dma("sync", s_o, dbg["d_ldbc"], ld_bc[:, :])

        p.wait("scalar", s_cast, cast_w)
        for nm in ("wr", "wk", "wv", "wg"):
            for ib in range(8):
                p.dma("scalar", s_tr, wT[nm][:, ib * C:(ib + 1) * C],
                      w_bf[nm][0:C, ib * 128:(ib + 1) * 128], transpose=True)
        tr_w = s_tr.n

        bank_evac = [0, 0]
        mix_done_v = 0
        proj_done_mm = 0
        attw_read_prev = 0
        a_xo_prev = 0
        v_sbf_prev = 0
        v_supd_prev = 0
        for s in range(NSLAB):
            t0 = s * SLAB
            p.wait("scalar", s_o, s_o.n)
            p.wait("vector", s_o, s_o.n)
            p.wait("scalar", s_cast, cast_x)
            p.wait("scalar", s_v, mix_done_v)
            for pb in range(8):
                cs = slice(pb * 128, (pb + 1) * 128)
                p.dma("scalar", s_xld, xc[:, pb * SLAB:(pb + 1) * SLAB],
                      x_bf[1 + t0:1 + t0 + SLAB, cs], transpose=True)
                p.dma("scalar", s_xld, xxc[:, pb * SLAB:(pb + 1) * SLAB],
                      x_bf[t0:t0 + SLAB, cs], transpose=True)
            xld = s_xld.n

            p.wait("vector", s_xld, xld)
            p.wait("vector", s_mm, proj_done_mm)
            for mi, nm in enumerate(("r", "k", "v", "g")):
                for pb in range(8):
                    sl = slice(pb * SLAB, (pb + 1) * SLAB)
                    p.op("vector", lambda e, sl=sl, nm=nm: e.tensor_sub(
                        mixed[nm][:, sl], xc[:, sl], xxc[:, sl]), s_v)
            v_sub = s_v.n
            p.wait("vector", s_v, v_sub)
            for mi, nm in enumerate(("r", "k", "v", "g")):
                for pb in range(8):
                    sl = slice(pb * SLAB, (pb + 1) * SLAB)
                    tcol = tm_cols[:, mi * 8 + pb:mi * 8 + pb + 1]
                    p.op("vector", lambda e, sl=sl, tcol=tcol, nm=nm: e.scalar_tensor_tensor(
                        mixed[nm][:, sl], mixed[nm][:, sl], tcol, xxc[:, sl],
                        ALU.mult, ALU.add), s_v)
            v_mix = s_v.n
            mix_done_v = v_mix

            def cmaj(nm_w, nm_x, dst):
                for ob in range(8):
                    bk = ob % 2
                    bank = ps_proj[bk]
                    p.wait("tensor", s_a, bank_evac[bk])
                    p.wait("tensor", s_v, v_mix)
                    p.wait("tensor", s_tr, tr_w)
                    for ib in range(8):
                        p.mm(bank[:, :],
                             wT[nm_w][:, ib * C + ob * 128: ib * C + (ob + 1) * 128],
                             mixed[nm_x][:, ib * SLAB:(ib + 1) * SLAB],
                             start=(ib == 0), stop=(ib == 7),
                             inc=(s_mm if ib == 7 else None))
                    p.wait("scalar", s_mm, s_mm.n)
                    sl = slice(ob * SLAB, (ob + 1) * SLAB)
                    p.op("scalar", lambda e, bank=bank, dst=dst, sl=sl:
                         e.copy(dst[:, sl], bank[:, :]), s_a)
                    bank_evac[bk] = s_a.n

            def tmaj(nm_w, nm_x, dst, act=None):
                for tb in range(4):
                    for oh in range(2):
                        bk = (tb * 2 + oh) % 2
                        bank = ps_proj[bk]
                        p.wait("tensor", s_a, bank_evac[bk])
                        p.wait("tensor", s_v, v_mix)
                        p.wait("tensor", s_tr, tr_w)
                        for ib in range(8):
                            p.mm(bank[:, :],
                                 mixed[nm_x][:, ib * SLAB + tb * 128: ib * SLAB + (tb + 1) * 128],
                                 wT[nm_w][:, ib * C + oh * 512: ib * C + (oh + 1) * 512],
                                 start=(ib == 0), stop=(ib == 7),
                                 inc=(s_mm if ib == 7 else None))
                        p.wait("scalar", s_mm, s_mm.n)
                        sl = slice(tb * C + oh * 512, tb * C + (oh + 1) * 512)
                        if act is None:
                            p.op("scalar", lambda e, bank=bank, sl=sl, dst=dst:
                                 e.copy(dst[:, sl], bank[:, :]), s_a)
                        else:
                            p.op("scalar", lambda e, bank=bank, sl=sl, dst=dst, act=act:
                                 e.activation(dst[:, sl], bank[:, :], act), s_a)
                        bank_evac[bk] = s_a.n

            cmaj("wr", "r", r_sl)
            cmaj("wk", "k", kcs)
            a_rk = s_a.n
            tmaj("wv", "v", vts)
            tmaj("wg", "g", g_sl, act=AF.Silu)
            a_proj = s_a.n
            proj_done_mm = s_mm.n

            p.wait("vector", s_a, a_rk)
            p.wait("vector", s_a, a_wb)
            for pb in range(8):
                for cnn in range(4):
                    sl = slice(pb * SLAB + cnn * CH, pb * SLAB + (cnn + 1) * CH)
                    wsl = slice(pb * CH, (pb + 1) * CH)
                    p.op("vector", lambda e, sl=sl, wsl=wsl: e.tensor_mul(
                        rwb_sl[:, sl], r_sl[:, sl], wb_ct[:, wsl]), s_v)
            v_rwb = s_v.n

            p.wait("sync", s_a, a_rk)
            p.dma("sync", s_kc,
                  kc_dram[0:C, t0:t0 + SLAB].rearrange("(p a) t -> a p t", a=128),
                  kcs.rearrange("a (p t) -> a p t", t=SLAB))
            kwrt = s_kc.n
            p.wait("scalar", s_kc, kwrt)
            p.wait("scalar", s_mm, s_mm.n)
            for tb in range(4):
                p.dma("scalar", s_tr, kts[:, tb * C:(tb + 1) * C],
                      kc_dram[0:C, t0 + tb * 128:t0 + (tb + 1) * 128], transpose=True)
            tr_k = s_tr.n

            if debug and s == 0:
                p.wait("sync", s_v, s_v.n)
                p.wait("sync", s_a, s_a.n)
                p.wait("sync", s_tr, s_tr.n)
                p.dma("sync", s_o, dbg["d_xc"], xc[:, :])
                p.dma("sync", s_o, dbg["d_xxc"], xxc[:, :])
                p.dma("sync", s_o, dbg["d_mixr"], mixed["r"][:, :])
                p.dma("sync", s_o, dbg["d_mixk"], mixed["k"][:, :])
                p.dma("sync", s_o, dbg["d_rsl"], r_sl[:, :])
                p.dma("sync", s_o, dbg["d_kcs"], kcs[:, :])
                p.dma("sync", s_o, dbg["d_kts"], kts[:, :])
                p.dma("sync", s_o, dbg["d_vts"], vts[:, :])
                p.wait("vector", s_o, s_o.n)
                p.wait("tensor", s_o, s_o.n)
                p.wait("sync", s_o, s_o.n)
                p.wait("scalar", s_o, s_o.n)
                p.wait("gpsimd", s_o, s_o.n)

            att_banks = [ps_att[0], ps_att[1], ps_proj[0], ps_proj[1]]
            for cnn in range(4):
                p.wait("vector", s_tr, tr_k)
                p.wait("vector", s_a, a_wk)
                p.wait("vector", s_mm, s_mm.n)
                for h in range(H):
                    p.op("vector", lambda e, h=h, cnn=cnn: e.tensor_scalar_mul(
                        kwk[:, h * KD:(h + 1) * KD],
                        kts[:, cnn * C + h * KD:cnn * C + (h + 1) * KD],
                        wk_cols[:, h:h + 1]), s_v)
                v_kwk = s_v.n

                p.wait("tensor", s_v, attw_read_prev)
                p.wait("tensor", s_a, a_proj)
                p.wait("tensor", s_v, v_rwb)
                att_mm = {}
                for h in range(H):
                    rows, pb = _head(h)
                    rsl = slice(pb * SLAB + cnn * CH, pb * SLAB + (cnn + 1) * CH)
                    bank = att_banks[h // 4]
                    csl = slice((h % 4) * CH, (h % 4 + 1) * CH)
                    p.mm(bank[:, csl], kcs[rows, rsl], r_sl[rows, rsl],
                         start=True, stop=True, inc=s_mm)
                    att_mm[h] = s_mm.n

                p.wait("vector", s_v, v_wmat)
                attw_v = {}
                for h in range(H):
                    bank = att_banks[h // 4]
                    csl = slice((h % 4) * CH, (h % 4 + 1) * CH)
                    p.wait("vector", s_mm, att_mm[h])
                    p.op("vector", lambda e, bank=bank, csl=csl, h=h: e.tensor_mul(
                        attw[:, h * CH:(h + 1) * CH], bank[:, csl],
                        wmatT[:, h * CH:(h + 1) * CH]), s_v)
                    attw_v[h] = s_v.n
                attw_read_prev = s_v.n

                p.wait("tensor", s_a, a_xo_prev)
                p.wait("tensor", s_v, v_sbf_prev)
                p.wait("tensor", s_v, v_supd_prev)
                p.wait("tensor", s_v, v_kwk)
                for h in range(H):
                    rows, pb = _head(h)
                    rsl = slice(pb * SLAB + cnn * CH, pb * SLAB + (cnn + 1) * CH)
                    p.wait("tensor", s_v, attw_v[h])
                    vsl = slice(cnn * C + h * KD, cnn * C + (h + 1) * KD)
                    p.mm(ps_out[:, h * KD:(h + 1) * KD],
                         attw[:, h * CH:(h + 1) * CH], vts[:, vsl],
                         start=True, stop=False)
                    srows = slice((h % 2) * KD, (h % 2) * KD + KD)
                    scols = slice((h // 2) * KD, (h // 2 + 1) * KD)
                    p.mm(ps_out[:, h * KD:(h + 1) * KD],
                         rwb_sl[rows, rsl], s_bf[srows, scols],
                         start=False, stop=True, inc=s_mm)
                    p.mm(ps_A[srows, scols], kwk[:, h * KD:(h + 1) * KD], vts[:, vsl],
                         start=True, stop=True, inc=s_mm,
                         tile_position=(0, (h % 2) * KD))
                mm_chunk = s_mm.n

                p.wait("vector", s_mm, mm_chunk)
                p.op("vector", lambda e: e.tensor_mul(s_tmp[:, :], s_sb[:, :], ws_tile[:, :]), s_v)
                p.wait("vector", s_v, s_v.n)
                p.op("vector", lambda e: e.tensor_add(s_sb[:, :], s_tmp[:, :], ps_A[:, :]), s_v)
                v_supd_prev = s_v.n
                p.wait("vector", s_v, v_supd_prev)
                p.op("vector", lambda e: e.tensor_copy(s_bf[:, :], s_sb[:, :]), s_v)
                v_sbf_prev = s_v.n

                p.wait("scalar", s_mm, mm_chunk)
                p.op("scalar", lambda e, cnn=cnn: e.mul(
                    xo_sl[:, cnn * C:(cnn + 1) * C], ps_out[:, :], 1.0 / HEAD_DIV), s_a)
                a_xo_prev = s_a.n

            a_slab = s_a.n
            v_slab = s_v.n
            p.wait("sync", s_a, a_slab)
            p.wait("sync", s_v, v_slab)
            p.dma("sync", s_o,
                  o_xo[t0:t0 + SLAB, :].rearrange("(tb p) c -> p tb c", p=128),
                  xo_sl.rearrange("a (tb c) -> a tb c", c=C))
            p.dma("sync", s_o,
                  o_g[t0:t0 + SLAB, :].rearrange("(tb p) c -> p tb c", p=128),
                  g_sl.rearrange("a (tb c) -> a tb c", c=C))
            p.dma("sync", s_o,
                  o_rwb.rearrange("a (p t) -> a p t", t=TOK)[:, :, t0:t0 + SLAB],
                  rwb_sl.rearrange("a (p t) -> a p t", t=SLAB))

        p.wait("sync", s_v, s_v.n)
        p.dma("sync", s_o, o_sc, s_sb[:, :])
        p.wait("sync", s_o, s_o.n)
        p.build_block()
    return nc



def build_l2():
    nc = bass.Bass("TRN2", target_bir_lowering=False)
    x_in = nc.dram_tensor("x", [TOK, C], F32, kind="ExternalInput").ap()
    xo_in = nc.dram_tensor("xo", [TOK, C], BF16, kind="ExternalInput").ap()
    rwb_in = nc.dram_tensor("rwb", [128, 8 * TOK], BF16, kind="ExternalInput").ap()
    g_in = nc.dram_tensor("g", [TOK, C], BF16, kind="ExternalInput").ap()
    sall_in = nc.dram_tensor("sall", [512, 512], F32, kind="ExternalInput").ap()
    coefk_in = nc.dram_tensor("coefk", [1, 4], F32, kind="ExternalInput").ap()
    mask_in = nc.dram_tensor("maskk", [1, 4], F32, kind="ExternalInput").ap()
    wo_in = nc.dram_tensor("wo", [C, C], F32, kind="ExternalInput").ap()
    td_in = nc.dram_tensor("td", [1, H], F32, kind="ExternalInput").ap()
    lnw_in = nc.dram_tensor("lnw", [1, C], F32, kind="ExternalInput").ap()
    lnb_in = nc.dram_tensor("lnb", [1, C], F32, kind="ExternalInput").ap()
    gw_in = nc.dram_tensor("gw", [16, C], F32, kind="ExternalInput").ap()
    lb_in = nc.dram_tensor("lb", [1, 8], F32, kind="ExternalInput").ap()

    o_xmid = nc.dram_tensor("o_xmid", [TOK, C], F32, kind="ExternalOutput").ap()
    o_m = nc.dram_tensor("o_m", [TOK, 16], F32, kind="ExternalOutput").ap()
    o_stats = nc.dram_tensor("o_stats", [1, 16], F32, kind="ExternalOutput").ap()

    wo_bf = nc.dram_tensor("wo_bf", [C, C], BF16, kind="Internal").ap()
    gw_bf = nc.dram_tensor("gw_bf", [16, C], BF16, kind="Internal").ap()
    yg_dram = nc.dram_tensor("yg_dram", [TOK, C], BF16, kind="Internal").ap()
    xmb_dram = nc.dram_tensor("xmb_dram", [TOK, C], BF16, kind="Internal").ap()
    sc2 = nc.dram_tensor("sc2", [1, 128], F32, kind="Internal").ap()

    with ExitStack() as ctx:
        p = Prog(nc, ctx)
        sb = lambda n, s, d=F32: ctx.enter_context(nc.sbuf_tensor(n, s, d))
        psl = lambda n, s: ctx.enter_context(nc.psum_tensor(n, s, F32))

        rwb = sb("rwb_s", [128, 8 * TOK], BF16)
        xo_res = sb("xo_res", [128, NCH * C], BF16)
        g_res = sb("g_res", [128, NCH * C], BF16)
        woT = sb("woT", [128, 8 * C], BF16)
        s_tiles = [sb("s_t%d" % i, [128, 512]) for i in range(4)]
        s_e = sb("s_e", [128, 512])
        s_ebf = sb("s_ebf", [128, 512], BF16)
        ws_tile = sb("ws_tile", [128, 512], F32)
        y_t = sb("y_t", [128, C])
        sq_t = sb("sq_t", [128, C])
        gn16 = sb("gn16", [128, 16])
        gn16b = sb("gn16b", [128, 16])
        rstd16 = sb("rstd16", [128, 16])
        yg_sl = sb("yg_sl", [128, 4 * C], BF16)
        ygc = sb("ygc", [128, 8 * SLAB], BF16)
        x_sl = sb("x_sl", [128, 4 * C], F32)
        xm_t = sb("xm_t", [128, C], F32)
        xmb_sl = sb("xmb_sl", [128, 4 * C], BF16)
        col1 = sb("col1", [128, 1])
        col2 = sb("col2", [128, 1])
        lnw_bc = sb("lnw_bc", [128, C], F32)
        lnb_bc = sb("lnb_bc", [128, C], F32)
        td_row = sb("td_row", [1, H], F32)
        ld_row = sb("ld_row", [1, H], F32)
        ws_row = sb("ws_row", [1, H], F32)
        ws_bc = sb("ws_bc", [128, H], F32)
        coefk_sb = sb("coefk_sb", [1, 4], F32)
        mask_sb = sb("mask_sb", [1, 4], F32)
        coefrow = sb("coefrow", [1, 4 * H], F32)
        coef_bc = sb("coef_bc", [128, 4 * H], F32)
        ones64 = sb("ones64", [64, 64], F32)
        gwT = sb("gwT", [128, 8 * 16], BF16)
        xmc_t = sb("xmc_t", [128, 8 * 128], BF16)
        iota8 = sb("iota8", [128, 8], F32)
        lb_bc = sb("lb_bc", [128, 8], F32)
        lg = sb("lg", [128, 8], F32)
        scores = sb("scores", [128, 8], F32)
        mx8 = sb("mx8", [128, 8], F32)
        mi8 = sb("mi8", [128, 8], U32)
        mif = sb("mif", [128, 8], F32)
        eq1 = sb("eq1", [128, 8], F32)
        eq2 = sb("eq2", [128, 8], F32)
        mw = sb("mw", [128, 16], F32)
        ones_col = sb("ones_col", [128, 1], F32)
        den = sb("den", [128, 1], F32)
        w1c = sb("w1c", [128, 1], F32)
        w2c = sb("w2c", [128, 1], F32)
        st_sb = sb("st_sb", [1, 16], F32)

        ps_corr = psl("ps_corr", [128, C])
        ps_proj = [psl("ps_proj%d" % i, [128, SLAB]) for i in range(2)]
        ps_g = psl("ps_g", [128, 8])
        ps_st = psl("ps_st", [1, 16])

        s_cast = p.sem("s_cast")
        s_ld = p.sem("s_ld")
        s_tr = p.sem("s_tr")
        s_kd = p.sem("s_kd")
        s_mm = p.sem("s_mm")
        s_v = p.sem("s_v")
        s_a = p.sem("s_a")
        s_g = p.sem("s_g")
        s_o = p.sem("s_o")

        p.dma("gpsimd", s_cast, wo_bf, wo_in)
        p.dma("gpsimd", s_cast, gw_bf, gw_in)
        cast_w = s_cast.n
        p.dma("gpsimd", s_cast, lnw_bc[:, :], lnw_in[0:1, :].partition_broadcast(128))
        p.dma("gpsimd", s_cast, lnb_bc[:, :], lnb_in[0:1, :].partition_broadcast(128))
        p.dma("gpsimd", s_cast, lb_bc[:, :], lb_in[0:1, :].partition_broadcast(128))
        cast_ln = s_cast.n

        p.dma("sync", s_ld, rwb[:, :], rwb_in)
        p.dma("sync", s_ld,
              xo_res.rearrange("a (tb c) -> a tb c", c=C),
              xo_in.rearrange("(tb p) c -> p tb c", p=128))
        p.dma("sync", s_ld,
              g_res.rearrange("a (tb c) -> a tb c", c=C),
              g_in.rearrange("(tb p) c -> p tb c", p=128))
        for i in range(4):
            p.dma("sync", s_ld, s_tiles[i][:, :], sall_in[i * 128:(i + 1) * 128, :])
        p.dma("sync", s_ld, td_row[0:1, :], td_in)
        p.dma("sync", s_ld, coefk_sb[0:1, :], coefk_in)
        p.dma("sync", s_ld, mask_sb[0:1, :], mask_in)
        ld0 = s_ld.n

        p.wait("scalar", s_ld, ld0)
        p.op("scalar", lambda e: e.activation(ld_row[0:1, :], td_row[0:1, :], AF.Exp), s_a)
        p.wait("vector", s_a, s_a.n)
        p.op("vector", lambda e: e.tensor_scalar_mul(ld_row[0:1, :], ld_row[0:1, :], -1.0), s_v)
        v_ld = s_v.n
        p.wait("scalar", s_v, v_ld)
        p.op("scalar", lambda e: e.activation(ws_row[0:1, :], ld_row[0:1, :], AF.Exp,
                                              scale=float(CH)), s_a)
        for qq in range(4):
            p.op("scalar", lambda e, qq=qq: e.activation(
                coefrow[0:1, qq * H:(qq + 1) * H], ld_row[0:1, :], AF.Exp,
                scale=coefk_sb[0:1, qq:qq + 1]), s_a)
        a_coef = s_a.n
        p.wait("vector", s_a, a_coef)
        for qq in range(4):
            p.op("vector", lambda e, qq=qq: e.tensor_scalar_mul(
                coefrow[0:1, qq * H:(qq + 1) * H], coefrow[0:1, qq * H:(qq + 1) * H],
                mask_sb[0:1, qq:qq + 1]), s_v)
        v_coef = s_v.n
        p.wait("sync", s_a, a_coef)
        p.dma("sync", s_kd, sc2[0:1, 0:H], ws_row[0:1, :])
        p.wait("sync", s_v, v_coef)
        p.dma("sync", s_kd, sc2[0:1, 16:16 + 4 * H], coefrow[0:1, :])
        kd0 = s_kd.n
        p.wait("gpsimd", s_kd, kd0)
        p.dma("gpsimd", s_cast, ws_bc[:, :], sc2[0:1, 0:H].partition_broadcast(128))
        p.dma("gpsimd", s_cast, coef_bc[:, :], sc2[0:1, 16:16 + 4 * H].partition_broadcast(128))
        cast_bc = s_cast.n

        p.op("gpsimd", lambda e: e.memset(ones64[:, :], 1.0), s_g)
        p.op("gpsimd", lambda e: e.iota(iota8[:, :], pattern=[[1, 8]], base=0,
                                        channel_multiplier=0,
                                        allow_small_or_imprecise_dtypes=True), s_g)
        p.op("gpsimd", lambda e: e.memset(ones_col[:, :], 1.0), s_g)
        g_0 = s_g.n

        p.wait("vector", s_cast, cast_bc)
        p.wait("vector", s_g, g_0)
        for hp in range(8):
            for half in range(2):
                rows = slice(half * 64, half * 64 + 64)
                colh = 2 * hp + half
                p.op("vector", lambda e, rows=rows, hp=hp, colh=colh: e.tensor_scalar_mul(
                    ws_tile[rows, hp * 64:(hp + 1) * 64], ones64[:, :],
                    ws_bc[rows, colh:colh + 1]), s_v)
        p.op("vector", lambda e: e.memset(s_e[:, :], 0.0), s_v)
        p.wait("vector", s_ld, ld0)
        p.wait("vector", s_v, s_v.n)
        for qq in range(4):
            for hp in range(8):
                for half in range(2):
                    rows = slice(half * 64, half * 64 + 64)
                    cols = slice(hp * 64, (hp + 1) * 64)
                    colh = qq * H + 2 * hp + half
                    if qq > 0:
                        p.wait("vector", s_v, s_v.n)
                    p.op("vector", lambda e, qq=qq, rows=rows, cols=cols, colh=colh:
                         e.scalar_tensor_tensor(s_e[rows, cols], s_tiles[qq][rows, cols],
                                                coef_bc[rows, colh:colh + 1], s_e[rows, cols],
                                                ALU.mult, ALU.add), s_v)
        p.wait("vector", s_v, s_v.n)
        p.op("vector", lambda e: e.tensor_copy(s_ebf[:, :], s_e[:, :]), s_v)
        v_se = s_v.n

        p.wait("scalar", s_cast, cast_w)
        for ib in range(8):
            p.dma("scalar", s_tr, woT[:, ib * C:(ib + 1) * C],
                  wo_bf[0:C, ib * 128:(ib + 1) * 128], transpose=True)
        for ib in range(8):
            p.dma("scalar", s_tr, gwT[:, ib * 16:(ib + 1) * 16],
                  gw_bf[0:16, ib * 128:(ib + 1) * 128], transpose=True)
        tr_w = s_tr.n

        for j in range(NCH):
            s_slab = j // 4
            cnn = j % 4
            p.wait("tensor", s_v, v_se)
            p.wait("tensor", s_ld, ld0)
            p.wait("tensor", s_v, s_v.n)
            for h in range(H):
                rows, pb = _head(h)
                rsl = slice(pb * TOK + j * CH, pb * TOK + (j + 1) * CH)
                srows = slice((h % 2) * KD, (h % 2) * KD + KD)
                scols = slice((h // 2) * KD, (h // 2 + 1) * KD)
                p.mm(ps_corr[:, h * KD:(h + 1) * KD], rwb[rows, rsl], s_ebf[srows, scols],
                     start=True, stop=True, inc=s_mm)
            mm_c = s_mm.n
            p.wait("vector", s_mm, mm_c)
            p.op("vector", lambda e, j=j: e.scalar_tensor_tensor(
                y_t[:, :], ps_corr[:, :], 1.0 / HEAD_DIV, xo_res[:, j * C:(j + 1) * C],
                ALU.mult, ALU.add), s_v)
            v_y = s_v.n
            p.wait("vector", s_v, v_y)
            p.op("vector", lambda e: e.tensor_mul(s_e[:, :], s_e[:, :], ws_tile[:, :]), s_v)
            p.wait("vector", s_v, s_v.n)
            p.op("vector", lambda e: e.tensor_copy(s_ebf[:, :], s_e[:, :]), s_v)
            v_se = s_v.n

            p.wait("vector", s_v, v_y)
            p.op("vector", lambda e: e.tensor_reduce(
                gn16[:, :], y_t.rearrange("p (h k) -> p h k", k=KD), axis=AX.X, op=ALU.add), s_v)
            p.wait("vector", s_v, s_v.n)
            p.op("vector", lambda e: e.tensor_scalar_mul(gn16[:, :], gn16[:, :], -1.0 / KD), s_v)
            v_nm = s_v.n
            p.wait("vector", s_v, v_nm)
            for h in range(H):
                sl = slice(h * KD, (h + 1) * KD)
                p.op("vector", lambda e, sl=sl, h=h: e.tensor_scalar_add(
                    y_t[:, sl], y_t[:, sl], gn16[:, h:h + 1]), s_v)
            v_yc = s_v.n
            p.wait("vector", s_v, v_yc)
            p.op("vector", lambda e: e.tensor_mul(sq_t[:, :], y_t[:, :], y_t[:, :]), s_v)
            p.wait("vector", s_v, s_v.n)
            p.op("vector", lambda e: e.tensor_reduce(
                gn16b[:, :], sq_t.rearrange("p (h k) -> p h k", k=KD), axis=AX.X, op=ALU.add), s_v)
            p.wait("vector", s_v, s_v.n)
            p.op("vector", lambda e: e.tensor_scalar(
                gn16b[:, :], gn16b[:, :], 1.0 / KD, EPS_GN, ALU.mult, ALU.add), s_v)
            p.wait("vector", s_v, s_v.n)
            p.op("vector", lambda e: e.reciprocal(gn16b[:, :], gn16b[:, :]), s_v)
            v_rcp = s_v.n
            p.wait("scalar", s_v, v_rcp)
            p.op("scalar", lambda e: e.activation(rstd16[:, :], gn16b[:, :], AF.Sqrt), s_a)
            a_rstd = s_a.n
            p.wait("vector", s_a, a_rstd)
            for h in range(H):
                sl = slice(h * KD, (h + 1) * KD)
                p.op("vector", lambda e, sl=sl, h=h: e.tensor_scalar_mul(
                    y_t[:, sl], y_t[:, sl], rstd16[:, h:h + 1]), s_v)
            p.wait("vector", s_v, s_v.n)
            p.wait("vector", s_cast, cast_ln)
            p.op("vector", lambda e: e.tensor_mul(y_t[:, :], y_t[:, :], lnw_bc[:, :]), s_v)
            p.wait("vector", s_v, s_v.n)
            p.op("vector", lambda e: e.tensor_add(y_t[:, :], y_t[:, :], lnb_bc[:, :]), s_v)
            p.wait("vector", s_v, s_v.n)
            p.op("vector", lambda e, j=j, cnn=cnn: e.tensor_mul(
                yg_sl[:, cnn * C:(cnn + 1) * C], y_t[:, :], g_res[:, j * C:(j + 1) * C]), s_v)
            v_yg = s_v.n

            if cnn == 3:
                t0 = s_slab * SLAB
                p.wait("sync", s_v, v_yg)
                p.dma("sync", s_kd,
                      yg_dram[t0:t0 + SLAB, :].rearrange("(tb p) c -> p tb c", p=128),
                      yg_sl.rearrange("a (tb c) -> a tb c", c=C))
                kw1 = s_kd.n
                p.wait("scalar", s_kd, kw1)
                p.wait("scalar", s_mm, s_mm.n)
                for pb in range(8):
                    cs = slice(pb * 128, (pb + 1) * 128)
                    p.dma("scalar", s_tr, ygc[:, pb * SLAB:(pb + 1) * SLAB],
                          yg_dram[t0:t0 + SLAB, cs], transpose=True)
                tr_yg = s_tr.n
                p.wait("sync", s_v, s_v.n)
                p.dma("sync", s_ld,
                      x_sl.rearrange("p (tb c) -> p tb c", c=C),
                      x_in[t0:t0 + SLAB, :].rearrange("(tb p) c -> p tb c", p=128))
                x_ld = s_ld.n

                p.wait("vector", s_kd, s_kd.n)
                p.wait("vector", s_o, s_o.n)
                for tb in range(4):
                    for oh in range(2):
                        bank = ps_proj[(tb * 2 + oh) % 2]
                        p.wait("tensor", s_v, s_v.n)
                        p.wait("tensor", s_tr, tr_yg)
                        for ib in range(8):
                            p.mm(bank[:, :],
                                 ygc[:, ib * SLAB + tb * 128: ib * SLAB + (tb + 1) * 128],
                                 woT[:, ib * C + oh * 512: ib * C + (oh + 1) * 512],
                                 start=(ib == 0), stop=(ib == 7),
                                 inc=(s_mm if ib == 7 else None))
                        p.wait("vector", s_mm, s_mm.n)
                        p.wait("vector", s_ld, x_ld)
                        p.wait("vector", s_o, s_o.n)
                        sl = slice(tb * C + oh * 512, tb * C + (oh + 1) * 512)
                        osl = slice(oh * 512, (oh + 1) * 512)
                        p.op("vector", lambda e, bank=bank, sl=sl, osl=osl: e.tensor_add(
                            xm_t[:, osl], x_sl[:, sl], bank[:, :]), s_v)
                    v_r1 = s_v.n
                    p.wait("vector", s_v, v_r1)
                    p.op("vector", lambda e: e.tensor_reduce(
                        col1[:, 0:1], xm_t[:, :], axis=AX.X, op=ALU.add), s_v)
                    p.wait("vector", s_v, s_v.n)
                    p.op("vector", lambda e: e.tensor_scalar_mul(col1[:, 0:1], col1[:, 0:1], -1.0 / C), s_v)
                    p.wait("vector", s_v, s_v.n)
                    p.op("vector", lambda e: e.tensor_scalar_add(xm_t[:, :], xm_t[:, :], col1[:, 0:1]), s_v)
                    p.wait("vector", s_v, s_v.n)
                    p.op("vector", lambda e: e.tensor_mul(sq_t[:, :], xm_t[:, :], xm_t[:, :]), s_v)
                    p.wait("vector", s_v, s_v.n)
                    p.op("vector", lambda e: e.tensor_reduce(
                        col2[:, 0:1], sq_t[:, :], axis=AX.X, op=ALU.add), s_v)
                    p.wait("vector", s_v, s_v.n)
                    p.op("vector", lambda e: e.tensor_scalar(
                        col2[:, 0:1], col2[:, 0:1], 1.0 / C, EPS_LN, ALU.mult, ALU.add), s_v)
                    p.wait("vector", s_v, s_v.n)
                    p.op("vector", lambda e: e.reciprocal(col2[:, 0:1], col2[:, 0:1]), s_v)
                    v_l1 = s_v.n
                    p.wait("scalar", s_v, v_l1)
                    p.op("scalar", lambda e: e.activation(col2[:, 0:1], col2[:, 0:1], AF.Sqrt), s_a)
                    a_l1 = s_a.n
                    p.wait("vector", s_a, a_l1)
                    p.op("vector", lambda e: e.tensor_scalar_mul(xm_t[:, :], xm_t[:, :], col2[:, 0:1]), s_v)
                    p.wait("vector", s_v, s_v.n)
                    p.op("vector", lambda e, tb=tb: e.tensor_copy(
                        xmb_sl[:, tb * C:(tb + 1) * C], xm_t[:, :]), s_v)
                    v_xm = s_v.n
                    p.wait("sync", s_v, v_xm)
                    p.dma("sync", s_o, o_xmid[t0 + tb * 128:t0 + (tb + 1) * 128, :], xm_t[:, :])
                p.wait("sync", s_v, s_v.n)
                p.dma("sync", s_kd,
                      xmb_dram[t0:t0 + SLAB, :].rearrange("(tb p) c -> p tb c", p=128),
                      xmb_sl.rearrange("a (tb c) -> a tb c", c=C))

        xmb_done = s_kd.n

        p.op("gpsimd", lambda e: e.memset(mw[:, :], 0.0), s_g)
        g_mw = s_g.n
        for ti in range(NCH):
            t0 = ti * CH
            p.wait("scalar", s_kd, xmb_done)
            p.wait("scalar", s_mm, s_mm.n)
            for ib in range(8):
                cs = slice(ib * 128, (ib + 1) * 128)
                p.dma("scalar", s_tr, xmc_t[:, ib * 128:(ib + 1) * 128],
                      xmb_dram[t0:t0 + CH, cs], transpose=True)
            tr_xm = s_tr.n
            p.wait("tensor", s_tr, tr_xm)
            p.wait("tensor", s_tr, tr_w)
            p.wait("tensor", s_v, s_v.n)
            for ib in range(8):
                p.mm(ps_g[:, :], xmc_t[:, ib * 128:(ib + 1) * 128], gwT[:, ib * 16:ib * 16 + 8],
                     start=(ib == 0), stop=(ib == 7), inc=(s_mm if ib == 7 else None))
            p.wait("vector", s_mm, s_mm.n)
            p.wait("vector", s_cast, cast_ln)
            p.op("vector", lambda e: e.tensor_add(lg[:, :], ps_g[:, :], lb_bc[:, :]), s_v)
            v_lg = s_v.n
            p.wait("scalar", s_v, v_lg)
            p.op("scalar", lambda e: e.activation(scores[:, :], lg[:, :], AF.Sigmoid), s_a)
            a_sc = s_a.n
            p.wait("vector", s_a, a_sc)
            p.op("vector", lambda e: e.max(mx8[:, :], scores[:, :]), s_v)
            p.wait("vector", s_v, s_v.n)
            p.op("vector", lambda e: e.max_index(mi8[:, :], mx8[:, :], scores[:, :]), s_v)
            p.wait("vector", s_v, s_v.n)
            p.op("vector", lambda e: e.tensor_copy(mif[:, :], mi8[:, :]), s_v)
            p.wait("vector", s_v, s_v.n)
            p.op("vector", lambda e: e.scalar_tensor_tensor(
                den[:, 0:1], mx8[:, 0:1], 1e-8, mx8[:, 1:2], ALU.add, ALU.add), s_v)
            p.wait("vector", s_v, s_v.n)
            p.op("vector", lambda e: e.reciprocal(den[:, 0:1], den[:, 0:1]), s_v)
            p.wait("vector", s_v, s_v.n)
            p.op("vector", lambda e: e.tensor_mul(w1c[:, 0:1], mx8[:, 0:1], den[:, 0:1]), s_v)
            p.op("vector", lambda e: e.tensor_mul(w2c[:, 0:1], mx8[:, 1:2], den[:, 0:1]), s_v)
            p.wait("vector", s_v, s_v.n)
            p.wait("vector", s_g, g_mw)
            p.op("vector", lambda e: e.tensor_scalar(
                eq1[:, :], iota8[:, :], mif[:, 0:1], None, ALU.is_equal), s_v)
            p.op("vector", lambda e: e.tensor_scalar(
                eq2[:, :], iota8[:, :], mif[:, 1:2], None, ALU.is_equal), s_v)
            p.wait("vector", s_v, s_v.n)
            p.wait("vector", s_mm, s_mm.n)
            p.wait("vector", s_o, s_o.n)
            p.op("vector", lambda e: e.tensor_add(mw[:, 0:8], eq1[:, :], eq2[:, :]), s_v)
            p.wait("vector", s_v, s_v.n)
            p.op("vector", lambda e: e.tensor_scalar_mul(mw[:, 8:16], eq1[:, :], w1c[:, 0:1]), s_v)
            p.wait("vector", s_v, s_v.n)
            p.op("vector", lambda e: e.scalar_tensor_tensor(
                mw[:, 8:16], eq2[:, :], w2c[:, 0:1], mw[:, 8:16], ALU.mult, ALU.add), s_v)
            v_mw = s_v.n
            p.wait("tensor", s_v, v_mw)
            p.wait("tensor", s_g, g_0)
            p.mm(ps_st[0:1, :], ones_col[:, 0:1], mw[:, :],
                 start=(ti == 0), stop=(ti == NCH - 1), inc=s_mm)
            p.wait("sync", s_v, v_mw)
            p.dma("sync", s_o, o_m[t0:t0 + CH, :], mw[:, 0:16])
        p.wait("scalar", s_mm, s_mm.n)
        p.op("scalar", lambda e: e.copy(st_sb[0:1, :], ps_st[0:1, :]), s_a)
        p.wait("sync", s_a, s_a.n)
        p.dma("sync", s_o, o_stats, st_sb[0:1, :])
        p.wait("sync", s_o, s_o.n)
        p.build_block()
    return nc



def build_l3():
    nc = bass.Bass("TRN2", target_bir_lowering=False)
    xm_in = nc.dram_tensor("xm", [TOK, C], F32, kind="ExternalInput").ap()
    m_in = nc.dram_tensor("m", [TOK, 128], F32, kind="ExternalInput").ap()
    stats_in = nc.dram_tensor("stats", [8, 16], F32, kind="ExternalInput").ap()
    eg_in = nc.dram_tensor("eg", [NEXP, EI, C], F32, kind="ExternalInput").ap()
    eu_in = nc.dram_tensor("eu", [NEXP, EI, C], F32, kind="ExternalInput").ap()
    ed_in = nc.dram_tensor("ed", [NEXP, C, EI], F32, kind="ExternalInput").ap()
    sg_in = nc.dram_tensor("sg", [SI, C], F32, kind="ExternalInput").ap()
    su_in = nc.dram_tensor("su", [SI, C], F32, kind="ExternalInput").ap()
    sd_in = nc.dram_tensor("sd", [C, SI], F32, kind="ExternalInput").ap()

    o_out = nc.dram_tensor("o_out", [TOK, C], F32, kind="ExternalOutput").ap()

    xm_bf = nc.dram_tensor("xm_bf", [TOK, C], BF16, kind="Internal").ap()
    m_bf = nc.dram_tensor("m_bf", [TOK, 128], BF16, kind="Internal").ap()
    eg_bf = nc.dram_tensor("eg_bf", [NEXP, EI, C], BF16, kind="Internal").ap()
    eu_bf = nc.dram_tensor("eu_bf", [NEXP, EI, C], BF16, kind="Internal").ap()
    ed_bf = nc.dram_tensor("ed_bf", [NEXP, C, EI], BF16, kind="Internal").ap()
    sg_bf = nc.dram_tensor("sg_bf", [SI, C], BF16, kind="Internal").ap()
    su_bf = nc.dram_tensor("su_bf", [SI, C], BF16, kind="Internal").ap()
    sd_bf = nc.dram_tensor("sd_bf", [C, SI], BF16, kind="Internal").ap()
    selwm_dram = nc.dram_tensor("selwm_dram", [16, TOK], BF16, kind="Internal").ap()
    guT_dram = nc.dram_tensor("guT_dram", [2 * (NEXP + 1), 128, 8 * C], BF16, kind="Internal").ap()
    wdT_dram = nc.dram_tensor("wdT_dram", [NEXP + 1, 128, 8 * C], BF16, kind="Internal").ap()
    sc3 = nc.dram_tensor("sc3", [1, 16], F32, kind="Internal").ap()

    with ExitStack() as ctx:
        p = Prog(nc, ctx)
        sb = lambda n, s, d=F32: ctx.enter_context(nc.sbuf_tensor(n, s, d))
        psl = lambda n, s: ctx.enter_context(nc.psum_tensor(n, s, F32))

        xmc = sb("xmc", [128, 8 * TOK], BF16)
        he_buf = [sb("he%d" % e, [128, 4 * SLAB], BF16) for e in range(NEXP)]
        hsh = sb("hsh", [128, 8 * SLAB], BF16)
        wg_buf = [sb("wg_buf0", [128, 8 * 512], BF16), sb("wg_buf1", [128, 8 * 512], BF16),
                  sb("wg_bufS", [128, 8 * C], BF16)]
        wu_buf = [sb("wu_buf0", [128, 8 * 512], BF16), sb("wu_buf1", [128, 8 * 512], BF16),
                  sb("wu_bufS", [128, 8 * C], BF16)]
        wd_buf = [sb("wd_buf0", [128, 4 * C], BF16), sb("wd_buf1", [128, 4 * C], BF16),
                  sb("wd_bufS", [128, 8 * C], BF16)]
        a_sl = sb("a_sl", [128, 8 * SLAB], BF16)
        u_sl = sb("u_sl", [128, 8 * SLAB], BF16)
        selwm_bc = sb("selwm_bc", [128, SLAB], BF16)
        xm_sl = sb("xm_sl", [128, 4 * C], F32)
        r2_t = sb("r2_t", [128, C], F32)
        col1 = sb("col1", [128, 1], F32)
        col2 = sb("col2", [128, 1], F32)
        st_in = sb("st_in", [8, 16], F32)
        st_sb = sb("st_sb", [1, 16], F32)
        wm_row = sb("wm_row", [1, 8], F32)
        wmcol = sb("wmcol", [16, 1], F32)
        ones8 = sb("ones8", [8, 1], F32)

        ps_dn = [psl("ps_dn%d" % i, [128, C]) for i in range(4)]

        s_cast = p.sem("s_cast")
        s_cex = [p.sem("s_cex%d" % i) for i in range(NEXP + 1)]
        s_csel = p.sem("s_csel")
        s_gu = [p.sem("s_gu%d" % i) for i in range(3)]
        s_wd = [p.sem("s_wd%d" % i) for i in range(3)]
        s_lds = p.sem("s_lds")
        s_ld = p.sem("s_ld")
        s_trm = p.sem("s_trm")
        s_trx = p.sem("s_trx")
        s_tr = p.sem("s_tr")
        s_trd = p.sem("s_trd")
        s_kd = p.sem("s_kd")
        s_mm = p.sem("s_mm")
        s_v = p.sem("s_v")
        s_a = p.sem("s_a")
        s_g = p.sem("s_g")
        s_o = p.sem("s_o")

        p.dma("gpsimd", s_cast, xm_bf, xm_in)
        p.dma("gpsimd", s_cast, m_bf, m_in)
        cast_xm = s_cast.n
        for e_ in range(NEXP):
            p.dma("gpsimd", s_cex[e_], eg_bf[e_], eg_in[e_])
            p.dma("gpsimd", s_cex[e_], eu_bf[e_], eu_in[e_])
            p.dma("gpsimd", s_cex[e_], ed_bf[e_], ed_in[e_])
        p.dma("gpsimd", s_cex[NEXP], sg_bf, sg_in)
        p.dma("gpsimd", s_cex[NEXP], su_bf, su_in)
        p.dma("gpsimd", s_cex[NEXP], sd_bf, sd_in)

        p.dma("sync", s_lds, st_in[:, :], stats_in)
        ld_st = s_lds.n
        p.op("gpsimd", lambda e: e.memset(ones8[:, :], 1.0), s_g)
        p.op("gpsimd", lambda e: e.memset(wmcol[:, :], 0.0), s_g)
        g_0 = s_g.n

        p.wait("tensor", s_lds, ld_st)
        p.wait("tensor", s_g, g_0)
        p.mm(ps_dn[3][0:1, 0:16], ones8[:, 0:1], st_in[:, :], start=True, stop=True, inc=s_mm)
        p.wait("scalar", s_mm, s_mm.n)
        p.op("scalar", lambda e: e.copy(st_sb[0:1, :], ps_dn[3][0:1, 0:16]), s_a)
        a_st = s_a.n
        p.wait("vector", s_a, a_st)
        p.op("vector", lambda e: e.tensor_scalar_max(st_sb[0:1, 0:8], st_sb[0:1, 0:8], 1.0), s_v)
        p.wait("vector", s_v, s_v.n)
        p.op("vector", lambda e: e.reciprocal(st_sb[0:1, 0:8], st_sb[0:1, 0:8]), s_v)
        p.wait("vector", s_v, s_v.n)
        p.op("vector", lambda e: e.tensor_mul(wm_row[0:1, :], st_sb[0:1, 8:16], st_sb[0:1, 0:8]), s_v)
        v_wm = s_v.n
        p.wait("sync", s_v, v_wm)
        p.dma("sync", s_kd, sc3[0:1, 0:8], wm_row[0:1, :])
        kd_wm = s_kd.n
        p.wait("sync", s_kd, kd_wm)
        p.wait("sync", s_g, g_0)
        p.dma("sync", s_kd, wmcol[0:8, 0:1], sc3[0:1, 0:8].rearrange("a b -> b a"),
              allow_slow_non_contiguous=True)
        kd_wc = s_kd.n

        mTv = a_sl.rearrange("p (a b) -> p (a b)") if False else a_sl
        p.wait("scalar", s_cast, cast_xm)
        p.dma("scalar", s_trm, mTv[:, 0:TOK], m_bf[0:TOK, 0:128], transpose=True)
        tr_m = s_trm.n
        p.wait("vector", s_trm, tr_m)
        p.wait("vector", s_kd, kd_wc)
        p.op("vector", lambda e: e.tensor_scalar_mul(mTv[0:16, 0:TOK], mTv[0:16, 0:TOK], wmcol[0:16, 0:1]), s_v)
        v_swm = s_v.n
        p.wait("sync", s_v, v_swm)
        p.dma("sync", s_kd, selwm_dram, mTv[0:16, 0:TOK])
        kd_swm = s_kd.n

        for pb in range(8):
            cs = slice(pb * 128, (pb + 1) * 128)
            p.dma("scalar", s_trx, xmc[:, pb * TOK:(pb + 1) * TOK],
                  xm_bf[0:TOK, cs], transpose=True)
        tr_xm = s_trx.n

        gu_used = [0, 0, 0]
        gu_tr = [0, 0, 0]
        wd_used = [0, 0, 0]
        wd_tr = [0, 0, 0]
        NE = NEXP + 1

        def _bi(ex_abs):
            return 2 if (ex_abs % NE) == NEXP else (ex_abs % 2)

        def load_gu(ex_abs):
            ex = ex_abs % NE
            tau = ex_abs // NE
            bi = _bi(ex_abs)
            shared = (ex == NEXP)
            nob = 8 if shared else 4
            ncols = nob * 128 * 8
            if tau == 0:
                p.wait("scalar", s_cex[ex], s_cex[ex].n)
            p.wait("scalar", s_mm, gu_used[bi])
            if tau == 0:
                p.wait("scalar", s_kd, s_kd.n)
                g_src = sg_bf if shared else eg_bf[ex]
                u_src = su_bf if shared else eu_bf[ex]
                for ib in range(8):
                    p.dma("scalar", s_gu[bi], wg_buf[bi][:, ib * (nob * 128):(ib + 1) * (nob * 128)],
                          g_src[0:nob * 128, ib * 128:(ib + 1) * 128], transpose=True)
                    p.dma("scalar", s_gu[bi], wu_buf[bi][:, ib * (nob * 128):(ib + 1) * (nob * 128)],
                          u_src[0:nob * 128, ib * 128:(ib + 1) * 128], transpose=True)
                gu_tr[bi] = s_gu[bi].n
                p.wait("sync", s_gu[bi], s_gu[bi].n)
                p.dma("sync", s_kd, guT_dram[2 * ex, 0:128, 0:ncols], wg_buf[bi][:, 0:ncols])
                p.dma("sync", s_kd, guT_dram[2 * ex + 1, 0:128, 0:ncols], wu_buf[bi][:, 0:ncols])
            else:
                p.wait("scalar", s_kd, s_kd.n)
                p.dma("scalar", s_gu[bi], wg_buf[bi][:, 0:ncols], guT_dram[2 * ex, 0:128, 0:ncols])
                p.dma("scalar", s_gu[bi], wu_buf[bi][:, 0:ncols], guT_dram[2 * ex + 1, 0:128, 0:ncols])
                gu_tr[bi] = s_gu[bi].n

        def load_wd(ex_abs):
            ex = ex_abs % NE
            tau = ex_abs // NE
            bi = _bi(ex_abs)
            shared = (ex == NEXP)
            nib = 8 if shared else 4
            ncols = nib * C
            if tau == 0:
                p.wait("scalar", s_cex[ex], s_cex[ex].n)
            p.wait("scalar", s_mm, wd_used[bi])
            if tau == 0:
                p.wait("scalar", s_kd, s_kd.n)
                d_src = sd_bf if shared else ed_bf[ex]
                for ib in range(nib):
                    p.dma("scalar", s_wd[bi], wd_buf[bi][:, ib * C:(ib + 1) * C],
                          d_src[0:C, ib * 128:(ib + 1) * 128], transpose=True)
                wd_tr[bi] = s_wd[bi].n
                p.wait("sync", s_wd[bi], s_wd[bi].n)
                p.dma("sync", s_kd, wdT_dram[ex, 0:128, 0:ncols], wd_buf[bi][:, 0:ncols])
            else:
                p.wait("scalar", s_kd, s_kd.n)
                p.dma("scalar", s_wd[bi], wd_buf[bi][:, 0:ncols], wdT_dram[ex, 0:128, 0:ncols])
                wd_tr[bi] = s_wd[bi].n

        d1_evac = [[0, 0], [0, 0]]
        psum01_read = 0
        load_gu(0)
        load_gu(1)
        p.wait("scalar", s_kd, s_kd.n)
        for tau in range(4):
            t0 = tau * SLAB
            p.wait("sync", s_v, s_v.n)
            p.dma("sync", s_ld,
                  xm_sl.rearrange("p (tb c) -> p tb c", c=C),
                  xm_in[t0:t0 + SLAB, :].rearrange("(tb p) c -> p tb c", p=128))
            x_ld = s_ld.n

            p.wait("tensor", s_v, psum01_read)
            for ex in range(NE):
                h_prev = s_v.n
                ex_abs = tau * NE + ex
                bi = _bi(ex_abs)
                shared = (ex == NEXP)
                nob = 8 if shared else 4
                if not shared:
                    p.wait("gpsimd", s_kd, s_kd.n)
                    p.wait("gpsimd", s_v, s_v.n)
                    p.dma("gpsimd", s_csel, selwm_bc[:, :],
                          selwm_dram[ex:ex + 1, t0:t0 + SLAB].partition_broadcast(128))
                sel_cast = s_csel.n

                for ob in range(nob):
                    bk = ob % 2
                    bank = ps_dn[bk][:, 0:512]
                    p.wait("tensor", s_a, d1_evac[bk][0])
                    p.wait("tensor", s_gu[bi], gu_tr[bi])
                    p.wait("tensor", s_trx, tr_xm)
                    for ib in range(8):
                        p.mm(bank,
                             wg_buf[bi][:, ib * (nob * 128) + ob * 128: ib * (nob * 128) + (ob + 1) * 128],
                             xmc[:, ib * TOK + t0: ib * TOK + t0 + SLAB],
                             start=(ib == 0), stop=(ib == 7),
                             inc=(s_mm if ib == 7 else None))
                    p.wait("scalar", s_mm, s_mm.n)
                    p.wait("scalar", s_v, h_prev)
                    p.op("scalar", lambda e, bank=bank, ob=ob: e.activation(
                        a_sl[:, ob * SLAB:(ob + 1) * SLAB], bank, AF.Silu), s_a)
                    d1_evac[bk][0] = s_a.n
                    bank2 = ps_dn[bk][:, 512:1024]
                    p.wait("tensor", s_a, d1_evac[bk][1])
                    for ib in range(8):
                        p.mm(bank2,
                             wu_buf[bi][:, ib * (nob * 128) + ob * 128: ib * (nob * 128) + (ob + 1) * 128],
                             xmc[:, ib * TOK + t0: ib * TOK + t0 + SLAB],
                             start=(ib == 0), stop=(ib == 7),
                             inc=(s_mm if ib == 7 else None))
                    p.wait("scalar", s_mm, s_mm.n)
                    p.wait("scalar", s_v, h_prev)
                    p.op("scalar", lambda e, bank2=bank2, ob=ob: e.copy(
                        u_sl[:, ob * SLAB:(ob + 1) * SLAB], bank2), s_a)
                    d1_evac[bk][1] = s_a.n
                gu_used[bi] = s_mm.n
                a_gu = s_a.n
                if ex_abs + 2 < 4 * NE:
                    load_gu(ex_abs + 2)
                if tau == 0 and ex == NE - 2:
                    load_wd(0)
                if tau == 0 and ex == NE - 1:
                    load_wd(1)
                p.wait("vector", s_a, a_gu)
                p.wait("vector", s_csel, sel_cast)
                dst = hsh if shared else he_buf[ex]
                for ob in range(nob):
                    sl = slice(ob * SLAB, (ob + 1) * SLAB)
                    p.op("vector", lambda e, sl=sl, dst=dst: e.tensor_mul(
                        dst[:, sl], a_sl[:, sl], u_sl[:, sl]), s_v)
                p.wait("vector", s_v, s_v.n)
                if not shared:
                    for ob in range(nob):
                        sl = slice(ob * SLAB, (ob + 1) * SLAB)
                        p.op("vector", lambda e, sl=sl, dst=dst: e.tensor_mul(
                            dst[:, sl], dst[:, sl], selwm_bc[:, :]), s_v)
            v_h = s_v.n

            for ex in range(NE):
                ex_abs = tau * NE + ex
                bi = _bi(ex_abs)
                shared = (ex == NEXP)
                nib = 8 if shared else 4
                buf = wd_buf[bi]
                p.wait("tensor", s_wd[bi], wd_tr[bi])
                p.wait("tensor", s_v, v_h)
                if ex == 0:
                    p.wait("tensor", s_a, s_a.n)
                hsrc = hsh if shared else he_buf[ex]
                for tb in range(4):
                    for oh in range(2):
                        for ib in range(nib):
                            last_of_ex = (tb == 3 and oh == 1 and ib == nib - 1)
                            p.mm(ps_dn[tb][:, oh * 512:(oh + 1) * 512],
                                 hsrc[:, ib * SLAB + tb * 128: ib * SLAB + (tb + 1) * 128],
                                 buf[:, ib * C + oh * 512: ib * C + (oh + 1) * 512],
                                 start=(ex == 0 and ib == 0),
                                 stop=(ex == NEXP and ib == nib - 1),
                                 inc=(s_mm if last_of_ex else None))
                wd_used[bi] = s_mm.n
                if ex_abs + 2 < 4 * NE:
                    load_wd(ex_abs + 2)
            mm_dn = s_mm.n

            for tb in range(4):
                p.wait("vector", s_mm, mm_dn)
                p.wait("vector", s_ld, x_ld)
                p.op("vector", lambda e, tb=tb: e.tensor_add(
                    r2_t[:, :], xm_sl[:, tb * C:(tb + 1) * C], ps_dn[tb][:, :]), s_v)
                p.wait("vector", s_v, s_v.n)
                p.op("vector", lambda e: e.tensor_reduce(
                    col1[:, 0:1], r2_t[:, :], axis=AX.X, op=ALU.add), s_v)
                p.wait("vector", s_v, s_v.n)
                p.op("vector", lambda e: e.tensor_scalar_mul(col1[:, 0:1], col1[:, 0:1], -1.0 / C), s_v)
                p.wait("vector", s_v, s_v.n)
                p.op("vector", lambda e: e.tensor_scalar_add(r2_t[:, :], r2_t[:, :], col1[:, 0:1]), s_v)
                p.wait("scalar", s_v, s_v.n)
                p.op("scalar", lambda e, tb=tb: e.activation(
                    xm_sl[:, tb * C:(tb + 1) * C], r2_t[:, :], AF.Square,
                    accum_out=col2[:, 0:1]), s_a)
                p.wait("vector", s_a, s_a.n)
                p.op("vector", lambda e: e.tensor_scalar(
                    col2[:, 0:1], col2[:, 0:1], 1.0 / C, EPS_LN, ALU.mult, ALU.add), s_v)
                p.wait("vector", s_v, s_v.n)
                p.op("vector", lambda e: e.reciprocal(col2[:, 0:1], col2[:, 0:1]), s_v)
                v_l2 = s_v.n
                p.wait("scalar", s_v, v_l2)
                p.op("scalar", lambda e: e.activation(col2[:, 0:1], col2[:, 0:1], AF.Sqrt), s_a)
                a_l2 = s_a.n
                p.wait("vector", s_a, a_l2)
                p.op("vector", lambda e: e.tensor_scalar_mul(r2_t[:, :], r2_t[:, :], col2[:, 0:1]), s_v)
                v_out = s_v.n
                p.wait("sync", s_v, v_out)
                p.dma("sync", s_o, o_out[t0 + tb * 128:t0 + (tb + 1) * 128, :], r2_t[:, :])
                p.wait("vector", s_o, s_o.n)
                if tb == 1:
                    psum01_read = s_v.n

        p.wait("sync", s_o, s_o.n)
        p.build_block()
    return nc



def kernel(**inputs):
    x = np.ascontiguousarray(np.asarray(inputs["L0_attn_xq"], np.float32))
    tm = np.ascontiguousarray(np.stack([
        np.asarray(inputs["L0_attn_time_mix_r"], np.float32).reshape(C),
        np.asarray(inputs["L0_attn_time_mix_k"], np.float32).reshape(C),
        np.asarray(inputs["L0_attn_time_mix_v"], np.float32).reshape(C),
        np.asarray(inputs["L0_attn_time_mix_g"], np.float32).reshape(C),
    ]))
    td = np.ascontiguousarray(np.asarray(inputs["L0_attn_time_decay"], np.float32).reshape(1, H))
    fa = np.ascontiguousarray(np.asarray(inputs["L0_attn_time_faaaa"], np.float32).reshape(1, H))
    f32c = lambda k: np.ascontiguousarray(np.asarray(inputs[k], np.float32))

    nc1 = build_l1()
    maps1 = []
    for c in range(NCORES):
        b, q = c // 4, c % 4
        t0 = q * TOK
        xp = np.zeros((TOK + 1, C), np.float32)
        xp[1:] = x[b, t0:t0 + TOK]
        if q > 0:
            xp[0] = x[b, t0 - 1]
        maps1.append({
            "xp": xp,
            "wr": f32c("L0_attn_W_r"), "wk": f32c("L0_attn_W_k"),
            "wv": f32c("L0_attn_W_v"), "wg": f32c("L0_attn_W_g"),
            "tm": tm, "td": td, "fa": fa,
        })
    res1 = run_bass_kernel_spmd(nc1, maps1, core_ids=list(range(NCORES)))
    r1 = res1.results

    nc2 = build_l2()
    maps2 = []
    for c in range(NCORES):
        b, q = c // 4, c % 4
        t0 = q * TOK
        sall = np.concatenate([np.asarray(r1[b * 4 + qq]["o_sc"], np.float32)
                               for qq in range(4)], axis=0)
        coefk = np.zeros((1, 4), np.float32)
        maskk = np.zeros((1, 4), np.float32)
        for qq in range(q):
            coefk[0, qq] = float(TOK * (q - 1 - qq))
            maskk[0, qq] = 1.0
        maps2.append({
            "x": x[b, t0:t0 + TOK],
            "xo": np.asarray(r1[c]["o_xo"]).view(np.uint16).view(ml_dtypes.bfloat16)
                  if np.asarray(r1[c]["o_xo"]).dtype != ml_dtypes.bfloat16
                  else np.asarray(r1[c]["o_xo"]),
            "rwb": np.asarray(r1[c]["o_rwb"]),
            "g": np.asarray(r1[c]["o_g"]),
            "sall": sall, "coefk": coefk, "maskk": maskk,
            "wo": f32c("L0_attn_W_o"), "td": td,
            "lnw": np.asarray(inputs["L0_attn_ln_w"], np.float32).reshape(1, C),
            "lnb": np.asarray(inputs["L0_attn_ln_b"], np.float32).reshape(1, C),
            "gw": np.concatenate([f32c("L0_ffn_gate_w"),
                                  np.zeros((8, C), np.float32)], axis=0),
            "lb": np.asarray(inputs["L0_ffn_lb_bias"], np.float32).reshape(1, 8),
        })
    res2 = run_bass_kernel_spmd(nc2, maps2, core_ids=list(range(NCORES)))
    r2 = res2.results

    nc3 = build_l3()
    stats_all = np.concatenate([np.asarray(r2[c]["o_stats"], np.float32)
                                for c in range(NCORES)], axis=0)
    maps3 = []
    for c in range(NCORES):
        maps3.append({
            "xm": np.asarray(r2[c]["o_xmid"], np.float32),
            "m": np.concatenate([np.asarray(r2[c]["o_m"], np.float32),
                                 np.zeros((TOK, 112), np.float32)], axis=1),
            "stats": stats_all,
            "eg": f32c("L0_ffn_expert_gate_w"),
            "eu": f32c("L0_ffn_expert_up_w"),
            "ed": f32c("L0_ffn_expert_down_w"),
            "sg": f32c("L0_ffn_shared_gate_w"),
            "su": f32c("L0_ffn_shared_up_w"),
            "sd": f32c("L0_ffn_shared_down_w"),
        })
    res3 = run_bass_kernel_spmd(nc3, maps3, core_ids=list(range(NCORES)))
    r3 = res3.results

    out = np.zeros((B, TT, C), np.float32)
    for c in range(NCORES):
        b, q = c // 4, c % 4
        out[b, q * TOK:(q + 1) * TOK] = np.asarray(r3[c]["o_out"], np.float32)
    return out


# version 10
# speedup vs baseline: 1.9884x; 1.0137x over previous
import numpy as np
import ml_dtypes
from contextlib import ExitStack

import concourse.bass as bass
import concourse.mybir as mybir
from concourse.bass_utils import run_bass_kernel_spmd

F32 = mybir.dt.float32
BF16 = mybir.dt.bfloat16
U32 = mybir.dt.uint32
AF = mybir.ActivationFunctionType
ALU = mybir.AluOpType
AX = mybir.AxisListType

B = 2
TT = 8192
C = 1024
H = 16
KD = 64
CH = 128
TOK = 2048
NCH = TOK // CH
SLAB = 512
NSLAB = TOK // SLAB
HEAD_DIV = 8.0
NCORES = 8
EPS_GN = 1e-5
EPS_LN = 1e-5
NEXP = 8
EI = 512
SI = 1024


class Ctr:
    def __init__(self, ctx, nc, name):
        self.h = ctx.enter_context(nc.semaphore(name))
        self.n = 0
        self.dma = False


class Prog:
    ENGINES = ("tensor", "vector", "scalar", "gpsimd", "sync")

    def __init__(self, nc, ctx):
        self.nc = nc
        self.ctx = ctx
        self.q = {e: [] for e in self.ENGINES}

    def sem(self, name):
        return Ctr(self.ctx, self.nc, name)

    def wait(self, eng, ctr, val=None):
        v = ctr.n if (val is None or ctr.dma) else val
        if v > 0:
            self.q[eng].append(lambda e, c=ctr, v=v: e.wait_ge(c.h, v))

    def mm(self, psum, lhsT, rhs, start, stop, inc=None, tile_position=None):
        try:
            nfree = rhs.shape[-1]
        except Exception:
            nfree = 512
        k = 8 if nfree >= 256 else 4

        def fn(e, psum=psum, lhsT=lhsT, rhs=rhs, start=start, stop=stop, inc=inc,
               tp=tile_position, k=k):
            ins = e.matmul(psum, lhsT, rhs, start=start, stop=stop, tile_position=tp)
            if inc is not None:
                ins.then_inc(inc.h, 1)
            for _ in range(k):
                e.nop()
        self.q["tensor"].append(fn)
        if inc is not None:
            inc.n += 1

    def op(self, eng, fn, inc=None):
        def wrapped(e, fn=fn, inc=inc):
            ins = fn(e)
            if inc is not None:
                ins.then_inc(inc.h, 1)
        self.q[eng].append(wrapped)
        if inc is not None:
            inc.n += 1

    def dma(self, eng, ctr, out, in_, transpose=False, **kw):
        ctr.dma = True
        def fn(e, out=out, in_=in_, transpose=transpose, kw=kw):
            e.dma_start(out, in_, transpose=transpose, **kw).then_inc(ctr.h, 16)
        self.q[eng].append(fn)
        ctr.n += 16

    def build_block(self):
        block = self.ctx.enter_context(self.nc.Block())
        q = self.q

        @block.tensor
        def _(e):
            for fn in q["tensor"]:
                fn(e)

        @block.vector
        def _(e):
            for fn in q["vector"]:
                fn(e)

        @block.scalar
        def _(e):
            for fn in q["scalar"]:
                fn(e)

        @block.gpsimd
        def _(e):
            for fn in q["gpsimd"]:
                fn(e)

        @block.sync
        def _(e):
            for fn in q["sync"]:
                fn(e)


def _head(h):
    return slice((h % 2) * KD, (h % 2) * KD + KD), h // 2



def build_l1(debug=False):
    nc = bass.Bass("TRN2", target_bir_lowering=False)
    xp = nc.dram_tensor("xp", [TOK + 1, C], F32, kind="ExternalInput").ap()
    w_in = {nm: nc.dram_tensor(nm, [C, C], F32, kind="ExternalInput").ap()
            for nm in ("wr", "wk", "wv", "wg")}
    tm_in = nc.dram_tensor("tm", [4, C], F32, kind="ExternalInput").ap()
    td_in = nc.dram_tensor("td", [1, H], F32, kind="ExternalInput").ap()
    fa_in = nc.dram_tensor("fa", [1, H], F32, kind="ExternalInput").ap()

    o_sc = nc.dram_tensor("o_sc", [128, 512], F32, kind="ExternalOutput").ap()
    o_xo = nc.dram_tensor("o_xo", [TOK, C], BF16, kind="ExternalOutput").ap()
    o_rwb = nc.dram_tensor("o_rwb", [128, 8 * TOK], BF16, kind="ExternalOutput").ap()
    o_g = nc.dram_tensor("o_g", [TOK, C], BF16, kind="ExternalOutput").ap()

    x_bf = nc.dram_tensor("x_bf", [TOK + 1, C], BF16, kind="Internal").ap()
    w_bf = {nm: nc.dram_tensor(nm + "_bf", [C, C], BF16, kind="Internal").ap()
            for nm in ("wr", "wk", "wv", "wg")}
    kc_dram = nc.dram_tensor("kc_dram", [C, TOK], BF16, kind="Internal").ap()
    sc1 = nc.dram_tensor("sc1", [1, 64], F32, kind="Internal").ap()

    dbg = {}
    if debug:
        for nm, shp, dt_ in [("d_wmat", [128, H * CH], F32), ("d_wb", [128, 8 * CH], BF16),
                             ("d_wk", [128, H], F32), ("d_ws", [128, 512], F32),
                             ("d_ldbc", [128, H], F32),
                             ("d_mixr", [128, 8 * SLAB], BF16), ("d_mixk", [128, 8 * SLAB], BF16),
                             ("d_rsl", [128, 8 * SLAB], BF16), ("d_kcs", [128, 8 * SLAB], BF16),
                             ("d_kts", [128, 4 * C], BF16), ("d_vts", [128, 4 * C], BF16),
                             ("d_xc", [128, 8 * SLAB], BF16), ("d_xxc", [128, 8 * SLAB], BF16)]:
            dbg[nm] = nc.dram_tensor(nm, shp, dt_, kind="ExternalOutput").ap()

    with ExitStack() as ctx:
        p = Prog(nc, ctx)
        sb = lambda n, s, d=F32: ctx.enter_context(nc.sbuf_tensor(n, s, d))
        psl = lambda n, s: ctx.enter_context(nc.psum_tensor(n, s, F32))

        wT = {nm: sb(nm + "T", [128, 8 * C], BF16) for nm in ("wr", "wk", "wv", "wg")}
        xc = sb("xc", [128, 8 * SLAB], BF16)
        xxc = sb("xxc", [128, 8 * SLAB], BF16)
        mixed = {nm: sb("mx_" + nm, [128, 8 * SLAB], BF16) for nm in ("r", "k", "v", "g")}
        r_sl = sb("r_sl", [128, 8 * SLAB], BF16)
        rwb_sl = sb("rwb_sl", [128, 8 * SLAB], BF16)
        kcs = sb("kcs", [128, 8 * SLAB], BF16)
        kts = sb("kts", [128, 4 * C], BF16)
        kwk = sb("kwk", [128, C], BF16)
        vts = sb("vts", [128, 4 * C], BF16)
        g_sl = sb("g_sl", [128, 4 * C], BF16)
        xo_sl = sb("xo_sl", [128, 4 * C], BF16)
        attw = sb("attw", [128, H * CH], BF16)
        s_sb = sb("s_sb", [128, 512], F32)
        s_tmp = sb("s_tmp", [128, 512], F32)
        s_bf = sb("s_bf", [128, 512], BF16)

        tm_cols = sb("tm_cols", [128, 32], F32)
        td_row = sb("td_row", [1, H], F32)
        ld_row = sb("ld_row", [1, H], F32)
        ws_row = sb("ws_row", [1, H], F32)
        fa_row = sb("fa_row", [1, H], F32)
        ld_bc = sb("ld_bc", [128, H], F32)
        u_bc = sb("u_bc", [128, H], F32)
        ws_bc = sb("ws_bc", [128, H], F32)
        ldcol8 = sb("ldcol8", [128, 8], F32)
        wmatT = sb("wmatT", [128, H * CH], F32)
        wb_ct = sb("wb_ct", [128, 8 * CH], BF16)
        wk_cols = sb("wk_cols", [128, H], F32)
        ws_tile = sb("ws_tile", [128, 512], F32)
        iota_d = sb("iota_d", [128, CH], F32)
        iota_f = sb("iota_f", [128, CH], F32)
        clampd = sb("clampd", [128, CH], F32)
        rev_col = sb("rev_col", [128, 1], F32)
        ones128 = sb("ones128", [128, CH], F32)
        eye = sb("eye", [128, CH], F32)

        ps_proj = [psl("ps_proj%d" % i, [128, SLAB]) for i in range(2)]
        ps_att = [psl("ps_att%d" % i, [128, 512]) for i in range(2)]
        ps_out = psl("ps_out", [128, C])
        ps_A = psl("ps_A", [128, 512])

        s_cast = p.sem("s_cast")
        s_ld = p.sem("s_ld")
        s_xld = p.sem("s_xld")
        s_tr = p.sem("s_tr")
        s_kd = p.sem("s_kd")
        s_kc = p.sem("s_kc")
        s_mm = p.sem("s_mm")
        s_v = p.sem("s_v")
        s_a = p.sem("s_a")
        s_g = p.sem("s_g")
        s_o = p.sem("s_o")

        p.dma("gpsimd", s_cast, x_bf, xp)
        cast_x = s_cast.n
        for nm in ("wr", "wk", "wv", "wg"):
            p.dma("gpsimd", s_cast, w_bf[nm], w_in[nm])
        cast_w = s_cast.n

        p.dma("sync", s_ld, td_row[0:1, :], td_in)
        p.dma("sync", s_ld, fa_row[0:1, :], fa_in)
        p.dma("sync", s_ld, tm_cols[:, :],
              tm_in.rearrange("m (p c) -> c (m p)", c=128),
              allow_slow_non_contiguous=True)
        ld0 = s_ld.n

        p.wait("scalar", s_ld, ld0)
        p.op("scalar", lambda e: e.activation(ld_row[0:1, :], td_row[0:1, :], AF.Exp), s_a)
        a_exp = s_a.n
        p.wait("vector", s_a, a_exp)
        p.op("vector", lambda e: e.tensor_scalar_mul(ld_row[0:1, :], ld_row[0:1, :], -1.0), s_v)
        v_ld = s_v.n
        p.wait("scalar", s_v, v_ld)
        p.op("scalar", lambda e: e.activation(ws_row[0:1, :], ld_row[0:1, :], AF.Exp,
                                              scale=float(CH)), s_a)
        a_ws = s_a.n
        p.wait("sync", s_v, v_ld)
        p.dma("sync", s_kd, sc1[0:1, 0:H], ld_row[0:1, :])
        p.wait("sync", s_a, a_ws)
        p.dma("sync", s_kd, sc1[0:1, 16:16 + H], ws_row[0:1, :])
        p.wait("sync", s_ld, ld0)
        p.dma("sync", s_kd, sc1[0:1, 32:32 + H], fa_row[0:1, :])
        kd0 = s_kd.n

        p.wait("gpsimd", s_kd, kd0)
        p.dma("gpsimd", s_cast, ld_bc[:, :], sc1[0:1, 0:H].partition_broadcast(128))
        p.dma("gpsimd", s_cast, ws_bc[:, :], sc1[0:1, 16:16 + H].partition_broadcast(128))
        p.dma("gpsimd", s_cast, u_bc[:, :], sc1[0:1, 32:32 + H].partition_broadcast(128))
        for pb in range(8):
            p.dma("gpsimd", s_cast, ldcol8[0:64, pb:pb + 1],
                  sc1[0:1, 2 * pb:2 * pb + 1].partition_broadcast(64))
            p.dma("gpsimd", s_cast, ldcol8[64:128, pb:pb + 1],
                  sc1[0:1, 2 * pb + 1:2 * pb + 2].partition_broadcast(64))
        cast_bc = s_cast.n

        p.op("gpsimd", lambda e: e.iota(iota_d[:, :], pattern=[[1, CH]], base=-1,
                                        channel_multiplier=-1,
                                        allow_small_or_imprecise_dtypes=True), s_g)
        p.op("gpsimd", lambda e: e.iota(iota_f[:, :], pattern=[[1, CH]], base=0,
                                        channel_multiplier=0,
                                        allow_small_or_imprecise_dtypes=True), s_g)
        p.op("gpsimd", lambda e: e.iota(rev_col[:, 0:1], pattern=[[0, 1]], base=127,
                                        channel_multiplier=-1,
                                        allow_small_or_imprecise_dtypes=True), s_g)
        p.op("gpsimd", lambda e: e.memset(ones128[:, :], 1.0), s_g)
        g_iota = s_g.n
        p.wait("gpsimd", s_g, g_iota)
        p.op("gpsimd", lambda e: e.affine_select(eye[:, :], ones128[:, :],
                                                 pattern=[[1, CH]], compare_op=ALU.is_equal,
                                                 fill=0.0, base=0, channel_multiplier=-1), s_g)
        g_eye = s_g.n

        p.wait("vector", s_g, g_iota)
        p.op("vector", lambda e: e.tensor_scalar_max(clampd[:, :], iota_d[:, :], 0.0), s_v)
        v_clamp = s_v.n

        p.wait("scalar", s_v, v_clamp)
        p.wait("scalar", s_cast, cast_bc)
        for h in range(H):
            sl = slice(h * CH, (h + 1) * CH)
            p.op("scalar", lambda e, sl=sl, h=h: e.activation(
                wmatT[:, sl], clampd[:, :], AF.Exp, scale=ld_bc[:, h:h + 1]), s_a)
        a_wpow = s_a.n
        p.wait("gpsimd", s_a, a_wpow)
        for h in range(H):
            sl = slice(h * CH, (h + 1) * CH)
            p.op("gpsimd", lambda e, sl=sl: e.affine_select(
                wmatT[:, sl], wmatT[:, sl], pattern=[[1, CH]], compare_op=ALU.is_ge,
                fill=0.0, base=-1, channel_multiplier=-1), s_g)
        g_wmask = s_g.n
        p.wait("vector", s_g, g_wmask)
        p.wait("vector", s_cast, cast_bc)
        for h in range(H):
            sl = slice(h * CH, (h + 1) * CH)
            p.op("vector", lambda e, sl=sl, h=h: e.scalar_tensor_tensor(
                wmatT[:, sl], eye[:, :], u_bc[:, h:h + 1], wmatT[:, sl], ALU.mult, ALU.add), s_v)
        v_wmat = s_v.n

        p.wait("scalar", s_cast, cast_bc)
        p.wait("scalar", s_g, g_iota)
        for pb in range(8):
            sl = slice(pb * CH, (pb + 1) * CH)
            p.op("scalar", lambda e, sl=sl, pb=pb: e.activation(
                wb_ct[:, sl], iota_f[:, :], AF.Exp, scale=ldcol8[:, pb:pb + 1]), s_a)
        a_wb = s_a.n

        for h in range(H):
            p.op("scalar", lambda e, h=h: e.activation(
                wk_cols[:, h:h + 1], rev_col[:, 0:1], AF.Exp, scale=ld_bc[:, h:h + 1]), s_a)
        a_wk = s_a.n

        p.wait("vector", s_cast, cast_bc)
        for hp in range(8):
            for half in range(2):
                rows = slice(half * 64, half * 64 + 64)
                colh = 2 * hp + half
                p.op("vector", lambda e, rows=rows, hp=hp, colh=colh: e.tensor_scalar_mul(
                    ws_tile[rows, hp * 64:(hp + 1) * 64], ones128[0:64, 0:64],
                    ws_bc[rows, colh:colh + 1]), s_v)
        p.op("vector", lambda e: e.memset(s_sb[:, :], 0.0), s_v)
        p.op("vector", lambda e: e.memset(s_bf[:, :], 0.0), s_v)
        v_consts = s_v.n

        if debug:
            p.wait("sync", s_v, s_v.n)
            p.wait("sync", s_a, s_a.n)
            p.wait("sync", s_g, s_g.n)
            p.dma("sync", s_o, dbg["d_wmat"], wmatT[:, :])
            p.dma("sync", s_o, dbg["d_wb"], wb_ct[:, :])
            p.dma("sync", s_o, dbg["d_wk"], wk_cols[:, :])
            p.dma("sync", s_o, dbg["d_ws"], ws_tile[:, :])
            p.dma("sync", s_o, dbg["d_ldbc"], ld_bc[:, :])

        p.wait("scalar", s_cast, cast_w)
        for nm in ("wr", "wk", "wv", "wg"):
            for ib in range(8):
                p.dma("scalar", s_tr, wT[nm][:, ib * C:(ib + 1) * C],
                      w_bf[nm][0:C, ib * 128:(ib + 1) * 128], transpose=True)
        tr_w = s_tr.n

        bank_evac = [0, 0]
        mix_done_v = 0
        proj_done_mm = 0
        attw_read_prev = 0
        a_xo_prev = 0
        v_sbf_prev = 0
        v_supd_prev = 0
        for s in range(NSLAB):
            t0 = s * SLAB
            p.wait("scalar", s_o, s_o.n)
            p.wait("vector", s_o, s_o.n)
            p.wait("scalar", s_cast, cast_x)
            p.wait("scalar", s_v, mix_done_v)
            for pb in range(8):
                cs = slice(pb * 128, (pb + 1) * 128)
                p.dma("scalar", s_xld, xc[:, pb * SLAB:(pb + 1) * SLAB],
                      x_bf[1 + t0:1 + t0 + SLAB, cs], transpose=True)
                p.dma("scalar", s_xld, xxc[:, pb * SLAB:(pb + 1) * SLAB],
                      x_bf[t0:t0 + SLAB, cs], transpose=True)
            xld = s_xld.n

            p.wait("vector", s_xld, xld)
            p.wait("vector", s_mm, proj_done_mm)
            for mi, nm in enumerate(("r", "k", "v", "g")):
                for pb in range(8):
                    sl = slice(pb * SLAB, (pb + 1) * SLAB)
                    p.op("vector", lambda e, sl=sl, nm=nm: e.tensor_sub(
                        mixed[nm][:, sl], xc[:, sl], xxc[:, sl]), s_v)
            v_sub = s_v.n
            p.wait("vector", s_v, v_sub)
            for mi, nm in enumerate(("r", "k", "v", "g")):
                for pb in range(8):
                    sl = slice(pb * SLAB, (pb + 1) * SLAB)
                    tcol = tm_cols[:, mi * 8 + pb:mi * 8 + pb + 1]
                    p.op("vector", lambda e, sl=sl, tcol=tcol, nm=nm: e.scalar_tensor_tensor(
                        mixed[nm][:, sl], mixed[nm][:, sl], tcol, xxc[:, sl],
                        ALU.mult, ALU.add), s_v)
            v_mix = s_v.n
            mix_done_v = v_mix

            def cmaj(nm_w, nm_x, dst):
                for ob in range(8):
                    bk = ob % 2
                    bank = ps_proj[bk]
                    p.wait("tensor", s_a, bank_evac[bk])
                    p.wait("tensor", s_v, v_mix)
                    p.wait("tensor", s_tr, tr_w)
                    for ib in range(8):
                        p.mm(bank[:, :],
                             wT[nm_w][:, ib * C + ob * 128: ib * C + (ob + 1) * 128],
                             mixed[nm_x][:, ib * SLAB:(ib + 1) * SLAB],
                             start=(ib == 0), stop=(ib == 7),
                             inc=(s_mm if ib == 7 else None))
                    p.wait("scalar", s_mm, s_mm.n)
                    sl = slice(ob * SLAB, (ob + 1) * SLAB)
                    p.op("scalar", lambda e, bank=bank, dst=dst, sl=sl:
                         e.copy(dst[:, sl], bank[:, :]), s_a)
                    bank_evac[bk] = s_a.n

            def tmaj(nm_w, nm_x, dst, act=None):
                for tb in range(4):
                    for oh in range(2):
                        bk = (tb * 2 + oh) % 2
                        bank = ps_proj[bk]
                        p.wait("tensor", s_a, bank_evac[bk])
                        p.wait("tensor", s_v, v_mix)
                        p.wait("tensor", s_tr, tr_w)
                        for ib in range(8):
                            p.mm(bank[:, :],
                                 mixed[nm_x][:, ib * SLAB + tb * 128: ib * SLAB + (tb + 1) * 128],
                                 wT[nm_w][:, ib * C + oh * 512: ib * C + (oh + 1) * 512],
                                 start=(ib == 0), stop=(ib == 7),
                                 inc=(s_mm if ib == 7 else None))
                        p.wait("scalar", s_mm, s_mm.n)
                        sl = slice(tb * C + oh * 512, tb * C + (oh + 1) * 512)
                        if act is None:
                            p.op("scalar", lambda e, bank=bank, sl=sl, dst=dst:
                                 e.copy(dst[:, sl], bank[:, :]), s_a)
                        else:
                            p.op("scalar", lambda e, bank=bank, sl=sl, dst=dst, act=act:
                                 e.activation(dst[:, sl], bank[:, :], act), s_a)
                        bank_evac[bk] = s_a.n

            cmaj("wr", "r", r_sl)
            cmaj("wk", "k", kcs)
            a_rk = s_a.n
            tmaj("wv", "v", vts)
            tmaj("wg", "g", g_sl, act=AF.Silu)
            a_proj = s_a.n
            proj_done_mm = s_mm.n

            p.wait("vector", s_a, a_rk)
            p.wait("vector", s_a, a_wb)
            for pb in range(8):
                for cnn in range(4):
                    sl = slice(pb * SLAB + cnn * CH, pb * SLAB + (cnn + 1) * CH)
                    wsl = slice(pb * CH, (pb + 1) * CH)
                    p.op("vector", lambda e, sl=sl, wsl=wsl: e.tensor_mul(
                        rwb_sl[:, sl], r_sl[:, sl], wb_ct[:, wsl]), s_v)
            v_rwb = s_v.n

            p.wait("sync", s_a, a_rk)
            p.dma("sync", s_kc,
                  kc_dram[0:C, t0:t0 + SLAB].rearrange("(p a) t -> a p t", a=128),
                  kcs.rearrange("a (p t) -> a p t", t=SLAB))
            kwrt = s_kc.n
            p.wait("scalar", s_kc, kwrt)
            p.wait("scalar", s_mm, s_mm.n)
            for tb in range(4):
                p.dma("scalar", s_tr, kts[:, tb * C:(tb + 1) * C],
                      kc_dram[0:C, t0 + tb * 128:t0 + (tb + 1) * 128], transpose=True)
            tr_k = s_tr.n

            if debug and s == 0:
                p.wait("sync", s_v, s_v.n)
                p.wait("sync", s_a, s_a.n)
                p.wait("sync", s_tr, s_tr.n)
                p.dma("sync", s_o, dbg["d_xc"], xc[:, :])
                p.dma("sync", s_o, dbg["d_xxc"], xxc[:, :])
                p.dma("sync", s_o, dbg["d_mixr"], mixed["r"][:, :])
                p.dma("sync", s_o, dbg["d_mixk"], mixed["k"][:, :])
                p.dma("sync", s_o, dbg["d_rsl"], r_sl[:, :])
                p.dma("sync", s_o, dbg["d_kcs"], kcs[:, :])
                p.dma("sync", s_o, dbg["d_kts"], kts[:, :])
                p.dma("sync", s_o, dbg["d_vts"], vts[:, :])
                p.wait("vector", s_o, s_o.n)
                p.wait("tensor", s_o, s_o.n)
                p.wait("sync", s_o, s_o.n)
                p.wait("scalar", s_o, s_o.n)
                p.wait("gpsimd", s_o, s_o.n)

            att_banks = [ps_att[0], ps_att[1], ps_proj[0], ps_proj[1]]
            for cnn in range(4):
                p.wait("vector", s_tr, tr_k)
                p.wait("vector", s_a, a_wk)
                p.wait("vector", s_mm, s_mm.n)
                for h in range(H):
                    p.op("vector", lambda e, h=h, cnn=cnn: e.tensor_scalar_mul(
                        kwk[:, h * KD:(h + 1) * KD],
                        kts[:, cnn * C + h * KD:cnn * C + (h + 1) * KD],
                        wk_cols[:, h:h + 1]), s_v)
                v_kwk = s_v.n

                p.wait("tensor", s_v, attw_read_prev)
                p.wait("tensor", s_a, a_proj)
                p.wait("tensor", s_v, v_rwb)
                att_mm = {}
                for h in range(H):
                    rows, pb = _head(h)
                    rsl = slice(pb * SLAB + cnn * CH, pb * SLAB + (cnn + 1) * CH)
                    bank = att_banks[h // 4]
                    csl = slice((h % 4) * CH, (h % 4 + 1) * CH)
                    p.mm(bank[:, csl], kcs[rows, rsl], r_sl[rows, rsl],
                         start=True, stop=True, inc=s_mm)
                    att_mm[h] = s_mm.n

                p.wait("vector", s_v, v_wmat)
                attw_v = {}
                for h in range(H):
                    bank = att_banks[h // 4]
                    csl = slice((h % 4) * CH, (h % 4 + 1) * CH)
                    p.wait("vector", s_mm, att_mm[h])
                    p.op("vector", lambda e, bank=bank, csl=csl, h=h: e.tensor_mul(
                        attw[:, h * CH:(h + 1) * CH], bank[:, csl],
                        wmatT[:, h * CH:(h + 1) * CH]), s_v)
                    attw_v[h] = s_v.n
                attw_read_prev = s_v.n

                p.wait("tensor", s_a, a_xo_prev)
                p.wait("tensor", s_v, v_sbf_prev)
                p.wait("tensor", s_v, v_supd_prev)
                p.wait("tensor", s_v, v_kwk)
                for h in range(H):
                    rows, pb = _head(h)
                    rsl = slice(pb * SLAB + cnn * CH, pb * SLAB + (cnn + 1) * CH)
                    p.wait("tensor", s_v, attw_v[h])
                    vsl = slice(cnn * C + h * KD, cnn * C + (h + 1) * KD)
                    p.mm(ps_out[:, h * KD:(h + 1) * KD],
                         attw[:, h * CH:(h + 1) * CH], vts[:, vsl],
                         start=True, stop=False)
                    srows = slice((h % 2) * KD, (h % 2) * KD + KD)
                    scols = slice((h // 2) * KD, (h // 2 + 1) * KD)
                    p.mm(ps_out[:, h * KD:(h + 1) * KD],
                         rwb_sl[rows, rsl], s_bf[srows, scols],
                         start=False, stop=True, inc=s_mm)
                    p.mm(ps_A[srows, scols], kwk[:, h * KD:(h + 1) * KD], vts[:, vsl],
                         start=True, stop=True, inc=s_mm,
                         tile_position=(0, (h % 2) * KD))
                mm_chunk = s_mm.n

                p.wait("vector", s_mm, mm_chunk)
                p.op("vector", lambda e: e.tensor_mul(s_tmp[:, :], s_sb[:, :], ws_tile[:, :]), s_v)
                p.wait("vector", s_v, s_v.n)
                p.op("vector", lambda e: e.tensor_add(s_sb[:, :], s_tmp[:, :], ps_A[:, :]), s_v)
                v_supd_prev = s_v.n
                p.wait("vector", s_v, v_supd_prev)
                p.op("vector", lambda e: e.tensor_copy(s_bf[:, :], s_sb[:, :]), s_v)
                v_sbf_prev = s_v.n

                p.wait("scalar", s_mm, mm_chunk)
                p.op("scalar", lambda e, cnn=cnn: e.mul(
                    xo_sl[:, cnn * C:(cnn + 1) * C], ps_out[:, :], 1.0 / HEAD_DIV), s_a)
                a_xo_prev = s_a.n

            a_slab = s_a.n
            v_slab = s_v.n
            p.wait("sync", s_a, a_slab)
            p.wait("sync", s_v, v_slab)
            p.dma("sync", s_o,
                  o_xo[t0:t0 + SLAB, :].rearrange("(tb p) c -> p tb c", p=128),
                  xo_sl.rearrange("a (tb c) -> a tb c", c=C))
            p.dma("sync", s_o,
                  o_g[t0:t0 + SLAB, :].rearrange("(tb p) c -> p tb c", p=128),
                  g_sl.rearrange("a (tb c) -> a tb c", c=C))
            p.dma("sync", s_o,
                  o_rwb.rearrange("a (p t) -> a p t", t=TOK)[:, :, t0:t0 + SLAB],
                  rwb_sl.rearrange("a (p t) -> a p t", t=SLAB))

        p.wait("sync", s_v, s_v.n)
        p.dma("sync", s_o, o_sc, s_sb[:, :])
        p.wait("sync", s_o, s_o.n)
        p.build_block()
    return nc



def build_l2():
    nc = bass.Bass("TRN2", target_bir_lowering=False)
    x_in = nc.dram_tensor("x", [TOK, C], F32, kind="ExternalInput").ap()
    xo_in = nc.dram_tensor("xo", [TOK, C], BF16, kind="ExternalInput").ap()
    rwb_in = nc.dram_tensor("rwb", [128, 8 * TOK], BF16, kind="ExternalInput").ap()
    g_in = nc.dram_tensor("g", [TOK, C], BF16, kind="ExternalInput").ap()
    sall_in = nc.dram_tensor("sall", [512, 512], F32, kind="ExternalInput").ap()
    coefk_in = nc.dram_tensor("coefk", [1, 4], F32, kind="ExternalInput").ap()
    mask_in = nc.dram_tensor("maskk", [1, 4], F32, kind="ExternalInput").ap()
    wo_in = nc.dram_tensor("wo", [C, C], F32, kind="ExternalInput").ap()
    td_in = nc.dram_tensor("td", [1, H], F32, kind="ExternalInput").ap()
    lnw_in = nc.dram_tensor("lnw", [1, C], F32, kind="ExternalInput").ap()
    lnb_in = nc.dram_tensor("lnb", [1, C], F32, kind="ExternalInput").ap()
    gw_in = nc.dram_tensor("gw", [16, C], F32, kind="ExternalInput").ap()
    lb_in = nc.dram_tensor("lb", [1, 8], F32, kind="ExternalInput").ap()

    o_xmid = nc.dram_tensor("o_xmid", [TOK, C], F32, kind="ExternalOutput").ap()
    o_m = nc.dram_tensor("o_m", [TOK, 16], F32, kind="ExternalOutput").ap()
    o_stats = nc.dram_tensor("o_stats", [1, 16], F32, kind="ExternalOutput").ap()

    wo_bf = nc.dram_tensor("wo_bf", [C, C], BF16, kind="Internal").ap()
    gw_bf = nc.dram_tensor("gw_bf", [16, C], BF16, kind="Internal").ap()
    yg_dram = nc.dram_tensor("yg_dram", [TOK, C], BF16, kind="Internal").ap()
    xmb_dram = nc.dram_tensor("xmb_dram", [TOK, C], BF16, kind="Internal").ap()
    sc2 = nc.dram_tensor("sc2", [1, 128], F32, kind="Internal").ap()

    with ExitStack() as ctx:
        p = Prog(nc, ctx)
        sb = lambda n, s, d=F32: ctx.enter_context(nc.sbuf_tensor(n, s, d))
        psl = lambda n, s: ctx.enter_context(nc.psum_tensor(n, s, F32))

        rwb = sb("rwb_s", [128, 8 * TOK], BF16)
        xo_res = sb("xo_res", [128, NCH * C], BF16)
        g_res = sb("g_res", [128, NCH * C], BF16)
        woT = sb("woT", [128, 8 * C], BF16)
        s_tiles = [sb("s_t%d" % i, [128, 512]) for i in range(4)]
        s_e = sb("s_e", [128, 512])
        s_ebf = sb("s_ebf", [128, 512], BF16)
        ws_tile = sb("ws_tile", [128, 512], F32)
        y_t = sb("y_t", [128, C])
        sq_t = sb("sq_t", [128, C])
        gn16 = sb("gn16", [128, 16])
        gn16b = sb("gn16b", [128, 16])
        rstd16 = sb("rstd16", [128, 16])
        yg_sl = sb("yg_sl", [128, 4 * C], BF16)
        ygc = sb("ygc", [128, 8 * SLAB], BF16)
        x_sl = sb("x_sl", [128, 4 * C], F32)
        xm_t = sb("xm_t", [128, C], F32)
        xmb_sl = sb("xmb_sl", [128, 4 * C], BF16)
        col1 = sb("col1", [128, 1])
        col2 = sb("col2", [128, 1])
        lnw_bc = sb("lnw_bc", [128, C], F32)
        lnb_bc = sb("lnb_bc", [128, C], F32)
        td_row = sb("td_row", [1, H], F32)
        ld_row = sb("ld_row", [1, H], F32)
        ws_row = sb("ws_row", [1, H], F32)
        ws_bc = sb("ws_bc", [128, H], F32)
        coefk_sb = sb("coefk_sb", [1, 4], F32)
        mask_sb = sb("mask_sb", [1, 4], F32)
        coefrow = sb("coefrow", [1, 4 * H], F32)
        coef_bc = sb("coef_bc", [128, 4 * H], F32)
        ones64 = sb("ones64", [64, 64], F32)
        gwT = sb("gwT", [128, 8 * 16], BF16)
        xmc_all = g_res
        iota8 = sb("iota8", [128, 8], F32)
        lb_bc = sb("lb_bc", [128, 8], F32)
        lg = sb("lg", [128, 8], F32)
        scores = sb("scores", [128, 8], F32)
        mx8 = sb("mx8", [128, 8], F32)
        mi8 = sb("mi8", [128, 8], U32)
        mif = sb("mif", [128, 8], F32)
        eq1 = sb("eq1", [128, 8], F32)
        eq2 = sb("eq2", [128, 8], F32)
        mw = sb("mw", [128, 16], F32)
        ones_col = sb("ones_col", [128, 1], F32)
        den = sb("den", [128, 1], F32)
        w1c = sb("w1c", [128, 1], F32)
        w2c = sb("w2c", [128, 1], F32)
        st_sb = sb("st_sb", [1, 16], F32)

        ps_corr = psl("ps_corr", [128, C])
        ps_proj = [psl("ps_proj%d" % i, [128, SLAB]) for i in range(2)]
        ps_g = psl("ps_g", [128, 8])
        ps_st = psl("ps_st", [1, 16])

        s_cast = p.sem("s_cast")
        s_ld = p.sem("s_ld")
        s_tr = p.sem("s_tr")
        s_kd = p.sem("s_kd")
        s_mm = p.sem("s_mm")
        s_v = p.sem("s_v")
        s_a = p.sem("s_a")
        s_g = p.sem("s_g")
        s_o = p.sem("s_o")

        p.dma("gpsimd", s_cast, wo_bf, wo_in)
        p.dma("gpsimd", s_cast, gw_bf, gw_in)
        cast_w = s_cast.n
        p.dma("gpsimd", s_cast, lnw_bc[:, :], lnw_in[0:1, :].partition_broadcast(128))
        p.dma("gpsimd", s_cast, lnb_bc[:, :], lnb_in[0:1, :].partition_broadcast(128))
        p.dma("gpsimd", s_cast, lb_bc[:, :], lb_in[0:1, :].partition_broadcast(128))
        cast_ln = s_cast.n

        p.dma("sync", s_ld, rwb[:, :], rwb_in)
        p.dma("sync", s_ld,
              xo_res.rearrange("a (tb c) -> a tb c", c=C),
              xo_in.rearrange("(tb p) c -> p tb c", p=128))
        p.dma("sync", s_ld,
              g_res.rearrange("a (tb c) -> a tb c", c=C),
              g_in.rearrange("(tb p) c -> p tb c", p=128))
        for i in range(4):
            p.dma("sync", s_ld, s_tiles[i][:, :], sall_in[i * 128:(i + 1) * 128, :])
        p.dma("sync", s_ld, td_row[0:1, :], td_in)
        p.dma("sync", s_ld, coefk_sb[0:1, :], coefk_in)
        p.dma("sync", s_ld, mask_sb[0:1, :], mask_in)
        ld0 = s_ld.n

        p.wait("scalar", s_ld, ld0)
        p.op("scalar", lambda e: e.activation(ld_row[0:1, :], td_row[0:1, :], AF.Exp), s_a)
        p.wait("vector", s_a, s_a.n)
        p.op("vector", lambda e: e.tensor_scalar_mul(ld_row[0:1, :], ld_row[0:1, :], -1.0), s_v)
        v_ld = s_v.n
        p.wait("scalar", s_v, v_ld)
        p.op("scalar", lambda e: e.activation(ws_row[0:1, :], ld_row[0:1, :], AF.Exp,
                                              scale=float(CH)), s_a)
        for qq in range(4):
            p.op("scalar", lambda e, qq=qq: e.activation(
                coefrow[0:1, qq * H:(qq + 1) * H], ld_row[0:1, :], AF.Exp,
                scale=coefk_sb[0:1, qq:qq + 1]), s_a)
        a_coef = s_a.n
        p.wait("vector", s_a, a_coef)
        for qq in range(4):
            p.op("vector", lambda e, qq=qq: e.tensor_scalar_mul(
                coefrow[0:1, qq * H:(qq + 1) * H], coefrow[0:1, qq * H:(qq + 1) * H],
                mask_sb[0:1, qq:qq + 1]), s_v)
        v_coef = s_v.n
        p.wait("sync", s_a, a_coef)
        p.dma("sync", s_kd, sc2[0:1, 0:H], ws_row[0:1, :])
        p.wait("sync", s_v, v_coef)
        p.dma("sync", s_kd, sc2[0:1, 16:16 + 4 * H], coefrow[0:1, :])
        kd0 = s_kd.n
        p.wait("gpsimd", s_kd, kd0)
        p.dma("gpsimd", s_cast, ws_bc[:, :], sc2[0:1, 0:H].partition_broadcast(128))
        p.dma("gpsimd", s_cast, coef_bc[:, :], sc2[0:1, 16:16 + 4 * H].partition_broadcast(128))
        cast_bc = s_cast.n

        p.op("gpsimd", lambda e: e.memset(ones64[:, :], 1.0), s_g)
        p.op("gpsimd", lambda e: e.iota(iota8[:, :], pattern=[[1, 8]], base=0,
                                        channel_multiplier=0,
                                        allow_small_or_imprecise_dtypes=True), s_g)
        p.op("gpsimd", lambda e: e.memset(ones_col[:, :], 1.0), s_g)
        g_0 = s_g.n

        p.wait("vector", s_cast, cast_bc)
        p.wait("vector", s_g, g_0)
        for hp in range(8):
            for half in range(2):
                rows = slice(half * 64, half * 64 + 64)
                colh = 2 * hp + half
                p.op("vector", lambda e, rows=rows, hp=hp, colh=colh: e.tensor_scalar_mul(
                    ws_tile[rows, hp * 64:(hp + 1) * 64], ones64[:, :],
                    ws_bc[rows, colh:colh + 1]), s_v)
        p.op("vector", lambda e: e.memset(s_e[:, :], 0.0), s_v)
        p.wait("vector", s_ld, ld0)
        p.wait("vector", s_v, s_v.n)
        for qq in range(4):
            for hp in range(8):
                for half in range(2):
                    rows = slice(half * 64, half * 64 + 64)
                    cols = slice(hp * 64, (hp + 1) * 64)
                    colh = qq * H + 2 * hp + half
                    if qq > 0:
                        p.wait("vector", s_v, s_v.n)
                    p.op("vector", lambda e, qq=qq, rows=rows, cols=cols, colh=colh:
                         e.scalar_tensor_tensor(s_e[rows, cols], s_tiles[qq][rows, cols],
                                                coef_bc[rows, colh:colh + 1], s_e[rows, cols],
                                                ALU.mult, ALU.add), s_v)
        p.wait("vector", s_v, s_v.n)
        p.op("vector", lambda e: e.tensor_copy(s_ebf[:, :], s_e[:, :]), s_v)
        v_se = s_v.n

        p.wait("scalar", s_cast, cast_w)
        for ib in range(8):
            p.dma("scalar", s_tr, woT[:, ib * C:(ib + 1) * C],
                  wo_bf[0:C, ib * 128:(ib + 1) * 128], transpose=True)
        for ib in range(8):
            p.dma("scalar", s_tr, gwT[:, ib * 16:(ib + 1) * 16],
                  gw_bf[0:16, ib * 128:(ib + 1) * 128], transpose=True)
        tr_w = s_tr.n

        for j in range(NCH):
            s_slab = j // 4
            cnn = j % 4
            p.wait("tensor", s_v, v_se)
            p.wait("tensor", s_ld, ld0)
            p.wait("tensor", s_v, s_v.n)
            for h in range(H):
                rows, pb = _head(h)
                rsl = slice(pb * TOK + j * CH, pb * TOK + (j + 1) * CH)
                srows = slice((h % 2) * KD, (h % 2) * KD + KD)
                scols = slice((h // 2) * KD, (h // 2 + 1) * KD)
                p.mm(ps_corr[:, h * KD:(h + 1) * KD], rwb[rows, rsl], s_ebf[srows, scols],
                     start=True, stop=True, inc=s_mm)
            mm_c = s_mm.n
            p.wait("vector", s_mm, mm_c)
            p.op("vector", lambda e, j=j: e.scalar_tensor_tensor(
                y_t[:, :], ps_corr[:, :], 1.0 / HEAD_DIV, xo_res[:, j * C:(j + 1) * C],
                ALU.mult, ALU.add), s_v)
            v_y = s_v.n
            p.wait("vector", s_v, v_y)
            p.op("vector", lambda e: e.tensor_mul(s_e[:, :], s_e[:, :], ws_tile[:, :]), s_v)
            p.wait("vector", s_v, s_v.n)
            p.op("vector", lambda e: e.tensor_copy(s_ebf[:, :], s_e[:, :]), s_v)
            v_se = s_v.n

            p.wait("vector", s_v, v_y)
            p.op("vector", lambda e: e.tensor_reduce(
                gn16[:, :], y_t.rearrange("p (h k) -> p h k", k=KD), axis=AX.X, op=ALU.add), s_v)
            p.wait("vector", s_v, s_v.n)
            p.op("vector", lambda e: e.tensor_scalar_mul(gn16[:, :], gn16[:, :], -1.0 / KD), s_v)
            v_nm = s_v.n
            p.wait("vector", s_v, v_nm)
            for h in range(H):
                sl = slice(h * KD, (h + 1) * KD)
                p.op("vector", lambda e, sl=sl, h=h: e.tensor_scalar_add(
                    y_t[:, sl], y_t[:, sl], gn16[:, h:h + 1]), s_v)
            v_yc = s_v.n
            p.wait("vector", s_v, v_yc)
            p.op("vector", lambda e: e.tensor_mul(sq_t[:, :], y_t[:, :], y_t[:, :]), s_v)
            p.wait("vector", s_v, s_v.n)
            p.op("vector", lambda e: e.tensor_reduce(
                gn16b[:, :], sq_t.rearrange("p (h k) -> p h k", k=KD), axis=AX.X, op=ALU.add), s_v)
            p.wait("vector", s_v, s_v.n)
            p.op("vector", lambda e: e.tensor_scalar(
                gn16b[:, :], gn16b[:, :], 1.0 / KD, EPS_GN, ALU.mult, ALU.add), s_v)
            p.wait("vector", s_v, s_v.n)
            p.op("vector", lambda e: e.reciprocal(gn16b[:, :], gn16b[:, :]), s_v)
            v_rcp = s_v.n
            p.wait("scalar", s_v, v_rcp)
            p.op("scalar", lambda e: e.activation(rstd16[:, :], gn16b[:, :], AF.Sqrt), s_a)
            a_rstd = s_a.n
            p.wait("vector", s_a, a_rstd)
            for h in range(H):
                sl = slice(h * KD, (h + 1) * KD)
                p.op("vector", lambda e, sl=sl, h=h: e.tensor_scalar_mul(
                    y_t[:, sl], y_t[:, sl], rstd16[:, h:h + 1]), s_v)
            p.wait("vector", s_v, s_v.n)
            p.wait("vector", s_cast, cast_ln)
            p.op("vector", lambda e: e.tensor_mul(y_t[:, :], y_t[:, :], lnw_bc[:, :]), s_v)
            p.wait("vector", s_v, s_v.n)
            p.op("vector", lambda e: e.tensor_add(y_t[:, :], y_t[:, :], lnb_bc[:, :]), s_v)
            p.wait("vector", s_v, s_v.n)
            p.op("vector", lambda e, j=j, cnn=cnn: e.tensor_mul(
                yg_sl[:, cnn * C:(cnn + 1) * C], y_t[:, :], g_res[:, j * C:(j + 1) * C]), s_v)
            v_yg = s_v.n

            if cnn == 3:
                t0 = s_slab * SLAB
                p.wait("sync", s_v, v_yg)
                p.dma("sync", s_kd,
                      yg_dram[t0:t0 + SLAB, :].rearrange("(tb p) c -> p tb c", p=128),
                      yg_sl.rearrange("a (tb c) -> a tb c", c=C))
                kw1 = s_kd.n
                p.wait("scalar", s_kd, kw1)
                p.wait("scalar", s_mm, s_mm.n)
                for pb in range(8):
                    cs = slice(pb * 128, (pb + 1) * 128)
                    p.dma("scalar", s_tr, ygc[:, pb * SLAB:(pb + 1) * SLAB],
                          yg_dram[t0:t0 + SLAB, cs], transpose=True)
                tr_yg = s_tr.n
                p.wait("sync", s_v, s_v.n)
                p.dma("sync", s_ld,
                      x_sl.rearrange("p (tb c) -> p tb c", c=C),
                      x_in[t0:t0 + SLAB, :].rearrange("(tb p) c -> p tb c", p=128))
                x_ld = s_ld.n

                p.wait("vector", s_kd, s_kd.n)
                p.wait("vector", s_o, s_o.n)
                for tb in range(4):
                    for oh in range(2):
                        bank = ps_proj[(tb * 2 + oh) % 2]
                        p.wait("tensor", s_v, s_v.n)
                        p.wait("tensor", s_tr, tr_yg)
                        for ib in range(8):
                            p.mm(bank[:, :],
                                 ygc[:, ib * SLAB + tb * 128: ib * SLAB + (tb + 1) * 128],
                                 woT[:, ib * C + oh * 512: ib * C + (oh + 1) * 512],
                                 start=(ib == 0), stop=(ib == 7),
                                 inc=(s_mm if ib == 7 else None))
                        p.wait("vector", s_mm, s_mm.n)
                        p.wait("vector", s_ld, x_ld)
                        p.wait("vector", s_o, s_o.n)
                        sl = slice(tb * C + oh * 512, tb * C + (oh + 1) * 512)
                        osl = slice(oh * 512, (oh + 1) * 512)
                        p.op("vector", lambda e, bank=bank, sl=sl, osl=osl: e.tensor_add(
                            xm_t[:, osl], x_sl[:, sl], bank[:, :]), s_v)
                    v_r1 = s_v.n
                    p.wait("vector", s_v, v_r1)
                    p.op("vector", lambda e: e.tensor_reduce(
                        col1[:, 0:1], xm_t[:, :], axis=AX.X, op=ALU.add), s_v)
                    p.wait("vector", s_v, s_v.n)
                    p.op("vector", lambda e: e.tensor_scalar_mul(col1[:, 0:1], col1[:, 0:1], -1.0 / C), s_v)
                    p.wait("vector", s_v, s_v.n)
                    p.op("vector", lambda e: e.tensor_scalar_add(xm_t[:, :], xm_t[:, :], col1[:, 0:1]), s_v)
                    p.wait("vector", s_v, s_v.n)
                    p.op("vector", lambda e: e.tensor_mul(sq_t[:, :], xm_t[:, :], xm_t[:, :]), s_v)
                    p.wait("vector", s_v, s_v.n)
                    p.op("vector", lambda e: e.tensor_reduce(
                        col2[:, 0:1], sq_t[:, :], axis=AX.X, op=ALU.add), s_v)
                    p.wait("vector", s_v, s_v.n)
                    p.op("vector", lambda e: e.tensor_scalar(
                        col2[:, 0:1], col2[:, 0:1], 1.0 / C, EPS_LN, ALU.mult, ALU.add), s_v)
                    p.wait("vector", s_v, s_v.n)
                    p.op("vector", lambda e: e.reciprocal(col2[:, 0:1], col2[:, 0:1]), s_v)
                    v_l1 = s_v.n
                    p.wait("scalar", s_v, v_l1)
                    p.op("scalar", lambda e: e.activation(col2[:, 0:1], col2[:, 0:1], AF.Sqrt), s_a)
                    a_l1 = s_a.n
                    p.wait("vector", s_a, a_l1)
                    p.op("vector", lambda e: e.tensor_scalar_mul(xm_t[:, :], xm_t[:, :], col2[:, 0:1]), s_v)
                    p.wait("vector", s_v, s_v.n)
                    p.op("vector", lambda e, tb=tb: e.tensor_copy(
                        xmb_sl[:, tb * C:(tb + 1) * C], xm_t[:, :]), s_v)
                    v_xm = s_v.n
                    p.wait("sync", s_v, v_xm)
                    p.dma("sync", s_o, o_xmid[t0 + tb * 128:t0 + (tb + 1) * 128, :], xm_t[:, :])
                p.wait("sync", s_v, s_v.n)
                p.dma("sync", s_kd,
                      xmb_dram[t0:t0 + SLAB, :].rearrange("(tb p) c -> p tb c", p=128),
                      xmb_sl.rearrange("a (tb c) -> a tb c", c=C))
                p.wait("scalar", s_kd, s_kd.n)
                p.wait("scalar", s_v, v_yg)
                for tj in range(4):
                    ti = s_slab * 4 + tj
                    for ib in range(8):
                        cs = slice(ib * 128, (ib + 1) * 128)
                        p.dma("scalar", s_tr, xmc_all[:, ti * C + ib * 128:ti * C + (ib + 1) * 128],
                              xmb_dram[ti * CH:(ti + 1) * CH, cs], transpose=True)

        tr_xm_all = s_tr.n

        p.op("gpsimd", lambda e: e.memset(mw[:, :], 0.0), s_g)
        g_mw = s_g.n
        for ti in range(NCH):
            t0 = ti * CH
            p.wait("tensor", s_tr, tr_xm_all)
            p.wait("tensor", s_tr, tr_w)
            p.wait("tensor", s_v, s_v.n)
            for ib in range(8):
                p.mm(ps_g[:, :], xmc_all[:, ti * C + ib * 128:ti * C + (ib + 1) * 128],
                     gwT[:, ib * 16:ib * 16 + 8],
                     start=(ib == 0), stop=(ib == 7), inc=(s_mm if ib == 7 else None))
            p.wait("vector", s_mm, s_mm.n)
            p.wait("vector", s_cast, cast_ln)
            p.op("vector", lambda e: e.tensor_add(lg[:, :], ps_g[:, :], lb_bc[:, :]), s_v)
            v_lg = s_v.n
            p.wait("scalar", s_v, v_lg)
            p.op("scalar", lambda e: e.activation(scores[:, :], lg[:, :], AF.Sigmoid), s_a)
            a_sc = s_a.n
            p.wait("vector", s_a, a_sc)
            p.op("vector", lambda e: e.max(mx8[:, :], scores[:, :]), s_v)
            p.wait("vector", s_v, s_v.n)
            p.op("vector", lambda e: e.max_index(mi8[:, :], mx8[:, :], scores[:, :]), s_v)
            p.wait("vector", s_v, s_v.n)
            p.op("vector", lambda e: e.tensor_copy(mif[:, :], mi8[:, :]), s_v)
            p.wait("vector", s_v, s_v.n)
            p.op("vector", lambda e: e.scalar_tensor_tensor(
                den[:, 0:1], mx8[:, 0:1], 1e-8, mx8[:, 1:2], ALU.add, ALU.add), s_v)
            p.wait("vector", s_v, s_v.n)
            p.op("vector", lambda e: e.reciprocal(den[:, 0:1], den[:, 0:1]), s_v)
            p.wait("vector", s_v, s_v.n)
            p.op("vector", lambda e: e.tensor_mul(w1c[:, 0:1], mx8[:, 0:1], den[:, 0:1]), s_v)
            p.op("vector", lambda e: e.tensor_mul(w2c[:, 0:1], mx8[:, 1:2], den[:, 0:1]), s_v)
            p.wait("vector", s_v, s_v.n)
            p.wait("vector", s_g, g_mw)
            p.op("vector", lambda e: e.tensor_scalar(
                eq1[:, :], iota8[:, :], mif[:, 0:1], None, ALU.is_equal), s_v)
            p.op("vector", lambda e: e.tensor_scalar(
                eq2[:, :], iota8[:, :], mif[:, 1:2], None, ALU.is_equal), s_v)
            p.wait("vector", s_v, s_v.n)
            p.wait("vector", s_mm, s_mm.n)
            p.wait("vector", s_o, s_o.n)
            p.op("vector", lambda e: e.tensor_add(mw[:, 0:8], eq1[:, :], eq2[:, :]), s_v)
            p.wait("vector", s_v, s_v.n)
            p.op("vector", lambda e: e.tensor_scalar_mul(mw[:, 8:16], eq1[:, :], w1c[:, 0:1]), s_v)
            p.wait("vector", s_v, s_v.n)
            p.op("vector", lambda e: e.scalar_tensor_tensor(
                mw[:, 8:16], eq2[:, :], w2c[:, 0:1], mw[:, 8:16], ALU.mult, ALU.add), s_v)
            v_mw = s_v.n
            p.wait("tensor", s_v, v_mw)
            p.wait("tensor", s_g, g_0)
            p.mm(ps_st[0:1, :], ones_col[:, 0:1], mw[:, :],
                 start=(ti == 0), stop=(ti == NCH - 1), inc=s_mm)
            p.wait("sync", s_v, v_mw)
            p.dma("sync", s_o, o_m[t0:t0 + CH, :], mw[:, 0:16])
        p.wait("scalar", s_mm, s_mm.n)
        p.op("scalar", lambda e: e.copy(st_sb[0:1, :], ps_st[0:1, :]), s_a)
        p.wait("sync", s_a, s_a.n)
        p.dma("sync", s_o, o_stats, st_sb[0:1, :])
        p.wait("sync", s_o, s_o.n)
        p.build_block()
    return nc



def build_l3():
    nc = bass.Bass("TRN2", target_bir_lowering=False)
    xm_in = nc.dram_tensor("xm", [TOK, C], F32, kind="ExternalInput").ap()
    m_in = nc.dram_tensor("m", [TOK, 128], F32, kind="ExternalInput").ap()
    stats_in = nc.dram_tensor("stats", [8, 16], F32, kind="ExternalInput").ap()
    eg_in = nc.dram_tensor("eg", [NEXP, EI, C], F32, kind="ExternalInput").ap()
    eu_in = nc.dram_tensor("eu", [NEXP, EI, C], F32, kind="ExternalInput").ap()
    ed_in = nc.dram_tensor("ed", [NEXP, C, EI], F32, kind="ExternalInput").ap()
    sg_in = nc.dram_tensor("sg", [SI, C], F32, kind="ExternalInput").ap()
    su_in = nc.dram_tensor("su", [SI, C], F32, kind="ExternalInput").ap()
    sd_in = nc.dram_tensor("sd", [C, SI], F32, kind="ExternalInput").ap()

    o_out = nc.dram_tensor("o_out", [TOK, C], F32, kind="ExternalOutput").ap()

    xm_bf = nc.dram_tensor("xm_bf", [TOK, C], BF16, kind="Internal").ap()
    m_bf = nc.dram_tensor("m_bf", [TOK, 128], BF16, kind="Internal").ap()
    eg_bf = nc.dram_tensor("eg_bf", [NEXP, EI, C], BF16, kind="Internal").ap()
    eu_bf = nc.dram_tensor("eu_bf", [NEXP, EI, C], BF16, kind="Internal").ap()
    ed_bf = nc.dram_tensor("ed_bf", [NEXP, C, EI], BF16, kind="Internal").ap()
    sg_bf = nc.dram_tensor("sg_bf", [SI, C], BF16, kind="Internal").ap()
    su_bf = nc.dram_tensor("su_bf", [SI, C], BF16, kind="Internal").ap()
    sd_bf = nc.dram_tensor("sd_bf", [C, SI], BF16, kind="Internal").ap()
    selwm_dram = nc.dram_tensor("selwm_dram", [16, TOK], BF16, kind="Internal").ap()
    guT_dram = nc.dram_tensor("guT_dram", [2 * (NEXP + 1), 128, 8 * C], BF16, kind="Internal").ap()
    wdT_dram = nc.dram_tensor("wdT_dram", [NEXP + 1, 128, 8 * C], BF16, kind="Internal").ap()
    sc3 = nc.dram_tensor("sc3", [1, 16], F32, kind="Internal").ap()

    with ExitStack() as ctx:
        p = Prog(nc, ctx)
        sb = lambda n, s, d=F32: ctx.enter_context(nc.sbuf_tensor(n, s, d))
        psl = lambda n, s: ctx.enter_context(nc.psum_tensor(n, s, F32))

        xmc = sb("xmc", [128, 8 * TOK], BF16)
        he_buf = [sb("he%d" % e, [128, 4 * SLAB], BF16) for e in range(NEXP)]
        hsh = sb("hsh", [128, 8 * SLAB], BF16)
        wg_buf = [sb("wg_buf0", [128, 8 * 512], BF16), sb("wg_buf1", [128, 8 * 512], BF16),
                  sb("wg_bufS", [128, 8 * C], BF16)]
        wu_buf = [sb("wu_buf0", [128, 8 * 512], BF16), sb("wu_buf1", [128, 8 * 512], BF16),
                  sb("wu_bufS", [128, 8 * C], BF16)]
        wd_buf = [sb("wd_buf0", [128, 4 * C], BF16), sb("wd_buf1", [128, 4 * C], BF16),
                  sb("wd_bufS", [128, 8 * C], BF16)]
        a_sl = sb("a_sl", [128, 8 * SLAB], BF16)
        u_sl = sb("u_sl", [128, 8 * SLAB], BF16)
        selwm_bc = sb("selwm_bc", [128, SLAB], BF16)
        xm_sl = sb("xm_sl", [128, 4 * C], F32)
        r2_t = sb("r2_t", [128, C], F32)
        col1 = sb("col1", [128, 1], F32)
        col2 = sb("col2", [128, 1], F32)
        st_in = sb("st_in", [8, 16], F32)
        st_sb = sb("st_sb", [1, 16], F32)
        wm_row = sb("wm_row", [1, 8], F32)
        wmcol = sb("wmcol", [16, 1], F32)
        ones8 = sb("ones8", [8, 1], F32)

        ps_dn = [psl("ps_dn%d" % i, [128, C]) for i in range(4)]

        s_cast = p.sem("s_cast")
        s_cex = [p.sem("s_cex%d" % i) for i in range(NEXP + 1)]
        s_csel = p.sem("s_csel")
        s_gu = [p.sem("s_gu%d" % i) for i in range(3)]
        s_wd = [p.sem("s_wd%d" % i) for i in range(3)]
        s_lds = p.sem("s_lds")
        s_ld = p.sem("s_ld")
        s_trm = p.sem("s_trm")
        s_trx = p.sem("s_trx")
        s_tr = p.sem("s_tr")
        s_trd = p.sem("s_trd")
        s_kd = p.sem("s_kd")
        s_mm = p.sem("s_mm")
        s_v = p.sem("s_v")
        s_a = p.sem("s_a")
        s_g = p.sem("s_g")
        s_o = p.sem("s_o")

        p.dma("gpsimd", s_cast, xm_bf, xm_in)
        p.dma("gpsimd", s_cast, m_bf, m_in)
        cast_xm = s_cast.n
        for e_ in range(NEXP):
            p.dma("gpsimd", s_cex[e_], eg_bf[e_], eg_in[e_])
            p.dma("gpsimd", s_cex[e_], eu_bf[e_], eu_in[e_])
            p.dma("gpsimd", s_cex[e_], ed_bf[e_], ed_in[e_])
        p.dma("gpsimd", s_cex[NEXP], sg_bf, sg_in)
        p.dma("gpsimd", s_cex[NEXP], su_bf, su_in)
        p.dma("gpsimd", s_cex[NEXP], sd_bf, sd_in)

        p.dma("sync", s_lds, st_in[:, :], stats_in)
        ld_st = s_lds.n
        p.op("gpsimd", lambda e: e.memset(ones8[:, :], 1.0), s_g)
        p.op("gpsimd", lambda e: e.memset(wmcol[:, :], 0.0), s_g)
        g_0 = s_g.n

        p.wait("tensor", s_lds, ld_st)
        p.wait("tensor", s_g, g_0)
        p.mm(ps_dn[3][0:1, 0:16], ones8[:, 0:1], st_in[:, :], start=True, stop=True, inc=s_mm)
        p.wait("scalar", s_mm, s_mm.n)
        p.op("scalar", lambda e: e.copy(st_sb[0:1, :], ps_dn[3][0:1, 0:16]), s_a)
        a_st = s_a.n
        p.wait("vector", s_a, a_st)
        p.op("vector", lambda e: e.tensor_scalar_max(st_sb[0:1, 0:8], st_sb[0:1, 0:8], 1.0), s_v)
        p.wait("vector", s_v, s_v.n)
        p.op("vector", lambda e: e.reciprocal(st_sb[0:1, 0:8], st_sb[0:1, 0:8]), s_v)
        p.wait("vector", s_v, s_v.n)
        p.op("vector", lambda e: e.tensor_mul(wm_row[0:1, :], st_sb[0:1, 8:16], st_sb[0:1, 0:8]), s_v)
        v_wm = s_v.n
        p.wait("sync", s_v, v_wm)
        p.dma("sync", s_kd, sc3[0:1, 0:8], wm_row[0:1, :])
        kd_wm = s_kd.n
        p.wait("sync", s_kd, kd_wm)
        p.wait("sync", s_g, g_0)
        p.dma("sync", s_kd, wmcol[0:8, 0:1], sc3[0:1, 0:8].rearrange("a b -> b a"),
              allow_slow_non_contiguous=True)
        kd_wc = s_kd.n

        mTv = a_sl.rearrange("p (a b) -> p (a b)") if False else a_sl
        p.wait("scalar", s_cast, cast_xm)
        p.dma("scalar", s_trm, mTv[:, 0:TOK], m_bf[0:TOK, 0:128], transpose=True)
        tr_m = s_trm.n
        p.wait("vector", s_trm, tr_m)
        p.wait("vector", s_kd, kd_wc)
        p.op("vector", lambda e: e.tensor_scalar_mul(mTv[0:16, 0:TOK], mTv[0:16, 0:TOK], wmcol[0:16, 0:1]), s_v)
        v_swm = s_v.n
        p.wait("sync", s_v, v_swm)
        p.dma("sync", s_kd, selwm_dram, mTv[0:16, 0:TOK])
        kd_swm = s_kd.n

        for pb in range(8):
            cs = slice(pb * 128, (pb + 1) * 128)
            p.dma("scalar", s_trx, xmc[:, pb * TOK:(pb + 1) * TOK],
                  xm_bf[0:TOK, cs], transpose=True)
        tr_xm = s_trx.n

        gu_used = [0, 0, 0]
        gu_tr = [0, 0, 0]
        wd_used = [0, 0, 0]
        wd_tr = [0, 0, 0]
        NE = NEXP + 1

        def _bi(ex_abs):
            return 2 if (ex_abs % NE) == NEXP else (ex_abs % 2)

        def load_gu(ex_abs):
            ex = ex_abs % NE
            tau = ex_abs // NE
            bi = _bi(ex_abs)
            shared = (ex == NEXP)
            nob = 8 if shared else 4
            ncols = nob * 128 * 8
            if tau == 0:
                p.wait("scalar", s_cex[ex], s_cex[ex].n)
            p.wait("scalar", s_mm, gu_used[bi])
            if tau == 0:
                p.wait("scalar", s_kd, s_kd.n)
                g_src = sg_bf if shared else eg_bf[ex]
                u_src = su_bf if shared else eu_bf[ex]
                for ib in range(8):
                    p.dma("scalar", s_gu[bi], wg_buf[bi][:, ib * (nob * 128):(ib + 1) * (nob * 128)],
                          g_src[0:nob * 128, ib * 128:(ib + 1) * 128], transpose=True)
                    p.dma("scalar", s_gu[bi], wu_buf[bi][:, ib * (nob * 128):(ib + 1) * (nob * 128)],
                          u_src[0:nob * 128, ib * 128:(ib + 1) * 128], transpose=True)
                gu_tr[bi] = s_gu[bi].n
                p.wait("sync", s_gu[bi], s_gu[bi].n)
                p.dma("sync", s_kd, guT_dram[2 * ex, 0:128, 0:ncols], wg_buf[bi][:, 0:ncols])
                p.dma("sync", s_kd, guT_dram[2 * ex + 1, 0:128, 0:ncols], wu_buf[bi][:, 0:ncols])
            else:
                p.wait("scalar", s_kd, s_kd.n)
                p.dma("scalar", s_gu[bi], wg_buf[bi][:, 0:ncols], guT_dram[2 * ex, 0:128, 0:ncols])
                p.dma("scalar", s_gu[bi], wu_buf[bi][:, 0:ncols], guT_dram[2 * ex + 1, 0:128, 0:ncols])
                gu_tr[bi] = s_gu[bi].n

        def load_wd(ex_abs):
            ex = ex_abs % NE
            tau = ex_abs // NE
            bi = _bi(ex_abs)
            shared = (ex == NEXP)
            nib = 8 if shared else 4
            ncols = nib * C
            if tau == 0:
                p.wait("scalar", s_cex[ex], s_cex[ex].n)
            p.wait("scalar", s_mm, wd_used[bi])
            if tau == 0:
                p.wait("scalar", s_kd, s_kd.n)
                d_src = sd_bf if shared else ed_bf[ex]
                for ib in range(nib):
                    p.dma("scalar", s_wd[bi], wd_buf[bi][:, ib * C:(ib + 1) * C],
                          d_src[0:C, ib * 128:(ib + 1) * 128], transpose=True)
                wd_tr[bi] = s_wd[bi].n
                p.wait("sync", s_wd[bi], s_wd[bi].n)
                p.dma("sync", s_kd, wdT_dram[ex, 0:128, 0:ncols], wd_buf[bi][:, 0:ncols])
            else:
                p.wait("scalar", s_kd, s_kd.n)
                p.dma("scalar", s_wd[bi], wd_buf[bi][:, 0:ncols], wdT_dram[ex, 0:128, 0:ncols])
                wd_tr[bi] = s_wd[bi].n

        d1_evac = [[0, 0], [0, 0]]
        psum01_read = 0
        load_gu(0)
        load_gu(1)
        p.wait("scalar", s_kd, s_kd.n)
        for tau in range(4):
            t0 = tau * SLAB
            p.wait("sync", s_v, s_v.n)
            p.dma("sync", s_ld,
                  xm_sl.rearrange("p (tb c) -> p tb c", c=C),
                  xm_in[t0:t0 + SLAB, :].rearrange("(tb p) c -> p tb c", p=128))
            x_ld = s_ld.n

            p.wait("tensor", s_v, psum01_read)
            for ex in range(NE):
                h_prev = s_v.n
                ex_abs = tau * NE + ex
                bi = _bi(ex_abs)
                shared = (ex == NEXP)
                nob = 8 if shared else 4
                if not shared:
                    p.wait("gpsimd", s_kd, s_kd.n)
                    p.wait("gpsimd", s_v, s_v.n)
                    p.dma("gpsimd", s_csel, selwm_bc[:, :],
                          selwm_dram[ex:ex + 1, t0:t0 + SLAB].partition_broadcast(128))
                sel_cast = s_csel.n

                for ob in range(nob):
                    bk = ob % 2
                    bank = ps_dn[bk][:, 0:512]
                    p.wait("tensor", s_a, d1_evac[bk][0])
                    p.wait("tensor", s_gu[bi], gu_tr[bi])
                    p.wait("tensor", s_trx, tr_xm)
                    for ib in range(8):
                        p.mm(bank,
                             wg_buf[bi][:, ib * (nob * 128) + ob * 128: ib * (nob * 128) + (ob + 1) * 128],
                             xmc[:, ib * TOK + t0: ib * TOK + t0 + SLAB],
                             start=(ib == 0), stop=(ib == 7),
                             inc=(s_mm if ib == 7 else None))
                    p.wait("scalar", s_mm, s_mm.n)
                    p.wait("scalar", s_v, h_prev)
                    p.op("scalar", lambda e, bank=bank, ob=ob: e.activation(
                        a_sl[:, ob * SLAB:(ob + 1) * SLAB], bank, AF.Silu), s_a)
                    d1_evac[bk][0] = s_a.n
                    bank2 = ps_dn[bk][:, 512:1024]
                    p.wait("tensor", s_a, d1_evac[bk][1])
                    for ib in range(8):
                        p.mm(bank2,
                             wu_buf[bi][:, ib * (nob * 128) + ob * 128: ib * (nob * 128) + (ob + 1) * 128],
                             xmc[:, ib * TOK + t0: ib * TOK + t0 + SLAB],
                             start=(ib == 0), stop=(ib == 7),
                             inc=(s_mm if ib == 7 else None))
                    p.wait("scalar", s_mm, s_mm.n)
                    p.wait("scalar", s_v, h_prev)
                    p.op("scalar", lambda e, bank2=bank2, ob=ob: e.copy(
                        u_sl[:, ob * SLAB:(ob + 1) * SLAB], bank2), s_a)
                    d1_evac[bk][1] = s_a.n
                gu_used[bi] = s_mm.n
                a_gu = s_a.n
                if ex_abs + 2 < 4 * NE:
                    load_gu(ex_abs + 2)
                if tau == 0 and ex == NE - 2:
                    load_wd(0)
                if tau == 0 and ex == NE - 1:
                    load_wd(1)
                p.wait("vector", s_a, a_gu)
                p.wait("vector", s_csel, sel_cast)
                dst = hsh if shared else he_buf[ex]
                for ob in range(nob):
                    sl = slice(ob * SLAB, (ob + 1) * SLAB)
                    p.op("vector", lambda e, sl=sl, dst=dst: e.tensor_mul(
                        dst[:, sl], a_sl[:, sl], u_sl[:, sl]), s_v)
                p.wait("vector", s_v, s_v.n)
                if not shared:
                    for ob in range(nob):
                        sl = slice(ob * SLAB, (ob + 1) * SLAB)
                        p.op("vector", lambda e, sl=sl, dst=dst: e.tensor_mul(
                            dst[:, sl], dst[:, sl], selwm_bc[:, :]), s_v)
            v_h = s_v.n

            for ex in range(NE):
                ex_abs = tau * NE + ex
                bi = _bi(ex_abs)
                shared = (ex == NEXP)
                nib = 8 if shared else 4
                buf = wd_buf[bi]
                p.wait("tensor", s_wd[bi], wd_tr[bi])
                p.wait("tensor", s_v, v_h)
                if ex == 0:
                    p.wait("tensor", s_a, s_a.n)
                hsrc = hsh if shared else he_buf[ex]
                for tb in range(4):
                    for oh in range(2):
                        for ib in range(nib):
                            last_of_ex = (tb == 3 and oh == 1 and ib == nib - 1)
                            p.mm(ps_dn[tb][:, oh * 512:(oh + 1) * 512],
                                 hsrc[:, ib * SLAB + tb * 128: ib * SLAB + (tb + 1) * 128],
                                 buf[:, ib * C + oh * 512: ib * C + (oh + 1) * 512],
                                 start=(ex == 0 and ib == 0),
                                 stop=(ex == NEXP and ib == nib - 1),
                                 inc=(s_mm if last_of_ex else None))
                wd_used[bi] = s_mm.n
                if ex_abs + 2 < 4 * NE:
                    load_wd(ex_abs + 2)
            mm_dn = s_mm.n

            for tb in range(4):
                p.wait("vector", s_mm, mm_dn)
                p.wait("vector", s_ld, x_ld)
                p.op("vector", lambda e, tb=tb: e.tensor_add(
                    r2_t[:, :], xm_sl[:, tb * C:(tb + 1) * C], ps_dn[tb][:, :]), s_v)
                p.wait("vector", s_v, s_v.n)
                p.op("vector", lambda e: e.tensor_reduce(
                    col1[:, 0:1], r2_t[:, :], axis=AX.X, op=ALU.add), s_v)
                p.wait("vector", s_v, s_v.n)
                p.op("vector", lambda e: e.tensor_scalar_mul(col1[:, 0:1], col1[:, 0:1], -1.0 / C), s_v)
                p.wait("vector", s_v, s_v.n)
                p.op("vector", lambda e: e.tensor_scalar_add(r2_t[:, :], r2_t[:, :], col1[:, 0:1]), s_v)
                p.wait("scalar", s_v, s_v.n)
                p.op("scalar", lambda e, tb=tb: e.activation(
                    xm_sl[:, tb * C:(tb + 1) * C], r2_t[:, :], AF.Square,
                    accum_out=col2[:, 0:1]), s_a)
                p.wait("vector", s_a, s_a.n)
                p.op("vector", lambda e: e.tensor_scalar(
                    col2[:, 0:1], col2[:, 0:1], 1.0 / C, EPS_LN, ALU.mult, ALU.add), s_v)
                p.wait("vector", s_v, s_v.n)
                p.op("vector", lambda e: e.reciprocal(col2[:, 0:1], col2[:, 0:1]), s_v)
                v_l2 = s_v.n
                p.wait("scalar", s_v, v_l2)
                p.op("scalar", lambda e: e.activation(col2[:, 0:1], col2[:, 0:1], AF.Sqrt), s_a)
                a_l2 = s_a.n
                p.wait("vector", s_a, a_l2)
                p.op("vector", lambda e: e.tensor_scalar_mul(r2_t[:, :], r2_t[:, :], col2[:, 0:1]), s_v)
                v_out = s_v.n
                p.wait("sync", s_v, v_out)
                p.dma("sync", s_o, o_out[t0 + tb * 128:t0 + (tb + 1) * 128, :], r2_t[:, :])
                p.wait("vector", s_o, s_o.n)
                if tb == 1:
                    psum01_read = s_v.n

        p.wait("sync", s_o, s_o.n)
        p.build_block()
    return nc



def kernel(**inputs):
    x = np.ascontiguousarray(np.asarray(inputs["L0_attn_xq"], np.float32))
    tm = np.ascontiguousarray(np.stack([
        np.asarray(inputs["L0_attn_time_mix_r"], np.float32).reshape(C),
        np.asarray(inputs["L0_attn_time_mix_k"], np.float32).reshape(C),
        np.asarray(inputs["L0_attn_time_mix_v"], np.float32).reshape(C),
        np.asarray(inputs["L0_attn_time_mix_g"], np.float32).reshape(C),
    ]))
    td = np.ascontiguousarray(np.asarray(inputs["L0_attn_time_decay"], np.float32).reshape(1, H))
    fa = np.ascontiguousarray(np.asarray(inputs["L0_attn_time_faaaa"], np.float32).reshape(1, H))
    f32c = lambda k: np.ascontiguousarray(np.asarray(inputs[k], np.float32))

    nc1 = build_l1()
    maps1 = []
    for c in range(NCORES):
        b, q = c // 4, c % 4
        t0 = q * TOK
        xp = np.zeros((TOK + 1, C), np.float32)
        xp[1:] = x[b, t0:t0 + TOK]
        if q > 0:
            xp[0] = x[b, t0 - 1]
        maps1.append({
            "xp": xp,
            "wr": f32c("L0_attn_W_r"), "wk": f32c("L0_attn_W_k"),
            "wv": f32c("L0_attn_W_v"), "wg": f32c("L0_attn_W_g"),
            "tm": tm, "td": td, "fa": fa,
        })
    res1 = run_bass_kernel_spmd(nc1, maps1, core_ids=list(range(NCORES)))
    r1 = res1.results

    nc2 = build_l2()
    maps2 = []
    for c in range(NCORES):
        b, q = c // 4, c % 4
        t0 = q * TOK
        sall = np.concatenate([np.asarray(r1[b * 4 + qq]["o_sc"], np.float32)
                               for qq in range(4)], axis=0)
        coefk = np.zeros((1, 4), np.float32)
        maskk = np.zeros((1, 4), np.float32)
        for qq in range(q):
            coefk[0, qq] = float(TOK * (q - 1 - qq))
            maskk[0, qq] = 1.0
        maps2.append({
            "x": x[b, t0:t0 + TOK],
            "xo": np.asarray(r1[c]["o_xo"]).view(np.uint16).view(ml_dtypes.bfloat16)
                  if np.asarray(r1[c]["o_xo"]).dtype != ml_dtypes.bfloat16
                  else np.asarray(r1[c]["o_xo"]),
            "rwb": np.asarray(r1[c]["o_rwb"]),
            "g": np.asarray(r1[c]["o_g"]),
            "sall": sall, "coefk": coefk, "maskk": maskk,
            "wo": f32c("L0_attn_W_o"), "td": td,
            "lnw": np.asarray(inputs["L0_attn_ln_w"], np.float32).reshape(1, C),
            "lnb": np.asarray(inputs["L0_attn_ln_b"], np.float32).reshape(1, C),
            "gw": np.concatenate([f32c("L0_ffn_gate_w"),
                                  np.zeros((8, C), np.float32)], axis=0),
            "lb": np.asarray(inputs["L0_ffn_lb_bias"], np.float32).reshape(1, 8),
        })
    res2 = run_bass_kernel_spmd(nc2, maps2, core_ids=list(range(NCORES)))
    r2 = res2.results

    nc3 = build_l3()
    stats_all = np.concatenate([np.asarray(r2[c]["o_stats"], np.float32)
                                for c in range(NCORES)], axis=0)
    maps3 = []
    for c in range(NCORES):
        maps3.append({
            "xm": np.asarray(r2[c]["o_xmid"], np.float32),
            "m": np.concatenate([np.asarray(r2[c]["o_m"], np.float32),
                                 np.zeros((TOK, 112), np.float32)], axis=1),
            "stats": stats_all,
            "eg": f32c("L0_ffn_expert_gate_w"),
            "eu": f32c("L0_ffn_expert_up_w"),
            "ed": f32c("L0_ffn_expert_down_w"),
            "sg": f32c("L0_ffn_shared_gate_w"),
            "su": f32c("L0_ffn_shared_up_w"),
            "sd": f32c("L0_ffn_shared_down_w"),
        })
    res3 = run_bass_kernel_spmd(nc3, maps3, core_ids=list(range(NCORES)))
    r3 = res3.results

    out = np.zeros((B, TT, C), np.float32)
    for c in range(NCORES):
        b, q = c // 4, c % 4
        out[b, q * TOK:(q + 1) * TOK] = np.asarray(r3[c]["o_out"], np.float32)
    return out
